# version 06
# baseline (speedup 1.0000x reference)
import numpy as np
import ml_dtypes
import concourse.bass as bass
import concourse.mybir as mybir
from concourse.bass_utils import run_bass_kernel_spmd

F32 = mybir.dt.float32
BF16 = mybir.dt.bfloat16
AF = mybir.ActivationFunctionType
OP = mybir.AluOpType
AX = mybir.AxisListType

D = 1024
T = 2048
NCH = 8
NTC = 4
LN_EPS = 1e-5

ENGS = ("tensor", "vector", "scalar", "gpsimd", "sync")


class P:

    def __init__(self, nc):
        self.nc = nc
        self.q = {e: [] for e in ENGS}
        self.sems = {}
        self.cnt = {}
        for name in ("v", "s", "g", "pe", "dx", "dg", "do"):
            self.sems[name] = nc.alloc_semaphore("sem_" + name)
            self.cnt[name] = 0
        self.bank_free = [[] for _ in range(8)]

    def wait(self, eng, sem, val):
        if val <= 0:
            return
        s = self.sems[sem]
        self.q[eng].append(lambda E: E.wait_ge(s, val))

    def op(self, eng, meth, *args, inc=None, incby=1, waits=(), **kw):
        for sem, val in waits:
            self.wait(eng, sem, val)
        n = None
        if inc is not None:
            self.cnt[inc] += incby
            n = self.cnt[inc]
            s = self.sems[inc]

            def fn(E, meth=meth, args=args, kw=kw, s=s, incby=incby):
                getattr(E, meth)(*args, **kw).then_inc(s, incby)
        else:
            def fn(E, meth=meth, args=args, kw=kw):
                getattr(E, meth)(*args, **kw)
        self.q[eng].append(fn)
        return n

    def vop(self, meth, *args, waits=(), **kw):
        w = [("v", self.cnt["v"])] + list(waits)
        return self.op("vector", meth, *args, inc="v", waits=w, **kw)

    def sop(self, meth, *args, waits=(), **kw):
        w = [("s", self.cnt["s"])] + list(waits)
        return self.op("scalar", meth, *args, inc="s", waits=w, **kw)

    def gop(self, meth, *args, waits=(), **kw):
        w = [("g", self.cnt["g"])] + list(waits)
        return self.op("gpsimd", meth, *args, inc="g", waits=w, **kw)

    def mm(self, out, lhsT, rhs, start, stop, waits=(), inc=None, **kw):
        return self.op("tensor", "matmul", out, lhsT, rhs, start=start, stop=stop,
                       waits=waits, inc=inc, **kw)

    def tr(self, out, in_, ident, waits=(), inc=None):
        return self.op("tensor", "transpose", out, in_, ident, waits=waits, inc=inc)

    def dma(self, queue, out, in_, inc, waits=(), **kw):
        return self.op(queue, "dma_start", out=out, in_=in_, inc=inc, incby=16,
                       waits=waits, **kw)

    def realize(self):
        with self.nc.Block() as block:
            for eng in ENGS:
                fns = self.q[eng]
                if fns:
                    getattr(block, eng)(lambda E, fns=fns: [f(E) for f in fns])
        self.q = {e: [] for e in ENGS}


def phase_barrier(p):
    snap = [(nm, p.cnt[nm]) for nm in ("v", "s", "g", "pe", "dx", "dg", "do")]
    for eng in ENGS:
        for nm, val in snap:
            p.wait(eng, nm, val)


class Buffers:
    dbg = {}
    adbg = {}
    dbg_sg = -1


def make_common(nc, p):
    B = Buffers()
    SB = nc.alloc_sbuf_tensor
    B.XR = SB("XR", [128, NCH * T], F32)
    B.XN = SB("XN", [128, NCH * T], BF16)
    B.ones128 = SB("ones128", [128, 128], BF16)
    B.ones128f = SB("ones128f", [128, 128], F32)
    B.onesrow = SB("onesrow", [1, 512], BF16)
    B.ident = SB("ident", [128, 128], BF16)
    B.identf = SB("identf", [128, 128], F32)
    B.iotacol = SB("iotacol", [128, 1], F32)
    B.epscol = SB("epscol", [128, 1], F32)
    B.iotarow32 = SB("iotarow32", [128, 32], F32)
    B.xbfs = SB("xbfs", [128, 2 * 512], BF16)
    B.xsqs = SB("xsqs", [128, 2 * 512], BF16)
    B.meanbc = SB("meanbc", [128, 512], F32)
    B.msqbc = SB("msqbc", [128, 512], F32)
    B.rstdbc = SB("rstdbc", [128, 512], F32)
    B.ytmp = SB("ytmp", [128, 512], F32)
    B.pb8h = nc.alloc_psum_tensor("pb8", [128, 4096], F32)
    B.pb8 = B.pb8h[:]
    B.pb = [B.pb8h[:, i * 512:(i + 1) * 512] for i in range(8)]
    p.gop("memset", B.ones128[:], 1.0)
    p.gop("memset", B.ones128f[:], 1.0)
    p.gop("memset", B.onesrow[:], 1.0)
    p.gop("memset", B.ident[:], 0.0)
    p.gop("affine_select", B.ident[:], B.ones128[:], [[1, 128]], OP.is_equal, 0.0,
          base=0, channel_multiplier=-1)
    p.gop("affine_select", B.identf[:], B.ones128f[:], [[1, 128]], OP.is_equal, 0.0,
          base=0, channel_multiplier=-1)
    p.gop("iota", B.iotacol[:], [[1, 1]], base=0, channel_multiplier=1,
          allow_small_or_imprecise_dtypes=True)
    p.gop("memset", B.epscol[:], LN_EPS)
    p.gop("iota", B.iotarow32[:], [[1, 32]], base=0, channel_multiplier=0,
          allow_small_or_imprecise_dtypes=True)
    return B


def load_x(nc, p, B, xin):
    for c in range(NCH):
        p.dma("sync", B.XR[:, c * T:(c + 1) * T], xin[c * 128:(c + 1) * 128, :], inc="dx")
    return p.cnt["dx"]


def store_x(nc, p, B, xout, vwait):
    for c in range(NCH):
        p.dma("sync", xout[c * 128:(c + 1) * 128, :], B.XR[:, c * T:(c + 1) * T],
              inc="do", waits=[("v", vwait)] if c == 0 else ())
    p.wait("sync", "do", p.cnt["do"])


def emit_ln(nc, p, B, x_ready_dx=0, v_ready=0, svar=True):
    done = []
    for tc in range(NTC):
        ia, ib = (2 * tc) % 4, (2 * tc) % 4 + 1
        bA, bB = B.pb[ia], B.pb[ib]
        ms = {}
        for c in range(NCH):
            db = c % 2
            sl = slice(db * 512, (db + 1) * 512)
            xsl = B.XR[:, c * T + tc * 512: c * T + (tc + 1) * 512]
            w = list([("dx", x_ready_dx), ("v", v_ready)] if (tc == 0 and c == 0) else [])
            if c >= 2:
                w.append(("pe", ms[c - 2]))
            vcp = p.vop("tensor_copy", B.xbfs[:, sl], xsl, waits=w)
            scp = p.sop("activation", B.xsqs[:, sl], xsl, AF.Square, waits=w)
            w2_ = [("v", vcp), ("s", scp)]
            if c == 0:
                w2_ += p.bank_free[ia] + p.bank_free[ib]
            p.mm(bA[:, 0:512], B.ones128[:], B.xbfs[:, sl],
                 start=(c == 0), stop=(c == NCH - 1), waits=w2_)
            ms[c] = p.mm(bB[:, 0:512], B.ones128[:], B.xsqs[:, sl],
                         start=(c == 0), stop=(c == NCH - 1), inc="pe")
        pe_done = p.cnt["pe"]
        p.vop("tensor_scalar", B.meanbc[:], bA[:, 0:512], 1.0 / D, None, OP.mult,
              waits=[("pe", pe_done)])
        p.vop("tensor_scalar", B.msqbc[:], bB[:, 0:512], 1.0 / D, None, OP.mult)
        p.vop("tensor_tensor", B.rstdbc[:], B.meanbc[:], B.meanbc[:], OP.mult)
        vfree = p.vop("tensor_tensor", B.msqbc[:], B.msqbc[:], B.rstdbc[:], OP.subtract)
        p.bank_free[ia] = p.bank_free[ib] = [("v", vfree)]
        p.sop("activation", B.rstdbc[:], B.msqbc[:], AF.Ln, bias=B.epscol[:],
              waits=[("v", vfree)])
        if svar:
            p.sop("activation", B.svarbc[:, tc * 512:(tc + 1) * 512], B.rstdbc[:],
                  AF.Exp, scale=0.5)
        ssq = p.sop("activation", B.rstdbc[:], B.rstdbc[:], AF.Exp, scale=-0.5)
        p.wait("vector", "s", ssq)
        last = None
        for c in range(NCH):
            xsl = B.XR[:, c * T + tc * 512: c * T + (tc + 1) * 512]
            nsl = B.XN[:, c * T + tc * 512: c * T + (tc + 1) * 512]
            p.vop("tensor_tensor", B.ytmp[:], xsl, B.meanbc[:], OP.subtract)
            last = p.vop("tensor_tensor", nsl, B.ytmp[:], B.rstdbc[:], OP.mult)
        done.append(last)
    return done


def moe_buffers(nc, B):
    SB = nc.alloc_sbuf_tensor
    if not hasattr(B, "svarbc"):
        B.svarbc = SB("svarbc", [128, T], BF16)
    B.rwt = SB("rwt", [128, NCH * 32], F32)
    B.rwsc = SB("rwsc", [128, 32], F32)
    B.rbrow = SB("rbrow", [1, 32], BF16)
    B.b1c = SB("sb_b1c", [128, 64], F32)
    B.b2sb = SB("b2sb", [32, 1024], BF16)
    B.idxcols = SB("idxcols", [128, 16], BF16)
    B.idxbc = SB("idxbc", [128, 2048], BF16)
    B.MT = SB("MT", [32, 2048], BF16)
    B.eqs = SB("eqs", [128, 512], BF16)
    B.lmax = SB("lmax", [128, 1], F32)
    B.lsel = SB("lsel", [128, 32], F32)
    B.HT = SB("HT", [128, 8 * 2 * 1024], BF16)
    B.Z2 = SB("Z2", [128, 8 * 1024], F32)
    B.w1sb = SB("w1sb", [128, 2 * 2048], BF16)
    B.w2sb = SB("w2sb", [128, 2 * 2048], BF16)


def emit_moe_weight_preload(nc, p, B, W):
    for c in range(NCH):
        p.dma("gpsimd", B.rwt[:, c * 32:(c + 1) * 32],
              W["rw_t"][c * 128:(c + 1) * 128, :], inc="dg")
    p.dma("gpsimd", B.rbrow[:], W["rb"], inc="dg")
    p.dma("gpsimd", B.b2sb[:], W["b2"], inc="dg")
    p.dma("sync", B.b1c[:], W["b1c"], inc="dx")
    B.b1c_dx = p.cnt["dx"]
    dg = p.cnt["dg"]
    bk = 4
    for c in range(NCH):
        p.mm(B.pb[bk][:, 0:32], B.ones128f[:], B.rwt[:, c * 32:(c + 1) * 32],
             start=(c == 0), stop=(c == NCH - 1),
             waits=[("dg", dg), ("g", 2)] if c == 0 else (),
             inc="pe" if c == NCH - 1 else None)
    p.vop("tensor_scalar", B.rwsc[:], B.pb[bk][:, 0:32], 1.0 / D, None, OP.mult,
          waits=[("pe", p.cnt["pe"])])
    vlast = 0
    for c in range(NCH):
        vlast = p.vop("tensor_tensor", B.rwt[:, c * 32:(c + 1) * 32],
                      B.rwt[:, c * 32:(c + 1) * 32], B.rwsc[:], OP.subtract)
    p.bank_free[bk] = [("v", vlast)]
    B.rw_ready = vlast
    return p.cnt["dg"]


def emit_moe(nc, p, B, W, ln_done, small_dg):
    B.relu_war = {}
    for tt in range(16):
        bk = 5 + tt % 2
        lb = B.pb[bk]
        w = list(p.bank_free[bk]) + [("v", ln_done[tt // 4])]
        if tt == 0:
            w += [("v", B.rw_ready), ("dg", small_dg)]
        for c in range(NCH):
            p.mm(lb[:, 0:32],
                 B.XR[:, c * T + tt * 128: c * T + (tt + 1) * 128],
                 B.rwt[:, c * 32:(c + 1) * 32],
                 start=(c == 0), stop=False, waits=w if c == 0 else ())
        p.mm(lb[:, 0:32], B.svarbc[0:1, tt * 128:(tt + 1) * 128],
             B.rbrow[0:1, 0:32], start=False, stop=True, inc="pe")
        pe_l = p.cnt["pe"]
        p.vop("tensor_reduce", B.lmax[:], lb[:, 0:32], axis=AX.X,
              op=OP.max, waits=[("pe", pe_l)])
        p.vop("tensor_scalar", B.lsel[:], lb[:, 0:32], B.lmax[:], None, OP.is_equal)
        p.vop("scalar_tensor_tensor", B.lsel[:], B.lsel[:], -10000.0, B.iotarow32[:],
              OP.mult, OP.add)
        p.vop("tensor_reduce", B.lmax[:], B.lsel[:], axis=AX.X, op=OP.min)
        vr = p.vop("tensor_scalar", B.idxcols[:, tt:tt + 1], B.lmax[:], 10000.0,
                   None, OP.add)
        p.bank_free[bk] = [("v", vr)]
    v_idx = p.cnt["v"]
    p7h = B.pb[7][:, 0:512].bitcast(BF16)
    p6h = B.pb[6][:, 0:512].bitcast(BF16)
    p.wait("tensor", "v", v_idx)
    for wv in p.bank_free[7] + p.bank_free[6]:
        p.wait("tensor", wv[0], wv[1])
    for tt in range(16):
        dst = p7h if tt < 8 else p6h
        p.tr(dst[0:1, (tt % 8) * 128:(tt % 8 + 1) * 128],
             B.idxcols[:, tt:tt + 1], B.ident[:],
             inc="pe" if tt == 15 else None)
    for half in range(2):
        ph = p7h if half == 0 else p6h
        vir = p.vop("tensor_copy", B.xbfs[0:1, 0:1024], ph[0:1, 0:1024],
                    waits=[("pe", p.cnt["pe"])] if half == 0 else ())
        for q in range(2):
            tc = half * 2 + q
            p.mm(B.pb[7][:, 0:512], B.ones128[0:1, :],
                 B.xbfs[0:1, q * 512:(q + 1) * 512],
                 start=True, stop=True,
                 waits=[("v", vir)] + p.bank_free[7], inc="pe")
            vb = p.vop("tensor_copy", B.idxbc[:, tc * 512:(tc + 1) * 512],
                       B.pb[7][:, 0:512], waits=[("pe", p.cnt["pe"])])
            p.bank_free[7] = [("v", vb)]
    p.vop("tensor_scalar", B.MT[:], B.idxbc[0:32, :], B.iotacol[0:32, :], None,
          OP.is_equal)
    v_mt = p.cnt["v"]
    if B.dbg:
        p.dma("sync", B.dbg["XN"], B.XN[:], inc="do", waits=[("v", ln_done[-1])])
        p.dma("sync", B.dbg["idxbc"], B.idxbc[:], inc="do", waits=[("v", v_mt)])

    w1_rd = {}
    w2_rd = {}
    for th in range(2):
        t0 = th * 1024
        for eg in range(4):
            for ei in range(8):
                e = eg * 8 + ei
                db = ei % 2
                dgw = p.dma("gpsimd", B.w1sb[:, db * 2048:(db + 1) * 2048],
                            W["w1"][e].rearrange("(c p) j -> p c j", p=128),
                            inc="dg", waits=[("pe", w1_rd.get(db, 0))])
                for tq in range(2):
                    veq = p.vop("tensor_scalar", B.eqs[:, 0:512],
                                B.idxbc[:, t0 + tq * 512: t0 + (tq + 1) * 512],
                                float(e), None, OP.is_equal,
                                waits=[("v", v_mt)] if (th == 0 and eg == 0 and ei == 0 and tq == 0) else ())
                    for jb in range(2):
                        bk = jb * 2 + tq
                        bank = B.pb[bk]
                        w = [("dg", dgw)] + p.bank_free[bk]
                        mlast = None
                        for ic in range(NCH):
                            mlast = p.mm(bank[:, 0:512],
                                 B.w1sb[:, db * 2048 + ic * 256 + jb * 128:
                                        db * 2048 + ic * 256 + (jb + 1) * 128],
                                 B.XN[:, ic * T + t0 + tq * 512:
                                      ic * T + t0 + (tq + 1) * 512],
                                 start=(ic == 0), stop=(ic == NCH - 1),
                                 waits=w if ic == 0 else (),
                                 inc="pe" if ic == NCH - 1 else None)
                        w1_rd[db] = mlast
                        hsl = B.HT[:, (ei * 2 + jb) * 1024 + tq * 512:
                                   (ei * 2 + jb) * 1024 + (tq + 1) * 512]
                        tsl = (B.xbfs if bk < 2 else B.xsqs)[:, (bk % 2) * 512:
                                                             (bk % 2) * 512 + 512]
                        srl = p.sop("activation", tsl, bank[:, 0:512], AF.Relu,
                                    bias=B.b1c[:, e * 2 + jb: e * 2 + jb + 1],
                                    waits=[("pe", mlast), ("dx", B.b1c_dx),
                                           ("v", B.relu_war.get(bk, 0))])
                        vh = p.vop("tensor_tensor", hsl, tsl,
                                   B.eqs[:, 0:512], OP.mult,
                                   waits=[("s", srl), ("v", veq)])
                        B.relu_war[bk] = vh
                        p.bank_free[bk] = [("v", vh)]
            v_ht = p.cnt["v"]
            if B.dbg and th == 0 and eg == 0:
                p.dma("sync", B.dbg["HT"], B.HT[:], inc="do", waits=[("v", v_ht)])
            for oh in range(2):
                for ei in range(8):
                    e = eg * 8 + ei
                    pr = (ei // 2) % 2
                    db = pr * 2 + ei % 2
                    if ei % 2 == 0:
                        dgw = p.dma("gpsimd", B.w2sb[:, pr * 2048:(pr + 1) * 2048],
                                    W["w2"][e:e + 2, :, oh * 512:(oh + 1) * 512]
                                    .rearrange("e (c p) o -> p e c o", p=128),
                                    inc="dg", waits=[("pe", w2_rd.get(pr, 0))])
                    for obi in range(4):
                        for tq in range(2):
                            bk = obi * 2 + tq
                            bank = B.pb[bk]
                            ob = oh * 4 + obi
                            if ei == 0:
                                w = [("dg", dgw), ("v", v_ht)] + p.bank_free[bk]
                                if eg == 0:
                                    p.mm(bank[:, 0:512],
                                         B.b2sb[:, ob * 128:(ob + 1) * 128],
                                         B.MT[:, t0 + tq * 512: t0 + (tq + 1) * 512],
                                         start=True, stop=False, waits=w)
                                    w = []
                            else:
                                w = [("dg", dgw)]
                            for jb in range(2):
                                start = (ei == 0 and jb == 0 and eg != 0)
                                islast = (ei == 7 and jb == 1)
                                expert_last = (obi == 3 and tq == 1 and jb == 1
                                               and ei % 2 == 1)
                                mn = p.mm(bank[:, 0:512],
                                     B.w2sb[:, db * 1024 + jb * 512 + obi * 128:
                                            db * 1024 + jb * 512 + (obi + 1) * 128],
                                     B.HT[:, (ei * 2 + jb) * 1024 + tq * 512:
                                          (ei * 2 + jb) * 1024 + (tq + 1) * 512],
                                     start=start, stop=islast,
                                     waits=w if jb == 0 else (),
                                     inc="pe" if (islast or expert_last) else None)
                                if expert_last:
                                    w2_rd[pr] = mn
                                w = []
                pe_grp = p.cnt["pe"]
                for obi in range(4):
                    for tq in range(2):
                        bk = obi * 2 + tq
                        bank = B.pb[bk]
                        ob = oh * 4 + obi
                        zsl = B.Z2[:, ob * 1024 + tq * 512: ob * 1024 + (tq + 1) * 512]
                        if eg == 0:
                            vz = p.vop("tensor_copy", zsl, bank[:, 0:512],
                                       waits=[("pe", pe_grp)])
                        else:
                            vz = p.vop("tensor_tensor", zsl, zsl, bank[:, 0:512],
                                       OP.add, waits=[("pe", pe_grp)])
                        p.bank_free[bk] = [("v", vz)]
        if B.dbg and th == 0:
            p.dma("sync", B.dbg["Z2"], B.Z2[:], inc="do", waits=[("v", p.cnt["v"])])
        for ob in range(8):
            for tq in range(2):
                zsl = B.Z2[:, ob * 1024 + tq * 512: ob * 1024 + (tq + 1) * 512]
                xsl = B.XR[:, ob * T + t0 + tq * 512: ob * T + t0 + (tq + 1) * 512]
                p.vop("tensor_scalar", B.ytmp[:], zsl, 0.0, 0.5, OP.max, OP.mult)
                p.vop("tensor_tensor", xsl, xsl, B.ytmp[:], OP.add)
    return p.cnt["v"]


def moe_dram(nc):
    return {
        "rw_t": nc.dram_tensor("rw_t", [1024, 32], F32, kind="ExternalInput").ap(),
        "rb": nc.dram_tensor("rb", [1, 32], F32, kind="ExternalInput").ap(),
        "w1": nc.dram_tensor("w1", [32, 1024, 256], F32, kind="ExternalInput").ap(),
        "b1c": nc.dram_tensor("b1c", [128, 64], F32, kind="ExternalInput").ap(),
        "w2": nc.dram_tensor("w2", [32, 256, 1024], F32, kind="ExternalInput").ap(),
        "b2": nc.dram_tensor("b2", [32, 1024], F32, kind="ExternalInput").ap(),
    }


def build_moe_program(debug=False):
    nc = bass.Bass("TRN2", target_bir_lowering=False)
    xin = nc.dram_tensor("xin", [D, T], F32, kind="ExternalInput").ap()
    W = moe_dram(nc)
    xout = nc.dram_tensor("xout", [D, T], F32, kind="ExternalOutput").ap()
    p = P(nc)
    B = make_common(nc, p)
    moe_buffers(nc, B)
    dbg = {}
    if debug:
        dbg["XN"] = nc.dram_tensor("dbg_XN", [128, NCH * T], BF16, kind="ExternalOutput").ap()
        dbg["idxbc"] = nc.dram_tensor("dbg_idxbc", [128, T], BF16, kind="ExternalOutput").ap()
        dbg["HT"] = nc.dram_tensor("dbg_HT", [128, 8 * 2 * 1024], BF16, kind="ExternalOutput").ap()
        dbg["Z2"] = nc.dram_tensor("dbg_Z2", [128, 8 * 1024], F32, kind="ExternalOutput").ap()
    B.dbg = dbg
    dx = load_x(nc, p, B, xin)
    small_dg = emit_moe_weight_preload(nc, p, B, W)
    ln_done = emit_ln(nc, p, B, dx)
    vend = emit_moe(nc, p, B, W, ln_done, small_dg)
    store_x(nc, p, B, xout, vend)
    p.realize()
    return nc


def moe_host_weights(inputs, prefix="L0_ffn"):
    return {
        "rw_t": np.ascontiguousarray(np.asarray(inputs[prefix + "_router_w"]).T),
        "rb": np.ascontiguousarray(np.asarray(inputs[prefix + "_router_b"])[None, :]),
        "w1": np.ascontiguousarray(np.asarray(inputs[prefix + "_weights1"])),
        "b1c": np.ascontiguousarray(np.asarray(inputs[prefix + "_bias1"]).reshape(32, 2, 128).transpose(2, 0, 1).reshape(128, 64)),
        "w2": np.ascontiguousarray(np.asarray(inputs[prefix + "_weights2"])),
        "b2": np.ascontiguousarray(np.asarray(inputs[prefix + "_bias2"])),
    }


def l1ffn_buffers(nc, B):
    SB = nc.alloc_sbuf_tensor
    B.GH = SB("GH", [128, 32768], BF16)
    B.Hv = B.XR[:].bitcast(BF16)
    B.wA = SB("wA", [128, 2 * 4096], BF16)
    B.wB = SB("wB", [128, 2 * 1024], BF16)
    B.wC = SB("wC", [128, 2 * 1024], BF16)
    B.s1c = SB("s1c", [128, 32], F32)
    B.s2c = SB("s2c", [128, 32], F32)
    B.sh1c = SB("sh1c", [128, 32], F32)
    B.sh2c = SB("sh2c", [128, 32], F32)
    B.s3c = SB("s3c", [128, 8], F32)
    B.s3h = SB("s3h", [128, 8], F32)



def emit_l1ffn_preload(nc, p, B, W):
    for nm in ("s1c", "s2c", "sh1c", "sh2c", "s3c"):
        p.dma("sync", getattr(B, nm)[:], W[nm], inc="dx")
    dxs = p.cnt["dx"]
    p.vop("tensor_scalar", B.s3h[:], B.s3c[:], 0.5, None, OP.mult,
          waits=[("dx", dxs)])
    return p.cnt["v"]


def emit_l1ffn(nc, p, B, W, xin, xout, ln_done, scales_v, hv_waits=()):
    wArd = {}
    wBrd = {}
    wCrd = {}
    for tch in range(2):
        t0 = tch * 1024
        for ob in range(32):
            db = ob % 2
            dg1 = p.dma("gpsimd", B.wB[:, db * 1024:(db + 1) * 1024],
                        W["w1"][ob], inc="dg", waits=[("pe", wBrd.get(db, 0))])
            dg2 = p.dma("gpsimd", B.wC[:, db * 1024:(db + 1) * 1024],
                        W["w2"][ob], inc="dg", waits=[("pe", wCrd.get(db, 0))])
            bk1 = [4 * db, 4 * db + 1]
            bk2 = [4 * db + 2, 4 * db + 3]
            for c in range(8):
                for n in range(2):
                    w = []
                    if c == 0:
                        w = [("dg", dg1), ("v", ln_done[tch * 2 + n])] + p.bank_free[bk1[n]]
                        if tch == 0 and ob == 0 and n == 0:
                            w.append(("v", scales_v))
                    p.mm(B.pb[bk1[n]][:, 0:512],
                         B.wB[:, db * 1024 + c * 128:(db * 1024) + (c + 1) * 128],
                         B.XN[:, c * T + t0 + n * 512: c * T + t0 + (n + 1) * 512],
                         start=(c == 0), stop=(c == 7), waits=w,
                         inc="pe" if c == 7 else None)
            wBrd[db] = p.cnt["pe"]
            pe1 = [p.cnt["pe"] - 1, p.cnt["pe"]]
            for c in range(8):
                for n in range(2):
                    w = []
                    if c == 0:
                        w = [("dg", dg2)] + p.bank_free[bk2[n]]
                    p.mm(B.pb[bk2[n]][:, 0:512],
                         B.wC[:, db * 1024 + c * 128:(db * 1024) + (c + 1) * 128],
                         B.XN[:, c * T + t0 + n * 512: c * T + t0 + (n + 1) * 512],
                         start=(c == 0), stop=(c == 7), waits=w,
                         inc="pe" if c == 7 else None)
            wCrd[db] = p.cnt["pe"]
            pe2 = [p.cnt["pe"] - 1, p.cnt["pe"]]
            silut = [B.meanbc, B.msqbc]
            ssl = []
            for n in range(2):
                sw = [("pe", pe1[n]), ("v", p.cnt["v"])]
                if tch == 0 and ob == 0:
                    sw.append(("v", scales_v))
                ssl.append(p.sop("activation", silut[n][:, 0:512],
                                 B.pb[bk1[n]][:, 0:512], AF.Silu,
                                 scale=B.s1c[:, ob:ob + 1], waits=sw))
            vh = None
            for n in range(2):
                hw = [("pe", pe2[n]), ("s", ssl[n])]
                if tch == 0 and ob == 0 and n == 0:
                    hw += list(hv_waits)
                vh = p.vop("scalar_tensor_tensor",
                           B.Hv[:, ob * 1024 + n * 512: ob * 1024 + (n + 1) * 512],
                           B.pb[bk2[n]][:, 0:512], B.s2c[:, ob:ob + 1],
                           silut[n][:, 0:512], OP.mult, OP.mult,
                           waits=hw)
                p.bank_free[bk2[n]] = [("v", vh)]
            p.bank_free[bk1[0]] = [("v", vh)]
            p.bank_free[bk1[1]] = [("v", vh)]
        v_h = p.cnt["v"]
        for chain, (wkey, scol, src, dst) in enumerate((
                ("h1", B.sh1c, B.Hv, B.GH), ("h2", B.sh2c, B.GH, B.Hv))):
            s_rel = None
            for ob in range(32):
                db = ob % 2
                dga = p.dma("gpsimd", B.wA[:, db * 4096:(db + 1) * 4096],
                            W[wkey][ob], inc="dg", waits=[("pe", wArd.get(db, 0))])
                bks = [4 * db, 4 * db + 1]
                for c in range(32):
                    for n in range(2):
                        w = []
                        if c == 0:
                            w = [("dg", dga)] + p.bank_free[bks[n]]
                            if ob == 0 and n == 0:
                                if chain == 0:
                                    w.append(("v", v_h))
                                else:
                                    w.append(("s", s_gel))
                        p.mm(B.pb[bks[n]][:, 0:512],
                             B.wA[:, db * 4096 + c * 128:(db * 4096) + (c + 1) * 128],
                             src[:, c * 1024 + n * 512: c * 1024 + (n + 1) * 512],
                             start=(c == 0), stop=(c == 31), waits=w,
                             inc="pe" if c == 31 else None)
                wArd[db] = p.cnt["pe"]
                pes = [p.cnt["pe"] - 1, p.cnt["pe"]]
                for n in range(2):
                    s_rel = p.sop("activation",
                                  dst[:, ob * 1024 + n * 512: ob * 1024 + (n + 1) * 512],
                                  B.pb[bks[n]][:, 0:512], AF.Gelu,
                                  scale=scol[:, ob:ob + 1], waits=[("pe", pes[n])])
                    p.bank_free[bks[n]] = [("s", s_rel)]
            s_gel = s_rel
        for ob in range(8):
            db = ob % 2
            dga = p.dma("gpsimd", B.wA[:, db * 4096:(db + 1) * 4096],
                        W["w3"][ob], inc="dg", waits=[("pe", wArd.get(db, 0))])
            bks = [4 * db, 4 * db + 1]
            for c in range(32):
                for n in range(2):
                    w = []
                    if c == 0:
                        w = [("dg", dga)] + p.bank_free[bks[n]]
                        if ob == 0 and n == 0:
                            w.append(("s", s_gel))
                    p.mm(B.pb[bks[n]][:, 0:512],
                         B.wA[:, db * 4096 + c * 128:(db * 4096) + (c + 1) * 128],
                         B.Hv[:, c * 1024 + n * 512: c * 1024 + (n + 1) * 512],
                         start=(c == 0), stop=(c == 31), waits=w,
                         inc="pe" if c == 31 else None)
            wArd[db] = p.cnt["pe"]
            pes = [p.cnt["pe"] - 1, p.cnt["pe"]]
            for n in range(2):
                dxr = p.dma("sync", B.ytmp[:, 0:512],
                            xin[ob * 128:(ob + 1) * 128, t0 + n * 512:t0 + (n + 1) * 512],
                            inc="dx", waits=[("do", p.cnt["do"]), ("v", p.cnt["v"])])
                vr = p.vop("scalar_tensor_tensor", B.ytmp[:, 0:512],
                           B.pb[bks[n]][:, 0:512], B.s3h[:, ob:ob + 1],
                           B.ytmp[:, 0:512], OP.mult, OP.add,
                           waits=[("pe", pes[n]), ("dx", dxr)])
                p.bank_free[bks[n]] = [("v", vr)]
                p.dma("sync", xout[ob * 128:(ob + 1) * 128, t0 + n * 512:t0 + (n + 1) * 512],
                      B.ytmp[:, 0:512], inc="do", waits=[("v", vr)])
    p.wait("sync", "do", p.cnt["do"])
    return p.cnt["v"]


def l1ffn_dram(nc):
    return {
        "w1": nc.dram_tensor("f_w1", [32, 128, 8, 128], F32, kind="ExternalInput").ap(),
        "w2": nc.dram_tensor("f_w2", [32, 128, 8, 128], F32, kind="ExternalInput").ap(),
        "h1": nc.dram_tensor("f_h1", [32, 128, 32, 128], F32, kind="ExternalInput").ap(),
        "h2": nc.dram_tensor("f_h2", [32, 128, 32, 128], F32, kind="ExternalInput").ap(),
        "w3": nc.dram_tensor("f_w3", [8, 128, 32, 128], F32, kind="ExternalInput").ap(),
        "s1c": nc.dram_tensor("f_s1c", [128, 32], F32, kind="ExternalInput").ap(),
        "s2c": nc.dram_tensor("f_s2c", [128, 32], F32, kind="ExternalInput").ap(),
        "sh1c": nc.dram_tensor("f_sh1c", [128, 32], F32, kind="ExternalInput").ap(),
        "sh2c": nc.dram_tensor("f_sh2c", [128, 32], F32, kind="ExternalInput").ap(),
        "s3c": nc.dram_tensor("f_s3c", [128, 8], F32, kind="ExternalInput").ap(),
    }


def build_l1ffn_program():
    nc = bass.Bass("TRN2", target_bir_lowering=False)
    xin = nc.dram_tensor("xin", [D, T], F32, kind="ExternalInput").ap()
    W = l1ffn_dram(nc)
    xout = nc.dram_tensor("xout", [D, T], F32, kind="ExternalOutput").ap()
    p = P(nc)
    B = make_common(nc, p)
    l1ffn_buffers(nc, B)
    dx = load_x(nc, p, B, xin)
    scales_v = emit_l1ffn_preload(nc, p, B, W)
    ln_done = emit_ln(nc, p, B, dx, svar=False)
    emit_l1ffn(nc, p, B, W, xin, xout, ln_done, scales_v)
    p.realize()
    return nc


def _tile_oc(w, P=128, Q=128):
    O, I = w.shape
    t = w.reshape(O // Q, Q, I // P, P)
    return np.ascontiguousarray(t.transpose(0, 3, 2, 1))


def l1ffn_host_weights(inputs, prefix="L1_ffn"):
    g = lambda n: np.asarray(inputs[prefix + "_" + n])
    colmaj = lambda s, nb: np.ascontiguousarray(s.reshape(nb, 128).T)
    return {
        "f_w1": _tile_oc(g("w1_w")), "f_w2": _tile_oc(g("w2_w")),
        "f_h1": _tile_oc(g("h1_w")), "f_h2": _tile_oc(g("h2_w")),
        "f_w3": _tile_oc(g("w3_w")),
        "f_s1c": colmaj(g("w1_s"), 32), "f_s2c": colmaj(g("w2_s"), 32),
        "f_sh1c": colmaj(g("h1_s"), 32), "f_sh2c": colmaj(g("h2_s"), 32),
        "f_s3c": colmaj(g("w3_s"), 8),
    }


def l1attn_buffers(nc, B):
    SB = nc.alloc_sbuf_tensor
    B.zbs = B.ytmp[:, 0:128]
    B.QTs = SB("QTs", [64, 16 * 512], BF16)
    B.KTs = SB("KTs", [64, 16 * 512], BF16)
    B.VA = SB("VA", [128, 4 * 16 * 128], BF16)
    B.E2 = SB("E2", [128, 2 * 2048], BF16)
    B.CTs = SB("CTs", [128, 8 * 512], BF16)
    B.ATTs = SB("ATTs", [128, 8 * 512], F32)
    B.WS = SB("WS", [128, 2 * 1024], BF16)
    B.WV = SB("WV", [128, 2 * 1024], BF16)
    B.bqc = SB("bqc", [128, 8], F32)
    B.bkc = SB("bkc", [128, 8], F32)
    B.boc = SB("boc", [128, 8], F32)
    B.bvrow = SB("bvrow", [1, 1024], BF16)
    B.gamc = SB("gamc", [128, 8], F32)
    B.betc = SB("betc", [128, 8], F32)
    B.neg4c = SB("neg4c", [128, 1], F32)
    B.rz = SB("rz", [1, 512], F32)


def emit_l1attn_preload(nc, p, B, W):
    for nm in ("bqc", "bkc", "boc", "gamc", "betc"):
        p.dma("sync", getattr(B, nm)[:], W[nm], inc="dx")
    p.dma("gpsimd", B.bvrow[:], W["bv"], inc="dg")
    p.gop("memset", B.neg4c[:], -4.0)
    return p.cnt["dx"], p.cnt["dg"], p.cnt["g"]


def emit_l1attn(nc, p, B, W, ln_done, pre):
    dxs, dgs, gs = pre
    ws_rd = {}
    wv_rd = {}
    p.wait("tensor", "dx", dxs)
    for sg in range(4):
        t0 = sg * 512
        lnv = ln_done[sg]
        g0 = p.gop("memset", B.VA[:], 0.0,
                   waits=[("pe", p.cnt["pe"])] if sg > 0 else ())
        va_ones = bass.AP(B.VA, 0, [[4 * 16 * 128, 128], [128, 64], [1, 1]])
        g1 = p.gop("memset", va_ones, 1.0)
        for ic in range(8):
            db = ic % 2
            dgw = p.dma("gpsimd", B.WV[:, db * 1024:(db + 1) * 1024],
                        W["wv"][ic * 128:(ic + 1) * 128, :], inc="dg",
                        waits=[("pe", wv_rd.get(db, 0))])
            for tt in range(4):
                for n in range(2):
                    bk = tt * 2 + n
                    w = []
                    if ic == 0:
                        w = [("dg", dgw), ("v", lnv)] + p.bank_free[bk]
                    else:
                        w = [("dg", dgw)] if (tt == 0 and n == 0) else []
                    mn = p.mm(B.pb[bk][:, 0:512],
                              B.XN[:, ic * T + t0 + tt * 128: ic * T + t0 + (tt + 1) * 128],
                              B.WV[:, db * 1024 + n * 512: db * 1024 + (n + 1) * 512],
                              start=(ic == 0), stop=False, waits=w,
                              inc="pe" if (tt == 3 and n == 1) else None)
                    if tt == 3 and n == 1:
                        wv_rd[db] = mn
        for tt in range(4):
            for n in range(2):
                bk = tt * 2 + n
                p.mm(B.pb[bk][:, 0:512], B.onesrow[0:1, 0:128],
                     B.bvrow[0:1, n * 512:(n + 1) * 512],
                     start=False, stop=True, inc="pe")
        wv_rd[0] = wv_rd[1] = pe_v = p.cnt["pe"]
        vva = None
        for tt in range(4):
            for n in range(2):
                bk = tt * 2 + n
                dst = bass.AP(B.VA, tt * 2048 + n * 8 * 128 + 64,
                              [[4 * 16 * 128, 128], [128, 8], [1, 64]])
                srcv = bass.AP(B.pb8h, bk * 512, [[4096, 128], [64, 8], [1, 64]])
                vva = p.vop("tensor_copy", dst, srcv,
                            waits=[("pe", pe_v), ("g", g1)])
                p.bank_free[bk] = [("v", vva)]
        qk_done = None
        for wi, (wkey, bcol, dstbuf) in enumerate((("wq", B.bqc, B.QTs),
                                                   ("wk", B.bkc, B.KTs))):
            for cb in range(8):
                db = (wi * 8 + cb) % 2
                dgw = p.dma("gpsimd", B.WS[:, db * 1024:(db + 1) * 1024],
                            W[wkey][cb], inc="dg", waits=[("pe", ws_rd.get(db, 0))])
                bk = (wi * 8 + cb) % 4
                for ic in range(8):
                    w = []
                    if ic == 0:
                        w = [("dg", dgw)] + p.bank_free[bk]
                    mn = p.mm(B.pb[bk][:, 0:512],
                              B.WS[:, db * 1024 + ic * 128: db * 1024 + (ic + 1) * 128],
                              B.XN[:, ic * T + t0: ic * T + t0 + 512],
                              start=(ic == 0), stop=(ic == 7), waits=w,
                              inc="pe")
                    ws_rd[db] = mn
                p.sop("activation", dstbuf[0:64, (2 * cb) * 512:(2 * cb + 1) * 512],
                      B.pb8[0:64, bk * 512:(bk + 1) * 512], AF.Identity,
                      bias=bcol[0:64, cb:cb + 1],
                      waits=[("pe", p.cnt["pe"])])
                qk_done = p.sop("activation",
                                dstbuf[0:64, (2 * cb + 1) * 512:(2 * cb + 2) * 512],
                                B.pb8[64:128, bk * 512:(bk + 1) * 512], AF.Identity,
                                bias=bcol[64:128, cb:cb + 1])
                p.bank_free[bk] = [("s", qk_done)]
        pend_norm = [None]

        def flush_norm(p=p, B=B):
            if pend_norm[0] is None:
                return
            vz_, cbk_, gi_, q4_ = pend_norm[0]
            pend_norm[0] = None
            p.mm(B.pb[4][:, 0:512], B.ones128f[0:1, 0:128], B.rz[0:1, 0:512],
                 start=True, stop=True, waits=[("v", vz_)] + p.bank_free[4],
                 inc="pe")
            vzb = p.vop("tensor_copy", B.ytmp[:], B.pb[4][:, 0:512],
                        waits=[("pe", p.cnt["pe"])])
            p.bank_free[4] = [("v", vzb)]
            vct = None
            for par in range(2):
                dst = bass.AP(B.CTs, par * 64 * 4096 + (q4_ * 2) * 512 + gi_ * 128,
                              [[4096, 64], [512, 2], [1, 128]])
                csrc = bass.AP(B.pb8h, cbk_ * 512 + 64 * 4096 + par * 128,
                               [[4096, 64], [256, 2], [1, 128]])
                zsrc = bass.AP(B.ytmp, 64 * 512 + par * 128,
                               [[512, 64], [256, 2], [1, 128]])
                vct = p.vop("tensor_tensor", dst, csrc, zsrc, OP.mult)
            p.bank_free[cbk_] = [("v", vct)]

        def emit_S(gi, cq):
            qv = B.QTs[0:64, cq * 512 + gi * 128: cq * 512 + (gi + 1) * 128]
            for ck in range(16):
                kv = B.KTs[0:64, ck * 512 + gi * 128: ck * 512 + (gi + 1) * 128]
                w = []
                if ck % 4 == 0:
                    w = list(p.bank_free[ck // 4])
                    if gi == 0 and cq == 0:
                        w.append(("s", qk_done))
                p.mm(B.pb8[:, ck * 128:(ck + 1) * 128],
                     kv, qv, start=True, stop=True, waits=w,
                     inc="pe" if ck in (7, 15) else None)
            pe_s = p.cnt["pe"]
            eslot = cq % 2
            se = None
            for half in range(2):
                se = p.sop("activation",
                           B.E2[:, eslot * 2048 + half * 1024:
                                eslot * 2048 + (half + 1) * 1024],
                           B.pb8[:, half * 1024:(half + 1) * 1024],
                           AF.Exp, scale=0.125, bias=B.neg4c[:, 0:1],
                           waits=[("pe", pe_s - 1 + half)])
            for i in range(4):
                p.bank_free[i] = [("s", se)]
            return se

        def emit_ctx(gi, cq, se):
            eslot = cq % 2
            cbk = 6 + (cq // 4) % 2
            slot = cq % 4
            for ck in range(16):
                w = []
                if ck == 0:
                    w = [("s", se)] + (p.bank_free[cbk] if slot == 0 else [])
                    if gi == 0 and cq == 0:
                        w += [("v", vva), ("g", g1)]
                p.mm(B.pb8[:, cbk * 512 + slot * 128: cbk * 512 + (slot + 1) * 128],
                     B.VA[:, (gi * 16 + ck) * 128:(gi * 16 + ck + 1) * 128],
                     B.E2[:, eslot * 2048 + ck * 128: eslot * 2048 + (ck + 1) * 128],
                     start=(ck == 0), stop=(ck == 15), waits=w,
                     inc="pe" if ck == 15 else None)
            if slot == 3:
                vz = p.vop("reciprocal", B.rz[:],
                           bass.AP(B.pb8h, cbk * 512, [[4096, 1], [1, 512]]),
                           waits=[("pe", p.cnt["pe"])])
                pend_norm[0] = (vz, cbk, gi, cq // 4)

        prev = None
        for gi in range(4):
            for cq in range(16):
                se = emit_S(gi, cq)
                if cq % 4 == 1:
                    flush_norm()
                if prev is not None:
                    emit_ctx(prev[0], prev[1], prev[2])
                prev = (gi, cq, se)
        emit_ctx(prev[0], prev[1], prev[2])
        flush_norm()
        v_ct = p.cnt["v"]
        if B.adbg and sg == B.dbg_sg:
            p.dma("sync", B.adbg["QTs"], B.QTs[:], inc="do", waits=[("v", v_ct)])
            p.dma("sync", B.adbg["KTs"], B.KTs[:], inc="do")
            p.dma("sync", B.adbg["VA"], B.VA[:], inc="do")
            p.dma("sync", B.adbg["CTs"], B.CTs[:], inc="do")
            p.dma("sync", B.adbg["E2"], B.E2[:], inc="do")
        for ob in range(8):
            db = ob % 2
            dgw = p.dma("gpsimd", B.WS[:, db * 1024:(db + 1) * 1024],
                        W["wo"][ob], inc="dg", waits=[("pe", ws_rd.get(db, 0))])
            bk = ob % 4
            for ec in range(8):
                w = []
                if ec == 0:
                    w = [("dg", dgw), ("v", v_ct)] + p.bank_free[bk]
                mn = p.mm(B.pb[bk][:, 0:512],
                          B.WS[:, db * 1024 + ec * 128: db * 1024 + (ec + 1) * 128],
                          B.CTs[:, ec * 512:(ec + 1) * 512],
                          start=(ec == 0), stop=(ec == 7), waits=w, inc="pe")
                ws_rd[db] = mn
            vo = p.vop("scalar_tensor_tensor", B.ATTs[:, ob * 512:(ob + 1) * 512],
                       B.pb[bk][:, 0:512], B.boc[:, ob:ob + 1],
                       B.XN[:, ob * T + t0: ob * T + t0 + 512], OP.add, OP.add,
                       waits=[("pe", p.cnt["pe"])] + ([("s", p.cnt["s"])] if ob == 0 else []))
            p.bank_free[bk] = [("v", vo)]
        if B.adbg and sg == B.dbg_sg:
            p.dma("sync", B.adbg["ATTs"], B.ATTs[:], inc="do", waits=[("v", p.cnt["v"])])
        v_att = p.cnt["v"]
        ia, ib = 4, 5
        ms = {}
        for c in range(NCH):
            db2 = c % 2
            sl = slice(db2 * 512, (db2 + 1) * 512)
            asl = B.ATTs[:, c * 512:(c + 1) * 512]
            w = [("pe", ms[c - 2])] if c >= 2 else []
            vcp = p.vop("tensor_copy", B.xbfs[:, sl], asl, waits=w)
            scp = p.sop("activation", B.xsqs[:, sl], asl, AF.Square,
                        waits=w + ([("v", v_att)] if c == 0 else []))
            w2_ = [("v", vcp), ("s", scp)]
            if c == 0:
                w2_ += p.bank_free[ia] + p.bank_free[ib]
            p.mm(B.pb[ia][:, 0:512], B.ones128[:], B.xbfs[:, sl],
                 start=(c == 0), stop=(c == NCH - 1), waits=w2_)
            ms[c] = p.mm(B.pb[ib][:, 0:512], B.ones128[:], B.xsqs[:, sl],
                         start=(c == 0), stop=(c == NCH - 1), inc="pe")
        pe_done = p.cnt["pe"]
        p.vop("tensor_scalar", B.meanbc[:], B.pb[ia][:, 0:512], 1.0 / D, None,
              OP.mult, waits=[("pe", pe_done)])
        p.vop("tensor_scalar", B.msqbc[:], B.pb[ib][:, 0:512], 1.0 / D, None, OP.mult)
        p.vop("tensor_tensor", B.rstdbc[:], B.meanbc[:], B.meanbc[:], OP.mult)
        vfree = p.vop("tensor_tensor", B.msqbc[:], B.msqbc[:], B.rstdbc[:], OP.subtract)
        p.bank_free[ia] = p.bank_free[ib] = [("v", vfree)]
        p.sop("activation", B.rstdbc[:], B.msqbc[:], AF.Ln, bias=B.epscol[:],
              waits=[("v", vfree)])
        ssq = p.sop("activation", B.rstdbc[:], B.rstdbc[:], AF.Exp, scale=-0.5)
        p.wait("vector", "s", ssq)
        for c in range(NCH):
            asl = B.ATTs[:, c * 512:(c + 1) * 512]
            xsl = B.XR[:, c * T + t0: c * T + t0 + 512]
            p.vop("tensor_tensor", B.ytmp[:], asl, B.meanbc[:], OP.subtract)
            p.vop("tensor_tensor", B.ytmp[:], B.ytmp[:], B.rstdbc[:], OP.mult)
            p.vop("tensor_scalar", B.ytmp[:], B.ytmp[:], B.gamc[:, c:c + 1],
                  B.betc[:, c:c + 1], OP.mult, OP.add)
            p.vop("tensor_tensor", xsl, xsl, B.ytmp[:], OP.add)
    return p.cnt["v"]


def l1attn_dram(nc):
    return {
        "wq": nc.dram_tensor("a_wq", [8, 128, 8, 128], F32, kind="ExternalInput").ap(),
        "wk": nc.dram_tensor("a_wk", [8, 128, 8, 128], F32, kind="ExternalInput").ap(),
        "wv": nc.dram_tensor("a_wv", [1024, 1024], F32, kind="ExternalInput").ap(),
        "wo": nc.dram_tensor("a_wo", [8, 128, 8, 128], F32, kind="ExternalInput").ap(),
        "bqc": nc.dram_tensor("a_bqc", [128, 8], F32, kind="ExternalInput").ap(),
        "bkc": nc.dram_tensor("a_bkc", [128, 8], F32, kind="ExternalInput").ap(),
        "bv": nc.dram_tensor("a_bv", [1, 1024], F32, kind="ExternalInput").ap(),
        "boc": nc.dram_tensor("a_boc", [128, 8], F32, kind="ExternalInput").ap(),
        "gamc": nc.dram_tensor("a_gamc", [128, 8], F32, kind="ExternalInput").ap(),
        "betc": nc.dram_tensor("a_betc", [128, 8], F32, kind="ExternalInput").ap(),
    }


def build_l1attn_program(debug=False, dbg_sg=0):
    nc = bass.Bass("TRN2", target_bir_lowering=False)
    xin = nc.dram_tensor("xin", [D, T], F32, kind="ExternalInput").ap()
    W = l1attn_dram(nc)
    xout = nc.dram_tensor("xout", [D, T], F32, kind="ExternalOutput").ap()
    p = P(nc)
    B = make_common(nc, p)
    l1attn_buffers(nc, B)
    B.adbg = {}
    B.dbg_sg = dbg_sg
    if debug:
        for nm, sh, dt in (("QTs", [64, 8192], BF16), ("KTs", [64, 8192], BF16),
                           ("VA", [128, 8192], BF16), ("CTs", [128, 4096], BF16),
                           ("ATTs", [128, 4096], F32), ("E2", [128, 4096], BF16)):
            B.adbg[nm] = nc.dram_tensor("dbg_" + nm, sh, dt, kind="ExternalOutput").ap()
    dx = load_x(nc, p, B, xin)
    pre = emit_l1attn_preload(nc, p, B, W)
    ln_done = emit_ln(nc, p, B, dx, svar=False)
    vend = emit_l1attn(nc, p, B, W, ln_done, pre)
    store_x(nc, p, B, xout, vend)
    p.realize()
    return nc


def l1attn_host_weights(inputs, prefix="L1_attn"):
    g = lambda n: np.asarray(inputs[prefix + "_" + n])
    colmaj = lambda s: np.ascontiguousarray(s.reshape(8, 128).T)
    def tile_nat(w):
        t = w.reshape(8, 128, 8, 128)
        return np.ascontiguousarray(w.T.reshape(8, 128, 8, 128).transpose(0, 3, 2, 1))
    def tile_lhsT(w):
        t = w.reshape(8, 128, 8, 128)
        return np.ascontiguousarray(t.transpose(2, 1, 0, 3))
    return {
        "a_wq": tile_lhsT(g("wq")), "a_wk": tile_lhsT(g("wk")),
        "a_wv": np.ascontiguousarray(g("wv")), "a_wo": tile_lhsT(g("wo")),
        "a_bqc": colmaj(g("bq")), "a_bkc": colmaj(g("bk")),
        "a_bv": np.ascontiguousarray(g("bv")[None, :]),
        "a_boc": colmaj(g("bo")), "a_gamc": colmaj(g("gamma")),
        "a_betc": colmaj(g("beta")),
    }


def l0attn_buffers(nc, B):
    SB = nc.alloc_sbuf_tensor
    B.EQs = SB("EQs", [128, 4 * 1024], BF16)
    B.EKs = SB("EKs", [128, 4 * 1024], BF16)
    B.Vs = SB("Vs", [128, 4 * 1024], BF16)
    B.QTi = SB("QTi", [64, 2048], BF16)
    B.KTi = SB("KTi", [64, 2048], BF16)
    B.VTi = SB("VTi", [64, 2048], BF16)
    B.At = SB("At", [16, 2048], BF16)
    B.Vr = SB("Vr", [16, 64 + 128 * 64], BF16)
    B.rQ = SB("rQ", [128, 16], F32)
    B.rK = SB("rK", [128, 64], F32)
    B.rKT = SB("rKT", [64, 128], F32)
    B.mxc = SB("mxc", [128, 1], F32)
    B.WQ3 = SB("WQ3", [128, 2 * 1024], BF16)


def emit_l0attn(nc, p, B, W, ln_done):
    w3_rd = {}
    ident = B.ident
    p.gop("memset", B.Vr[0:16, 0:64], 0.0)
    for sg in range(4):
        t0 = sg * 512
        for wi, wkey in enumerate(("wqT", "wkT", "wvT")):
            dst = (B.EQs, B.EKs, B.Vs)[wi]
            for ic in range(8):
                db = ic % 2
                dgw = p.dma("gpsimd", B.WQ3[:, db * 1024:(db + 1) * 1024],
                            W[wkey][ic * 128:(ic + 1) * 128, :], inc="dg",
                            waits=[("pe", w3_rd.get(db, 0))])
                for tt in range(4):
                    for n in range(2):
                        bk = tt * 2 + n
                        w = []
                        if ic == 0:
                            w = [("dg", dgw)] + p.bank_free[bk]
                            if wi == 0 and tt == 0 and n == 0:
                                w.append(("v", ln_done[sg]))
                        elif tt == 0 and n == 0:
                            w = [("dg", dgw)]
                        mn = p.mm(B.pb[bk][:, 0:512],
                                  B.XN[:, ic * T + t0 + tt * 128: ic * T + t0 + (tt + 1) * 128],
                                  B.WQ3[:, db * 1024 + n * 512: db * 1024 + (n + 1) * 512],
                                  start=(ic == 0), stop=(ic == 7), waits=w,
                                  inc="pe" if (tt == 3 and n == 1) or ic == 7 else None)
                        if tt == 3 and n == 1:
                            w3_rd[db] = mn
            for tt in range(4):
                pe_t = p.cnt["pe"]
                if wi < 2:
                    vmx = p.vop("tensor_reduce", B.mxc[:],
                                bass.AP(B.pb8h, tt * 1024, [[4096, 128], [1, 1024]]),
                                axis=AX.X, op=OP.max, waits=[("pe", pe_t)])
                    p.vop("tensor_scalar_mul", B.mxc[:], B.mxc[:], -1.0)
                    p.wait("scalar", "v", p.cnt["v"])
                    se = p.sop("activation", dst[:, tt * 1024:(tt + 1) * 1024],
                               bass.AP(B.pb8h, tt * 1024, [[4096, 128], [1, 1024]]),
                               AF.Exp, bias=B.mxc[:], waits=[("pe", pe_t)])
                    rel = [("s", se)]
                else:
                    sv = p.sop("activation", dst[:, tt * 1024:(tt + 1) * 1024],
                               bass.AP(B.pb8h, tt * 1024, [[4096, 128], [1, 1024]]),
                               AF.Identity, waits=[("pe", pe_t)])
                    rel = [("s", sv)]
                p.bank_free[tt * 2] = rel
                p.bank_free[tt * 2 + 1] = rel
        for tt in range(4):
            toff = t0 + tt * 128
            eq = B.EQs[:, tt * 1024:(tt + 1) * 1024]
            ek = B.EKs[:, tt * 1024:(tt + 1) * 1024]
            vv = B.Vs[:, tt * 1024:(tt + 1) * 1024]
            p.wait("vector", "s", p.cnt["s"])
            p.vop("tensor_reduce", B.rQ[:],
                  bass.AP(B.EQs, tt * 1024, [[4096, 128], [64, 16], [1, 64]]),
                  axis=AX.X, op=OP.add)
            p.vop("reciprocal", B.rQ[:], B.rQ[:])
            p.vop("tensor_tensor",
                  bass.AP(B.EQs, tt * 1024, [[4096, 128], [64, 16], [1, 64]]),
                  bass.AP(B.EQs, tt * 1024, [[4096, 128], [64, 16], [1, 64]]),
                  bass.AP(B.rQ, 0, [[16, 128], [1, 16], [0, 64]]), OP.mult)
            p.vop("tensor_reduce", B.rK[:],
                  bass.AP(B.EKs, tt * 1024, [[4096, 128], [1, 64], [64, 16]]),
                  axis=AX.X, op=OP.add)
            vrk = p.vop("reciprocal", B.rK[:], B.rK[:])
            p2h = B.pb8h[:].bitcast(BF16)
            for src_, dstb, base in ((eq, B.QTi, 0), (ek, B.KTi, 2048), (vv, B.VTi, 4096)):
                pe_w = [("v", p.cnt["v"])]
                for h in range(16):
                    p.tr(p2h[0:64, base + h * 128:base + (h + 1) * 128],
                         src_[:, h * 64:(h + 1) * 64], ident[:],
                         waits=pe_w if h == 0 else (),
                         inc="pe" if h == 15 else None)
            p.tr(bass.AP(B.pb8h, 3072, [[4096, 64], [1, 128]]),
                 B.rK[:], B.identf[:], waits=[("v", vrk)], inc="pe")
            pe_tr = p.cnt["pe"]
            import dataclasses as _dc
            def _pview(off_bf16, pat):
                return _dc.replace(p2h, offset=off_bf16, ap=pat)
            vr1 = p.vop("tensor_copy", B.rKT[:],
                        bass.AP(B.pb8h, 3072, [[4096, 64], [1, 128]]),
                        waits=[("pe", pe_tr)])
            p.vop("tensor_copy",
                  bass.AP(B.QTi, 0, [[2048, 64], [1, 16], [16, 128]]),
                  _pview(0, [[8192, 64], [128, 16], [1, 128]]))
            v_qk = p.vop("scalar_tensor_tensor",
                         bass.AP(B.KTi, 0, [[2048, 64], [1, 16], [16, 128]]),
                         _pview(2048, [[8192, 64], [128, 16], [1, 128]]),
                         1.0,
                         bass.AP(B.rKT, 0, [[128, 64], [0, 16], [1, 128]]),
                         OP.mult, OP.mult)
            v_v = p.vop("tensor_copy",
                        bass.AP(B.VTi, 0, [[2048, 64], [1, 16], [16, 128]]),
                        _pview(4096, [[8192, 64], [128, 16], [1, 128]]))
            p.wait("tensor", "v", v_qk)
            for half in range(2):
                abase = 3072 if half == 0 else 2048
                bks = (6, 7) if half == 0 else (4, 5)
                for ti in range(64):
                    tok = half * 64 + ti
                    w0 = []
                    if ti == 0:
                        w0 = p.bank_free[bks[0]] + p.bank_free[bks[1]]
                        w0.append(("v", vr1 if half == 0 else v_v))
                    p.mm(bass.AP(B.pb8h, abase + ti * 16, [[4096, 16], [1, 16]]),
                         B.KTi[0:64, tok * 16:(tok + 1) * 16],
                         B.QTi[0:64, tok * 16:(tok + 1) * 16],
                         start=True, stop=True, waits=w0,
                         inc="pe" if ti == 63 else None)
                vsrc = bass.AP(B.pb8h, abase, [[4096, 16], [16, 64], [1, 2], [2, 8]])
                vdst = bass.AP(B.At, half * 1024, [[2048, 16], [16, 64], [8, 2], [1, 8]])
                va = p.vop("tensor_copy", vdst, vsrc, waits=[("pe", p.cnt["pe"])])
                p.bank_free[bks[0]] = [("v", va)]
                p.bank_free[bks[1]] = [("v", va)]
            for grp in range(4):
                vb0 = (grp % 2) * 2
                for ti in range(32):
                    tok = grp * 32 + ti
                    p.tr(p2h[0:16, vb0 * 1024 + ti * 64:vb0 * 1024 + (ti + 1) * 64],
                         B.VTi[0:64, tok * 16:(tok + 1) * 16], ident[0:64, 0:64],
                         waits=((p.bank_free[vb0] + p.bank_free[vb0 + 1]) if ti == 0 else ()),
                         inc="pe" if ti == 31 else None)
                vv2 = p.vop("tensor_copy",
                            B.Vr[0:16, 64 + grp * 2048: 64 + (grp + 1) * 2048],
                            p2h[0:16, vb0 * 1024:vb0 * 1024 + 2048],
                            waits=[("pe", p.cnt["pe"])])
                p.bank_free[vb0] = [("v", vv2)]
                p.bank_free[vb0 + 1] = [("v", vv2)]
            v_vr = p.cnt["v"]
            p.wait("tensor", "v", v_vr)
            for half in range(2):
                ebk, obk = (0, 1) if half == 0 else (2, 3)
                for ti in range(64):
                    tok = half * 64 + ti
                    w_e = (p.bank_free[ebk] + p.bank_free[obk]) if ti == 0 else ()
                    p.mm(bass.AP(B.pb8h, ebk * 512 + ti * 8, [[4096, 64], [1, 8]]),
                         B.Vr[0:16, 64 + tok * 64: 64 + (tok + 1) * 64],
                         B.At[0:16, tok * 16: tok * 16 + 8],
                         start=True, stop=True, waits=w_e)
                    p.mm(bass.AP(B.pb8h, obk * 512 + ti * 8, [[4096, 128], [1, 8]]),
                         B.Vr[0:16, tok * 64: tok * 64 + 128],
                         B.At[0:16, tok * 16 + 8: tok * 16 + 16],
                         start=True, stop=True,
                         inc="pe" if ti == 63 else None)
                pe_s2 = p.cnt["pe"]
                vlast = p.vop("scalar_tensor_tensor",
                              bass.AP(B.XR, toff + half * 64, [[16384, 64], [2048, 8], [1, 64]]),
                              bass.AP(B.pb8h, ebk * 512, [[4096, 64], [1, 8], [8, 64]]),
                              1.0,
                              bass.AP(B.XR, toff + half * 64, [[16384, 64], [2048, 8], [1, 64]]),
                              OP.mult, OP.add, waits=[("pe", pe_s2)])
                vlast = p.vop("scalar_tensor_tensor",
                              bass.AP(B.XR, 64 * 16384 + toff + half * 64,
                                      [[16384, 64], [2048, 8], [1, 64]]),
                              bass.AP(B.pb8h, obk * 512 + 64 * 4096, [[4096, 64], [1, 8], [8, 64]]),
                              1.0,
                              bass.AP(B.XR, 64 * 16384 + toff + half * 64,
                                      [[16384, 64], [2048, 8], [1, 64]]),
                              OP.mult, OP.add)
                p.bank_free[ebk] = [("v", vlast)]
                p.bank_free[obk] = [("v", vlast)]
    return p.cnt["v"]


def l0attn_dram(nc):
    return {
        "wqT": nc.dram_tensor("l0_wqT", [1024, 1024], F32, kind="ExternalInput").ap(),
        "wkT": nc.dram_tensor("l0_wkT", [1024, 1024], F32, kind="ExternalInput").ap(),
        "wvT": nc.dram_tensor("l0_wvT", [1024, 1024], F32, kind="ExternalInput").ap(),
    }


def build_l0attn_program():
    nc = bass.Bass("TRN2", target_bir_lowering=False)
    xin = nc.dram_tensor("xin", [D, T], F32, kind="ExternalInput").ap()
    W = l0attn_dram(nc)
    xout = nc.dram_tensor("xout", [D, T], F32, kind="ExternalOutput").ap()
    p = P(nc)
    B = make_common(nc, p)
    l0attn_buffers(nc, B)
    dx = load_x(nc, p, B, xin)
    ln_done = emit_ln(nc, p, B, dx, svar=False)
    vend = emit_l0attn(nc, p, B, W, ln_done)
    store_x(nc, p, B, xout, vend)
    p.realize()
    return nc


def l0attn_host_weights(inputs):
    return {
        "l0_wqT": np.ascontiguousarray(np.asarray(inputs["L0_attn_W_Q"]).T),
        "l0_wkT": np.ascontiguousarray(np.asarray(inputs["L0_attn_W_K"]).T),
        "l0_wvT": np.ascontiguousarray(np.asarray(inputs["L0_attn_W_V"]).T),
    }


_SBUF_TOP = 229344

_ARENA_GROUPS = {
    "moe": [("svarbc", [128, T], "BF16"), ("rwt", [128, NCH * 32], "F32"),
            ("rwsc", [128, 32], "F32"), ("rbrow", [1, 32], "BF16"),
            ("b1c", [128, 64], "F32"), ("b2sb", [32, 1024], "BF16"),
            ("idxcols", [128, 16], "BF16"), ("idxbc", [128, 2048], "BF16"),
            ("MT", [32, 2048], "BF16"), ("eqs", [128, 512], "BF16"),
            ("lmax", [128, 1], "F32"), ("lsel", [128, 32], "F32"),
            ("HT", [128, 8 * 2 * 1024], "BF16"), ("Z2", [128, 8 * 1024], "F32"),
            ("w1sb", [128, 2 * 2048], "BF16"), ("w2sb", [128, 2 * 2048], "BF16")],
    "l0a": [("EQs", [128, 4 * 1024], "BF16"), ("EKs", [128, 4 * 1024], "BF16"),
            ("Vs", [128, 4 * 1024], "BF16"), ("QTi", [64, 2048], "BF16"),
            ("KTi", [64, 2048], "BF16"), ("VTi", [64, 2048], "BF16"),
            ("At", [16, 2048], "BF16"), ("Vr", [16, 64 + 128 * 64], "BF16"),
            ("rQ", [128, 16], "F32"), ("rK", [128, 64], "F32"),
            ("rKT", [64, 128], "F32"), ("mxc", [128, 1], "F32"),
            ("WQ3", [128, 2 * 1024], "BF16")],
    "l1f": [("GH", [128, 32768], "BF16"), ("wA", [128, 2 * 4096], "BF16"),
            ("wB", [128, 2 * 1024], "BF16"), ("wC", [128, 2 * 1024], "BF16"),
            ("s1c", [128, 32], "F32"), ("s2c", [128, 32], "F32"),
            ("sh1c", [128, 32], "F32"), ("sh2c", [128, 32], "F32"),
            ("s3c", [128, 8], "F32"), ("s3h", [128, 8], "F32")],
    "l1a": [("QTs", [64, 16 * 512], "BF16"), ("KTs", [64, 16 * 512], "BF16"),
            ("VA", [128, 4 * 16 * 128], "BF16"), ("E2", [128, 2 * 2048], "BF16"),
            ("CTs", [128, 8 * 512], "BF16"), ("ATTs", [128, 8 * 512], "F32"),
            ("WS", [128, 2 * 1024], "BF16"), ("WV", [128, 2 * 1024], "BF16"),
            ("bqc", [128, 8], "F32"), ("bkc", [128, 8], "F32"),
            ("boc", [128, 8], "F32"), ("bvrow", [1, 1024], "BF16"),
            ("gamc", [128, 8], "F32"), ("betc", [128, 8], "F32"),
            ("neg4c", [128, 1], "F32"), ("rz", [1, 512], "F32")],
}


def _alloc_arena(nc, base):
    dts = {"F32": F32, "BF16": BF16}
    handles = {}
    maxend = base
    for phase, members in _ARENA_GROUPS.items():
        off = base
        hs = {}
        for name, shape, dt in members:
            nb = shape[-1] * (4 if dt == "F32" else 2)
            for s in shape[1:-1]:
                nb *= s
            off = (off + 31) & ~31
            hs[name] = nc.alloc_sbuf_tensor_at(phase + "_" + name, shape, dts[dt],
                                               offset=off)
            off += nb
        assert off <= _SBUF_TOP, (phase, off)
        maxend = max(maxend, off)
        handles[phase] = hs
    return handles


def build_full_program():
    nc = bass.Bass("TRN2", target_bir_lowering=False)
    xin = nc.dram_tensor("xin", [D, T], F32, kind="ExternalInput").ap()
    Wm = moe_dram(nc)
    W0 = l0attn_dram(nc)
    Wf = l1ffn_dram(nc)
    Wa = l1attn_dram(nc)
    xout = nc.dram_tensor("xout", [D, T], F32, kind="ExternalOutput").ap()
    xsp = nc.dram_tensor("xspill", [D, T], F32, kind="Internal").ap()
    p = P(nc)
    B = make_common(nc, p)
    base = _SBUF_TOP - nc.sbuf_bytes_remaining
    handles = _alloc_arena(nc, base)

    def bind(phase):
        for name, h in handles[phase].items():
            setattr(B, name, h)
        if phase == "l1a":
            B.zbs = B.ytmp[:, 0:128]
        if phase == "l1f":
            B.Hv = B.XR[:].bitcast(BF16)

    dx = load_x(nc, p, B, xin)
    bind("moe")
    sdg = emit_moe_weight_preload(nc, p, B, Wm)
    ln = emit_ln(nc, p, B, x_ready_dx=dx, svar=True)
    emit_moe(nc, p, B, Wm, ln, sdg)
    phase_barrier(p)
    bind("l0a")
    ln = emit_ln(nc, p, B, v_ready=p.cnt["v"], svar=False)
    emit_l0attn(nc, p, B, W0, ln)
    phase_barrier(p)
    bind("moe")
    sdg = emit_moe_weight_preload(nc, p, B, Wm)
    ln = emit_ln(nc, p, B, v_ready=p.cnt["v"], svar=True)
    emit_moe(nc, p, B, Wm, ln, sdg)
    phase_barrier(p)
    bind("l1f")
    sv = emit_l1ffn_preload(nc, p, B, Wf)
    ln = emit_ln(nc, p, B, v_ready=p.cnt["v"], svar=False)
    for c in range(NCH):
        p.dma("sync", xsp[c * 128:(c + 1) * 128, :], B.XR[:, c * T:(c + 1) * T],
              inc="do")
    spill_do = p.cnt["do"]
    emit_l1ffn(nc, p, B, Wf, xsp, xsp, ln, sv, hv_waits=[("do", spill_do)])
    phase_barrier(p)
    bind("l1a")
    for c in range(NCH):
        p.dma("sync", B.XR[:, c * T:(c + 1) * T], xsp[c * 128:(c + 1) * 128, :],
              inc="dx", waits=[("do", p.cnt["do"])] if c == 0 else ())
    dx2 = p.cnt["dx"]
    pre = emit_l1attn_preload(nc, p, B, Wa)
    ln = emit_ln(nc, p, B, x_ready_dx=dx2, svar=False)
    emit_l1attn(nc, p, B, Wa, ln, pre)
    phase_barrier(p)
    bind("l1f")
    sv = emit_l1ffn_preload(nc, p, B, Wf)
    ln = emit_ln(nc, p, B, v_ready=p.cnt["v"], svar=False)
    for c in range(NCH):
        p.dma("sync", xsp[c * 128:(c + 1) * 128, :], B.XR[:, c * T:(c + 1) * T],
              inc="do")
    spill_do = p.cnt["do"]
    emit_l1ffn(nc, p, B, Wf, xsp, xout, ln, sv, hv_waits=[("do", spill_do)])
    p.realize()
    return nc


_PROGRAMS = {}


def _get_program(name):
    if name not in _PROGRAMS:
        _PROGRAMS[name] = {"moe": build_moe_program, "l1ffn": build_l1ffn_program, "l1attn": build_l1attn_program, "l0attn": build_l0attn_program, "full": build_full_program}[name]()
    return _PROGRAMS[name]


def kernel(**inputs):
    x = np.asarray(inputs["L0_attn_input_seq_embs"])
    xt = [np.ascontiguousarray(x[i].T) for i in range(8)]
    wmap = {}
    wmap.update(moe_host_weights(inputs))
    wmap.update(l0attn_host_weights(inputs))
    wmap.update(l1ffn_host_weights(inputs))
    wmap.update(l1attn_host_weights(inputs))
    nc = _get_program("full")
    in_maps = [dict(wmap, xin=xt[i]) for i in range(8)]
    res = run_bass_kernel_spmd(nc, in_maps, core_ids=list(range(8)))
    return np.stack([np.ascontiguousarray(r["xout"].T) for r in res.results])


# version 08
# speedup vs baseline: 1.0609x; 1.0609x over previous
import numpy as np
import ml_dtypes
import concourse.bass as bass
import concourse.mybir as mybir
from concourse.bass_utils import run_bass_kernel_spmd

F32 = mybir.dt.float32
BF16 = mybir.dt.bfloat16
I32 = mybir.dt.int32
AF = mybir.ActivationFunctionType
OP = mybir.AluOpType
AX = mybir.AxisListType

D = 1024
T = 2048
NCH = 8
NTC = 4
LN_EPS = 1e-5

ENGS = ("tensor", "vector", "scalar", "gpsimd", "sync")


class P:

    def __init__(self, nc):
        self.nc = nc
        self.q = {e: [] for e in ENGS}
        self.sems = {}
        self.cnt = {}
        for name in ("v", "s", "g", "pe", "dx", "dg", "do"):
            self.sems[name] = nc.alloc_semaphore("sem_" + name)
            self.cnt[name] = 0
        self.bank_free = [[] for _ in range(8)]

    def wait(self, eng, sem, val):
        if val <= 0:
            return
        s = self.sems[sem]
        self.q[eng].append(lambda E: E.wait_ge(s, val))

    def op(self, eng, meth, *args, inc=None, incby=1, waits=(), **kw):
        for sem, val in waits:
            self.wait(eng, sem, val)
        n = None
        if inc is not None:
            self.cnt[inc] += incby
            n = self.cnt[inc]
            s = self.sems[inc]

            def fn(E, meth=meth, args=args, kw=kw, s=s, incby=incby):
                getattr(E, meth)(*args, **kw).then_inc(s, incby)
        else:
            def fn(E, meth=meth, args=args, kw=kw):
                getattr(E, meth)(*args, **kw)
        self.q[eng].append(fn)
        return n

    def vop(self, meth, *args, waits=(), **kw):
        w = [("v", self.cnt["v"])] + list(waits)
        return self.op("vector", meth, *args, inc="v", waits=w, **kw)

    def sop(self, meth, *args, waits=(), **kw):
        w = [("s", self.cnt["s"])] + list(waits)
        return self.op("scalar", meth, *args, inc="s", waits=w, **kw)

    def gop(self, meth, *args, waits=(), **kw):
        w = [("g", self.cnt["g"])] + list(waits)
        return self.op("gpsimd", meth, *args, inc="g", waits=w, **kw)

    def mm(self, out, lhsT, rhs, start, stop, waits=(), inc=None, **kw):
        return self.op("tensor", "matmul", out, lhsT, rhs, start=start, stop=stop,
                       waits=waits, inc=inc, **kw)

    def tr(self, out, in_, ident, waits=(), inc=None):
        return self.op("tensor", "transpose", out, in_, ident, waits=waits, inc=inc)

    def dma(self, queue, out, in_, inc, waits=(), **kw):
        return self.op(queue, "dma_start", out=out, in_=in_, inc=inc, incby=16,
                       waits=waits, **kw)

    def idma(self, queue, out, out_off, in_, in_off, inc, waits=()):
        return self.op(queue, "indirect_dma_start", out, out_off, in_, in_off,
                       inc=inc, incby=16, waits=waits)

    def realize(self):
        with self.nc.Block() as block:
            for eng in ENGS:
                fns = self.q[eng]
                if fns:
                    getattr(block, eng)(lambda E, fns=fns: [f(E) for f in fns])
        self.q = {e: [] for e in ENGS}


def phase_barrier(p):
    snap = [(nm, p.cnt[nm]) for nm in ("v", "s", "g", "pe", "dx", "dg", "do")]
    for eng in ENGS:
        for nm, val in snap:
            p.wait(eng, nm, val)


class Buffers:
    dbg = {}
    adbg = {}
    dbg_sg = -1


def make_common(nc, p):
    B = Buffers()
    SB = nc.alloc_sbuf_tensor
    B.XR = SB("XR", [128, NCH * T], F32)
    B.XN = SB("XN", [128, NCH * T], BF16)
    B.ones128 = SB("ones128", [128, 128], BF16)
    B.ones128f = SB("ones128f", [128, 128], F32)
    B.onesbig = SB("onesbig", [128, 2048], BF16)
    B.onesrow = SB("onesrow", [1, 512], BF16)
    B.ident = SB("ident", [128, 128], BF16)
    B.identf = SB("identf", [128, 128], F32)
    B.iotacol = SB("iotacol", [128, 1], F32)
    B.epscol = SB("epscol", [128, 1], F32)
    B.iotarow32 = SB("iotarow32", [128, 32], F32)
    B.xbfs = SB("xbfs", [128, 2 * 512], BF16)
    B.xsqs = SB("xsqs", [128, 2 * 512], BF16)
    B.meanbc = SB("meanbc", [128, 512], F32)
    B.msqbc = SB("msqbc", [128, 512], F32)
    B.rstdbc = SB("rstdbc", [128, 512], F32)
    B.ytmp = SB("ytmp", [128, 512], F32)
    B.pb8h = nc.alloc_psum_tensor("pb8", [128, 4096], F32)
    B.pb8 = B.pb8h[:]
    B.pb = [B.pb8h[:, i * 512:(i + 1) * 512] for i in range(8)]
    p.gop("memset", B.ones128[:], 1.0)
    p.gop("memset", B.ones128f[:], 1.0)
    p.gop("memset", B.onesbig[:], 1.0)
    p.gop("memset", B.onesrow[:], 1.0)
    p.gop("memset", B.ident[:], 0.0)
    p.gop("affine_select", B.ident[:], B.ones128[:], [[1, 128]], OP.is_equal, 0.0,
          base=0, channel_multiplier=-1)
    p.gop("affine_select", B.identf[:], B.ones128f[:], [[1, 128]], OP.is_equal, 0.0,
          base=0, channel_multiplier=-1)
    p.gop("iota", B.iotacol[:], [[1, 1]], base=0, channel_multiplier=1,
          allow_small_or_imprecise_dtypes=True)
    p.gop("memset", B.epscol[:], LN_EPS)
    p.gop("iota", B.iotarow32[:], [[1, 32]], base=0, channel_multiplier=0,
          allow_small_or_imprecise_dtypes=True)
    return B


def load_x(nc, p, B, xin):
    for c in range(NCH):
        p.dma("sync", B.XR[:, c * T:(c + 1) * T], xin[c * 128:(c + 1) * 128, :], inc="dx")
    return p.cnt["dx"]


def store_x(nc, p, B, xout, vwait):
    for c in range(NCH):
        p.dma("sync", xout[c * 128:(c + 1) * 128, :], B.XR[:, c * T:(c + 1) * T],
              inc="do", waits=[("v", vwait)] if c == 0 else ())
    p.wait("sync", "do", p.cnt["do"])


def emit_ln(nc, p, B, x_ready_dx=0, v_ready=0, svar=True):
    done = []
    for tc in range(NTC):
        ia, ib = (2 * tc) % 4, (2 * tc) % 4 + 1
        bA, bB = B.pb[ia], B.pb[ib]
        ms = {}
        for c in range(NCH):
            db = c % 2
            sl = slice(db * 512, (db + 1) * 512)
            xsl = B.XR[:, c * T + tc * 512: c * T + (tc + 1) * 512]
            w = list([("dx", x_ready_dx), ("v", v_ready)] if (tc == 0 and c == 0) else [])
            if c >= 2:
                w.append(("pe", ms[c - 2]))
            vcp = p.vop("tensor_copy", B.xbfs[:, sl], xsl, waits=w)
            scp = p.sop("activation", B.xsqs[:, sl], xsl, AF.Square, waits=w)
            w2_ = [("v", vcp), ("s", scp)]
            if c == 0:
                w2_ += p.bank_free[ia] + p.bank_free[ib]
            p.mm(bA[:, 0:512], B.ones128[:], B.xbfs[:, sl],
                 start=(c == 0), stop=(c == NCH - 1), waits=w2_)
            ms[c] = p.mm(bB[:, 0:512], B.ones128[:], B.xsqs[:, sl],
                         start=(c == 0), stop=(c == NCH - 1), inc="pe")
        pe_done = p.cnt["pe"]
        p.vop("tensor_scalar", B.meanbc[:], bA[:, 0:512], 1.0 / D, None, OP.mult,
              waits=[("pe", pe_done)])
        p.vop("tensor_scalar", B.msqbc[:], bB[:, 0:512], 1.0 / D, None, OP.mult)
        p.vop("tensor_tensor", B.rstdbc[:], B.meanbc[:], B.meanbc[:], OP.mult)
        vfree = p.vop("tensor_tensor", B.msqbc[:], B.msqbc[:], B.rstdbc[:], OP.subtract)
        p.bank_free[ia] = p.bank_free[ib] = [("v", vfree)]
        p.sop("activation", B.rstdbc[:], B.msqbc[:], AF.Ln, bias=B.epscol[:],
              waits=[("v", vfree)])
        if svar:
            p.sop("activation", B.svarbc[:, tc * 512:(tc + 1) * 512], B.rstdbc[:],
                  AF.Exp, scale=0.5)
        ssq = p.sop("activation", B.rstdbc[:], B.rstdbc[:], AF.Exp, scale=-0.5)
        p.wait("vector", "s", ssq)
        last = None
        for c in range(NCH):
            xsl = B.XR[:, c * T + tc * 512: c * T + (tc + 1) * 512]
            nsl = B.XN[:, c * T + tc * 512: c * T + (tc + 1) * 512]
            p.vop("tensor_tensor", B.ytmp[:], xsl, B.meanbc[:], OP.subtract)
            last = p.vop("tensor_tensor", nsl, B.ytmp[:], B.rstdbc[:], OP.mult)
        done.append(last)
    return done


def moe_buffers(nc, B):
    SB = nc.alloc_sbuf_tensor
    if not hasattr(B, "svarbc"):
        B.svarbc = SB("svarbc", [128, T], BF16)
    B.rwt = SB("rwt", [128, NCH * 32], F32)
    B.rwsc = SB("rwsc", [128, 32], F32)
    B.rbrow = SB("rbrow", [1, 32], BF16)
    B.b1c = SB("sb_b1c", [128, 64], F32)
    B.b2sb = SB("b2sb", [32, 1024], BF16)
    B.idxcols = SB("idxcols", [128, 16], BF16)
    B.idxbc = SB("idxbc", [128, 2048], BF16)
    B.MT = SB("MT", [32, 2048], BF16)
    B.eqs = SB("eqs", [128, 512], BF16)
    B.lmax = SB("lmax", [128, 1], F32)
    B.lsel = SB("lsel", [128, 32], F32)
    B.HT = SB("HT", [128, 8 * 2 * 1024], BF16)
    B.Z2 = SB("Z2", [128, 8 * 1024], F32)
    B.w1sb = SB("w1sb", [128, 2 * 2048], BF16)
    B.w2sb = SB("w2sb", [128, 2 * 2048], BF16)


def emit_moe_weight_preload(nc, p, B, W):
    for c in range(NCH):
        p.dma("gpsimd", B.rwt[:, c * 32:(c + 1) * 32],
              W["rw_t"][c * 128:(c + 1) * 128, :], inc="dg")
    p.dma("gpsimd", B.rbrow[:], W["rb"], inc="dg")
    p.dma("gpsimd", B.b2sb[:], W["b2"], inc="dg")
    p.dma("sync", B.b1c[:], W["b1c"], inc="dx")
    B.b1c_dx = p.cnt["dx"]
    dg = p.cnt["dg"]
    bk = 4
    for c in range(NCH):
        p.mm(B.pb[bk][:, 0:32], B.ones128f[:], B.rwt[:, c * 32:(c + 1) * 32],
             start=(c == 0), stop=(c == NCH - 1),
             waits=[("dg", dg), ("g", 2)] if c == 0 else (),
             inc="pe" if c == NCH - 1 else None)
    p.vop("tensor_scalar", B.rwsc[:], B.pb[bk][:, 0:32], 1.0 / D, None, OP.mult,
          waits=[("pe", p.cnt["pe"])])
    vlast = 0
    for c in range(NCH):
        vlast = p.vop("tensor_tensor", B.rwt[:, c * 32:(c + 1) * 32],
                      B.rwt[:, c * 32:(c + 1) * 32], B.rwsc[:], OP.subtract)
    p.bank_free[bk] = [("v", vlast)]
    B.rw_ready = vlast
    return p.cnt["dg"]


def emit_moe(nc, p, B, W, ln_done, small_dg):
    B.relu_war = {}
    for tt in range(16):
        bk = 5 + tt % 2
        lb = B.pb[bk]
        w = list(p.bank_free[bk]) + [("v", ln_done[tt // 4])]
        if tt == 0:
            w += [("v", B.rw_ready), ("dg", small_dg)]
        for c in range(NCH):
            p.mm(lb[:, 0:32],
                 B.XR[:, c * T + tt * 128: c * T + (tt + 1) * 128],
                 B.rwt[:, c * 32:(c + 1) * 32],
                 start=(c == 0), stop=False, waits=w if c == 0 else ())
        p.mm(lb[:, 0:32], B.svarbc[0:1, tt * 128:(tt + 1) * 128],
             B.rbrow[0:1, 0:32], start=False, stop=True, inc="pe")
        pe_l = p.cnt["pe"]
        p.vop("tensor_reduce", B.lmax[:], lb[:, 0:32], axis=AX.X,
              op=OP.max, waits=[("pe", pe_l)])
        p.vop("tensor_scalar", B.lsel[:], lb[:, 0:32], B.lmax[:], None, OP.is_equal)
        p.vop("scalar_tensor_tensor", B.lsel[:], B.lsel[:], -10000.0, B.iotarow32[:],
              OP.mult, OP.add)
        p.vop("tensor_reduce", B.lmax[:], B.lsel[:], axis=AX.X, op=OP.min)
        vr = p.vop("tensor_scalar", B.idxcols[:, tt:tt + 1], B.lmax[:], 10000.0,
                   None, OP.add)
        p.bank_free[bk] = [("v", vr)]
    v_idx = p.cnt["v"]
    p7h = B.pb[7][:, 0:512].bitcast(BF16)
    p6h = B.pb[6][:, 0:512].bitcast(BF16)
    p.wait("tensor", "v", v_idx)
    for wv in p.bank_free[7] + p.bank_free[6]:
        p.wait("tensor", wv[0], wv[1])
    for tt in range(16):
        dst = p7h if tt < 8 else p6h
        p.tr(dst[0:1, (tt % 8) * 128:(tt % 8 + 1) * 128],
             B.idxcols[:, tt:tt + 1], B.ident[:],
             inc="pe" if tt == 15 else None)
    for half in range(2):
        ph = p7h if half == 0 else p6h
        vir = p.vop("tensor_copy", B.xbfs[0:1, 0:1024], ph[0:1, 0:1024],
                    waits=[("pe", p.cnt["pe"])] if half == 0 else ())
        for q in range(2):
            tc = half * 2 + q
            p.mm(B.pb[7][:, 0:512], B.ones128[0:1, :],
                 B.xbfs[0:1, q * 512:(q + 1) * 512],
                 start=True, stop=True,
                 waits=[("v", vir)] + p.bank_free[7], inc="pe")
            vb = p.vop("tensor_copy", B.idxbc[:, tc * 512:(tc + 1) * 512],
                       B.pb[7][:, 0:512], waits=[("pe", p.cnt["pe"])])
            p.bank_free[7] = [("v", vb)]
    p.vop("tensor_scalar", B.MT[:], B.idxbc[0:32, :], B.iotacol[0:32, :], None,
          OP.is_equal)
    v_mt = p.cnt["v"]
    if B.dbg:
        p.dma("sync", B.dbg["XN"], B.XN[:], inc="do", waits=[("v", ln_done[-1])])
        p.dma("sync", B.dbg["idxbc"], B.idxbc[:], inc="do", waits=[("v", v_mt)])

    w1_rd = {}
    w2_rd = {}
    for th in range(2):
        t0 = th * 1024
        for eg in range(4):
            for ei in range(8):
                e = eg * 8 + ei
                db = ei % 2
                dgw = p.dma("gpsimd", B.w1sb[:, db * 2048:(db + 1) * 2048],
                            W["w1"][e].rearrange("(c p) j -> p c j", p=128),
                            inc="dg", waits=[("pe", w1_rd.get(db, 0))])
                for tq in range(2):
                    veq = p.vop("tensor_scalar", B.eqs[:, 0:512],
                                B.idxbc[:, t0 + tq * 512: t0 + (tq + 1) * 512],
                                float(e), None, OP.is_equal,
                                waits=[("v", v_mt)] if (th == 0 and eg == 0 and ei == 0 and tq == 0) else ())
                    for jb in range(2):
                        bk = jb * 2 + tq
                        bank = B.pb[bk]
                        w = [("dg", dgw)] + p.bank_free[bk]
                        mlast = None
                        for ic in range(NCH):
                            mlast = p.mm(bank[:, 0:512],
                                 B.w1sb[:, db * 2048 + ic * 256 + jb * 128:
                                        db * 2048 + ic * 256 + (jb + 1) * 128],
                                 B.XN[:, ic * T + t0 + tq * 512:
                                      ic * T + t0 + (tq + 1) * 512],
                                 start=(ic == 0), stop=(ic == NCH - 1),
                                 waits=w if ic == 0 else (),
                                 inc="pe" if ic == NCH - 1 else None)
                        w1_rd[db] = mlast
                        hsl = B.HT[:, (ei * 2 + jb) * 1024 + tq * 512:
                                   (ei * 2 + jb) * 1024 + (tq + 1) * 512]
                        tsl = (B.xbfs if bk < 2 else B.xsqs)[:, (bk % 2) * 512:
                                                             (bk % 2) * 512 + 512]
                        srl = p.sop("activation", tsl, bank[:, 0:512], AF.Relu,
                                    bias=B.b1c[:, e * 2 + jb: e * 2 + jb + 1],
                                    waits=[("pe", mlast), ("dx", B.b1c_dx),
                                           ("v", B.relu_war.get(bk, 0))])
                        vh = p.vop("tensor_tensor", hsl, tsl,
                                   B.eqs[:, 0:512], OP.mult,
                                   waits=[("s", srl), ("v", veq)])
                        B.relu_war[bk] = vh
                        p.bank_free[bk] = [("v", vh)]
            v_ht = p.cnt["v"]
            if B.dbg and th == 0 and eg == 0:
                p.dma("sync", B.dbg["HT"], B.HT[:], inc="do", waits=[("v", v_ht)])
            for oh in range(2):
                for ei in range(8):
                    e = eg * 8 + ei
                    pr = (ei // 2) % 2
                    db = pr * 2 + ei % 2
                    if ei % 2 == 0:
                        dgw = p.dma("gpsimd", B.w2sb[:, pr * 2048:(pr + 1) * 2048],
                                    W["w2"][e:e + 2, :, oh * 512:(oh + 1) * 512]
                                    .rearrange("e (c p) o -> p e c o", p=128),
                                    inc="dg", waits=[("pe", w2_rd.get(pr, 0))])
                    for obi in range(4):
                        for tq in range(2):
                            bk = obi * 2 + tq
                            bank = B.pb[bk]
                            ob = oh * 4 + obi
                            if ei == 0:
                                w = [("dg", dgw), ("v", v_ht)] + p.bank_free[bk]
                                if eg == 0:
                                    p.mm(bank[:, 0:512],
                                         B.b2sb[:, ob * 128:(ob + 1) * 128],
                                         B.MT[:, t0 + tq * 512: t0 + (tq + 1) * 512],
                                         start=True, stop=False, waits=w)
                                    w = []
                            else:
                                w = [("dg", dgw)]
                            for jb in range(2):
                                start = (ei == 0 and jb == 0 and eg != 0)
                                islast = (ei == 7 and jb == 1)
                                expert_last = (obi == 3 and tq == 1 and jb == 1
                                               and ei % 2 == 1)
                                mn = p.mm(bank[:, 0:512],
                                     B.w2sb[:, db * 1024 + jb * 512 + obi * 128:
                                            db * 1024 + jb * 512 + (obi + 1) * 128],
                                     B.HT[:, (ei * 2 + jb) * 1024 + tq * 512:
                                          (ei * 2 + jb) * 1024 + (tq + 1) * 512],
                                     start=start, stop=islast,
                                     waits=w if jb == 0 else (),
                                     inc="pe" if (islast or expert_last) else None)
                                if expert_last:
                                    w2_rd[pr] = mn
                                w = []
                pe_grp = p.cnt["pe"]
                for obi in range(4):
                    for tq in range(2):
                        bk = obi * 2 + tq
                        bank = B.pb[bk]
                        ob = oh * 4 + obi
                        zsl = B.Z2[:, ob * 1024 + tq * 512: ob * 1024 + (tq + 1) * 512]
                        if eg == 0:
                            vz = p.vop("tensor_copy", zsl, bank[:, 0:512],
                                       waits=[("pe", pe_grp)])
                        else:
                            vz = p.vop("tensor_tensor", zsl, zsl, bank[:, 0:512],
                                       OP.add, waits=[("pe", pe_grp)])
                        p.bank_free[bk] = [("v", vz)]
        if B.dbg and th == 0:
            p.dma("sync", B.dbg["Z2"], B.Z2[:], inc="do", waits=[("v", p.cnt["v"])])
        for ob in range(8):
            for tq in range(2):
                zsl = B.Z2[:, ob * 1024 + tq * 512: ob * 1024 + (tq + 1) * 512]
                xsl = B.XR[:, ob * T + t0 + tq * 512: ob * T + t0 + (tq + 1) * 512]
                p.vop("tensor_scalar", B.ytmp[:], zsl, 0.0, 0.5, OP.max, OP.mult)
                p.vop("tensor_tensor", xsl, xsl, B.ytmp[:], OP.add)
    return p.cnt["v"]


def moer_buffers(nc, B):
    SB = nc.alloc_sbuf_tensor
    if not hasattr(B, "svarbc"):
        B.svarbc = SB("svarbc", [128, T], BF16)
    B.rwt = SB("rwt", [128, NCH * 32], F32)
    B.rwsc = SB("rwsc", [128, 32], F32)
    B.rbrow = SB("rbrow", [1, 32], BF16)
    B.b1c = SB("sb_b1c", [128, 64], F32)
    B.b2t = SB("sb_b2t", [128, 256], F32)
    B.idxcols = SB("idxcols", [128, 16], F32)
    B.MT = SB("MT", [32, 2048], BF16)
    B.lmax = SB("lmax", [128, 1], F32)
    B.lsel = SB("lsel", [128, 32], F32)
    B.iote = SB("iote", [32, 1], BF16)
    B.Mtm = SB("Mtm", [128, 16 * 32], BF16)
    B.Ltile = SB("Ltile", [128, 2 * 2048], BF16)
    B.ranksb = SB("ranksb", [32, 2048], BF16)
    B.rrow = SB("rrow", [1, 2048], F32)
    B.irow = SB("irow", [1, 2048], F32)
    B.srow = SB("srow", [1, 2048], F32)
    B.scoli = SB("scoli", [128, 16], I32)
    B.tokids = SB("tokids", [128, 16], I32)
    B.xtms = SB("xtms", [128, 2 * 1024], BF16)
    B.xg = SB("xg", [128, 2 * 1024], BF16)
    B.xgT = SB("xgT", [128, 2 * 1024], BF16)
    B.hT = SB("hT", [128, 2 * 256], BF16)
    B.yT = SB("yT", [128, 2 * 1024], BF16)
    B.ytm = SB("ytm", [128, 2 * 1024], BF16)
    B.permsb = SB("permsb", [128, 32], I32)
    B.w1sb = SB("w1sb", [128, 2 * 2048], BF16)
    B.w2sb = SB("w2sb", [128, 2 * 2048], BF16)


def emit_moer_preload(nc, p, B, W):
    for c in range(NCH):
        p.dma("gpsimd", B.rwt[:, c * 32:(c + 1) * 32],
              W["rw_t"][c * 128:(c + 1) * 128, :], inc="dg")
    p.dma("gpsimd", B.rbrow[:], W["rb"], inc="dg")
    p.dma("sync", B.b1c[:], W["b1c"], inc="dx")
    p.dma("sync", B.b2t[:], W["b2t"], inc="dx")
    B.b1c_dx = p.cnt["dx"]
    dg = p.cnt["dg"]
    p.vop("tensor_copy", B.iote[:], B.iotacol[0:32, :], waits=[("g", p.cnt["g"])])
    bk = 4
    for c in range(NCH):
        p.mm(B.pb[bk][:, 0:32], B.ones128f[:], B.rwt[:, c * 32:(c + 1) * 32],
             start=(c == 0), stop=(c == NCH - 1),
             waits=([("dg", dg), ("g", 2)] + p.bank_free[bk]) if c == 0 else (),
             inc="pe" if c == NCH - 1 else None)
    p.vop("tensor_scalar", B.rwsc[:], B.pb[bk][:, 0:32], 1.0 / D, None, OP.mult,
          waits=[("pe", p.cnt["pe"])])
    vlast = 0
    for c in range(NCH):
        vlast = p.vop("tensor_tensor", B.rwt[:, c * 32:(c + 1) * 32],
                      B.rwt[:, c * 32:(c + 1) * 32], B.rwsc[:], OP.subtract)
    p.bank_free[bk] = [("v", vlast)]
    B.rw_ready = vlast
    return p.cnt["dg"]


def emit_moe_routed(nc, p, B, W, ln_done, small_dg, scr):
    import dataclasses as _dc
    p2h = B.pb8h[:].bitcast(BF16)

    for tt in range(16):
        bk = 5 + tt % 2
        lb = B.pb[bk]
        w = list(p.bank_free[bk]) + [("v", ln_done[tt // 4])]
        if tt == 0:
            w += [("v", B.rw_ready), ("dg", small_dg), ("dx", B.b1c_dx)]
        for c in range(NCH):
            p.mm(lb[:, 0:32],
                 B.XR[:, c * T + tt * 128: c * T + (tt + 1) * 128],
                 B.rwt[:, c * 32:(c + 1) * 32],
                 start=(c == 0), stop=False, waits=w if c == 0 else ())
        p.mm(lb[:, 0:32], B.svarbc[0:1, tt * 128:(tt + 1) * 128],
             B.rbrow[0:1, 0:32], start=False, stop=True, inc="pe")
        pe_l = p.cnt["pe"]
        p.vop("tensor_reduce", B.lmax[:], lb[:, 0:32], axis=AX.X,
              op=OP.max, waits=[("pe", pe_l)])
        p.vop("tensor_scalar", B.lsel[:], lb[:, 0:32], B.lmax[:], None, OP.is_equal)
        p.vop("scalar_tensor_tensor", B.lsel[:], B.lsel[:], -10000.0, B.iotarow32[:],
              OP.mult, OP.add)
        p.vop("tensor_reduce", B.lmax[:], B.lsel[:], axis=AX.X, op=OP.min)
        p.vop("tensor_scalar", B.idxcols[:, tt:tt + 1], B.lmax[:], 10000.0,
              None, OP.add)
        vr = p.vop("tensor_scalar", B.Mtm[:, tt * 32:(tt + 1) * 32], B.iotarow32[:],
                   B.idxcols[:, tt:tt + 1], None, OP.is_equal)
        p.bank_free[bk] = [("v", vr)]
    v_idx = p.cnt["v"]

    st_dma = {}
    trp = {}
    for tile in range(16):
        par = tile % 2
        pbase = (4 + par) * 1024
        pw = [("v", v_idx)] if tile == 0 else []
        if tile >= 2:
            pw += p.bank_free[4 + par]
        for c in range(NCH):
            trp[par] = p.tr(p2h[:, pbase + c * 128: pbase + (c + 1) * 128],
                            B.XN[:, c * T + tile * 128: c * T + tile * 128 + 128],
                            B.ident[:], waits=pw if c == 0 else (),
                            inc="pe" if c == NCH - 1 else None)
        vw = [("pe", trp[par])]
        if tile >= 2:
            vw.append(("do", st_dma[tile - 2]))
        vv = p.vop("tensor_copy", B.xtms[:, par * 1024:(par + 1) * 1024],
                   p2h[:, pbase:pbase + 1024], waits=vw)
        p.bank_free[4 + par] = [("v", vv)]
        st_dma[tile] = p.dma("sync", scr["xn_tm"][tile * 128:(tile + 1) * 128, :],
                             B.xtms[:, par * 1024:(par + 1) * 1024], inc="do",
                             waits=[("v", vv)])
    xn_tm_done = p.cnt["do"]

    p.wait("tensor", "v", v_idx)
    for w_ in p.bank_free[0] + p.bank_free[1]:
        p.wait("tensor", w_[0], w_[1])
    for tile in range(16):
        p.tr(p2h[0:32, tile * 128:(tile + 1) * 128],
             B.Mtm[:, tile * 32:(tile + 1) * 32], B.ident[:],
             inc="pe" if tile == 15 else None)
    vmt = p.vop("tensor_copy", B.MT[:], p2h[0:32, 0:2048],
                waits=[("pe", p.cnt["pe"])])
    p.bank_free[0] = p.bank_free[1] = [("v", vmt)]

    g_l = {}
    mm_l = {}
    for c in range(16):
        par = c % 2
        g_l[c] = p.gop("affine_select", B.Ltile[:, par * 2048:(par + 1) * 2048],
                       B.onesbig[:], [[1, 2048]], OP.is_gt, 0.0,
                       base=-(c * 128), channel_multiplier=-1,
                       waits=[("pe", mm_l[c - 2])] if c >= 2 else ())
        w = [("g", g_l[c])]
        if c == 0:
            w += p.bank_free[0] + p.bank_free[1] + p.bank_free[2] + p.bank_free[3]
        for n in range(4):
            mm_l[c] = p.mm(B.pb8[0:32, n * 512:(n + 1) * 512],
                           B.Mtm[:, c * 32:(c + 1) * 32],
                           B.Ltile[:, par * 2048 + n * 512: par * 2048 + (n + 1) * 512],
                           start=(c == 0), stop=(c == 15),
                           waits=w if n == 0 else (), inc="pe")
    pe_rank = p.cnt["pe"]
    p.vop("tensor_scalar", B.ranksb[:], B.pb8[0:32, 0:2048], 127.0, None, OP.min,
          waits=[("pe", pe_rank)])
    v_mt = p.vop("tensor_tensor", B.ranksb[:], B.ranksb[:], B.MT[:], OP.mult)
    for i in range(4):
        p.bank_free[i] = [("v", v_mt)]
    p.wait("tensor", "v", v_mt)
    for i in range(4, 8):
        for w_ in p.bank_free[i]:
            p.wait("tensor", w_[0], w_[1])
        p.bank_free[i] = []
    for n in range(4):
        p.mm(B.pb8[0:1, n * 512:(n + 1) * 512], B.ones128[0:32, 0:1],
             B.ranksb[:, n * 512:(n + 1) * 512], start=True, stop=True,
             inc="pe" if n == 3 else None)
        p.mm(B.pb8[0:1, 2048 + n * 512:2048 + (n + 1) * 512], B.iote[:],
             B.MT[:, n * 512:(n + 1) * 512], start=True, stop=True,
             inc="pe" if n == 3 else None)
    p.vop("tensor_copy", B.rrow[:], B.pb8[0:1, 0:2048], waits=[("pe", p.cnt["pe"])])
    p.vop("tensor_copy", B.irow[:], B.pb8[0:1, 2048:4096])
    p.vop("tensor_scalar", B.rrow[:], B.rrow[:], 32.0, None, OP.mult)
    v_srow = p.vop("tensor_tensor", B.srow[:], B.rrow[:], B.irow[:], OP.add)
    for i in range(4):
        p.bank_free[i] = [("v", v_srow)]
    p.wait("tensor", "v", v_srow)
    for tile in range(16):
        p.tr(bass.AP(B.pb8h, tile, [[4096, 128], [1, 1]]),
             B.srow[0:1, tile * 128:(tile + 1) * 128], B.identf[0:1, 0:1],
             inc="pe" if tile == 15 else None)
    p.vop("tensor_copy", B.scoli[:],
          bass.AP(B.pb8h, 0, [[4096, 128], [1, 16]]),
          waits=[("pe", p.cnt["pe"])])
    vsc = p.cnt["v"]
    p.bank_free[0] = [("v", vsc)]
    g0 = p.gop("memset", B.tokids[:], 1 << 20)
    gi = None
    for hf in range(2):
        gi = p.dma("gpsimd",
                   scr["perm2"].rearrange("(r e) one -> r (e one)", r=128)[:, hf * 16:(hf + 1) * 16],
                   B.tokids[:], inc="dg", waits=[("g", g0)] if hf == 0 else ())
    g1 = p.gop("iota", B.tokids[:], [[128, 16]], base=0, channel_multiplier=1,
               allow_small_or_imprecise_dtypes=True, waits=[("dg", gi)])
    gsc = None
    for tile in range(16):
        gsc = p.idma("gpsimd", scr["perm2"],
                     bass.IndirectOffsetOnAxis(ap=B.scoli[:, tile:tile + 1], axis=0),
                     B.tokids[:, tile:tile + 1], None, inc="dg",
                     waits=[("g", g1), ("v", vsc)] if tile == 0 else ())
    gld = p.dma("sync", B.permsb[:],
                scr["perm2"].rearrange("(r e) one -> r (e one)", r=128),
                inc="dx", waits=[("dg", gsc)])

    w1_rd = {}
    w2_rd = {}
    xg_rd = {}
    for e in range(32):
        par = e % 2
        bko = 0 if par == 0 else 4
        dg1 = p.dma("gpsimd", B.w1sb[:, par * 2048:(par + 1) * 2048],
                    W["w1"][e].rearrange("(c p) j -> p c j", p=128),
                    inc="dg", waits=[("pe", w1_rd.get(par, 0))])
        dg2 = p.dma("gpsimd", B.w2sb[:, par * 2048:(par + 1) * 2048],
                    W["w2"][e].rearrange("(c p) o -> p c o", p=128),
                    inc="dg", waits=[("pe", w2_rd.get(par, 0))])
        gw = [("dx", gld), ("do", xn_tm_done)] if e == 0 else []
        gath = p.idma("gpsimd", B.xg[:, par * 1024:(par + 1) * 1024], None,
                      scr["xn_tm"],
                      bass.IndirectOffsetOnAxis(ap=B.permsb[:, e:e + 1], axis=0),
                      inc="dg", waits=gw + [("pe", xg_rd.get(par, 0))])
        for c in range(NCH):
            p.tr(p2h[:, bko * 1024 + c * 128: bko * 1024 + (c + 1) * 128],
                 B.xg[:, par * 1024 + c * 128: par * 1024 + (c + 1) * 128],
                 B.ident[:],
                 waits=([("dg", gath)] + p.bank_free[bko]) if c == 0 else (),
                 inc="pe" if c == NCH - 1 else None)
        xg_rd[par] = p.cnt["pe"]
        vxgt = p.vop("tensor_copy", B.xgT[:, par * 1024:(par + 1) * 1024],
                     p2h[:, bko * 1024:(bko + 1) * 1024],
                     waits=[("pe", p.cnt["pe"])])
        p.bank_free[bko] = [("v", vxgt)]
        p.wait("tensor", "v", vxgt)
        for jb in range(2):
            for ic in range(NCH):
                p.mm(B.pb8[:, (bko + 1) * 512 + jb * 128:(bko + 1) * 512 + (jb + 1) * 128],
                     B.w1sb[:, par * 2048 + ic * 256 + jb * 128:
                            par * 2048 + ic * 256 + (jb + 1) * 128],
                     B.xgT[:, par * 1024 + ic * 128: par * 1024 + (ic + 1) * 128],
                     start=(ic == 0), stop=(ic == NCH - 1),
                     waits=([("dg", dg1)] + p.bank_free[bko + 1])
                           if (jb == 0 and ic == 0) else (),
                     inc="pe" if ic == NCH - 1 else None)
        w1_rd[par] = p.cnt["pe"]
        sh = None
        for jb in range(2):
            sh = p.sop("activation",
                       B.hT[:, par * 256 + jb * 128: par * 256 + (jb + 1) * 128],
                       B.pb8[:, (bko + 1) * 512 + jb * 128:
                             (bko + 1) * 512 + (jb + 1) * 128],
                       AF.Relu, bias=B.b1c[:, e * 2 + jb: e * 2 + jb + 1],
                       waits=[("pe", w1_rd[par])])
        p.bank_free[bko + 1] = [("s", sh)]
        p.wait("tensor", "s", sh)
        for ob in range(8):
            obk = bko + 2 + ob // 4
            slot = ob % 4
            for jb in range(2):
                p.mm(B.pb8[:, obk * 512 + slot * 128: obk * 512 + (slot + 1) * 128],
                     B.w2sb[:, par * 2048 + jb * 1024 + ob * 128:
                            par * 2048 + jb * 1024 + (ob + 1) * 128],
                     B.hT[:, par * 256 + jb * 128: par * 256 + (jb + 1) * 128],
                     start=(jb == 0), stop=(jb == 1),
                     waits=([("dg", dg2)] + p.bank_free[obk])
                           if (slot == 0 and jb == 0) else (),
                     inc="pe" if jb == 1 else None)
        w2_rd[par] = p.cnt["pe"]
        sy = None
        for ob in range(8):
            obk = bko + 2 + ob // 4
            slot = ob % 4
            sy = p.sop("activation",
                       B.yT[:, par * 1024 + ob * 128: par * 1024 + (ob + 1) * 128],
                       B.pb8[:, obk * 512 + slot * 128: obk * 512 + (slot + 1) * 128],
                       AF.Relu, bias=B.b2t[:, e * 8 + ob: e * 8 + ob + 1],
                       waits=[("pe", w2_rd[par])])
        p.bank_free[bko + 2] = [("s", sy)]
        p.bank_free[bko + 3] = [("s", sy)]
        p.wait("tensor", "s", sy)
        for c in range(NCH):
            p.tr(p2h[:, bko * 1024 + c * 128: bko * 1024 + (c + 1) * 128],
                 B.yT[:, par * 1024 + c * 128: par * 1024 + (c + 1) * 128],
                 B.ident[:], waits=p.bank_free[bko] if c == 0 else (),
                 inc="pe" if c == NCH - 1 else None)
        vytm = p.vop("tensor_copy", B.ytm[:, par * 1024:(par + 1) * 1024],
                     p2h[:, bko * 1024:(bko + 1) * 1024],
                     waits=[("pe", p.cnt["pe"])])
        p.bank_free[bko] = [("v", vytm)]
        p.idma("gpsimd", scr["y_tm"],
               bass.IndirectOffsetOnAxis(ap=B.permsb[:, e:e + 1], axis=0),
               B.ytm[:, par * 1024:(par + 1) * 1024], None,
               inc="dg", waits=[("v", vytm)])
    y_done = p.cnt["dg"]

    ld = {}
    tr_rd = {}
    for tile in range(16):
        par = tile % 2
        bko = 0 if par == 0 else 4
        ld[tile] = p.dma("sync", B.xg[:, par * 1024:(par + 1) * 1024],
                         scr["y_tm"][tile * 128:(tile + 1) * 128, :], inc="dx",
                         waits=([("dg", y_done)] if tile == 0 else []) +
                               ([("pe", tr_rd[par])] if tile >= 2 else []))
        for c in range(NCH):
            tr_rd[par] = p.tr(
                p2h[:, bko * 1024 + c * 128: bko * 1024 + (c + 1) * 128],
                B.xg[:, par * 1024 + c * 128: par * 1024 + (c + 1) * 128],
                B.ident[:],
                waits=([("dx", ld[tile])] + p.bank_free[bko]) if c == 0 else (),
                inc="pe")
        vz = p.vop("scalar_tensor_tensor",
                   bass.AP(B.XR, tile * 128, [[16384, 128], [2048, 8], [1, 128]]),
                   _dc.replace(p2h, offset=bko * 1024, ap=[[8192, 128], [128, 8], [1, 128]]),
                   0.5,
                   bass.AP(B.XR, tile * 128, [[16384, 128], [2048, 8], [1, 128]]),
                   OP.mult, OP.add, waits=[("pe", tr_rd[par])])
        p.bank_free[bko] = [("v", vz)]
    return p.cnt["v"]


def moe_dram(nc):
    return {
        "rw_t": nc.dram_tensor("rw_t", [1024, 32], F32, kind="ExternalInput").ap(),
        "rb": nc.dram_tensor("rb", [1, 32], F32, kind="ExternalInput").ap(),
        "w1": nc.dram_tensor("w1", [32, 1024, 256], F32, kind="ExternalInput").ap(),
        "b1c": nc.dram_tensor("b1c", [128, 64], F32, kind="ExternalInput").ap(),
        "w2": nc.dram_tensor("w2", [32, 256, 1024], F32, kind="ExternalInput").ap(),
        "b2": nc.dram_tensor("b2", [32, 1024], F32, kind="ExternalInput").ap(),
    }


def build_moe_program(debug=False):
    nc = bass.Bass("TRN2", target_bir_lowering=False)
    xin = nc.dram_tensor("xin", [D, T], F32, kind="ExternalInput").ap()
    W = moe_dram(nc)
    xout = nc.dram_tensor("xout", [D, T], F32, kind="ExternalOutput").ap()
    p = P(nc)
    B = make_common(nc, p)
    moe_buffers(nc, B)
    dbg = {}
    if debug:
        dbg["XN"] = nc.dram_tensor("dbg_XN", [128, NCH * T], BF16, kind="ExternalOutput").ap()
        dbg["idxbc"] = nc.dram_tensor("dbg_idxbc", [128, T], BF16, kind="ExternalOutput").ap()
        dbg["HT"] = nc.dram_tensor("dbg_HT", [128, 8 * 2 * 1024], BF16, kind="ExternalOutput").ap()
        dbg["Z2"] = nc.dram_tensor("dbg_Z2", [128, 8 * 1024], F32, kind="ExternalOutput").ap()
    B.dbg = dbg
    dx = load_x(nc, p, B, xin)
    small_dg = emit_moe_weight_preload(nc, p, B, W)
    ln_done = emit_ln(nc, p, B, dx)
    vend = emit_moe(nc, p, B, W, ln_done, small_dg)
    store_x(nc, p, B, xout, vend)
    p.realize()
    return nc


def moe_host_weights(inputs, prefix="L0_ffn"):
    return {
        "rw_t": np.ascontiguousarray(np.asarray(inputs[prefix + "_router_w"]).T),
        "rb": np.ascontiguousarray(np.asarray(inputs[prefix + "_router_b"])[None, :]),
        "w1": np.ascontiguousarray(np.asarray(inputs[prefix + "_weights1"])),
        "b1c": np.ascontiguousarray(np.asarray(inputs[prefix + "_bias1"]).reshape(32, 2, 128).transpose(2, 0, 1).reshape(128, 64)),
        "w2": np.ascontiguousarray(np.asarray(inputs[prefix + "_weights2"])),
        "b2": np.ascontiguousarray(np.asarray(inputs[prefix + "_bias2"])),
    }


def l1ffn_buffers(nc, B):
    SB = nc.alloc_sbuf_tensor
    B.GH = SB("GH", [128, 32768], BF16)
    B.Hv = B.XR[:].bitcast(BF16)
    B.wA = SB("wA", [128, 2 * 4096], BF16)
    B.wB = SB("wB", [128, 2 * 1024], BF16)
    B.wC = SB("wC", [128, 2 * 1024], BF16)
    B.s1c = SB("s1c", [128, 32], F32)
    B.s2c = SB("s2c", [128, 32], F32)
    B.sh1c = SB("sh1c", [128, 32], F32)
    B.sh2c = SB("sh2c", [128, 32], F32)
    B.s3c = SB("s3c", [128, 8], F32)
    B.s3h = SB("s3h", [128, 8], F32)



def emit_l1ffn_preload(nc, p, B, W):
    for nm in ("s1c", "s2c", "sh1c", "sh2c", "s3c"):
        p.dma("sync", getattr(B, nm)[:], W[nm], inc="dx")
    dxs = p.cnt["dx"]
    p.vop("tensor_scalar", B.s3h[:], B.s3c[:], 0.5, None, OP.mult,
          waits=[("dx", dxs)])
    return p.cnt["v"]


def emit_l1ffn(nc, p, B, W, xin, xout, ln_done, scales_v, hv_waits=()):
    wArd = {}
    wBrd = {}
    wCrd = {}
    for tch in range(2):
        t0 = tch * 1024
        for ob in range(32):
            db = ob % 2
            dg1 = p.dma("gpsimd", B.wB[:, db * 1024:(db + 1) * 1024],
                        W["w1"][ob], inc="dg", waits=[("pe", wBrd.get(db, 0))])
            dg2 = p.dma("gpsimd", B.wC[:, db * 1024:(db + 1) * 1024],
                        W["w2"][ob], inc="dg", waits=[("pe", wCrd.get(db, 0))])
            bk1 = [4 * db, 4 * db + 1]
            bk2 = [4 * db + 2, 4 * db + 3]
            for c in range(8):
                for n in range(2):
                    w = []
                    if c == 0:
                        w = [("dg", dg1), ("v", ln_done[tch * 2 + n])] + p.bank_free[bk1[n]]
                        if tch == 0 and ob == 0 and n == 0:
                            w.append(("v", scales_v))
                    p.mm(B.pb[bk1[n]][:, 0:512],
                         B.wB[:, db * 1024 + c * 128:(db * 1024) + (c + 1) * 128],
                         B.XN[:, c * T + t0 + n * 512: c * T + t0 + (n + 1) * 512],
                         start=(c == 0), stop=(c == 7), waits=w,
                         inc="pe" if c == 7 else None)
            wBrd[db] = p.cnt["pe"]
            pe1 = [p.cnt["pe"] - 1, p.cnt["pe"]]
            for c in range(8):
                for n in range(2):
                    w = []
                    if c == 0:
                        w = [("dg", dg2)] + p.bank_free[bk2[n]]
                    p.mm(B.pb[bk2[n]][:, 0:512],
                         B.wC[:, db * 1024 + c * 128:(db * 1024) + (c + 1) * 128],
                         B.XN[:, c * T + t0 + n * 512: c * T + t0 + (n + 1) * 512],
                         start=(c == 0), stop=(c == 7), waits=w,
                         inc="pe" if c == 7 else None)
            wCrd[db] = p.cnt["pe"]
            pe2 = [p.cnt["pe"] - 1, p.cnt["pe"]]
            silut = [B.meanbc, B.msqbc]
            ssl = []
            for n in range(2):
                sw = [("pe", pe1[n]), ("v", p.cnt["v"])]
                if tch == 0 and ob == 0:
                    sw.append(("v", scales_v))
                ssl.append(p.sop("activation", silut[n][:, 0:512],
                                 B.pb[bk1[n]][:, 0:512], AF.Silu,
                                 scale=B.s1c[:, ob:ob + 1], waits=sw))
            vh = None
            for n in range(2):
                hw = [("pe", pe2[n]), ("s", ssl[n])]
                if tch == 0 and ob == 0 and n == 0:
                    hw += list(hv_waits)
                vh = p.vop("scalar_tensor_tensor",
                           B.Hv[:, ob * 1024 + n * 512: ob * 1024 + (n + 1) * 512],
                           B.pb[bk2[n]][:, 0:512], B.s2c[:, ob:ob + 1],
                           silut[n][:, 0:512], OP.mult, OP.mult,
                           waits=hw)
                p.bank_free[bk2[n]] = [("v", vh)]
            p.bank_free[bk1[0]] = [("v", vh)]
            p.bank_free[bk1[1]] = [("v", vh)]
        v_h = p.cnt["v"]
        for chain, (wkey, scol, src, dst) in enumerate((
                ("h1", B.sh1c, B.Hv, B.GH), ("h2", B.sh2c, B.GH, B.Hv))):
            s_rel = None
            for ob in range(32):
                db = ob % 2
                dga = p.dma("gpsimd", B.wA[:, db * 4096:(db + 1) * 4096],
                            W[wkey][ob], inc="dg", waits=[("pe", wArd.get(db, 0))])
                bks = [4 * db, 4 * db + 1]
                for c in range(32):
                    for n in range(2):
                        w = []
                        if c == 0:
                            w = [("dg", dga)] + p.bank_free[bks[n]]
                            if ob == 0 and n == 0:
                                if chain == 0:
                                    w.append(("v", v_h))
                                else:
                                    w.append(("s", s_gel))
                        p.mm(B.pb[bks[n]][:, 0:512],
                             B.wA[:, db * 4096 + c * 128:(db * 4096) + (c + 1) * 128],
                             src[:, c * 1024 + n * 512: c * 1024 + (n + 1) * 512],
                             start=(c == 0), stop=(c == 31), waits=w,
                             inc="pe" if c == 31 else None)
                wArd[db] = p.cnt["pe"]
                pes = [p.cnt["pe"] - 1, p.cnt["pe"]]
                for n in range(2):
                    s_rel = p.sop("activation",
                                  dst[:, ob * 1024 + n * 512: ob * 1024 + (n + 1) * 512],
                                  B.pb[bks[n]][:, 0:512], AF.Gelu,
                                  scale=scol[:, ob:ob + 1], waits=[("pe", pes[n])])
                    p.bank_free[bks[n]] = [("s", s_rel)]
            s_gel = s_rel
        for ob in range(8):
            db = ob % 2
            dga = p.dma("gpsimd", B.wA[:, db * 4096:(db + 1) * 4096],
                        W["w3"][ob], inc="dg", waits=[("pe", wArd.get(db, 0))])
            bks = [4 * db, 4 * db + 1]
            for c in range(32):
                for n in range(2):
                    w = []
                    if c == 0:
                        w = [("dg", dga)] + p.bank_free[bks[n]]
                        if ob == 0 and n == 0:
                            w.append(("s", s_gel))
                    p.mm(B.pb[bks[n]][:, 0:512],
                         B.wA[:, db * 4096 + c * 128:(db * 4096) + (c + 1) * 128],
                         B.Hv[:, c * 1024 + n * 512: c * 1024 + (n + 1) * 512],
                         start=(c == 0), stop=(c == 31), waits=w,
                         inc="pe" if c == 31 else None)
            wArd[db] = p.cnt["pe"]
            pes = [p.cnt["pe"] - 1, p.cnt["pe"]]
            for n in range(2):
                dxr = p.dma("sync", B.ytmp[:, 0:512],
                            xin[ob * 128:(ob + 1) * 128, t0 + n * 512:t0 + (n + 1) * 512],
                            inc="dx", waits=[("do", p.cnt["do"]), ("v", p.cnt["v"])])
                vr = p.vop("scalar_tensor_tensor", B.ytmp[:, 0:512],
                           B.pb[bks[n]][:, 0:512], B.s3h[:, ob:ob + 1],
                           B.ytmp[:, 0:512], OP.mult, OP.add,
                           waits=[("pe", pes[n]), ("dx", dxr)])
                p.bank_free[bks[n]] = [("v", vr)]
                p.dma("sync", xout[ob * 128:(ob + 1) * 128, t0 + n * 512:t0 + (n + 1) * 512],
                      B.ytmp[:, 0:512], inc="do", waits=[("v", vr)])
    p.wait("sync", "do", p.cnt["do"])
    return p.cnt["v"]


def l1ffn_dram(nc):
    return {
        "w1": nc.dram_tensor("f_w1", [32, 128, 8, 128], F32, kind="ExternalInput").ap(),
        "w2": nc.dram_tensor("f_w2", [32, 128, 8, 128], F32, kind="ExternalInput").ap(),
        "h1": nc.dram_tensor("f_h1", [32, 128, 32, 128], F32, kind="ExternalInput").ap(),
        "h2": nc.dram_tensor("f_h2", [32, 128, 32, 128], F32, kind="ExternalInput").ap(),
        "w3": nc.dram_tensor("f_w3", [8, 128, 32, 128], F32, kind="ExternalInput").ap(),
        "s1c": nc.dram_tensor("f_s1c", [128, 32], F32, kind="ExternalInput").ap(),
        "s2c": nc.dram_tensor("f_s2c", [128, 32], F32, kind="ExternalInput").ap(),
        "sh1c": nc.dram_tensor("f_sh1c", [128, 32], F32, kind="ExternalInput").ap(),
        "sh2c": nc.dram_tensor("f_sh2c", [128, 32], F32, kind="ExternalInput").ap(),
        "s3c": nc.dram_tensor("f_s3c", [128, 8], F32, kind="ExternalInput").ap(),
    }


def build_l1ffn_program():
    nc = bass.Bass("TRN2", target_bir_lowering=False)
    xin = nc.dram_tensor("xin", [D, T], F32, kind="ExternalInput").ap()
    W = l1ffn_dram(nc)
    xout = nc.dram_tensor("xout", [D, T], F32, kind="ExternalOutput").ap()
    p = P(nc)
    B = make_common(nc, p)
    l1ffn_buffers(nc, B)
    dx = load_x(nc, p, B, xin)
    scales_v = emit_l1ffn_preload(nc, p, B, W)
    ln_done = emit_ln(nc, p, B, dx, svar=False)
    emit_l1ffn(nc, p, B, W, xin, xout, ln_done, scales_v)
    p.realize()
    return nc


def _tile_oc(w, P=128, Q=128):
    O, I = w.shape
    t = w.reshape(O // Q, Q, I // P, P)
    return np.ascontiguousarray(t.transpose(0, 3, 2, 1))


def l1ffn_host_weights(inputs, prefix="L1_ffn"):
    g = lambda n: np.asarray(inputs[prefix + "_" + n])
    colmaj = lambda s, nb: np.ascontiguousarray(s.reshape(nb, 128).T)
    return {
        "f_w1": _tile_oc(g("w1_w")), "f_w2": _tile_oc(g("w2_w")),
        "f_h1": _tile_oc(g("h1_w")), "f_h2": _tile_oc(g("h2_w")),
        "f_w3": _tile_oc(g("w3_w")),
        "f_s1c": colmaj(g("w1_s"), 32), "f_s2c": colmaj(g("w2_s"), 32),
        "f_sh1c": colmaj(g("h1_s"), 32), "f_sh2c": colmaj(g("h2_s"), 32),
        "f_s3c": colmaj(g("w3_s"), 8),
    }


def l1attn_buffers(nc, B):
    SB = nc.alloc_sbuf_tensor
    B.zbs = B.ytmp[:, 0:128]
    B.QTs = SB("QTs", [64, 16 * 512], BF16)
    B.KTs = SB("KTs", [64, 16 * 512], BF16)
    B.VA = SB("VA", [128, 4 * 16 * 128], BF16)
    B.E2 = SB("E2", [128, 2 * 2048], BF16)
    B.CTs = SB("CTs", [128, 8 * 512], BF16)
    B.ATTs = SB("ATTs", [128, 8 * 512], F32)
    B.WS = SB("WS", [128, 2 * 1024], BF16)
    B.WV = SB("WV", [128, 2 * 1024], BF16)
    B.bqc = SB("bqc", [128, 8], F32)
    B.bkc = SB("bkc", [128, 8], F32)
    B.boc = SB("boc", [128, 8], F32)
    B.bvrow = SB("bvrow", [1, 1024], BF16)
    B.gamc = SB("gamc", [128, 8], F32)
    B.betc = SB("betc", [128, 8], F32)
    B.neg4c = SB("neg4c", [128, 1], F32)
    B.rz = SB("rz", [1, 512], F32)


def emit_l1attn_preload(nc, p, B, W):
    for nm in ("bqc", "bkc", "boc", "gamc", "betc"):
        p.dma("sync", getattr(B, nm)[:], W[nm], inc="dx")
    p.dma("gpsimd", B.bvrow[:], W["bv"], inc="dg")
    p.gop("memset", B.neg4c[:], -4.0)
    return p.cnt["dx"], p.cnt["dg"], p.cnt["g"]


def emit_l1attn(nc, p, B, W, ln_done, pre):
    dxs, dgs, gs = pre
    ws_rd = {}
    wv_rd = {}
    p.wait("tensor", "dx", dxs)
    for sg in range(4):
        t0 = sg * 512
        lnv = ln_done[sg]
        g0 = p.gop("memset", B.VA[:], 0.0,
                   waits=[("pe", p.cnt["pe"])] if sg > 0 else ())
        va_ones = bass.AP(B.VA, 0, [[4 * 16 * 128, 128], [128, 64], [1, 1]])
        g1 = p.gop("memset", va_ones, 1.0)
        for ic in range(8):
            db = ic % 2
            dgw = p.dma("gpsimd", B.WV[:, db * 1024:(db + 1) * 1024],
                        W["wv"][ic * 128:(ic + 1) * 128, :], inc="dg",
                        waits=[("pe", wv_rd.get(db, 0))])
            for tt in range(4):
                for n in range(2):
                    bk = tt * 2 + n
                    w = []
                    if ic == 0:
                        w = [("dg", dgw), ("v", lnv)] + p.bank_free[bk]
                    else:
                        w = [("dg", dgw)] if (tt == 0 and n == 0) else []
                    mn = p.mm(B.pb[bk][:, 0:512],
                              B.XN[:, ic * T + t0 + tt * 128: ic * T + t0 + (tt + 1) * 128],
                              B.WV[:, db * 1024 + n * 512: db * 1024 + (n + 1) * 512],
                              start=(ic == 0), stop=False, waits=w,
                              inc="pe" if (tt == 3 and n == 1) else None)
                    if tt == 3 and n == 1:
                        wv_rd[db] = mn
        for tt in range(4):
            for n in range(2):
                bk = tt * 2 + n
                p.mm(B.pb[bk][:, 0:512], B.onesrow[0:1, 0:128],
                     B.bvrow[0:1, n * 512:(n + 1) * 512],
                     start=False, stop=True, inc="pe")
        wv_rd[0] = wv_rd[1] = pe_v = p.cnt["pe"]
        vva = None
        for tt in range(4):
            for n in range(2):
                bk = tt * 2 + n
                dst = bass.AP(B.VA, tt * 2048 + n * 8 * 128 + 64,
                              [[4 * 16 * 128, 128], [128, 8], [1, 64]])
                srcv = bass.AP(B.pb8h, bk * 512, [[4096, 128], [64, 8], [1, 64]])
                vva = p.vop("tensor_copy", dst, srcv,
                            waits=[("pe", pe_v), ("g", g1)])
                p.bank_free[bk] = [("v", vva)]
        qk_done = None
        for wi, (wkey, bcol, dstbuf) in enumerate((("wq", B.bqc, B.QTs),
                                                   ("wk", B.bkc, B.KTs))):
            for cb in range(8):
                db = (wi * 8 + cb) % 2
                dgw = p.dma("gpsimd", B.WS[:, db * 1024:(db + 1) * 1024],
                            W[wkey][cb], inc="dg", waits=[("pe", ws_rd.get(db, 0))])
                bk = (wi * 8 + cb) % 4
                for ic in range(8):
                    w = []
                    if ic == 0:
                        w = [("dg", dgw)] + p.bank_free[bk]
                    mn = p.mm(B.pb[bk][:, 0:512],
                              B.WS[:, db * 1024 + ic * 128: db * 1024 + (ic + 1) * 128],
                              B.XN[:, ic * T + t0: ic * T + t0 + 512],
                              start=(ic == 0), stop=(ic == 7), waits=w,
                              inc="pe")
                    ws_rd[db] = mn
                p.sop("activation", dstbuf[0:64, (2 * cb) * 512:(2 * cb + 1) * 512],
                      B.pb8[0:64, bk * 512:(bk + 1) * 512], AF.Identity,
                      bias=bcol[0:64, cb:cb + 1],
                      waits=[("pe", p.cnt["pe"])])
                qk_done = p.sop("activation",
                                dstbuf[0:64, (2 * cb + 1) * 512:(2 * cb + 2) * 512],
                                B.pb8[64:128, bk * 512:(bk + 1) * 512], AF.Identity,
                                bias=bcol[64:128, cb:cb + 1])
                p.bank_free[bk] = [("s", qk_done)]
        pend_norm = [None]

        def flush_norm(p=p, B=B):
            if pend_norm[0] is None:
                return
            vz_, cbk_, gi_, q4_ = pend_norm[0]
            pend_norm[0] = None
            p.mm(B.pb[4][:, 0:512], B.ones128f[0:1, 0:128], B.rz[0:1, 0:512],
                 start=True, stop=True, waits=[("v", vz_)] + p.bank_free[4],
                 inc="pe")
            vzb = p.vop("tensor_copy", B.ytmp[:], B.pb[4][:, 0:512],
                        waits=[("pe", p.cnt["pe"])])
            p.bank_free[4] = [("v", vzb)]
            vct = None
            for par in range(2):
                dst = bass.AP(B.CTs, par * 64 * 4096 + (q4_ * 2) * 512 + gi_ * 128,
                              [[4096, 64], [512, 2], [1, 128]])
                csrc = bass.AP(B.pb8h, cbk_ * 512 + 64 * 4096 + par * 128,
                               [[4096, 64], [256, 2], [1, 128]])
                zsrc = bass.AP(B.ytmp, 64 * 512 + par * 128,
                               [[512, 64], [256, 2], [1, 128]])
                vct = p.vop("tensor_tensor", dst, csrc, zsrc, OP.mult)
            p.bank_free[cbk_] = [("v", vct)]

        def emit_S(gi, cq):
            qv = B.QTs[0:64, cq * 512 + gi * 128: cq * 512 + (gi + 1) * 128]
            for ck in range(16):
                kv = B.KTs[0:64, ck * 512 + gi * 128: ck * 512 + (gi + 1) * 128]
                w = []
                if ck % 4 == 0:
                    w = list(p.bank_free[ck // 4])
                    if gi == 0 and cq == 0:
                        w.append(("s", qk_done))
                p.mm(B.pb8[:, ck * 128:(ck + 1) * 128],
                     kv, qv, start=True, stop=True, waits=w,
                     inc="pe" if ck in (7, 15) else None)
            pe_s = p.cnt["pe"]
            eslot = cq % 2
            se = None
            for half in range(2):
                se = p.sop("activation",
                           B.E2[:, eslot * 2048 + half * 1024:
                                eslot * 2048 + (half + 1) * 1024],
                           B.pb8[:, half * 1024:(half + 1) * 1024],
                           AF.Exp, scale=0.125, bias=B.neg4c[:, 0:1],
                           waits=[("pe", pe_s - 1 + half)])
            for i in range(4):
                p.bank_free[i] = [("s", se)]
            return se

        def emit_ctx(gi, cq, se):
            eslot = cq % 2
            cbk = 6 + (cq // 4) % 2
            slot = cq % 4
            for ck in range(16):
                w = []
                if ck == 0:
                    w = [("s", se)] + (p.bank_free[cbk] if slot == 0 else [])
                    if gi == 0 and cq == 0:
                        w += [("v", vva), ("g", g1)]
                p.mm(B.pb8[:, cbk * 512 + slot * 128: cbk * 512 + (slot + 1) * 128],
                     B.VA[:, (gi * 16 + ck) * 128:(gi * 16 + ck + 1) * 128],
                     B.E2[:, eslot * 2048 + ck * 128: eslot * 2048 + (ck + 1) * 128],
                     start=(ck == 0), stop=(ck == 15), waits=w,
                     inc="pe" if ck == 15 else None)
            if slot == 3:
                vz = p.vop("reciprocal", B.rz[:],
                           bass.AP(B.pb8h, cbk * 512, [[4096, 1], [1, 512]]),
                           waits=[("pe", p.cnt["pe"])])
                pend_norm[0] = (vz, cbk, gi, cq // 4)

        prev = None
        for gi in range(4):
            for cq in range(16):
                se = emit_S(gi, cq)
                if cq % 4 == 1:
                    flush_norm()
                if prev is not None:
                    emit_ctx(prev[0], prev[1], prev[2])
                prev = (gi, cq, se)
        emit_ctx(prev[0], prev[1], prev[2])
        flush_norm()
        v_ct = p.cnt["v"]
        if B.adbg and sg == B.dbg_sg:
            p.dma("sync", B.adbg["QTs"], B.QTs[:], inc="do", waits=[("v", v_ct)])
            p.dma("sync", B.adbg["KTs"], B.KTs[:], inc="do")
            p.dma("sync", B.adbg["VA"], B.VA[:], inc="do")
            p.dma("sync", B.adbg["CTs"], B.CTs[:], inc="do")
            p.dma("sync", B.adbg["E2"], B.E2[:], inc="do")
        for ob in range(8):
            db = ob % 2
            dgw = p.dma("gpsimd", B.WS[:, db * 1024:(db + 1) * 1024],
                        W["wo"][ob], inc="dg", waits=[("pe", ws_rd.get(db, 0))])
            bk = ob % 4
            for ec in range(8):
                w = []
                if ec == 0:
                    w = [("dg", dgw), ("v", v_ct)] + p.bank_free[bk]
                mn = p.mm(B.pb[bk][:, 0:512],
                          B.WS[:, db * 1024 + ec * 128: db * 1024 + (ec + 1) * 128],
                          B.CTs[:, ec * 512:(ec + 1) * 512],
                          start=(ec == 0), stop=(ec == 7), waits=w, inc="pe")
                ws_rd[db] = mn
            vo = p.vop("scalar_tensor_tensor", B.ATTs[:, ob * 512:(ob + 1) * 512],
                       B.pb[bk][:, 0:512], B.boc[:, ob:ob + 1],
                       B.XN[:, ob * T + t0: ob * T + t0 + 512], OP.add, OP.add,
                       waits=[("pe", p.cnt["pe"])] + ([("s", p.cnt["s"])] if ob == 0 else []))
            p.bank_free[bk] = [("v", vo)]
        if B.adbg and sg == B.dbg_sg:
            p.dma("sync", B.adbg["ATTs"], B.ATTs[:], inc="do", waits=[("v", p.cnt["v"])])
        v_att = p.cnt["v"]
        ia, ib = 4, 5
        ms = {}
        for c in range(NCH):
            db2 = c % 2
            sl = slice(db2 * 512, (db2 + 1) * 512)
            asl = B.ATTs[:, c * 512:(c + 1) * 512]
            w = [("pe", ms[c - 2])] if c >= 2 else []
            vcp = p.vop("tensor_copy", B.xbfs[:, sl], asl, waits=w)
            scp = p.sop("activation", B.xsqs[:, sl], asl, AF.Square,
                        waits=w + ([("v", v_att)] if c == 0 else []))
            w2_ = [("v", vcp), ("s", scp)]
            if c == 0:
                w2_ += p.bank_free[ia] + p.bank_free[ib]
            p.mm(B.pb[ia][:, 0:512], B.ones128[:], B.xbfs[:, sl],
                 start=(c == 0), stop=(c == NCH - 1), waits=w2_)
            ms[c] = p.mm(B.pb[ib][:, 0:512], B.ones128[:], B.xsqs[:, sl],
                         start=(c == 0), stop=(c == NCH - 1), inc="pe")
        pe_done = p.cnt["pe"]
        p.vop("tensor_scalar", B.meanbc[:], B.pb[ia][:, 0:512], 1.0 / D, None,
              OP.mult, waits=[("pe", pe_done)])
        p.vop("tensor_scalar", B.msqbc[:], B.pb[ib][:, 0:512], 1.0 / D, None, OP.mult)
        p.vop("tensor_tensor", B.rstdbc[:], B.meanbc[:], B.meanbc[:], OP.mult)
        vfree = p.vop("tensor_tensor", B.msqbc[:], B.msqbc[:], B.rstdbc[:], OP.subtract)
        p.bank_free[ia] = p.bank_free[ib] = [("v", vfree)]
        p.sop("activation", B.rstdbc[:], B.msqbc[:], AF.Ln, bias=B.epscol[:],
              waits=[("v", vfree)])
        ssq = p.sop("activation", B.rstdbc[:], B.rstdbc[:], AF.Exp, scale=-0.5)
        p.wait("vector", "s", ssq)
        for c in range(NCH):
            asl = B.ATTs[:, c * 512:(c + 1) * 512]
            xsl = B.XR[:, c * T + t0: c * T + t0 + 512]
            p.vop("tensor_tensor", B.ytmp[:], asl, B.meanbc[:], OP.subtract)
            p.vop("tensor_tensor", B.ytmp[:], B.ytmp[:], B.rstdbc[:], OP.mult)
            p.vop("tensor_scalar", B.ytmp[:], B.ytmp[:], B.gamc[:, c:c + 1],
                  B.betc[:, c:c + 1], OP.mult, OP.add)
            p.vop("tensor_tensor", xsl, xsl, B.ytmp[:], OP.add)
    return p.cnt["v"]


def l1attn_dram(nc):
    return {
        "wq": nc.dram_tensor("a_wq", [8, 128, 8, 128], F32, kind="ExternalInput").ap(),
        "wk": nc.dram_tensor("a_wk", [8, 128, 8, 128], F32, kind="ExternalInput").ap(),
        "wv": nc.dram_tensor("a_wv", [1024, 1024], F32, kind="ExternalInput").ap(),
        "wo": nc.dram_tensor("a_wo", [8, 128, 8, 128], F32, kind="ExternalInput").ap(),
        "bqc": nc.dram_tensor("a_bqc", [128, 8], F32, kind="ExternalInput").ap(),
        "bkc": nc.dram_tensor("a_bkc", [128, 8], F32, kind="ExternalInput").ap(),
        "bv": nc.dram_tensor("a_bv", [1, 1024], F32, kind="ExternalInput").ap(),
        "boc": nc.dram_tensor("a_boc", [128, 8], F32, kind="ExternalInput").ap(),
        "gamc": nc.dram_tensor("a_gamc", [128, 8], F32, kind="ExternalInput").ap(),
        "betc": nc.dram_tensor("a_betc", [128, 8], F32, kind="ExternalInput").ap(),
    }


def build_l1attn_program(debug=False, dbg_sg=0):
    nc = bass.Bass("TRN2", target_bir_lowering=False)
    xin = nc.dram_tensor("xin", [D, T], F32, kind="ExternalInput").ap()
    W = l1attn_dram(nc)
    xout = nc.dram_tensor("xout", [D, T], F32, kind="ExternalOutput").ap()
    p = P(nc)
    B = make_common(nc, p)
    l1attn_buffers(nc, B)
    B.adbg = {}
    B.dbg_sg = dbg_sg
    if debug:
        for nm, sh, dt in (("QTs", [64, 8192], BF16), ("KTs", [64, 8192], BF16),
                           ("VA", [128, 8192], BF16), ("CTs", [128, 4096], BF16),
                           ("ATTs", [128, 4096], F32), ("E2", [128, 4096], BF16)):
            B.adbg[nm] = nc.dram_tensor("dbg_" + nm, sh, dt, kind="ExternalOutput").ap()
    dx = load_x(nc, p, B, xin)
    pre = emit_l1attn_preload(nc, p, B, W)
    ln_done = emit_ln(nc, p, B, dx, svar=False)
    vend = emit_l1attn(nc, p, B, W, ln_done, pre)
    store_x(nc, p, B, xout, vend)
    p.realize()
    return nc


def l1attn_host_weights(inputs, prefix="L1_attn"):
    g = lambda n: np.asarray(inputs[prefix + "_" + n])
    colmaj = lambda s: np.ascontiguousarray(s.reshape(8, 128).T)
    def tile_nat(w):
        t = w.reshape(8, 128, 8, 128)
        return np.ascontiguousarray(w.T.reshape(8, 128, 8, 128).transpose(0, 3, 2, 1))
    def tile_lhsT(w):
        t = w.reshape(8, 128, 8, 128)
        return np.ascontiguousarray(t.transpose(2, 1, 0, 3))
    return {
        "a_wq": tile_lhsT(g("wq")), "a_wk": tile_lhsT(g("wk")),
        "a_wv": np.ascontiguousarray(g("wv")), "a_wo": tile_lhsT(g("wo")),
        "a_bqc": colmaj(g("bq")), "a_bkc": colmaj(g("bk")),
        "a_bv": np.ascontiguousarray(g("bv")[None, :]),
        "a_boc": colmaj(g("bo")), "a_gamc": colmaj(g("gamma")),
        "a_betc": colmaj(g("beta")),
    }


def l0attn_buffers(nc, B):
    SB = nc.alloc_sbuf_tensor
    B.EQs = SB("EQs", [128, 4 * 1024], BF16)
    B.EKs = SB("EKs", [128, 4 * 1024], BF16)
    B.Vs = SB("Vs", [128, 4 * 1024], BF16)
    B.QTi = SB("QTi", [64, 2048], BF16)
    B.KTi = SB("KTi", [64, 2048], BF16)
    B.VTi = SB("VTi", [64, 2048], BF16)
    B.At = SB("At", [16, 2048], BF16)
    B.Vr = SB("Vr", [16, 64 + 128 * 64], BF16)
    B.rQ = SB("rQ", [128, 16], F32)
    B.rK = SB("rK", [128, 64], F32)
    B.rKT = SB("rKT", [64, 128], F32)
    B.mxc = SB("mxc", [128, 1], F32)
    B.WQ3 = SB("WQ3", [128, 2 * 1024], BF16)


def emit_l0attn(nc, p, B, W, ln_done):
    w3_rd = {}
    ident = B.ident
    p.gop("memset", B.Vr[0:16, 0:64], 0.0)
    for sg in range(4):
        t0 = sg * 512
        for wi, wkey in enumerate(("wqT", "wkT", "wvT")):
            dst = (B.EQs, B.EKs, B.Vs)[wi]
            for ic in range(8):
                db = ic % 2
                dgw = p.dma("gpsimd", B.WQ3[:, db * 1024:(db + 1) * 1024],
                            W[wkey][ic * 128:(ic + 1) * 128, :], inc="dg",
                            waits=[("pe", w3_rd.get(db, 0))])
                for tt in range(4):
                    for n in range(2):
                        bk = tt * 2 + n
                        w = []
                        if ic == 0:
                            w = [("dg", dgw)] + p.bank_free[bk]
                            if wi == 0 and tt == 0 and n == 0:
                                w.append(("v", ln_done[sg]))
                        elif tt == 0 and n == 0:
                            w = [("dg", dgw)]
                        mn = p.mm(B.pb[bk][:, 0:512],
                                  B.XN[:, ic * T + t0 + tt * 128: ic * T + t0 + (tt + 1) * 128],
                                  B.WQ3[:, db * 1024 + n * 512: db * 1024 + (n + 1) * 512],
                                  start=(ic == 0), stop=(ic == 7), waits=w,
                                  inc="pe" if (tt == 3 and n == 1) or ic == 7 else None)
                        if tt == 3 and n == 1:
                            w3_rd[db] = mn
            for tt in range(4):
                pe_t = p.cnt["pe"]
                if wi < 2:
                    vmx = p.vop("tensor_reduce", B.mxc[:],
                                bass.AP(B.pb8h, tt * 1024, [[4096, 128], [1, 1024]]),
                                axis=AX.X, op=OP.max, waits=[("pe", pe_t)])
                    p.vop("tensor_scalar_mul", B.mxc[:], B.mxc[:], -1.0)
                    p.wait("scalar", "v", p.cnt["v"])
                    se = p.sop("activation", dst[:, tt * 1024:(tt + 1) * 1024],
                               bass.AP(B.pb8h, tt * 1024, [[4096, 128], [1, 1024]]),
                               AF.Exp, bias=B.mxc[:], waits=[("pe", pe_t)])
                    rel = [("s", se)]
                else:
                    sv = p.sop("activation", dst[:, tt * 1024:(tt + 1) * 1024],
                               bass.AP(B.pb8h, tt * 1024, [[4096, 128], [1, 1024]]),
                               AF.Identity, waits=[("pe", pe_t)])
                    rel = [("s", sv)]
                p.bank_free[tt * 2] = rel
                p.bank_free[tt * 2 + 1] = rel
        for tt in range(4):
            toff = t0 + tt * 128
            eq = B.EQs[:, tt * 1024:(tt + 1) * 1024]
            ek = B.EKs[:, tt * 1024:(tt + 1) * 1024]
            vv = B.Vs[:, tt * 1024:(tt + 1) * 1024]
            p.wait("vector", "s", p.cnt["s"])
            p.vop("tensor_reduce", B.rQ[:],
                  bass.AP(B.EQs, tt * 1024, [[4096, 128], [64, 16], [1, 64]]),
                  axis=AX.X, op=OP.add)
            p.vop("reciprocal", B.rQ[:], B.rQ[:])
            p.vop("tensor_tensor",
                  bass.AP(B.EQs, tt * 1024, [[4096, 128], [64, 16], [1, 64]]),
                  bass.AP(B.EQs, tt * 1024, [[4096, 128], [64, 16], [1, 64]]),
                  bass.AP(B.rQ, 0, [[16, 128], [1, 16], [0, 64]]), OP.mult)
            p.vop("tensor_reduce", B.rK[:],
                  bass.AP(B.EKs, tt * 1024, [[4096, 128], [1, 64], [64, 16]]),
                  axis=AX.X, op=OP.add)
            vrk = p.vop("reciprocal", B.rK[:], B.rK[:])
            p2h = B.pb8h[:].bitcast(BF16)
            for src_, dstb, base in ((eq, B.QTi, 0), (ek, B.KTi, 2048), (vv, B.VTi, 4096)):
                pe_w = [("v", p.cnt["v"])]
                for h in range(16):
                    p.tr(p2h[0:64, base + h * 128:base + (h + 1) * 128],
                         src_[:, h * 64:(h + 1) * 64], ident[:],
                         waits=pe_w if h == 0 else (),
                         inc="pe" if h == 15 else None)
            p.tr(bass.AP(B.pb8h, 3072, [[4096, 64], [1, 128]]),
                 B.rK[:], B.identf[:], waits=[("v", vrk)], inc="pe")
            pe_tr = p.cnt["pe"]
            import dataclasses as _dc
            def _pview(off_bf16, pat):
                return _dc.replace(p2h, offset=off_bf16, ap=pat)
            vr1 = p.vop("tensor_copy", B.rKT[:],
                        bass.AP(B.pb8h, 3072, [[4096, 64], [1, 128]]),
                        waits=[("pe", pe_tr)])
            p.vop("tensor_copy",
                  bass.AP(B.QTi, 0, [[2048, 64], [1, 16], [16, 128]]),
                  _pview(0, [[8192, 64], [128, 16], [1, 128]]))
            v_qk = p.vop("scalar_tensor_tensor",
                         bass.AP(B.KTi, 0, [[2048, 64], [1, 16], [16, 128]]),
                         _pview(2048, [[8192, 64], [128, 16], [1, 128]]),
                         1.0,
                         bass.AP(B.rKT, 0, [[128, 64], [0, 16], [1, 128]]),
                         OP.mult, OP.mult)
            v_v = p.vop("tensor_copy",
                        bass.AP(B.VTi, 0, [[2048, 64], [1, 16], [16, 128]]),
                        _pview(4096, [[8192, 64], [128, 16], [1, 128]]))
            p.wait("tensor", "v", v_qk)
            for half in range(2):
                abase = 3072 if half == 0 else 2048
                bks = (6, 7) if half == 0 else (4, 5)
                for ti in range(64):
                    tok = half * 64 + ti
                    w0 = []
                    if ti == 0:
                        w0 = p.bank_free[bks[0]] + p.bank_free[bks[1]]
                        w0.append(("v", vr1 if half == 0 else v_v))
                    p.mm(bass.AP(B.pb8h, abase + ti * 16, [[4096, 16], [1, 16]]),
                         B.KTi[0:64, tok * 16:(tok + 1) * 16],
                         B.QTi[0:64, tok * 16:(tok + 1) * 16],
                         start=True, stop=True, waits=w0,
                         inc="pe" if ti == 63 else None)
                vsrc = bass.AP(B.pb8h, abase, [[4096, 16], [16, 64], [1, 2], [2, 8]])
                vdst = bass.AP(B.At, half * 1024, [[2048, 16], [16, 64], [8, 2], [1, 8]])
                va = p.vop("tensor_copy", vdst, vsrc, waits=[("pe", p.cnt["pe"])])
                p.bank_free[bks[0]] = [("v", va)]
                p.bank_free[bks[1]] = [("v", va)]
            for grp in range(4):
                vb0 = (grp % 2) * 2
                for ti in range(32):
                    tok = grp * 32 + ti
                    p.tr(p2h[0:16, vb0 * 1024 + ti * 64:vb0 * 1024 + (ti + 1) * 64],
                         B.VTi[0:64, tok * 16:(tok + 1) * 16], ident[0:64, 0:64],
                         waits=((p.bank_free[vb0] + p.bank_free[vb0 + 1]) if ti == 0 else ()),
                         inc="pe" if ti == 31 else None)
                vv2 = p.vop("tensor_copy",
                            B.Vr[0:16, 64 + grp * 2048: 64 + (grp + 1) * 2048],
                            p2h[0:16, vb0 * 1024:vb0 * 1024 + 2048],
                            waits=[("pe", p.cnt["pe"])])
                p.bank_free[vb0] = [("v", vv2)]
                p.bank_free[vb0 + 1] = [("v", vv2)]
            v_vr = p.cnt["v"]
            p.wait("tensor", "v", v_vr)
            for half in range(2):
                ebk, obk = (0, 1) if half == 0 else (2, 3)
                for ti in range(64):
                    tok = half * 64 + ti
                    w_e = (p.bank_free[ebk] + p.bank_free[obk]) if ti == 0 else ()
                    p.mm(bass.AP(B.pb8h, ebk * 512 + ti * 8, [[4096, 64], [1, 8]]),
                         B.Vr[0:16, 64 + tok * 64: 64 + (tok + 1) * 64],
                         B.At[0:16, tok * 16: tok * 16 + 8],
                         start=True, stop=True, waits=w_e)
                    p.mm(bass.AP(B.pb8h, obk * 512 + ti * 8, [[4096, 128], [1, 8]]),
                         B.Vr[0:16, tok * 64: tok * 64 + 128],
                         B.At[0:16, tok * 16 + 8: tok * 16 + 16],
                         start=True, stop=True,
                         inc="pe" if ti == 63 else None)
                pe_s2 = p.cnt["pe"]
                vlast = p.vop("scalar_tensor_tensor",
                              bass.AP(B.XR, toff + half * 64, [[16384, 64], [2048, 8], [1, 64]]),
                              bass.AP(B.pb8h, ebk * 512, [[4096, 64], [1, 8], [8, 64]]),
                              1.0,
                              bass.AP(B.XR, toff + half * 64, [[16384, 64], [2048, 8], [1, 64]]),
                              OP.mult, OP.add, waits=[("pe", pe_s2)])
                vlast = p.vop("scalar_tensor_tensor",
                              bass.AP(B.XR, 64 * 16384 + toff + half * 64,
                                      [[16384, 64], [2048, 8], [1, 64]]),
                              bass.AP(B.pb8h, obk * 512 + 64 * 4096, [[4096, 64], [1, 8], [8, 64]]),
                              1.0,
                              bass.AP(B.XR, 64 * 16384 + toff + half * 64,
                                      [[16384, 64], [2048, 8], [1, 64]]),
                              OP.mult, OP.add)
                p.bank_free[ebk] = [("v", vlast)]
                p.bank_free[obk] = [("v", vlast)]
    return p.cnt["v"]


def l0attn_dram(nc):
    return {
        "wqT": nc.dram_tensor("l0_wqT", [1024, 1024], F32, kind="ExternalInput").ap(),
        "wkT": nc.dram_tensor("l0_wkT", [1024, 1024], F32, kind="ExternalInput").ap(),
        "wvT": nc.dram_tensor("l0_wvT", [1024, 1024], F32, kind="ExternalInput").ap(),
    }


def build_l0attn_program():
    nc = bass.Bass("TRN2", target_bir_lowering=False)
    xin = nc.dram_tensor("xin", [D, T], F32, kind="ExternalInput").ap()
    W = l0attn_dram(nc)
    xout = nc.dram_tensor("xout", [D, T], F32, kind="ExternalOutput").ap()
    p = P(nc)
    B = make_common(nc, p)
    l0attn_buffers(nc, B)
    dx = load_x(nc, p, B, xin)
    ln_done = emit_ln(nc, p, B, dx, svar=False)
    vend = emit_l0attn(nc, p, B, W, ln_done)
    store_x(nc, p, B, xout, vend)
    p.realize()
    return nc


def l0attn_host_weights(inputs):
    return {
        "l0_wqT": np.ascontiguousarray(np.asarray(inputs["L0_attn_W_Q"]).T),
        "l0_wkT": np.ascontiguousarray(np.asarray(inputs["L0_attn_W_K"]).T),
        "l0_wvT": np.ascontiguousarray(np.asarray(inputs["L0_attn_W_V"]).T),
    }


_SBUF_TOP = 229344

_ARENA_GROUPS = {
    "moe": [("svarbc", [128, T], "BF16"), ("rwt", [128, NCH * 32], "F32"),
            ("rwsc", [128, 32], "F32"), ("rbrow", [1, 32], "BF16"),
            ("b1c", [128, 64], "F32"), ("b2t", [128, 256], "F32"),
            ("idxcols", [128, 16], "F32"), ("MT", [32, 2048], "BF16"),
            ("lmax", [128, 1], "F32"), ("lsel", [128, 32], "F32"),
            ("iote", [32, 1], "BF16"), ("Mtm", [128, 16 * 32], "BF16"),
            ("Ltile", [128, 2 * 2048], "BF16"), ("ranksb", [32, 2048], "BF16"),
            ("rrow", [1, 2048], "F32"), ("irow", [1, 2048], "F32"),
            ("srow", [1, 2048], "F32"),
            ("tokids", [128, 16], "I32"),
            ("xtms", [128, 2 * 1024], "BF16"), ("xg", [128, 2 * 1024], "BF16"),
            ("xgT", [128, 2 * 1024], "BF16"), ("hT", [128, 2 * 256], "BF16"),
            ("yT", [128, 2 * 1024], "BF16"), ("ytm", [128, 2 * 1024], "BF16"),
            ("w1sb", [128, 2 * 2048], "BF16"), ("w2sb", [128, 2 * 2048], "BF16")],
    "l0a": [("EQs", [128, 4 * 1024], "BF16"), ("EKs", [128, 4 * 1024], "BF16"),
            ("Vs", [128, 4 * 1024], "BF16"), ("QTi", [64, 2048], "BF16"),
            ("KTi", [64, 2048], "BF16"), ("VTi", [64, 2048], "BF16"),
            ("At", [16, 2048], "BF16"), ("Vr", [16, 64 + 128 * 64], "BF16"),
            ("rQ", [128, 16], "F32"), ("rK", [128, 64], "F32"),
            ("rKT", [64, 128], "F32"), ("mxc", [128, 1], "F32"),
            ("WQ3", [128, 2 * 1024], "BF16")],
    "l1f": [("GH", [128, 32768], "BF16"), ("wA", [128, 2 * 4096], "BF16"),
            ("wB", [128, 2 * 1024], "BF16"), ("wC", [128, 2 * 1024], "BF16"),
            ("s1c", [128, 32], "F32"), ("s2c", [128, 32], "F32"),
            ("sh1c", [128, 32], "F32"), ("sh2c", [128, 32], "F32"),
            ("s3c", [128, 8], "F32"), ("s3h", [128, 8], "F32")],
    "l1a": [("QTs", [64, 16 * 512], "BF16"), ("KTs", [64, 16 * 512], "BF16"),
            ("VA", [128, 4 * 16 * 128], "BF16"), ("E2", [128, 2 * 2048], "BF16"),
            ("CTs", [128, 8 * 512], "BF16"), ("ATTs", [128, 8 * 512], "F32"),
            ("WS", [128, 2 * 1024], "BF16"), ("WV", [128, 2 * 1024], "BF16"),
            ("bqc", [128, 8], "F32"), ("bkc", [128, 8], "F32"),
            ("boc", [128, 8], "F32"), ("bvrow", [1, 1024], "BF16"),
            ("gamc", [128, 8], "F32"), ("betc", [128, 8], "F32"),
            ("neg4c", [128, 1], "F32"), ("rz", [1, 512], "F32")],
}


def _alloc_arena(nc, base):
    dts = {"F32": F32, "BF16": BF16, "I32": I32}
    handles = {}
    maxend = base
    for phase, members in _ARENA_GROUPS.items():
        off = base
        hs = {}
        for name, shape, dt in members:
            nb = shape[-1] * (4 if dt == "F32" else 2)
            for s in shape[1:-1]:
                nb *= s
            off = (off + 31) & ~31
            hs[name] = nc.alloc_sbuf_tensor_at(phase + "_" + name, shape, dts[dt],
                                               offset=off)
            off += nb
        assert off <= _SBUF_TOP, (phase, off)
        maxend = max(maxend, off)
        handles[phase] = hs
    return handles


def build_full_program(skip=()):
    nc = bass.Bass("TRN2", target_bir_lowering=False)
    xin = nc.dram_tensor("xin", [D, T], F32, kind="ExternalInput").ap()
    Wm = moer_dram(nc)
    W0 = l0attn_dram(nc)
    Wf = l1ffn_dram(nc)
    Wa = l1attn_dram(nc)
    xout = nc.dram_tensor("xout", [D, T], F32, kind="ExternalOutput").ap()
    xsp = nc.dram_tensor("xspill", [D, T], F32, kind="Internal").ap()
    scr = {
        "xn_tm": nc.dram_tensor("s_xn_tm", [T, D], BF16, kind="Internal").ap(),
        "y_tm": nc.dram_tensor("s_y_tm", [T, D], BF16, kind="Internal").ap(),
        "perm2": nc.dram_tensor("s_perm2", [32 * 128, 1], I32, kind="Internal").ap(),
    }
    p = P(nc)
    B = make_common(nc, p)
    B.scoli = nc.alloc_sbuf_tensor("fx_scoli", [128, 16], I32)
    B.permsb = nc.alloc_sbuf_tensor("fx_permsb", [128, 32], I32)
    base = _SBUF_TOP - nc.sbuf_bytes_remaining
    handles = _alloc_arena(nc, base)

    def bind(phase):
        for name, h in handles[phase].items():
            setattr(B, name, h)
        if phase == "l1a":
            B.zbs = B.ytmp[:, 0:128]
        if phase == "l1f":
            B.Hv = B.XR[:].bitcast(BF16)

    dx = load_x(nc, p, B, xin)
    if 1 not in skip:
        bind("moe")
        B.rdbg = {}
        sdg = emit_moer_preload(nc, p, B, Wm)
        ln = emit_ln(nc, p, B, x_ready_dx=dx, svar=True)
        emit_moe_routed(nc, p, B, Wm, ln, sdg, scr)
        phase_barrier(p)
    if 2 not in skip:
        bind("l0a")
        ln = emit_ln(nc, p, B, x_ready_dx=dx, v_ready=p.cnt["v"], svar=False)
        emit_l0attn(nc, p, B, W0, ln)
        phase_barrier(p)
    if 3 not in skip:
        bind("moe")
        B.rdbg = {}
        sdg = emit_moer_preload(nc, p, B, Wm)
        ln = emit_ln(nc, p, B, x_ready_dx=dx, v_ready=p.cnt["v"], svar=True)
        emit_moe_routed(nc, p, B, Wm, ln, sdg, scr)
        phase_barrier(p)
    if 4 not in skip:
        bind("l1f")
        sv = emit_l1ffn_preload(nc, p, B, Wf)
        ln = emit_ln(nc, p, B, x_ready_dx=dx, v_ready=p.cnt["v"], svar=False)
        for c in range(NCH):
            p.dma("sync", xsp[c * 128:(c + 1) * 128, :], B.XR[:, c * T:(c + 1) * T],
                  inc="do")
        spill_do = p.cnt["do"]
        emit_l1ffn(nc, p, B, Wf, xsp, xsp, ln, sv, hv_waits=[("do", spill_do)])
        phase_barrier(p)
    if 5 not in skip:
        bind("l1a")
        if 4 not in skip:
            for c in range(NCH):
                p.dma("sync", B.XR[:, c * T:(c + 1) * T], xsp[c * 128:(c + 1) * 128, :],
                      inc="dx", waits=[("do", p.cnt["do"])] if c == 0 else ())
        dx2 = p.cnt["dx"]
        pre = emit_l1attn_preload(nc, p, B, Wa)
        ln = emit_ln(nc, p, B, x_ready_dx=dx2, svar=False)
        emit_l1attn(nc, p, B, Wa, ln, pre)
        phase_barrier(p)
    if 6 not in skip:
        bind("l1f")
        sv = emit_l1ffn_preload(nc, p, B, Wf)
        ln = emit_ln(nc, p, B, x_ready_dx=dx, v_ready=p.cnt["v"], svar=False)
        for c in range(NCH):
            p.dma("sync", xsp[c * 128:(c + 1) * 128, :], B.XR[:, c * T:(c + 1) * T],
                  inc="do")
        spill_do = p.cnt["do"]
        emit_l1ffn(nc, p, B, Wf, xsp, xout, ln, sv, hv_waits=[("do", spill_do)])
    else:
        store_x(nc, p, B, xout, p.cnt["v"])
    p.realize()
    return nc


def build_moer_program(debug=False):
    nc = bass.Bass("TRN2", target_bir_lowering=False)
    xin = nc.dram_tensor("xin", [D, T], F32, kind="ExternalInput").ap()
    W = moer_dram(nc)
    xout = nc.dram_tensor("xout", [D, T], F32, kind="ExternalOutput").ap()
    scr = {
        "xn_tm": nc.dram_tensor("s_xn_tm", [T, D], BF16, kind="Internal").ap(),
        "y_tm": nc.dram_tensor("s_y_tm", [T, D], BF16, kind="Internal").ap(),
        "perm2": nc.dram_tensor("s_perm2", [32 * 128, 1], I32, kind="Internal").ap(),
    }
    p = P(nc)
    B = make_common(nc, p)
    moer_buffers(nc, B)
    B.rdbg = {}
    if debug:
        for nm, sh, dt in (("srow", [1, 2048], F32), ("permsb", [128, 32], I32),
                           ("MT", [32, 2048], BF16), ("ranksb", [32, 2048], BF16),
                           ("xn_tm", [T, D], BF16), ("y_tm", [T, D], BF16),
                           ("xg", [128, 2048], BF16), ("hT", [128, 512], BF16)):
            B.rdbg[nm] = nc.dram_tensor("dbg_" + nm, sh, dt, kind="ExternalOutput").ap()
    dx = load_x(nc, p, B, xin)
    small_dg = emit_moer_preload(nc, p, B, W)
    ln_done = emit_ln(nc, p, B, x_ready_dx=dx, svar=True)
    vend = emit_moe_routed(nc, p, B, W, ln_done, small_dg, scr)
    if debug:
        p.dma("sync", B.rdbg["srow"], B.srow[:], inc="do", waits=[("v", vend)])
        p.dma("sync", B.rdbg["permsb"], B.permsb[:], inc="do")
        p.dma("sync", B.rdbg["MT"], B.MT[:], inc="do")
        p.dma("sync", B.rdbg["ranksb"], B.ranksb[:], inc="do")
        p.dma("sync", B.rdbg["xn_tm"], scr["xn_tm"], inc="do")
        p.dma("sync", B.rdbg["y_tm"], scr["y_tm"], inc="do")
        p.dma("sync", B.rdbg["xg"], B.xg[:], inc="do")
        p.dma("sync", B.rdbg["hT"], B.hT[:], inc="do")
    store_x(nc, p, B, xout, vend)
    p.realize()
    return nc


def moer_dram(nc):
    return {
        "rw_t": nc.dram_tensor("rw_t", [1024, 32], F32, kind="ExternalInput").ap(),
        "rb": nc.dram_tensor("rb", [1, 32], F32, kind="ExternalInput").ap(),
        "w1": nc.dram_tensor("w1", [32, 1024, 256], F32, kind="ExternalInput").ap(),
        "b1c": nc.dram_tensor("b1c", [128, 64], F32, kind="ExternalInput").ap(),
        "w2": nc.dram_tensor("w2", [32, 256, 1024], F32, kind="ExternalInput").ap(),
        "b2t": nc.dram_tensor("b2t", [128, 256], F32, kind="ExternalInput").ap(),
    }


def moer_host_weights(inputs, prefix="L0_ffn"):
    b2 = np.asarray(inputs[prefix + "_bias2"])
    return {
        "rw_t": np.ascontiguousarray(np.asarray(inputs[prefix + "_router_w"]).T),
        "rb": np.ascontiguousarray(np.asarray(inputs[prefix + "_router_b"])[None, :]),
        "w1": np.ascontiguousarray(np.asarray(inputs[prefix + "_weights1"])),
        "b1c": np.ascontiguousarray(np.asarray(inputs[prefix + "_bias1"]).reshape(32, 2, 128).transpose(2, 0, 1).reshape(128, 64)),
        "w2": np.ascontiguousarray(np.asarray(inputs[prefix + "_weights2"])),
        "b2t": np.ascontiguousarray(b2.reshape(32, 8, 128).transpose(2, 0, 1).reshape(128, 256)),
    }


_PROGRAMS = {}


def _get_program(name):
    if name not in _PROGRAMS:
        _PROGRAMS[name] = {"moe": build_moe_program, "l1ffn": build_l1ffn_program, "l1attn": build_l1attn_program, "l0attn": build_l0attn_program, "full": build_full_program, "moer": build_moer_program}[name]()
    return _PROGRAMS[name]


def kernel(**inputs):
    x = np.asarray(inputs["L0_attn_input_seq_embs"])
    xt = [np.ascontiguousarray(x[i].T) for i in range(8)]
    wmap = {}
    wmap.update(moer_host_weights(inputs))
    wmap.update(l0attn_host_weights(inputs))
    wmap.update(l1ffn_host_weights(inputs))
    wmap.update(l1attn_host_weights(inputs))
    nc = _get_program("full")
    in_maps = [dict(wmap, xin=xt[i]) for i in range(8)]
    res = run_bass_kernel_spmd(nc, in_maps, core_ids=list(range(8)))
    return np.stack([np.ascontiguousarray(r["xout"].T) for r in res.results])


# version 09
# speedup vs baseline: 1.0716x; 1.0101x over previous
import numpy as np
import ml_dtypes
import concourse.bass as bass
import concourse.mybir as mybir
from concourse.bass_utils import run_bass_kernel_spmd

F32 = mybir.dt.float32
BF16 = mybir.dt.bfloat16
I32 = mybir.dt.int32
AF = mybir.ActivationFunctionType
OP = mybir.AluOpType
AX = mybir.AxisListType

D = 1024
T = 2048
NCH = 8
NTC = 4
LN_EPS = 1e-5

ENGS = ("tensor", "vector", "scalar", "gpsimd", "sync")


class P:

    def __init__(self, nc):
        self.nc = nc
        self.q = {e: [] for e in ENGS}
        self.sems = {}
        self.cnt = {}
        for name in ("v", "s", "g", "pe", "dx", "dg", "do"):
            self.sems[name] = nc.alloc_semaphore("sem_" + name)
            self.cnt[name] = 0
        self.bank_free = [[] for _ in range(8)]

    def wait(self, eng, sem, val):
        if val <= 0:
            return
        s = self.sems[sem]
        self.q[eng].append(lambda E: E.wait_ge(s, val))

    def op(self, eng, meth, *args, inc=None, incby=1, waits=(), **kw):
        for sem, val in waits:
            self.wait(eng, sem, val)
        n = None
        if inc is not None:
            self.cnt[inc] += incby
            n = self.cnt[inc]
            s = self.sems[inc]

            def fn(E, meth=meth, args=args, kw=kw, s=s, incby=incby):
                getattr(E, meth)(*args, **kw).then_inc(s, incby)
        else:
            def fn(E, meth=meth, args=args, kw=kw):
                getattr(E, meth)(*args, **kw)
        self.q[eng].append(fn)
        return n

    def vop(self, meth, *args, waits=(), **kw):
        w = [("v", self.cnt["v"])] + list(waits)
        return self.op("vector", meth, *args, inc="v", waits=w, **kw)

    def sop(self, meth, *args, waits=(), **kw):
        w = [("s", self.cnt["s"])] + list(waits)
        return self.op("scalar", meth, *args, inc="s", waits=w, **kw)

    def gop(self, meth, *args, waits=(), **kw):
        w = [("g", self.cnt["g"])] + list(waits)
        return self.op("gpsimd", meth, *args, inc="g", waits=w, **kw)

    def mm(self, out, lhsT, rhs, start, stop, waits=(), inc=None, **kw):
        return self.op("tensor", "matmul", out, lhsT, rhs, start=start, stop=stop,
                       waits=waits, inc=inc, **kw)

    def tr(self, out, in_, ident, waits=(), inc=None):
        return self.op("tensor", "transpose", out, in_, ident, waits=waits, inc=inc)

    def dma(self, queue, out, in_, inc, waits=(), **kw):
        return self.op(queue, "dma_start", out=out, in_=in_, inc=inc, incby=16,
                       waits=waits, **kw)

    def idma(self, queue, out, out_off, in_, in_off, inc, waits=()):
        return self.op(queue, "indirect_dma_start", out, out_off, in_, in_off,
                       inc=inc, incby=16, waits=waits)

    def realize(self):
        with self.nc.Block() as block:
            for eng in ENGS:
                fns = self.q[eng]
                if fns:
                    getattr(block, eng)(lambda E, fns=fns: [f(E) for f in fns])
        self.q = {e: [] for e in ENGS}


def phase_barrier(p):
    snap = [(nm, p.cnt[nm]) for nm in ("v", "s", "g", "pe", "dx", "dg", "do")]
    for eng in ENGS:
        for nm, val in snap:
            p.wait(eng, nm, val)


class Buffers:
    dbg = {}
    adbg = {}
    dbg_sg = -1


def make_common(nc, p):
    B = Buffers()
    SB = nc.alloc_sbuf_tensor
    B.XR = SB("XR", [128, NCH * T], F32)
    B.XN = SB("XN", [128, NCH * T], BF16)
    B.ones128 = SB("ones128", [128, 128], BF16)
    B.ones128f = SB("ones128f", [128, 128], F32)
    B.onesbig = SB("onesbig", [128, 2048], BF16)
    B.onesrow = SB("onesrow", [1, 512], BF16)
    B.ident = SB("ident", [128, 128], BF16)
    B.identf = SB("identf", [128, 128], F32)
    B.iotacol = SB("iotacol", [128, 1], F32)
    B.epscol = SB("epscol", [128, 1], F32)
    B.iotarow32 = SB("iotarow32", [128, 32], F32)
    B.xbfs = SB("xbfs", [128, 2 * 512], BF16)
    B.xsqs = SB("xsqs", [128, 2 * 512], BF16)
    B.meanbc = SB("meanbc", [128, 512], F32)
    B.msqbc = SB("msqbc", [128, 512], F32)
    B.rstdbc = SB("rstdbc", [128, 512], F32)
    B.ytmp = SB("ytmp", [128, 512], F32)
    B.pb8h = nc.alloc_psum_tensor("pb8", [128, 4096], F32)
    B.pb8 = B.pb8h[:]
    B.pb = [B.pb8h[:, i * 512:(i + 1) * 512] for i in range(8)]
    p.gop("memset", B.ones128[:], 1.0)
    p.gop("memset", B.ones128f[:], 1.0)
    p.gop("memset", B.onesbig[:], 1.0)
    p.gop("memset", B.onesrow[:], 1.0)
    p.gop("memset", B.ident[:], 0.0)
    p.gop("affine_select", B.ident[:], B.ones128[:], [[1, 128]], OP.is_equal, 0.0,
          base=0, channel_multiplier=-1)
    p.gop("affine_select", B.identf[:], B.ones128f[:], [[1, 128]], OP.is_equal, 0.0,
          base=0, channel_multiplier=-1)
    p.gop("iota", B.iotacol[:], [[1, 1]], base=0, channel_multiplier=1,
          allow_small_or_imprecise_dtypes=True)
    p.gop("memset", B.epscol[:], LN_EPS)
    p.gop("iota", B.iotarow32[:], [[1, 32]], base=0, channel_multiplier=0,
          allow_small_or_imprecise_dtypes=True)
    return B


def load_x(nc, p, B, xin):
    for c in range(NCH):
        p.dma("sync", B.XR[:, c * T:(c + 1) * T], xin[c * 128:(c + 1) * 128, :], inc="dx")
    return p.cnt["dx"]


def store_x(nc, p, B, xout, vwait):
    for c in range(NCH):
        p.dma("sync", xout[c * 128:(c + 1) * 128, :], B.XR[:, c * T:(c + 1) * T],
              inc="do", waits=[("v", vwait)] if c == 0 else ())
    p.wait("sync", "do", p.cnt["do"])


def emit_ln(nc, p, B, x_ready_dx=0, v_ready=0, svar=True):
    done = []
    for tc in range(NTC):
        ia, ib = (2 * tc) % 4, (2 * tc) % 4 + 1
        bA, bB = B.pb[ia], B.pb[ib]
        ms = {}
        for c in range(NCH):
            db = c % 2
            sl = slice(db * 512, (db + 1) * 512)
            xsl = B.XR[:, c * T + tc * 512: c * T + (tc + 1) * 512]
            w = list([("dx", x_ready_dx), ("v", v_ready)] if (tc == 0 and c == 0) else [])
            if c >= 2:
                w.append(("pe", ms[c - 2]))
            vcp = p.vop("tensor_copy", B.xbfs[:, sl], xsl, waits=w)
            scp = p.sop("activation", B.xsqs[:, sl], xsl, AF.Square, waits=w)
            w2_ = [("v", vcp), ("s", scp)]
            if c == 0:
                w2_ += p.bank_free[ia] + p.bank_free[ib]
            p.mm(bA[:, 0:512], B.ones128[:], B.xbfs[:, sl],
                 start=(c == 0), stop=(c == NCH - 1), waits=w2_)
            ms[c] = p.mm(bB[:, 0:512], B.ones128[:], B.xsqs[:, sl],
                         start=(c == 0), stop=(c == NCH - 1), inc="pe")
        pe_done = p.cnt["pe"]
        p.vop("tensor_scalar", B.meanbc[:], bA[:, 0:512], 1.0 / D, None, OP.mult,
              waits=[("pe", pe_done)])
        p.vop("tensor_scalar", B.msqbc[:], bB[:, 0:512], 1.0 / D, None, OP.mult)
        p.vop("tensor_tensor", B.rstdbc[:], B.meanbc[:], B.meanbc[:], OP.mult)
        vfree = p.vop("tensor_tensor", B.msqbc[:], B.msqbc[:], B.rstdbc[:], OP.subtract)
        p.bank_free[ia] = p.bank_free[ib] = [("v", vfree)]
        p.sop("activation", B.rstdbc[:], B.msqbc[:], AF.Ln, bias=B.epscol[:],
              waits=[("v", vfree)])
        if svar:
            p.sop("activation", B.svarbc[:, tc * 512:(tc + 1) * 512], B.rstdbc[:],
                  AF.Exp, scale=0.5)
        ssq = p.sop("activation", B.rstdbc[:], B.rstdbc[:], AF.Exp, scale=-0.5)
        p.wait("vector", "s", ssq)
        last = None
        for c in range(NCH):
            xsl = B.XR[:, c * T + tc * 512: c * T + (tc + 1) * 512]
            nsl = B.XN[:, c * T + tc * 512: c * T + (tc + 1) * 512]
            p.vop("tensor_tensor", B.ytmp[:], xsl, B.meanbc[:], OP.subtract)
            last = p.vop("tensor_tensor", nsl, B.ytmp[:], B.rstdbc[:], OP.mult)
        done.append(last)
    return done


def moe_buffers(nc, B):
    SB = nc.alloc_sbuf_tensor
    if not hasattr(B, "svarbc"):
        B.svarbc = SB("svarbc", [128, T], BF16)
    B.rwt = SB("rwt", [128, NCH * 32], F32)
    B.rwsc = SB("rwsc", [128, 32], F32)
    B.rbrow = SB("rbrow", [1, 32], BF16)
    B.b1c = SB("sb_b1c", [128, 64], F32)
    B.b2sb = SB("b2sb", [32, 1024], BF16)
    B.idxcols = SB("idxcols", [128, 16], BF16)
    B.idxbc = SB("idxbc", [128, 2048], BF16)
    B.MT = SB("MT", [32, 2048], BF16)
    B.eqs = SB("eqs", [128, 512], BF16)
    B.lmax = SB("lmax", [128, 1], F32)
    B.lsel = SB("lsel", [128, 32], F32)
    B.HT = SB("HT", [128, 8 * 2 * 1024], BF16)
    B.Z2 = SB("Z2", [128, 8 * 1024], F32)
    B.w1sb = SB("w1sb", [128, 2 * 2048], BF16)
    B.w2sb = SB("w2sb", [128, 2 * 2048], BF16)


def emit_moe_weight_preload(nc, p, B, W):
    for c in range(NCH):
        p.dma("gpsimd", B.rwt[:, c * 32:(c + 1) * 32],
              W["rw_t"][c * 128:(c + 1) * 128, :], inc="dg")
    p.dma("gpsimd", B.rbrow[:], W["rb"], inc="dg")
    p.dma("gpsimd", B.b2sb[:], W["b2"], inc="dg")
    p.dma("sync", B.b1c[:], W["b1c"], inc="dx")
    B.b1c_dx = p.cnt["dx"]
    dg = p.cnt["dg"]
    bk = 4
    for c in range(NCH):
        p.mm(B.pb[bk][:, 0:32], B.ones128f[:], B.rwt[:, c * 32:(c + 1) * 32],
             start=(c == 0), stop=(c == NCH - 1),
             waits=[("dg", dg), ("g", 2)] if c == 0 else (),
             inc="pe" if c == NCH - 1 else None)
    p.vop("tensor_scalar", B.rwsc[:], B.pb[bk][:, 0:32], 1.0 / D, None, OP.mult,
          waits=[("pe", p.cnt["pe"])])
    vlast = 0
    for c in range(NCH):
        vlast = p.vop("tensor_tensor", B.rwt[:, c * 32:(c + 1) * 32],
                      B.rwt[:, c * 32:(c + 1) * 32], B.rwsc[:], OP.subtract)
    p.bank_free[bk] = [("v", vlast)]
    B.rw_ready = vlast
    return p.cnt["dg"]


def emit_moe(nc, p, B, W, ln_done, small_dg):
    B.relu_war = {}
    for tt in range(16):
        bk = 5 + tt % 2
        lb = B.pb[bk]
        w = list(p.bank_free[bk]) + [("v", ln_done[tt // 4])]
        if tt == 0:
            w += [("v", B.rw_ready), ("dg", small_dg)]
        for c in range(NCH):
            p.mm(lb[:, 0:32],
                 B.XR[:, c * T + tt * 128: c * T + (tt + 1) * 128],
                 B.rwt[:, c * 32:(c + 1) * 32],
                 start=(c == 0), stop=False, waits=w if c == 0 else ())
        p.mm(lb[:, 0:32], B.svarbc[0:1, tt * 128:(tt + 1) * 128],
             B.rbrow[0:1, 0:32], start=False, stop=True, inc="pe")
        pe_l = p.cnt["pe"]
        p.vop("tensor_reduce", B.lmax[:], lb[:, 0:32], axis=AX.X,
              op=OP.max, waits=[("pe", pe_l)])
        p.vop("tensor_scalar", B.lsel[:], lb[:, 0:32], B.lmax[:], None, OP.is_equal)
        p.vop("scalar_tensor_tensor", B.lsel[:], B.lsel[:], -10000.0, B.iotarow32[:],
              OP.mult, OP.add)
        p.vop("tensor_reduce", B.lmax[:], B.lsel[:], axis=AX.X, op=OP.min)
        vr = p.vop("tensor_scalar", B.idxcols[:, tt:tt + 1], B.lmax[:], 10000.0,
                   None, OP.add)
        p.bank_free[bk] = [("v", vr)]
    v_idx = p.cnt["v"]
    p7h = B.pb[7][:, 0:512].bitcast(BF16)
    p6h = B.pb[6][:, 0:512].bitcast(BF16)
    p.wait("tensor", "v", v_idx)
    for wv in p.bank_free[7] + p.bank_free[6]:
        p.wait("tensor", wv[0], wv[1])
    for tt in range(16):
        dst = p7h if tt < 8 else p6h
        p.tr(dst[0:1, (tt % 8) * 128:(tt % 8 + 1) * 128],
             B.idxcols[:, tt:tt + 1], B.ident[:],
             inc="pe" if tt == 15 else None)
    for half in range(2):
        ph = p7h if half == 0 else p6h
        vir = p.vop("tensor_copy", B.xbfs[0:1, 0:1024], ph[0:1, 0:1024],
                    waits=[("pe", p.cnt["pe"])] if half == 0 else ())
        for q in range(2):
            tc = half * 2 + q
            p.mm(B.pb[7][:, 0:512], B.ones128[0:1, :],
                 B.xbfs[0:1, q * 512:(q + 1) * 512],
                 start=True, stop=True,
                 waits=[("v", vir)] + p.bank_free[7], inc="pe")
            vb = p.vop("tensor_copy", B.idxbc[:, tc * 512:(tc + 1) * 512],
                       B.pb[7][:, 0:512], waits=[("pe", p.cnt["pe"])])
            p.bank_free[7] = [("v", vb)]
    p.vop("tensor_scalar", B.MT[:], B.idxbc[0:32, :], B.iotacol[0:32, :], None,
          OP.is_equal)
    v_mt = p.cnt["v"]
    if B.dbg:
        p.dma("sync", B.dbg["XN"], B.XN[:], inc="do", waits=[("v", ln_done[-1])])
        p.dma("sync", B.dbg["idxbc"], B.idxbc[:], inc="do", waits=[("v", v_mt)])

    w1_rd = {}
    w2_rd = {}
    for th in range(2):
        t0 = th * 1024
        for eg in range(4):
            for ei in range(8):
                e = eg * 8 + ei
                db = ei % 2
                dgw = p.dma("gpsimd", B.w1sb[:, db * 2048:(db + 1) * 2048],
                            W["w1"][e].rearrange("(c p) j -> p c j", p=128),
                            inc="dg", waits=[("pe", w1_rd.get(db, 0))])
                for tq in range(2):
                    veq = p.vop("tensor_scalar", B.eqs[:, 0:512],
                                B.idxbc[:, t0 + tq * 512: t0 + (tq + 1) * 512],
                                float(e), None, OP.is_equal,
                                waits=[("v", v_mt)] if (th == 0 and eg == 0 and ei == 0 and tq == 0) else ())
                    for jb in range(2):
                        bk = jb * 2 + tq
                        bank = B.pb[bk]
                        w = [("dg", dgw)] + p.bank_free[bk]
                        mlast = None
                        for ic in range(NCH):
                            mlast = p.mm(bank[:, 0:512],
                                 B.w1sb[:, db * 2048 + ic * 256 + jb * 128:
                                        db * 2048 + ic * 256 + (jb + 1) * 128],
                                 B.XN[:, ic * T + t0 + tq * 512:
                                      ic * T + t0 + (tq + 1) * 512],
                                 start=(ic == 0), stop=(ic == NCH - 1),
                                 waits=w if ic == 0 else (),
                                 inc="pe" if ic == NCH - 1 else None)
                        w1_rd[db] = mlast
                        hsl = B.HT[:, (ei * 2 + jb) * 1024 + tq * 512:
                                   (ei * 2 + jb) * 1024 + (tq + 1) * 512]
                        tsl = (B.xbfs if bk < 2 else B.xsqs)[:, (bk % 2) * 512:
                                                             (bk % 2) * 512 + 512]
                        srl = p.sop("activation", tsl, bank[:, 0:512], AF.Relu,
                                    bias=B.b1c[:, e * 2 + jb: e * 2 + jb + 1],
                                    waits=[("pe", mlast), ("dx", B.b1c_dx),
                                           ("v", B.relu_war.get(bk, 0))])
                        vh = p.vop("tensor_tensor", hsl, tsl,
                                   B.eqs[:, 0:512], OP.mult,
                                   waits=[("s", srl), ("v", veq)])
                        B.relu_war[bk] = vh
                        p.bank_free[bk] = [("v", vh)]
            v_ht = p.cnt["v"]
            if B.dbg and th == 0 and eg == 0:
                p.dma("sync", B.dbg["HT"], B.HT[:], inc="do", waits=[("v", v_ht)])
            for oh in range(2):
                for ei in range(8):
                    e = eg * 8 + ei
                    pr = (ei // 2) % 2
                    db = pr * 2 + ei % 2
                    if ei % 2 == 0:
                        dgw = p.dma("gpsimd", B.w2sb[:, pr * 2048:(pr + 1) * 2048],
                                    W["w2"][e:e + 2, :, oh * 512:(oh + 1) * 512]
                                    .rearrange("e (c p) o -> p e c o", p=128),
                                    inc="dg", waits=[("pe", w2_rd.get(pr, 0))])
                    for obi in range(4):
                        for tq in range(2):
                            bk = obi * 2 + tq
                            bank = B.pb[bk]
                            ob = oh * 4 + obi
                            if ei == 0:
                                w = [("dg", dgw), ("v", v_ht)] + p.bank_free[bk]
                                if eg == 0:
                                    p.mm(bank[:, 0:512],
                                         B.b2sb[:, ob * 128:(ob + 1) * 128],
                                         B.MT[:, t0 + tq * 512: t0 + (tq + 1) * 512],
                                         start=True, stop=False, waits=w)
                                    w = []
                            else:
                                w = [("dg", dgw)]
                            for jb in range(2):
                                start = (ei == 0 and jb == 0 and eg != 0)
                                islast = (ei == 7 and jb == 1)
                                expert_last = (obi == 3 and tq == 1 and jb == 1
                                               and ei % 2 == 1)
                                mn = p.mm(bank[:, 0:512],
                                     B.w2sb[:, db * 1024 + jb * 512 + obi * 128:
                                            db * 1024 + jb * 512 + (obi + 1) * 128],
                                     B.HT[:, (ei * 2 + jb) * 1024 + tq * 512:
                                          (ei * 2 + jb) * 1024 + (tq + 1) * 512],
                                     start=start, stop=islast,
                                     waits=w if jb == 0 else (),
                                     inc="pe" if (islast or expert_last) else None)
                                if expert_last:
                                    w2_rd[pr] = mn
                                w = []
                pe_grp = p.cnt["pe"]
                for obi in range(4):
                    for tq in range(2):
                        bk = obi * 2 + tq
                        bank = B.pb[bk]
                        ob = oh * 4 + obi
                        zsl = B.Z2[:, ob * 1024 + tq * 512: ob * 1024 + (tq + 1) * 512]
                        if eg == 0:
                            vz = p.vop("tensor_copy", zsl, bank[:, 0:512],
                                       waits=[("pe", pe_grp)])
                        else:
                            vz = p.vop("tensor_tensor", zsl, zsl, bank[:, 0:512],
                                       OP.add, waits=[("pe", pe_grp)])
                        p.bank_free[bk] = [("v", vz)]
        if B.dbg and th == 0:
            p.dma("sync", B.dbg["Z2"], B.Z2[:], inc="do", waits=[("v", p.cnt["v"])])
        for ob in range(8):
            for tq in range(2):
                zsl = B.Z2[:, ob * 1024 + tq * 512: ob * 1024 + (tq + 1) * 512]
                xsl = B.XR[:, ob * T + t0 + tq * 512: ob * T + t0 + (tq + 1) * 512]
                p.vop("tensor_scalar", B.ytmp[:], zsl, 0.0, 0.5, OP.max, OP.mult)
                p.vop("tensor_tensor", xsl, xsl, B.ytmp[:], OP.add)
    return p.cnt["v"]


def moer_buffers(nc, B):
    SB = nc.alloc_sbuf_tensor
    if not hasattr(B, "svarbc"):
        B.svarbc = SB("svarbc", [128, T], BF16)
    B.rwt = SB("rwt", [128, NCH * 32], F32)
    B.rwsc = SB("rwsc", [128, 32], F32)
    B.rbrow = SB("rbrow", [1, 32], BF16)
    B.b1c = SB("sb_b1c", [128, 64], F32)
    B.b2t = SB("sb_b2t", [128, 256], F32)
    B.idxcols = SB("idxcols", [128, 16], F32)
    B.MT = SB("MT", [32, 2048], BF16)
    B.lmax = SB("lmax", [128, 1], F32)
    B.lsel = SB("lsel", [128, 32], F32)
    B.iote = SB("iote", [32, 1], BF16)
    B.Mtm = SB("Mtm", [128, 16 * 32], BF16)
    B.Ltile = SB("Ltile", [128, 2 * 2048], BF16)
    B.ranksb = SB("ranksb", [32, 2048], BF16)
    B.w1f32 = SB("w1f32", [128, 2 * 2048], F32)
    B.w2f32 = SB("w2f32", [128, 2 * 2048], F32)
    B.srow = B.w1f32[0:1, 0:2048]
    B.scoli = SB("scoli", [128, 16], I32)
    B.tokids = SB("tokids", [128, 16], I32)
    B.xtms = SB("xtms", [128, 2 * 1024], BF16)
    B.xg = SB("xg", [128, 2 * 1024], BF16)
    B.xgT = SB("xgT", [128, 2 * 1024], BF16)
    B.hT = SB("hT", [128, 2 * 256], BF16)
    B.yT = SB("yT", [128, 2 * 1024], BF16)
    B.ytm = SB("ytm", [128, 2 * 1024], BF16)
    B.permsb = SB("permsb", [128, 32], I32)
    B.w1sb = SB("w1sb", [128, 2 * 2048], BF16)
    B.w2sb = SB("w2sb", [128, 2 * 2048], BF16)


def emit_moer_preload(nc, p, B, W):
    for c in range(NCH):
        p.dma("gpsimd", B.rwt[:, c * 32:(c + 1) * 32],
              W["rw_t"][c * 128:(c + 1) * 128, :], inc="dg")
    p.dma("gpsimd", B.rbrow[:], W["rb"], inc="dg")
    p.dma("sync", B.b1c[:], W["b1c"], inc="dx")
    p.dma("sync", B.b2t[:], W["b2t"], inc="dx")
    B.b1c_dx = p.cnt["dx"]
    dg = p.cnt["dg"]
    p.vop("tensor_copy", B.iote[:], B.iotacol[0:32, :], waits=[("g", p.cnt["g"])])
    bk = 4
    for c in range(NCH):
        p.mm(B.pb[bk][:, 0:32], B.ones128f[:], B.rwt[:, c * 32:(c + 1) * 32],
             start=(c == 0), stop=(c == NCH - 1),
             waits=([("dg", dg), ("g", 2)] + p.bank_free[bk]) if c == 0 else (),
             inc="pe" if c == NCH - 1 else None)
    p.vop("tensor_scalar", B.rwsc[:], B.pb[bk][:, 0:32], 1.0 / D, None, OP.mult,
          waits=[("pe", p.cnt["pe"])])
    vlast = 0
    for c in range(NCH):
        vlast = p.vop("tensor_tensor", B.rwt[:, c * 32:(c + 1) * 32],
                      B.rwt[:, c * 32:(c + 1) * 32], B.rwsc[:], OP.subtract)
    p.bank_free[bk] = [("v", vlast)]
    B.rw_ready = vlast
    return p.cnt["dg"]


def emit_moe_routed(nc, p, B, W, ln_done, small_dg, scr):
    import dataclasses as _dc
    p2h = B.pb8h[:].bitcast(BF16)

    for tt in range(16):
        bk = 5 + tt % 2
        lb = B.pb[bk]
        w = list(p.bank_free[bk]) + [("v", ln_done[tt // 4])]
        if tt == 0:
            w += [("v", B.rw_ready), ("dg", small_dg), ("dx", B.b1c_dx)]
        for c in range(NCH):
            p.mm(lb[:, 0:32],
                 B.XR[:, c * T + tt * 128: c * T + (tt + 1) * 128],
                 B.rwt[:, c * 32:(c + 1) * 32],
                 start=(c == 0), stop=False, waits=w if c == 0 else ())
        p.mm(lb[:, 0:32], B.svarbc[0:1, tt * 128:(tt + 1) * 128],
             B.rbrow[0:1, 0:32], start=False, stop=True, inc="pe")
        pe_l = p.cnt["pe"]
        p.vop("tensor_reduce", B.lmax[:], lb[:, 0:32], axis=AX.X,
              op=OP.max, waits=[("pe", pe_l)])
        p.vop("tensor_scalar", B.lsel[:], lb[:, 0:32], B.lmax[:], None, OP.is_equal)
        p.vop("scalar_tensor_tensor", B.lsel[:], B.lsel[:], -10000.0, B.iotarow32[:],
              OP.mult, OP.add)
        p.vop("tensor_reduce", B.lmax[:], B.lsel[:], axis=AX.X, op=OP.min)
        p.vop("tensor_scalar", B.idxcols[:, tt:tt + 1], B.lmax[:], 10000.0,
              None, OP.add)
        vr = p.vop("tensor_scalar", B.Mtm[:, tt * 32:(tt + 1) * 32], B.iotarow32[:],
                   B.idxcols[:, tt:tt + 1], None, OP.is_equal)
        p.bank_free[bk] = [("v", vr)]
    v_idx = p.cnt["v"]

    st_dma = {}
    trp = {}
    for tile in range(16):
        par = tile % 2
        pbase = (4 + par) * 1024
        pw = [("v", v_idx)] if tile == 0 else []
        if tile >= 2:
            pw += p.bank_free[4 + par]
        for c in range(NCH):
            trp[par] = p.tr(p2h[:, pbase + c * 128: pbase + (c + 1) * 128],
                            B.XN[:, c * T + tile * 128: c * T + tile * 128 + 128],
                            B.ident[:], waits=pw if c == 0 else (),
                            inc="pe" if c == NCH - 1 else None)
        vw = [("pe", trp[par])]
        if tile >= 2:
            vw.append(("do", st_dma[tile - 2]))
        vv = p.vop("tensor_copy", B.xtms[:, par * 1024:(par + 1) * 1024],
                   p2h[:, pbase:pbase + 1024], waits=vw)
        p.bank_free[4 + par] = [("v", vv)]
        st_dma[tile] = p.dma("sync", scr["xn_tm"][tile * 128:(tile + 1) * 128, :],
                             B.xtms[:, par * 1024:(par + 1) * 1024], inc="do",
                             waits=[("v", vv)])
    xn_tm_done = p.cnt["do"]

    p.wait("tensor", "v", v_idx)
    for w_ in p.bank_free[0] + p.bank_free[1]:
        p.wait("tensor", w_[0], w_[1])
    for tile in range(16):
        p.tr(p2h[0:32, tile * 128:(tile + 1) * 128],
             B.Mtm[:, tile * 32:(tile + 1) * 32], B.ident[:],
             inc="pe" if tile == 15 else None)
    vmt = p.vop("tensor_copy", B.MT[:], p2h[0:32, 0:2048],
                waits=[("pe", p.cnt["pe"])])
    p.bank_free[0] = p.bank_free[1] = [("v", vmt)]

    g_l = {}
    mm_l = {}
    for c in range(16):
        par = c % 2
        g_l[c] = p.gop("affine_select", B.Ltile[:, par * 2048:(par + 1) * 2048],
                       B.onesbig[:], [[1, 2048]], OP.is_gt, 0.0,
                       base=-(c * 128), channel_multiplier=-1,
                       waits=[("pe", mm_l[c - 2])] if c >= 2 else ())
        w = [("g", g_l[c])]
        if c == 0:
            w += p.bank_free[0] + p.bank_free[1] + p.bank_free[2] + p.bank_free[3]
        for n in range(4):
            mm_l[c] = p.mm(B.pb8[0:32, n * 512:(n + 1) * 512],
                           B.Mtm[:, c * 32:(c + 1) * 32],
                           B.Ltile[:, par * 2048 + n * 512: par * 2048 + (n + 1) * 512],
                           start=(c == 0), stop=(c == 15),
                           waits=w if n == 0 else (), inc="pe")
    pe_rank = p.cnt["pe"]
    p.vop("tensor_scalar", B.ranksb[:], B.pb8[0:32, 0:2048], 127.0, 32.0,
          OP.min, OP.mult, waits=[("pe", pe_rank)])
    v_mt = p.vop("tensor_tensor", B.ranksb[:], B.ranksb[:], B.MT[:], OP.mult)
    for i in range(4):
        p.bank_free[i] = [("v", v_mt)]
    p.wait("tensor", "v", v_mt)
    for n in range(4):
        p.mm(B.pb8[0:1, n * 512:(n + 1) * 512], B.ones128[0:32, 0:1],
             B.ranksb[:, n * 512:(n + 1) * 512], start=True, stop=False)
        p.mm(B.pb8[0:1, n * 512:(n + 1) * 512], B.iote[:],
             B.MT[:, n * 512:(n + 1) * 512], start=False, stop=True,
             inc="pe" if n == 3 else None)
    v_srow = p.vop("tensor_copy", B.srow, B.pb8[0:1, 0:2048],
                   waits=[("pe", p.cnt["pe"])])
    for i in range(4):
        p.bank_free[i] = [("v", v_srow)]
    p.wait("tensor", "v", v_srow)
    for tile in range(16):
        p.tr(bass.AP(B.pb8h, tile, [[4096, 128], [1, 1]]),
             B.srow[0:1, tile * 128:(tile + 1) * 128], B.identf[0:1, 0:1],
             inc="pe" if tile == 15 else None)
    B.pe_scoli = p.cnt["pe"]
    p.vop("tensor_copy", B.scoli[:],
          bass.AP(B.pb8h, 0, [[4096, 128], [1, 16]]),
          waits=[("pe", p.cnt["pe"])])
    vsc = p.cnt["v"]
    p.bank_free[0] = [("v", vsc)]
    g0 = p.gop("memset", B.tokids[:], 1 << 20)
    gi = None
    for hf in range(2):
        gi = p.dma("gpsimd",
                   scr["perm2"].rearrange("(r e) one -> r (e one)", r=128)[:, hf * 16:(hf + 1) * 16],
                   B.tokids[:], inc="dg", waits=[("g", g0)] if hf == 0 else ())
    g1 = p.gop("iota", B.tokids[:], [[128, 16]], base=0, channel_multiplier=1,
               allow_small_or_imprecise_dtypes=True, waits=[("dg", gi)])
    gsc = None
    for tile in range(16):
        gsc = p.idma("gpsimd", scr["perm2"],
                     bass.IndirectOffsetOnAxis(ap=B.scoli[:, tile:tile + 1], axis=0),
                     B.tokids[:, tile:tile + 1], None, inc="dg",
                     waits=[("g", g1), ("v", vsc)] if tile == 0 else ())
    gld = p.dma("sync", B.permsb[:],
                scr["perm2"].rearrange("(r e) one -> r (e one)", r=128),
                inc="dx", waits=[("dg", gsc)])

    w1_rd = {}
    w2_rd = {}
    xg_rd = {}
    cast_s = {}
    for e in range(32):
        par = e % 2
        bko = 0 if par == 0 else 4
        df1 = p.dma("sync", B.w1f32[:, par * 2048:(par + 1) * 2048],
                    W["w1"][e].rearrange("(c p) j -> p c j", p=128),
                    inc="dx", waits=[("s", cast_s.get(("w1", par), 0))] +
                    ([("pe", B.pe_scoli)] if e == 0 else []))
        df2 = p.dma("sync", B.w2f32[:, par * 2048:(par + 1) * 2048],
                    W["w2"][e].rearrange("(c p) o -> p c o", p=128),
                    inc="dx", waits=[("s", cast_s.get(("w2", par), 0))])
        cast_s[("w1", par)] = p.sop("activation", B.w1sb[:, par * 2048:(par + 1) * 2048],
                                    B.w1f32[:, par * 2048:(par + 1) * 2048], AF.Identity,
                                    waits=[("dx", df1), ("pe", w1_rd.get(par, 0))])
        dg1 = cast_s[("w1", par)]
        cast_s[("w2", par)] = p.sop("activation", B.w2sb[:, par * 2048:(par + 1) * 2048],
                                    B.w2f32[:, par * 2048:(par + 1) * 2048], AF.Identity,
                                    waits=[("dx", df2), ("pe", w2_rd.get(par, 0))])
        dg2 = cast_s[("w2", par)]
        gw = [("dx", gld), ("do", xn_tm_done)] if e == 0 else []
        gath = p.idma("gpsimd", B.xg[:, par * 1024:(par + 1) * 1024], None,
                      scr["xn_tm"],
                      bass.IndirectOffsetOnAxis(ap=B.permsb[:, e:e + 1], axis=0),
                      inc="dg", waits=gw + [("pe", xg_rd.get(par, 0))])
        for c in range(NCH):
            p.tr(p2h[:, bko * 1024 + c * 128: bko * 1024 + (c + 1) * 128],
                 B.xg[:, par * 1024 + c * 128: par * 1024 + (c + 1) * 128],
                 B.ident[:],
                 waits=([("dg", gath)] + p.bank_free[bko]) if c == 0 else (),
                 inc="pe" if c == NCH - 1 else None)
        xg_rd[par] = p.cnt["pe"]
        vxgt = p.vop("tensor_copy", B.xgT[:, par * 1024:(par + 1) * 1024],
                     p2h[:, bko * 1024:(bko + 1) * 1024],
                     waits=[("pe", p.cnt["pe"])])
        p.bank_free[bko] = [("v", vxgt)]
        p.wait("tensor", "v", vxgt)
        for jb in range(2):
            for ic in range(NCH):
                p.mm(B.pb8[:, (bko + 1) * 512 + jb * 128:(bko + 1) * 512 + (jb + 1) * 128],
                     B.w1sb[:, par * 2048 + ic * 256 + jb * 128:
                            par * 2048 + ic * 256 + (jb + 1) * 128],
                     B.xgT[:, par * 1024 + ic * 128: par * 1024 + (ic + 1) * 128],
                     start=(ic == 0), stop=(ic == NCH - 1),
                     waits=([("s", dg1)] + p.bank_free[bko + 1])
                           if (jb == 0 and ic == 0) else (),
                     inc="pe" if ic == NCH - 1 else None)
        w1_rd[par] = p.cnt["pe"]
        sh = None
        for jb in range(2):
            sh = p.sop("activation",
                       B.hT[:, par * 256 + jb * 128: par * 256 + (jb + 1) * 128],
                       B.pb8[:, (bko + 1) * 512 + jb * 128:
                             (bko + 1) * 512 + (jb + 1) * 128],
                       AF.Relu, bias=B.b1c[:, e * 2 + jb: e * 2 + jb + 1],
                       waits=[("pe", w1_rd[par])])
        p.bank_free[bko + 1] = [("s", sh)]
        p.wait("tensor", "s", sh)
        for ob in range(8):
            obk = bko + 2 + ob // 4
            slot = ob % 4
            for jb in range(2):
                p.mm(B.pb8[:, obk * 512 + slot * 128: obk * 512 + (slot + 1) * 128],
                     B.w2sb[:, par * 2048 + jb * 1024 + ob * 128:
                            par * 2048 + jb * 1024 + (ob + 1) * 128],
                     B.hT[:, par * 256 + jb * 128: par * 256 + (jb + 1) * 128],
                     start=(jb == 0), stop=(jb == 1),
                     waits=([("s", dg2)] + p.bank_free[obk])
                           if (slot == 0 and jb == 0) else (),
                     inc="pe" if jb == 1 else None)
        w2_rd[par] = p.cnt["pe"]
        sy = None
        for ob in range(8):
            obk = bko + 2 + ob // 4
            slot = ob % 4
            sy = p.sop("activation",
                       B.yT[:, par * 1024 + ob * 128: par * 1024 + (ob + 1) * 128],
                       B.pb8[:, obk * 512 + slot * 128: obk * 512 + (slot + 1) * 128],
                       AF.Relu, bias=B.b2t[:, e * 8 + ob: e * 8 + ob + 1],
                       waits=[("pe", w2_rd[par])])
        p.bank_free[bko + 2] = [("s", sy)]
        p.bank_free[bko + 3] = [("s", sy)]
        p.wait("tensor", "s", sy)
        for c in range(NCH):
            p.tr(p2h[:, bko * 1024 + c * 128: bko * 1024 + (c + 1) * 128],
                 B.yT[:, par * 1024 + c * 128: par * 1024 + (c + 1) * 128],
                 B.ident[:], waits=p.bank_free[bko] if c == 0 else (),
                 inc="pe" if c == NCH - 1 else None)
        vytm = p.vop("tensor_copy", B.ytm[:, par * 1024:(par + 1) * 1024],
                     p2h[:, bko * 1024:(bko + 1) * 1024],
                     waits=[("pe", p.cnt["pe"])])
        p.bank_free[bko] = [("v", vytm)]
        p.idma("gpsimd", scr["y_tm"],
               bass.IndirectOffsetOnAxis(ap=B.permsb[:, e:e + 1], axis=0),
               B.ytm[:, par * 1024:(par + 1) * 1024], None,
               inc="dg", waits=[("v", vytm)])
    y_done = p.cnt["dg"]

    ld = {}
    tr_rd = {}
    for tile in range(16):
        par = tile % 2
        bko = 0 if par == 0 else 4
        ld[tile] = p.dma("sync", B.xg[:, par * 1024:(par + 1) * 1024],
                         scr["y_tm"][tile * 128:(tile + 1) * 128, :], inc="dx",
                         waits=([("dg", y_done)] if tile == 0 else []) +
                               ([("pe", tr_rd[par])] if tile >= 2 else []))
        for c in range(NCH):
            tr_rd[par] = p.tr(
                p2h[:, bko * 1024 + c * 128: bko * 1024 + (c + 1) * 128],
                B.xg[:, par * 1024 + c * 128: par * 1024 + (c + 1) * 128],
                B.ident[:],
                waits=([("dx", ld[tile])] + p.bank_free[bko]) if c == 0 else (),
                inc="pe")
        vz = p.vop("scalar_tensor_tensor",
                   bass.AP(B.XR, tile * 128, [[16384, 128], [2048, 8], [1, 128]]),
                   _dc.replace(p2h, offset=bko * 1024, ap=[[8192, 128], [128, 8], [1, 128]]),
                   0.5,
                   bass.AP(B.XR, tile * 128, [[16384, 128], [2048, 8], [1, 128]]),
                   OP.mult, OP.add, waits=[("pe", tr_rd[par])])
        p.bank_free[bko] = [("v", vz)]
    return p.cnt["v"]


def moe_dram(nc):
    return {
        "rw_t": nc.dram_tensor("rw_t", [1024, 32], F32, kind="ExternalInput").ap(),
        "rb": nc.dram_tensor("rb", [1, 32], F32, kind="ExternalInput").ap(),
        "w1": nc.dram_tensor("w1", [32, 1024, 256], F32, kind="ExternalInput").ap(),
        "b1c": nc.dram_tensor("b1c", [128, 64], F32, kind="ExternalInput").ap(),
        "w2": nc.dram_tensor("w2", [32, 256, 1024], F32, kind="ExternalInput").ap(),
        "b2": nc.dram_tensor("b2", [32, 1024], F32, kind="ExternalInput").ap(),
    }


def build_moe_program(debug=False):
    nc = bass.Bass("TRN2", target_bir_lowering=False)
    xin = nc.dram_tensor("xin", [D, T], F32, kind="ExternalInput").ap()
    W = moe_dram(nc)
    xout = nc.dram_tensor("xout", [D, T], F32, kind="ExternalOutput").ap()
    p = P(nc)
    B = make_common(nc, p)
    moe_buffers(nc, B)
    dbg = {}
    if debug:
        dbg["XN"] = nc.dram_tensor("dbg_XN", [128, NCH * T], BF16, kind="ExternalOutput").ap()
        dbg["idxbc"] = nc.dram_tensor("dbg_idxbc", [128, T], BF16, kind="ExternalOutput").ap()
        dbg["HT"] = nc.dram_tensor("dbg_HT", [128, 8 * 2 * 1024], BF16, kind="ExternalOutput").ap()
        dbg["Z2"] = nc.dram_tensor("dbg_Z2", [128, 8 * 1024], F32, kind="ExternalOutput").ap()
    B.dbg = dbg
    dx = load_x(nc, p, B, xin)
    small_dg = emit_moe_weight_preload(nc, p, B, W)
    ln_done = emit_ln(nc, p, B, dx)
    vend = emit_moe(nc, p, B, W, ln_done, small_dg)
    store_x(nc, p, B, xout, vend)
    p.realize()
    return nc


def moe_host_weights(inputs, prefix="L0_ffn"):
    return {
        "rw_t": np.ascontiguousarray(np.asarray(inputs[prefix + "_router_w"]).T),
        "rb": np.ascontiguousarray(np.asarray(inputs[prefix + "_router_b"])[None, :]),
        "w1": np.ascontiguousarray(np.asarray(inputs[prefix + "_weights1"])),
        "b1c": np.ascontiguousarray(np.asarray(inputs[prefix + "_bias1"]).reshape(32, 2, 128).transpose(2, 0, 1).reshape(128, 64)),
        "w2": np.ascontiguousarray(np.asarray(inputs[prefix + "_weights2"])),
        "b2": np.ascontiguousarray(np.asarray(inputs[prefix + "_bias2"])),
    }


def l1ffn_buffers(nc, B):
    SB = nc.alloc_sbuf_tensor
    B.GH = SB("GH", [128, 32768], BF16)
    B.Hv = B.XR[:].bitcast(BF16)
    B.wA = SB("wA", [128, 2 * 4096], BF16)
    B.wB = SB("wB", [128, 2 * 1024], BF16)
    B.wC = SB("wC", [128, 2 * 1024], BF16)
    B.s1c = SB("s1c", [128, 32], F32)
    B.s2c = SB("s2c", [128, 32], F32)
    B.sh1c = SB("sh1c", [128, 32], F32)
    B.sh2c = SB("sh2c", [128, 32], F32)
    B.s3c = SB("s3c", [128, 8], F32)
    B.s3h = SB("s3h", [128, 8], F32)



def emit_l1ffn_preload(nc, p, B, W):
    for nm in ("s1c", "s2c", "sh1c", "sh2c", "s3c"):
        p.dma("sync", getattr(B, nm)[:], W[nm], inc="dx")
    dxs = p.cnt["dx"]
    p.vop("tensor_scalar", B.s3h[:], B.s3c[:], 0.5, None, OP.mult,
          waits=[("dx", dxs)])
    return p.cnt["v"]


def emit_l1ffn(nc, p, B, W, xin, xout, ln_done, scales_v, hv_waits=()):
    wArd = {}
    wBrd = {}
    wCrd = {}
    for tch in range(2):
        t0 = tch * 1024
        for ob in range(32):
            db = ob % 2
            dg1 = p.dma("gpsimd", B.wB[:, db * 1024:(db + 1) * 1024],
                        W["w1"][ob], inc="dg", waits=[("pe", wBrd.get(db, 0))])
            dg2 = p.dma("gpsimd", B.wC[:, db * 1024:(db + 1) * 1024],
                        W["w2"][ob], inc="dg", waits=[("pe", wCrd.get(db, 0))])
            bk1 = [4 * db, 4 * db + 1]
            bk2 = [4 * db + 2, 4 * db + 3]
            for c in range(8):
                for n in range(2):
                    w = []
                    if c == 0:
                        w = [("dg", dg1), ("v", ln_done[tch * 2 + n])] + p.bank_free[bk1[n]]
                        if tch == 0 and ob == 0 and n == 0:
                            w.append(("v", scales_v))
                    p.mm(B.pb[bk1[n]][:, 0:512],
                         B.wB[:, db * 1024 + c * 128:(db * 1024) + (c + 1) * 128],
                         B.XN[:, c * T + t0 + n * 512: c * T + t0 + (n + 1) * 512],
                         start=(c == 0), stop=(c == 7), waits=w,
                         inc="pe" if c == 7 else None)
            wBrd[db] = p.cnt["pe"]
            pe1 = [p.cnt["pe"] - 1, p.cnt["pe"]]
            for c in range(8):
                for n in range(2):
                    w = []
                    if c == 0:
                        w = [("dg", dg2)] + p.bank_free[bk2[n]]
                    p.mm(B.pb[bk2[n]][:, 0:512],
                         B.wC[:, db * 1024 + c * 128:(db * 1024) + (c + 1) * 128],
                         B.XN[:, c * T + t0 + n * 512: c * T + t0 + (n + 1) * 512],
                         start=(c == 0), stop=(c == 7), waits=w,
                         inc="pe" if c == 7 else None)
            wCrd[db] = p.cnt["pe"]
            pe2 = [p.cnt["pe"] - 1, p.cnt["pe"]]
            silut = [B.meanbc, B.msqbc]
            ssl = []
            for n in range(2):
                sw = [("pe", pe1[n]), ("v", p.cnt["v"])]
                if tch == 0 and ob == 0:
                    sw.append(("v", scales_v))
                ssl.append(p.sop("activation", silut[n][:, 0:512],
                                 B.pb[bk1[n]][:, 0:512], AF.Silu,
                                 scale=B.s1c[:, ob:ob + 1], waits=sw))
            vh = None
            for n in range(2):
                hw = [("pe", pe2[n]), ("s", ssl[n])]
                if tch == 0 and ob == 0 and n == 0:
                    hw += list(hv_waits)
                vh = p.vop("scalar_tensor_tensor",
                           B.Hv[:, ob * 1024 + n * 512: ob * 1024 + (n + 1) * 512],
                           B.pb[bk2[n]][:, 0:512], B.s2c[:, ob:ob + 1],
                           silut[n][:, 0:512], OP.mult, OP.mult,
                           waits=hw)
                p.bank_free[bk2[n]] = [("v", vh)]
            p.bank_free[bk1[0]] = [("v", vh)]
            p.bank_free[bk1[1]] = [("v", vh)]
        v_h = p.cnt["v"]
        for chain, (wkey, scol, src, dst) in enumerate((
                ("h1", B.sh1c, B.Hv, B.GH), ("h2", B.sh2c, B.GH, B.Hv))):
            s_rel = None
            for ob in range(32):
                db = ob % 2
                dga = p.dma("gpsimd", B.wA[:, db * 4096:(db + 1) * 4096],
                            W[wkey][ob], inc="dg", waits=[("pe", wArd.get(db, 0))])
                bks = [4 * db, 4 * db + 1]
                for c in range(32):
                    for n in range(2):
                        w = []
                        if c == 0:
                            w = [("dg", dga)] + p.bank_free[bks[n]]
                            if ob == 0 and n == 0:
                                if chain == 0:
                                    w.append(("v", v_h))
                                else:
                                    w.append(("s", s_gel))
                        p.mm(B.pb[bks[n]][:, 0:512],
                             B.wA[:, db * 4096 + c * 128:(db * 4096) + (c + 1) * 128],
                             src[:, c * 1024 + n * 512: c * 1024 + (n + 1) * 512],
                             start=(c == 0), stop=(c == 31), waits=w,
                             inc="pe" if c == 31 else None)
                wArd[db] = p.cnt["pe"]
                pes = [p.cnt["pe"] - 1, p.cnt["pe"]]
                for n in range(2):
                    s_rel = p.sop("activation",
                                  dst[:, ob * 1024 + n * 512: ob * 1024 + (n + 1) * 512],
                                  B.pb[bks[n]][:, 0:512], AF.Gelu,
                                  scale=scol[:, ob:ob + 1], waits=[("pe", pes[n])])
                    p.bank_free[bks[n]] = [("s", s_rel)]
            s_gel = s_rel
        for ob in range(8):
            db = ob % 2
            dga = p.dma("gpsimd", B.wA[:, db * 4096:(db + 1) * 4096],
                        W["w3"][ob], inc="dg", waits=[("pe", wArd.get(db, 0))])
            bks = [4 * db, 4 * db + 1]
            for c in range(32):
                for n in range(2):
                    w = []
                    if c == 0:
                        w = [("dg", dga)] + p.bank_free[bks[n]]
                        if ob == 0 and n == 0:
                            w.append(("s", s_gel))
                    p.mm(B.pb[bks[n]][:, 0:512],
                         B.wA[:, db * 4096 + c * 128:(db * 4096) + (c + 1) * 128],
                         B.Hv[:, c * 1024 + n * 512: c * 1024 + (n + 1) * 512],
                         start=(c == 0), stop=(c == 31), waits=w,
                         inc="pe" if c == 31 else None)
            wArd[db] = p.cnt["pe"]
            pes = [p.cnt["pe"] - 1, p.cnt["pe"]]
            for n in range(2):
                dxr = p.dma("sync", B.ytmp[:, 0:512],
                            xin[ob * 128:(ob + 1) * 128, t0 + n * 512:t0 + (n + 1) * 512],
                            inc="dx", waits=[("do", p.cnt["do"]), ("v", p.cnt["v"])])
                vr = p.vop("scalar_tensor_tensor", B.ytmp[:, 0:512],
                           B.pb[bks[n]][:, 0:512], B.s3h[:, ob:ob + 1],
                           B.ytmp[:, 0:512], OP.mult, OP.add,
                           waits=[("pe", pes[n]), ("dx", dxr)])
                p.bank_free[bks[n]] = [("v", vr)]
                p.dma("sync", xout[ob * 128:(ob + 1) * 128, t0 + n * 512:t0 + (n + 1) * 512],
                      B.ytmp[:, 0:512], inc="do", waits=[("v", vr)])
    p.wait("sync", "do", p.cnt["do"])
    return p.cnt["v"]


def l1ffn_dram(nc):
    return {
        "w1": nc.dram_tensor("f_w1", [32, 128, 8, 128], F32, kind="ExternalInput").ap(),
        "w2": nc.dram_tensor("f_w2", [32, 128, 8, 128], F32, kind="ExternalInput").ap(),
        "h1": nc.dram_tensor("f_h1", [32, 128, 32, 128], F32, kind="ExternalInput").ap(),
        "h2": nc.dram_tensor("f_h2", [32, 128, 32, 128], F32, kind="ExternalInput").ap(),
        "w3": nc.dram_tensor("f_w3", [8, 128, 32, 128], F32, kind="ExternalInput").ap(),
        "s1c": nc.dram_tensor("f_s1c", [128, 32], F32, kind="ExternalInput").ap(),
        "s2c": nc.dram_tensor("f_s2c", [128, 32], F32, kind="ExternalInput").ap(),
        "sh1c": nc.dram_tensor("f_sh1c", [128, 32], F32, kind="ExternalInput").ap(),
        "sh2c": nc.dram_tensor("f_sh2c", [128, 32], F32, kind="ExternalInput").ap(),
        "s3c": nc.dram_tensor("f_s3c", [128, 8], F32, kind="ExternalInput").ap(),
    }


def build_l1ffn_program():
    nc = bass.Bass("TRN2", target_bir_lowering=False)
    xin = nc.dram_tensor("xin", [D, T], F32, kind="ExternalInput").ap()
    W = l1ffn_dram(nc)
    xout = nc.dram_tensor("xout", [D, T], F32, kind="ExternalOutput").ap()
    p = P(nc)
    B = make_common(nc, p)
    l1ffn_buffers(nc, B)
    dx = load_x(nc, p, B, xin)
    scales_v = emit_l1ffn_preload(nc, p, B, W)
    ln_done = emit_ln(nc, p, B, dx, svar=False)
    emit_l1ffn(nc, p, B, W, xin, xout, ln_done, scales_v)
    p.realize()
    return nc


def _tile_oc(w, P=128, Q=128):
    O, I = w.shape
    t = w.reshape(O // Q, Q, I // P, P)
    return np.ascontiguousarray(t.transpose(0, 3, 2, 1))


def l1ffn_host_weights(inputs, prefix="L1_ffn"):
    g = lambda n: np.asarray(inputs[prefix + "_" + n])
    colmaj = lambda s, nb: np.ascontiguousarray(s.reshape(nb, 128).T)
    return {
        "f_w1": _tile_oc(g("w1_w")), "f_w2": _tile_oc(g("w2_w")),
        "f_h1": _tile_oc(g("h1_w")), "f_h2": _tile_oc(g("h2_w")),
        "f_w3": _tile_oc(g("w3_w")),
        "f_s1c": colmaj(g("w1_s"), 32), "f_s2c": colmaj(g("w2_s"), 32),
        "f_sh1c": colmaj(g("h1_s"), 32), "f_sh2c": colmaj(g("h2_s"), 32),
        "f_s3c": colmaj(g("w3_s"), 8),
    }


def l1attn_buffers(nc, B):
    SB = nc.alloc_sbuf_tensor
    B.zbs = B.ytmp[:, 0:128]
    B.QTs = SB("QTs", [64, 16 * 512], BF16)
    B.KTs = SB("KTs", [64, 16 * 512], BF16)
    B.VA = SB("VA", [128, 4 * 16 * 128], BF16)
    B.E2 = SB("E2", [128, 2 * 2048], BF16)
    B.CTs = SB("CTs", [128, 8 * 512], BF16)
    B.ATTs = SB("ATTs", [128, 8 * 512], F32)
    B.WS = SB("WS", [128, 2 * 1024], BF16)
    B.WV = SB("WV", [128, 2 * 1024], BF16)
    B.bqc = SB("bqc", [128, 8], F32)
    B.bkc = SB("bkc", [128, 8], F32)
    B.boc = SB("boc", [128, 8], F32)
    B.bvrow = SB("bvrow", [1, 1024], BF16)
    B.gamc = SB("gamc", [128, 8], F32)
    B.betc = SB("betc", [128, 8], F32)
    B.neg4c = SB("neg4c", [128, 1], F32)
    B.rz = SB("rz", [1, 512], F32)


def emit_l1attn_preload(nc, p, B, W):
    for nm in ("bqc", "bkc", "boc", "gamc", "betc"):
        p.dma("sync", getattr(B, nm)[:], W[nm], inc="dx")
    p.dma("gpsimd", B.bvrow[:], W["bv"], inc="dg")
    p.gop("memset", B.neg4c[:], -4.0)
    return p.cnt["dx"], p.cnt["dg"], p.cnt["g"]


def emit_l1attn(nc, p, B, W, ln_done, pre):
    dxs, dgs, gs = pre
    ws_rd = {}
    wv_rd = {}
    p.wait("tensor", "dx", dxs)
    for sg in range(4):
        t0 = sg * 512
        lnv = ln_done[sg]
        g0 = p.gop("memset", B.VA[:], 0.0,
                   waits=[("pe", p.cnt["pe"])] if sg > 0 else ())
        va_ones = bass.AP(B.VA, 0, [[4 * 16 * 128, 128], [128, 64], [1, 1]])
        g1 = p.gop("memset", va_ones, 1.0)
        for ic in range(8):
            db = ic % 2
            dgw = p.dma("gpsimd", B.WV[:, db * 1024:(db + 1) * 1024],
                        W["wv"][ic * 128:(ic + 1) * 128, :], inc="dg",
                        waits=[("pe", wv_rd.get(db, 0))])
            for tt in range(4):
                for n in range(2):
                    bk = tt * 2 + n
                    w = []
                    if ic == 0:
                        w = [("dg", dgw), ("v", lnv)] + p.bank_free[bk]
                    else:
                        w = [("dg", dgw)] if (tt == 0 and n == 0) else []
                    mn = p.mm(B.pb[bk][:, 0:512],
                              B.XN[:, ic * T + t0 + tt * 128: ic * T + t0 + (tt + 1) * 128],
                              B.WV[:, db * 1024 + n * 512: db * 1024 + (n + 1) * 512],
                              start=(ic == 0), stop=False, waits=w,
                              inc="pe" if (tt == 3 and n == 1) else None)
                    if tt == 3 and n == 1:
                        wv_rd[db] = mn
        for tt in range(4):
            for n in range(2):
                bk = tt * 2 + n
                p.mm(B.pb[bk][:, 0:512], B.onesrow[0:1, 0:128],
                     B.bvrow[0:1, n * 512:(n + 1) * 512],
                     start=False, stop=True, inc="pe")
        wv_rd[0] = wv_rd[1] = pe_v = p.cnt["pe"]
        vva = None
        for tt in range(4):
            for n in range(2):
                bk = tt * 2 + n
                dst = bass.AP(B.VA, tt * 2048 + n * 8 * 128 + 64,
                              [[4 * 16 * 128, 128], [128, 8], [1, 64]])
                srcv = bass.AP(B.pb8h, bk * 512, [[4096, 128], [64, 8], [1, 64]])
                vva = p.vop("tensor_copy", dst, srcv,
                            waits=[("pe", pe_v), ("g", g1)])
                p.bank_free[bk] = [("v", vva)]
        qk_done = None
        for wi, (wkey, bcol, dstbuf) in enumerate((("wq", B.bqc, B.QTs),
                                                   ("wk", B.bkc, B.KTs))):
            for cb in range(8):
                db = (wi * 8 + cb) % 2
                dgw = p.dma("gpsimd", B.WS[:, db * 1024:(db + 1) * 1024],
                            W[wkey][cb], inc="dg", waits=[("pe", ws_rd.get(db, 0))])
                bk = (wi * 8 + cb) % 4
                for ic in range(8):
                    w = []
                    if ic == 0:
                        w = [("dg", dgw)] + p.bank_free[bk]
                    mn = p.mm(B.pb[bk][:, 0:512],
                              B.WS[:, db * 1024 + ic * 128: db * 1024 + (ic + 1) * 128],
                              B.XN[:, ic * T + t0: ic * T + t0 + 512],
                              start=(ic == 0), stop=(ic == 7), waits=w,
                              inc="pe")
                    ws_rd[db] = mn
                p.sop("activation", dstbuf[0:64, (2 * cb) * 512:(2 * cb + 1) * 512],
                      B.pb8[0:64, bk * 512:(bk + 1) * 512], AF.Identity,
                      bias=bcol[0:64, cb:cb + 1],
                      waits=[("pe", p.cnt["pe"])])
                qk_done = p.sop("activation",
                                dstbuf[0:64, (2 * cb + 1) * 512:(2 * cb + 2) * 512],
                                B.pb8[64:128, bk * 512:(bk + 1) * 512], AF.Identity,
                                bias=bcol[64:128, cb:cb + 1])
                p.bank_free[bk] = [("s", qk_done)]
        pend_norm = [None]

        def flush_norm(p=p, B=B):
            if pend_norm[0] is None:
                return
            vz_, cbk_, gi_, q4_ = pend_norm[0]
            pend_norm[0] = None
            p.mm(B.pb[4][:, 0:512], B.ones128f[0:1, 0:128], B.rz[0:1, 0:512],
                 start=True, stop=True, waits=[("v", vz_)] + p.bank_free[4],
                 inc="pe")
            vzb = p.vop("tensor_copy", B.ytmp[:], B.pb[4][:, 0:512],
                        waits=[("pe", p.cnt["pe"])])
            p.bank_free[4] = [("v", vzb)]
            vct = None
            for par in range(2):
                dst = bass.AP(B.CTs, par * 64 * 4096 + (q4_ * 2) * 512 + gi_ * 128,
                              [[4096, 64], [512, 2], [1, 128]])
                csrc = bass.AP(B.pb8h, cbk_ * 512 + 64 * 4096 + par * 128,
                               [[4096, 64], [256, 2], [1, 128]])
                zsrc = bass.AP(B.ytmp, 64 * 512 + par * 128,
                               [[512, 64], [256, 2], [1, 128]])
                vct = p.vop("tensor_tensor", dst, csrc, zsrc, OP.mult)
            p.bank_free[cbk_] = [("v", vct)]

        def emit_S(gi, cq):
            qv = B.QTs[0:64, cq * 512 + gi * 128: cq * 512 + (gi + 1) * 128]
            for ck in range(16):
                kv = B.KTs[0:64, ck * 512 + gi * 128: ck * 512 + (gi + 1) * 128]
                w = []
                if ck % 4 == 0:
                    w = list(p.bank_free[ck // 4])
                    if gi == 0 and cq == 0:
                        w.append(("s", qk_done))
                p.mm(B.pb8[:, ck * 128:(ck + 1) * 128],
                     kv, qv, start=True, stop=True, waits=w,
                     inc="pe" if ck in (7, 15) else None)
            pe_s = p.cnt["pe"]
            eslot = cq % 2
            se = None
            for half in range(2):
                se = p.sop("activation",
                           B.E2[:, eslot * 2048 + half * 1024:
                                eslot * 2048 + (half + 1) * 1024],
                           B.pb8[:, half * 1024:(half + 1) * 1024],
                           AF.Exp, scale=0.125, bias=B.neg4c[:, 0:1],
                           waits=[("pe", pe_s - 1 + half)])
            for i in range(4):
                p.bank_free[i] = [("s", se)]
            return se

        def emit_ctx(gi, cq, se):
            eslot = cq % 2
            cbk = 6 + (cq // 4) % 2
            slot = cq % 4
            for ck in range(16):
                w = []
                if ck == 0:
                    w = [("s", se)] + (p.bank_free[cbk] if slot == 0 else [])
                    if gi == 0 and cq == 0:
                        w += [("v", vva), ("g", g1)]
                p.mm(B.pb8[:, cbk * 512 + slot * 128: cbk * 512 + (slot + 1) * 128],
                     B.VA[:, (gi * 16 + ck) * 128:(gi * 16 + ck + 1) * 128],
                     B.E2[:, eslot * 2048 + ck * 128: eslot * 2048 + (ck + 1) * 128],
                     start=(ck == 0), stop=(ck == 15), waits=w,
                     inc="pe" if ck == 15 else None)
            if slot == 3:
                vz = p.vop("reciprocal", B.rz[:],
                           bass.AP(B.pb8h, cbk * 512, [[4096, 1], [1, 512]]),
                           waits=[("pe", p.cnt["pe"])])
                pend_norm[0] = (vz, cbk, gi, cq // 4)

        prev = None
        for gi in range(4):
            for cq in range(16):
                se = emit_S(gi, cq)
                if cq % 4 == 1:
                    flush_norm()
                if prev is not None:
                    emit_ctx(prev[0], prev[1], prev[2])
                prev = (gi, cq, se)
        emit_ctx(prev[0], prev[1], prev[2])
        flush_norm()
        v_ct = p.cnt["v"]
        if B.adbg and sg == B.dbg_sg:
            p.dma("sync", B.adbg["QTs"], B.QTs[:], inc="do", waits=[("v", v_ct)])
            p.dma("sync", B.adbg["KTs"], B.KTs[:], inc="do")
            p.dma("sync", B.adbg["VA"], B.VA[:], inc="do")
            p.dma("sync", B.adbg["CTs"], B.CTs[:], inc="do")
            p.dma("sync", B.adbg["E2"], B.E2[:], inc="do")
        for ob in range(8):
            db = ob % 2
            dgw = p.dma("gpsimd", B.WS[:, db * 1024:(db + 1) * 1024],
                        W["wo"][ob], inc="dg", waits=[("pe", ws_rd.get(db, 0))])
            bk = ob % 4
            for ec in range(8):
                w = []
                if ec == 0:
                    w = [("dg", dgw), ("v", v_ct)] + p.bank_free[bk]
                mn = p.mm(B.pb[bk][:, 0:512],
                          B.WS[:, db * 1024 + ec * 128: db * 1024 + (ec + 1) * 128],
                          B.CTs[:, ec * 512:(ec + 1) * 512],
                          start=(ec == 0), stop=(ec == 7), waits=w, inc="pe")
                ws_rd[db] = mn
            vo = p.vop("scalar_tensor_tensor", B.ATTs[:, ob * 512:(ob + 1) * 512],
                       B.pb[bk][:, 0:512], B.boc[:, ob:ob + 1],
                       B.XN[:, ob * T + t0: ob * T + t0 + 512], OP.add, OP.add,
                       waits=[("pe", p.cnt["pe"])] + ([("s", p.cnt["s"])] if ob == 0 else []))
            p.bank_free[bk] = [("v", vo)]
        if B.adbg and sg == B.dbg_sg:
            p.dma("sync", B.adbg["ATTs"], B.ATTs[:], inc="do", waits=[("v", p.cnt["v"])])
        v_att = p.cnt["v"]
        ia, ib = 4, 5
        ms = {}
        for c in range(NCH):
            db2 = c % 2
            sl = slice(db2 * 512, (db2 + 1) * 512)
            asl = B.ATTs[:, c * 512:(c + 1) * 512]
            w = [("pe", ms[c - 2])] if c >= 2 else []
            vcp = p.vop("tensor_copy", B.xbfs[:, sl], asl, waits=w)
            scp = p.sop("activation", B.xsqs[:, sl], asl, AF.Square,
                        waits=w + ([("v", v_att)] if c == 0 else []))
            w2_ = [("v", vcp), ("s", scp)]
            if c == 0:
                w2_ += p.bank_free[ia] + p.bank_free[ib]
            p.mm(B.pb[ia][:, 0:512], B.ones128[:], B.xbfs[:, sl],
                 start=(c == 0), stop=(c == NCH - 1), waits=w2_)
            ms[c] = p.mm(B.pb[ib][:, 0:512], B.ones128[:], B.xsqs[:, sl],
                         start=(c == 0), stop=(c == NCH - 1), inc="pe")
        pe_done = p.cnt["pe"]
        p.vop("tensor_scalar", B.meanbc[:], B.pb[ia][:, 0:512], 1.0 / D, None,
              OP.mult, waits=[("pe", pe_done)])
        p.vop("tensor_scalar", B.msqbc[:], B.pb[ib][:, 0:512], 1.0 / D, None, OP.mult)
        p.vop("tensor_tensor", B.rstdbc[:], B.meanbc[:], B.meanbc[:], OP.mult)
        vfree = p.vop("tensor_tensor", B.msqbc[:], B.msqbc[:], B.rstdbc[:], OP.subtract)
        p.bank_free[ia] = p.bank_free[ib] = [("v", vfree)]
        p.sop("activation", B.rstdbc[:], B.msqbc[:], AF.Ln, bias=B.epscol[:],
              waits=[("v", vfree)])
        ssq = p.sop("activation", B.rstdbc[:], B.rstdbc[:], AF.Exp, scale=-0.5)
        p.wait("vector", "s", ssq)
        for c in range(NCH):
            asl = B.ATTs[:, c * 512:(c + 1) * 512]
            xsl = B.XR[:, c * T + t0: c * T + t0 + 512]
            p.vop("tensor_tensor", B.ytmp[:], asl, B.meanbc[:], OP.subtract)
            p.vop("tensor_tensor", B.ytmp[:], B.ytmp[:], B.rstdbc[:], OP.mult)
            p.vop("tensor_scalar", B.ytmp[:], B.ytmp[:], B.gamc[:, c:c + 1],
                  B.betc[:, c:c + 1], OP.mult, OP.add)
            p.vop("tensor_tensor", xsl, xsl, B.ytmp[:], OP.add)
    return p.cnt["v"]


def l1attn_dram(nc):
    return {
        "wq": nc.dram_tensor("a_wq", [8, 128, 8, 128], F32, kind="ExternalInput").ap(),
        "wk": nc.dram_tensor("a_wk", [8, 128, 8, 128], F32, kind="ExternalInput").ap(),
        "wv": nc.dram_tensor("a_wv", [1024, 1024], F32, kind="ExternalInput").ap(),
        "wo": nc.dram_tensor("a_wo", [8, 128, 8, 128], F32, kind="ExternalInput").ap(),
        "bqc": nc.dram_tensor("a_bqc", [128, 8], F32, kind="ExternalInput").ap(),
        "bkc": nc.dram_tensor("a_bkc", [128, 8], F32, kind="ExternalInput").ap(),
        "bv": nc.dram_tensor("a_bv", [1, 1024], F32, kind="ExternalInput").ap(),
        "boc": nc.dram_tensor("a_boc", [128, 8], F32, kind="ExternalInput").ap(),
        "gamc": nc.dram_tensor("a_gamc", [128, 8], F32, kind="ExternalInput").ap(),
        "betc": nc.dram_tensor("a_betc", [128, 8], F32, kind="ExternalInput").ap(),
    }


def build_l1attn_program(debug=False, dbg_sg=0):
    nc = bass.Bass("TRN2", target_bir_lowering=False)
    xin = nc.dram_tensor("xin", [D, T], F32, kind="ExternalInput").ap()
    W = l1attn_dram(nc)
    xout = nc.dram_tensor("xout", [D, T], F32, kind="ExternalOutput").ap()
    p = P(nc)
    B = make_common(nc, p)
    l1attn_buffers(nc, B)
    B.adbg = {}
    B.dbg_sg = dbg_sg
    if debug:
        for nm, sh, dt in (("QTs", [64, 8192], BF16), ("KTs", [64, 8192], BF16),
                           ("VA", [128, 8192], BF16), ("CTs", [128, 4096], BF16),
                           ("ATTs", [128, 4096], F32), ("E2", [128, 4096], BF16)):
            B.adbg[nm] = nc.dram_tensor("dbg_" + nm, sh, dt, kind="ExternalOutput").ap()
    dx = load_x(nc, p, B, xin)
    pre = emit_l1attn_preload(nc, p, B, W)
    ln_done = emit_ln(nc, p, B, dx, svar=False)
    vend = emit_l1attn(nc, p, B, W, ln_done, pre)
    store_x(nc, p, B, xout, vend)
    p.realize()
    return nc


def l1attn_host_weights(inputs, prefix="L1_attn"):
    g = lambda n: np.asarray(inputs[prefix + "_" + n])
    colmaj = lambda s: np.ascontiguousarray(s.reshape(8, 128).T)
    def tile_nat(w):
        t = w.reshape(8, 128, 8, 128)
        return np.ascontiguousarray(w.T.reshape(8, 128, 8, 128).transpose(0, 3, 2, 1))
    def tile_lhsT(w):
        t = w.reshape(8, 128, 8, 128)
        return np.ascontiguousarray(t.transpose(2, 1, 0, 3))
    return {
        "a_wq": tile_lhsT(g("wq")), "a_wk": tile_lhsT(g("wk")),
        "a_wv": np.ascontiguousarray(g("wv")), "a_wo": tile_lhsT(g("wo")),
        "a_bqc": colmaj(g("bq")), "a_bkc": colmaj(g("bk")),
        "a_bv": np.ascontiguousarray(g("bv")[None, :]),
        "a_boc": colmaj(g("bo")), "a_gamc": colmaj(g("gamma")),
        "a_betc": colmaj(g("beta")),
    }


def l0attn_buffers(nc, B):
    SB = nc.alloc_sbuf_tensor
    B.EQs = SB("EQs", [128, 4 * 1024], BF16)
    B.EKs = SB("EKs", [128, 4 * 1024], BF16)
    B.Vs = SB("Vs", [128, 4 * 1024], BF16)
    B.QTi = SB("QTi", [64, 2048], BF16)
    B.KTi = SB("KTi", [64, 2048], BF16)
    B.VTi = SB("VTi", [64, 2048], BF16)
    B.At = SB("At", [16, 2048], BF16)
    B.Vr = SB("Vr", [16, 64 + 128 * 64], BF16)
    B.rQ = SB("rQ", [128, 16], F32)
    B.rK = SB("rK", [128, 64], F32)
    B.rKT = SB("rKT", [64, 128], F32)
    B.mxc = SB("mxc", [128, 1], F32)
    B.WQ3 = SB("WQ3", [128, 2 * 1024], BF16)


def emit_l0attn(nc, p, B, W, ln_done):
    w3_rd = {}
    ident = B.ident
    p.gop("memset", B.Vr[0:16, 0:64], 0.0)
    for sg in range(4):
        t0 = sg * 512
        for wi, wkey in enumerate(("wqT", "wkT", "wvT")):
            dst = (B.EQs, B.EKs, B.Vs)[wi]
            for ic in range(8):
                db = ic % 2
                dgw = p.dma("gpsimd", B.WQ3[:, db * 1024:(db + 1) * 1024],
                            W[wkey][ic * 128:(ic + 1) * 128, :], inc="dg",
                            waits=[("pe", w3_rd.get(db, 0))])
                for tt in range(4):
                    for n in range(2):
                        bk = tt * 2 + n
                        w = []
                        if ic == 0:
                            w = [("dg", dgw)] + p.bank_free[bk]
                            if wi == 0 and tt == 0 and n == 0:
                                w.append(("v", ln_done[sg]))
                        elif tt == 0 and n == 0:
                            w = [("dg", dgw)]
                        mn = p.mm(B.pb[bk][:, 0:512],
                                  B.XN[:, ic * T + t0 + tt * 128: ic * T + t0 + (tt + 1) * 128],
                                  B.WQ3[:, db * 1024 + n * 512: db * 1024 + (n + 1) * 512],
                                  start=(ic == 0), stop=(ic == 7), waits=w,
                                  inc="pe" if (tt == 3 and n == 1) or ic == 7 else None)
                        if tt == 3 and n == 1:
                            w3_rd[db] = mn
            for tt in range(4):
                pe_t = p.cnt["pe"]
                if wi < 2:
                    vmx = p.vop("tensor_reduce", B.mxc[:],
                                bass.AP(B.pb8h, tt * 1024, [[4096, 128], [1, 1024]]),
                                axis=AX.X, op=OP.max, waits=[("pe", pe_t)])
                    p.vop("tensor_scalar_mul", B.mxc[:], B.mxc[:], -1.0)
                    p.wait("scalar", "v", p.cnt["v"])
                    se = p.sop("activation", dst[:, tt * 1024:(tt + 1) * 1024],
                               bass.AP(B.pb8h, tt * 1024, [[4096, 128], [1, 1024]]),
                               AF.Exp, bias=B.mxc[:], waits=[("pe", pe_t)])
                    rel = [("s", se)]
                else:
                    sv = p.sop("activation", dst[:, tt * 1024:(tt + 1) * 1024],
                               bass.AP(B.pb8h, tt * 1024, [[4096, 128], [1, 1024]]),
                               AF.Identity, waits=[("pe", pe_t)])
                    rel = [("s", sv)]
                p.bank_free[tt * 2] = rel
                p.bank_free[tt * 2 + 1] = rel
        for tt in range(4):
            toff = t0 + tt * 128
            eq = B.EQs[:, tt * 1024:(tt + 1) * 1024]
            ek = B.EKs[:, tt * 1024:(tt + 1) * 1024]
            vv = B.Vs[:, tt * 1024:(tt + 1) * 1024]
            p.wait("vector", "s", p.cnt["s"])
            p.vop("tensor_reduce", B.rQ[:],
                  bass.AP(B.EQs, tt * 1024, [[4096, 128], [64, 16], [1, 64]]),
                  axis=AX.X, op=OP.add)
            p.vop("reciprocal", B.rQ[:], B.rQ[:])
            p.vop("tensor_tensor",
                  bass.AP(B.EQs, tt * 1024, [[4096, 128], [64, 16], [1, 64]]),
                  bass.AP(B.EQs, tt * 1024, [[4096, 128], [64, 16], [1, 64]]),
                  bass.AP(B.rQ, 0, [[16, 128], [1, 16], [0, 64]]), OP.mult)
            p.vop("tensor_reduce", B.rK[:],
                  bass.AP(B.EKs, tt * 1024, [[4096, 128], [1, 64], [64, 16]]),
                  axis=AX.X, op=OP.add)
            vrk = p.vop("reciprocal", B.rK[:], B.rK[:])
            p2h = B.pb8h[:].bitcast(BF16)
            for src_, dstb, base in ((eq, B.QTi, 0), (ek, B.KTi, 2048), (vv, B.VTi, 4096)):
                pe_w = [("v", p.cnt["v"])]
                for h in range(16):
                    p.tr(p2h[0:64, base + h * 128:base + (h + 1) * 128],
                         src_[:, h * 64:(h + 1) * 64], ident[:],
                         waits=pe_w if h == 0 else (),
                         inc="pe" if h == 15 else None)
            p.tr(bass.AP(B.pb8h, 3072, [[4096, 64], [1, 128]]),
                 B.rK[:], B.identf[:], waits=[("v", vrk)], inc="pe")
            pe_tr = p.cnt["pe"]
            import dataclasses as _dc
            def _pview(off_bf16, pat):
                return _dc.replace(p2h, offset=off_bf16, ap=pat)
            vr1 = p.vop("tensor_copy", B.rKT[:],
                        bass.AP(B.pb8h, 3072, [[4096, 64], [1, 128]]),
                        waits=[("pe", pe_tr)])
            p.vop("tensor_copy",
                  bass.AP(B.QTi, 0, [[2048, 64], [1, 16], [16, 128]]),
                  _pview(0, [[8192, 64], [128, 16], [1, 128]]))
            v_qk = p.vop("scalar_tensor_tensor",
                         bass.AP(B.KTi, 0, [[2048, 64], [1, 16], [16, 128]]),
                         _pview(2048, [[8192, 64], [128, 16], [1, 128]]),
                         1.0,
                         bass.AP(B.rKT, 0, [[128, 64], [0, 16], [1, 128]]),
                         OP.mult, OP.mult)
            v_v = p.vop("tensor_copy",
                        bass.AP(B.VTi, 0, [[2048, 64], [1, 16], [16, 128]]),
                        _pview(4096, [[8192, 64], [128, 16], [1, 128]]))
            p.wait("tensor", "v", v_qk)
            for half in range(2):
                abase = 3072 if half == 0 else 2048
                bks = (6, 7) if half == 0 else (4, 5)
                for ti in range(64):
                    tok = half * 64 + ti
                    w0 = []
                    if ti == 0:
                        w0 = p.bank_free[bks[0]] + p.bank_free[bks[1]]
                        w0.append(("v", vr1 if half == 0 else v_v))
                    p.mm(bass.AP(B.pb8h, abase + ti * 16, [[4096, 16], [1, 16]]),
                         B.KTi[0:64, tok * 16:(tok + 1) * 16],
                         B.QTi[0:64, tok * 16:(tok + 1) * 16],
                         start=True, stop=True, waits=w0,
                         inc="pe" if ti == 63 else None)
                vsrc = bass.AP(B.pb8h, abase, [[4096, 16], [16, 64], [1, 2], [2, 8]])
                vdst = bass.AP(B.At, half * 1024, [[2048, 16], [16, 64], [8, 2], [1, 8]])
                va = p.vop("tensor_copy", vdst, vsrc, waits=[("pe", p.cnt["pe"])])
                p.bank_free[bks[0]] = [("v", va)]
                p.bank_free[bks[1]] = [("v", va)]
            for grp in range(4):
                vb0 = (grp % 2) * 2
                for ti in range(32):
                    tok = grp * 32 + ti
                    p.tr(p2h[0:16, vb0 * 1024 + ti * 64:vb0 * 1024 + (ti + 1) * 64],
                         B.VTi[0:64, tok * 16:(tok + 1) * 16], ident[0:64, 0:64],
                         waits=((p.bank_free[vb0] + p.bank_free[vb0 + 1]) if ti == 0 else ()),
                         inc="pe" if ti == 31 else None)
                vv2 = p.vop("tensor_copy",
                            B.Vr[0:16, 64 + grp * 2048: 64 + (grp + 1) * 2048],
                            p2h[0:16, vb0 * 1024:vb0 * 1024 + 2048],
                            waits=[("pe", p.cnt["pe"])])
                p.bank_free[vb0] = [("v", vv2)]
                p.bank_free[vb0 + 1] = [("v", vv2)]
            v_vr = p.cnt["v"]
            p.wait("tensor", "v", v_vr)
            for half in range(2):
                ebk, obk = (0, 1) if half == 0 else (2, 3)
                for ti in range(64):
                    tok = half * 64 + ti
                    w_e = (p.bank_free[ebk] + p.bank_free[obk]) if ti == 0 else ()
                    p.mm(bass.AP(B.pb8h, ebk * 512 + ti * 8, [[4096, 64], [1, 8]]),
                         B.Vr[0:16, 64 + tok * 64: 64 + (tok + 1) * 64],
                         B.At[0:16, tok * 16: tok * 16 + 8],
                         start=True, stop=True, waits=w_e)
                    p.mm(bass.AP(B.pb8h, obk * 512 + ti * 8, [[4096, 128], [1, 8]]),
                         B.Vr[0:16, tok * 64: tok * 64 + 128],
                         B.At[0:16, tok * 16 + 8: tok * 16 + 16],
                         start=True, stop=True,
                         inc="pe" if ti == 63 else None)
                pe_s2 = p.cnt["pe"]
                vlast = p.vop("scalar_tensor_tensor",
                              bass.AP(B.XR, toff + half * 64, [[16384, 64], [2048, 8], [1, 64]]),
                              bass.AP(B.pb8h, ebk * 512, [[4096, 64], [1, 8], [8, 64]]),
                              1.0,
                              bass.AP(B.XR, toff + half * 64, [[16384, 64], [2048, 8], [1, 64]]),
                              OP.mult, OP.add, waits=[("pe", pe_s2)])
                vlast = p.vop("scalar_tensor_tensor",
                              bass.AP(B.XR, 64 * 16384 + toff + half * 64,
                                      [[16384, 64], [2048, 8], [1, 64]]),
                              bass.AP(B.pb8h, obk * 512 + 64 * 4096, [[4096, 64], [1, 8], [8, 64]]),
                              1.0,
                              bass.AP(B.XR, 64 * 16384 + toff + half * 64,
                                      [[16384, 64], [2048, 8], [1, 64]]),
                              OP.mult, OP.add)
                p.bank_free[ebk] = [("v", vlast)]
                p.bank_free[obk] = [("v", vlast)]
    return p.cnt["v"]


def l0attn_dram(nc):
    return {
        "wqT": nc.dram_tensor("l0_wqT", [1024, 1024], F32, kind="ExternalInput").ap(),
        "wkT": nc.dram_tensor("l0_wkT", [1024, 1024], F32, kind="ExternalInput").ap(),
        "wvT": nc.dram_tensor("l0_wvT", [1024, 1024], F32, kind="ExternalInput").ap(),
    }


def build_l0attn_program():
    nc = bass.Bass("TRN2", target_bir_lowering=False)
    xin = nc.dram_tensor("xin", [D, T], F32, kind="ExternalInput").ap()
    W = l0attn_dram(nc)
    xout = nc.dram_tensor("xout", [D, T], F32, kind="ExternalOutput").ap()
    p = P(nc)
    B = make_common(nc, p)
    l0attn_buffers(nc, B)
    dx = load_x(nc, p, B, xin)
    ln_done = emit_ln(nc, p, B, dx, svar=False)
    vend = emit_l0attn(nc, p, B, W, ln_done)
    store_x(nc, p, B, xout, vend)
    p.realize()
    return nc


def l0attn_host_weights(inputs):
    return {
        "l0_wqT": np.ascontiguousarray(np.asarray(inputs["L0_attn_W_Q"]).T),
        "l0_wkT": np.ascontiguousarray(np.asarray(inputs["L0_attn_W_K"]).T),
        "l0_wvT": np.ascontiguousarray(np.asarray(inputs["L0_attn_W_V"]).T),
    }


_SBUF_TOP = 229344

_ARENA_GROUPS = {
    "moe": [("svarbc", [128, T], "BF16"), ("rwt", [128, NCH * 32], "F32"),
            ("rwsc", [128, 32], "F32"), ("rbrow", [1, 32], "BF16"),
            ("b1c", [128, 64], "F32"), ("b2t", [128, 256], "F32"),
            ("idxcols", [128, 16], "F32"), ("MT", [32, 2048], "BF16"),
            ("lmax", [128, 1], "F32"), ("lsel", [128, 32], "F32"),
            ("iote", [32, 1], "BF16"), ("Mtm", [128, 16 * 32], "BF16"),
            ("Ltile", [128, 2 * 2048], "BF16"), ("ranksb", [32, 2048], "BF16"),
            ("w1f32", [128, 2 * 2048], "F32"), ("w2f32", [128, 2 * 2048], "F32"),
            ("tokids", [128, 16], "I32"),
            ("xtms", [128, 2 * 1024], "BF16"), ("xg", [128, 2 * 1024], "BF16"),
            ("xgT", [128, 2 * 1024], "BF16"), ("hT", [128, 2 * 256], "BF16"),
            ("yT", [128, 2 * 1024], "BF16"), ("ytm", [128, 2 * 1024], "BF16"),
            ("w1sb", [128, 2 * 2048], "BF16"), ("w2sb", [128, 2 * 2048], "BF16")],
    "l0a": [("EQs", [128, 4 * 1024], "BF16"), ("EKs", [128, 4 * 1024], "BF16"),
            ("Vs", [128, 4 * 1024], "BF16"), ("QTi", [64, 2048], "BF16"),
            ("KTi", [64, 2048], "BF16"), ("VTi", [64, 2048], "BF16"),
            ("At", [16, 2048], "BF16"), ("Vr", [16, 64 + 128 * 64], "BF16"),
            ("rQ", [128, 16], "F32"), ("rK", [128, 64], "F32"),
            ("rKT", [64, 128], "F32"), ("mxc", [128, 1], "F32"),
            ("WQ3", [128, 2 * 1024], "BF16")],
    "l1f": [("GH", [128, 32768], "BF16"), ("wA", [128, 2 * 4096], "BF16"),
            ("wB", [128, 2 * 1024], "BF16"), ("wC", [128, 2 * 1024], "BF16"),
            ("s1c", [128, 32], "F32"), ("s2c", [128, 32], "F32"),
            ("sh1c", [128, 32], "F32"), ("sh2c", [128, 32], "F32"),
            ("s3c", [128, 8], "F32"), ("s3h", [128, 8], "F32")],
    "l1a": [("QTs", [64, 16 * 512], "BF16"), ("KTs", [64, 16 * 512], "BF16"),
            ("VA", [128, 4 * 16 * 128], "BF16"), ("E2", [128, 2 * 2048], "BF16"),
            ("CTs", [128, 8 * 512], "BF16"), ("ATTs", [128, 8 * 512], "F32"),
            ("WS", [128, 2 * 1024], "BF16"), ("WV", [128, 2 * 1024], "BF16"),
            ("bqc", [128, 8], "F32"), ("bkc", [128, 8], "F32"),
            ("boc", [128, 8], "F32"), ("bvrow", [1, 1024], "BF16"),
            ("gamc", [128, 8], "F32"), ("betc", [128, 8], "F32"),
            ("neg4c", [128, 1], "F32"), ("rz", [1, 512], "F32")],
}


def _alloc_arena(nc, base):
    dts = {"F32": F32, "BF16": BF16, "I32": I32}
    handles = {}
    maxend = base
    for phase, members in _ARENA_GROUPS.items():
        off = base
        hs = {}
        for name, shape, dt in members:
            nb = shape[-1] * (4 if dt == "F32" else 2)
            for s in shape[1:-1]:
                nb *= s
            off = (off + 31) & ~31
            hs[name] = nc.alloc_sbuf_tensor_at(phase + "_" + name, shape, dts[dt],
                                               offset=off)
            off += nb
        assert off <= _SBUF_TOP, (phase, off)
        maxend = max(maxend, off)
        handles[phase] = hs
    return handles


def build_full_program(skip=()):
    nc = bass.Bass("TRN2", target_bir_lowering=False)
    xin = nc.dram_tensor("xin", [D, T], F32, kind="ExternalInput").ap()
    Wm = moer_dram(nc)
    W0 = l0attn_dram(nc)
    Wf = l1ffn_dram(nc)
    Wa = l1attn_dram(nc)
    xout = nc.dram_tensor("xout", [D, T], F32, kind="ExternalOutput").ap()
    xsp = nc.dram_tensor("xspill", [D, T], F32, kind="Internal").ap()
    scr = {
        "xn_tm": nc.dram_tensor("s_xn_tm", [T, D], BF16, kind="Internal").ap(),
        "y_tm": nc.dram_tensor("s_y_tm", [T, D], BF16, kind="Internal").ap(),
        "perm2": nc.dram_tensor("s_perm2", [32 * 128, 1], I32, kind="Internal").ap(),
    }
    p = P(nc)
    B = make_common(nc, p)
    B.scoli = nc.alloc_sbuf_tensor("fx_scoli", [128, 16], I32)
    B.permsb = nc.alloc_sbuf_tensor("fx_permsb", [128, 32], I32)
    base = _SBUF_TOP - nc.sbuf_bytes_remaining
    handles = _alloc_arena(nc, base)

    def bind(phase):
        for name, h in handles[phase].items():
            setattr(B, name, h)
        if phase == "l1a":
            B.zbs = B.ytmp[:, 0:128]
        if phase == "moe":
            B.srow = B.w1f32[0:1, 0:2048]
        if phase == "l1f":
            B.Hv = B.XR[:].bitcast(BF16)

    dx = load_x(nc, p, B, xin)
    if 1 not in skip:
        bind("moe")
        B.rdbg = {}
        sdg = emit_moer_preload(nc, p, B, Wm)
        ln = emit_ln(nc, p, B, x_ready_dx=dx, svar=True)
        emit_moe_routed(nc, p, B, Wm, ln, sdg, scr)
        phase_barrier(p)
    if 2 not in skip:
        bind("l0a")
        ln = emit_ln(nc, p, B, x_ready_dx=dx, v_ready=p.cnt["v"], svar=False)
        emit_l0attn(nc, p, B, W0, ln)
        phase_barrier(p)
    if 3 not in skip:
        bind("moe")
        B.rdbg = {}
        sdg = emit_moer_preload(nc, p, B, Wm)
        ln = emit_ln(nc, p, B, x_ready_dx=dx, v_ready=p.cnt["v"], svar=True)
        emit_moe_routed(nc, p, B, Wm, ln, sdg, scr)
        phase_barrier(p)
    if 4 not in skip:
        bind("l1f")
        sv = emit_l1ffn_preload(nc, p, B, Wf)
        ln = emit_ln(nc, p, B, x_ready_dx=dx, v_ready=p.cnt["v"], svar=False)
        for c in range(NCH):
            p.dma("sync", xsp[c * 128:(c + 1) * 128, :], B.XR[:, c * T:(c + 1) * T],
                  inc="do")
        spill_do = p.cnt["do"]
        emit_l1ffn(nc, p, B, Wf, xsp, xsp, ln, sv, hv_waits=[("do", spill_do)])
        phase_barrier(p)
    if 5 not in skip:
        bind("l1a")
        if 4 not in skip:
            for c in range(NCH):
                p.dma("sync", B.XR[:, c * T:(c + 1) * T], xsp[c * 128:(c + 1) * 128, :],
                      inc="dx", waits=[("do", p.cnt["do"])] if c == 0 else ())
        dx2 = p.cnt["dx"]
        pre = emit_l1attn_preload(nc, p, B, Wa)
        ln = emit_ln(nc, p, B, x_ready_dx=dx2, svar=False)
        emit_l1attn(nc, p, B, Wa, ln, pre)
        phase_barrier(p)
    if 6 not in skip:
        bind("l1f")
        sv = emit_l1ffn_preload(nc, p, B, Wf)
        ln = emit_ln(nc, p, B, x_ready_dx=dx, v_ready=p.cnt["v"], svar=False)
        for c in range(NCH):
            p.dma("sync", xsp[c * 128:(c + 1) * 128, :], B.XR[:, c * T:(c + 1) * T],
                  inc="do")
        spill_do = p.cnt["do"]
        emit_l1ffn(nc, p, B, Wf, xsp, xout, ln, sv, hv_waits=[("do", spill_do)])
    else:
        store_x(nc, p, B, xout, p.cnt["v"])
    p.realize()
    return nc


def build_moer_program(debug=False):
    nc = bass.Bass("TRN2", target_bir_lowering=False)
    xin = nc.dram_tensor("xin", [D, T], F32, kind="ExternalInput").ap()
    W = moer_dram(nc)
    xout = nc.dram_tensor("xout", [D, T], F32, kind="ExternalOutput").ap()
    scr = {
        "xn_tm": nc.dram_tensor("s_xn_tm", [T, D], BF16, kind="Internal").ap(),
        "y_tm": nc.dram_tensor("s_y_tm", [T, D], BF16, kind="Internal").ap(),
        "perm2": nc.dram_tensor("s_perm2", [32 * 128, 1], I32, kind="Internal").ap(),
    }
    p = P(nc)
    B = make_common(nc, p)
    moer_buffers(nc, B)
    B.rdbg = {}
    if debug:
        for nm, sh, dt in (("srow", [1, 2048], F32), ("permsb", [128, 32], I32),
                           ("MT", [32, 2048], BF16), ("ranksb", [32, 2048], BF16),
                           ("xn_tm", [T, D], BF16), ("y_tm", [T, D], BF16),
                           ("xg", [128, 2048], BF16), ("hT", [128, 512], BF16)):
            B.rdbg[nm] = nc.dram_tensor("dbg_" + nm, sh, dt, kind="ExternalOutput").ap()
    dx = load_x(nc, p, B, xin)
    small_dg = emit_moer_preload(nc, p, B, W)
    ln_done = emit_ln(nc, p, B, x_ready_dx=dx, svar=True)
    vend = emit_moe_routed(nc, p, B, W, ln_done, small_dg, scr)
    if debug:
        p.dma("sync", B.rdbg["srow"], B.srow, inc="do", waits=[("v", vend)])
        p.dma("sync", B.rdbg["permsb"], B.permsb[:], inc="do")
        p.dma("sync", B.rdbg["MT"], B.MT[:], inc="do")
        p.dma("sync", B.rdbg["ranksb"], B.ranksb[:], inc="do")
        p.dma("sync", B.rdbg["xn_tm"], scr["xn_tm"], inc="do")
        p.dma("sync", B.rdbg["y_tm"], scr["y_tm"], inc="do")
        p.dma("sync", B.rdbg["xg"], B.xg[:], inc="do")
        p.dma("sync", B.rdbg["hT"], B.hT[:], inc="do")
    store_x(nc, p, B, xout, vend)
    p.realize()
    return nc


def moer_dram(nc):
    return {
        "rw_t": nc.dram_tensor("rw_t", [1024, 32], F32, kind="ExternalInput").ap(),
        "rb": nc.dram_tensor("rb", [1, 32], F32, kind="ExternalInput").ap(),
        "w1": nc.dram_tensor("w1", [32, 1024, 256], F32, kind="ExternalInput").ap(),
        "b1c": nc.dram_tensor("b1c", [128, 64], F32, kind="ExternalInput").ap(),
        "w2": nc.dram_tensor("w2", [32, 256, 1024], F32, kind="ExternalInput").ap(),
        "b2t": nc.dram_tensor("b2t", [128, 256], F32, kind="ExternalInput").ap(),
    }


def moer_host_weights(inputs, prefix="L0_ffn"):
    b2 = np.asarray(inputs[prefix + "_bias2"])
    return {
        "rw_t": np.ascontiguousarray(np.asarray(inputs[prefix + "_router_w"]).T),
        "rb": np.ascontiguousarray(np.asarray(inputs[prefix + "_router_b"])[None, :]),
        "w1": np.ascontiguousarray(np.asarray(inputs[prefix + "_weights1"])),
        "b1c": np.ascontiguousarray(np.asarray(inputs[prefix + "_bias1"]).reshape(32, 2, 128).transpose(2, 0, 1).reshape(128, 64)),
        "w2": np.ascontiguousarray(np.asarray(inputs[prefix + "_weights2"])),
        "b2t": np.ascontiguousarray(b2.reshape(32, 8, 128).transpose(2, 0, 1).reshape(128, 256)),
    }


_PROGRAMS = {}


def _get_program(name):
    if name not in _PROGRAMS:
        _PROGRAMS[name] = {"moe": build_moe_program, "l1ffn": build_l1ffn_program, "l1attn": build_l1attn_program, "l0attn": build_l0attn_program, "full": build_full_program, "moer": build_moer_program}[name]()
    return _PROGRAMS[name]


def kernel(**inputs):
    x = np.asarray(inputs["L0_attn_input_seq_embs"])
    xt = [np.ascontiguousarray(x[i].T) for i in range(8)]
    wmap = {}
    wmap.update(moer_host_weights(inputs))
    wmap.update(l0attn_host_weights(inputs))
    wmap.update(l1ffn_host_weights(inputs))
    wmap.update(l1attn_host_weights(inputs))
    nc = _get_program("full")
    in_maps = [dict(wmap, xin=xt[i]) for i in range(8)]
    res = run_bass_kernel_spmd(nc, in_maps, core_ids=list(range(8)))
    return np.stack([np.ascontiguousarray(r["xout"].T) for r in res.results])


# version 11
# speedup vs baseline: 1.0889x; 1.0162x over previous
import numpy as np
import ml_dtypes
import concourse.bass as bass
import concourse.mybir as mybir
from concourse.bass_utils import run_bass_kernel_spmd

F32 = mybir.dt.float32
BF16 = mybir.dt.bfloat16
I32 = mybir.dt.int32
AF = mybir.ActivationFunctionType
OP = mybir.AluOpType
AX = mybir.AxisListType

D = 1024
T = 2048
NCH = 8
NTC = 4
LN_EPS = 1e-5

ENGS = ("tensor", "vector", "scalar", "gpsimd", "sync")


class P:

    def __init__(self, nc):
        self.nc = nc
        self.q = {e: [] for e in ENGS}
        self.sems = {}
        self.cnt = {}
        for name in ("v", "s", "g", "pe", "dx", "dg", "do"):
            self.sems[name] = nc.alloc_semaphore("sem_" + name)
            self.cnt[name] = 0
        self.bank_free = [[] for _ in range(8)]

    def wait(self, eng, sem, val):
        if val <= 0:
            return
        s = self.sems[sem]
        self.q[eng].append(lambda E: E.wait_ge(s, val))

    def op(self, eng, meth, *args, inc=None, incby=1, waits=(), **kw):
        for sem, val in waits:
            self.wait(eng, sem, val)
        n = None
        if inc is not None:
            self.cnt[inc] += incby
            n = self.cnt[inc]
            s = self.sems[inc]

            def fn(E, meth=meth, args=args, kw=kw, s=s, incby=incby):
                getattr(E, meth)(*args, **kw).then_inc(s, incby)
        else:
            def fn(E, meth=meth, args=args, kw=kw):
                getattr(E, meth)(*args, **kw)
        self.q[eng].append(fn)
        return n

    def vop(self, meth, *args, waits=(), **kw):
        w = [("v", self.cnt["v"])] + list(waits)
        return self.op("vector", meth, *args, inc="v", waits=w, **kw)

    def sop(self, meth, *args, waits=(), **kw):
        w = [("s", self.cnt["s"])] + list(waits)
        return self.op("scalar", meth, *args, inc="s", waits=w, **kw)

    def gop(self, meth, *args, waits=(), **kw):
        w = [("g", self.cnt["g"])] + list(waits)
        return self.op("gpsimd", meth, *args, inc="g", waits=w, **kw)

    def mm(self, out, lhsT, rhs, start, stop, waits=(), inc=None, **kw):
        return self.op("tensor", "matmul", out, lhsT, rhs, start=start, stop=stop,
                       waits=waits, inc=inc, **kw)

    def tr(self, out, in_, ident, waits=(), inc=None):
        return self.op("tensor", "transpose", out, in_, ident, waits=waits, inc=inc)

    def dma(self, queue, out, in_, inc, waits=(), **kw):
        return self.op(queue, "dma_start", out=out, in_=in_, inc=inc, incby=16,
                       waits=waits, **kw)

    def idma(self, queue, out, out_off, in_, in_off, inc, waits=()):
        return self.op(queue, "indirect_dma_start", out, out_off, in_, in_off,
                       inc=inc, incby=16, waits=waits)

    def realize(self):
        with self.nc.Block() as block:
            for eng in ENGS:
                fns = self.q[eng]
                if fns:
                    getattr(block, eng)(lambda E, fns=fns: [f(E) for f in fns])
        self.q = {e: [] for e in ENGS}


def phase_barrier(p):
    snap = [(nm, p.cnt[nm]) for nm in ("v", "s", "g", "pe", "dx", "dg", "do")]
    for eng in ENGS:
        for nm, val in snap:
            p.wait(eng, nm, val)


class Buffers:
    dbg = {}
    adbg = {}
    dbg_sg = -1


def make_common(nc, p):
    B = Buffers()
    SB = nc.alloc_sbuf_tensor
    B.XR = SB("XR", [128, NCH * T], F32)
    B.XN = SB("XN", [128, NCH * T], BF16)
    B.ones128 = SB("ones128", [128, 128], BF16)
    B.ones128f = SB("ones128f", [128, 128], F32)
    B.onesbig = SB("onesbig", [128, 2048], BF16)
    B.onesrow = SB("onesrow", [1, 512], BF16)
    B.ident = SB("ident", [128, 128], BF16)
    B.identf = SB("identf", [128, 128], F32)
    B.iotacol = SB("iotacol", [128, 1], F32)
    B.epscol = SB("epscol", [128, 1], F32)
    B.iotarow32 = SB("iotarow32", [128, 32], F32)
    B.xbfs = SB("xbfs", [128, 2 * 512], BF16)
    B.xsqs = SB("xsqs", [128, 2 * 512], BF16)
    B.meanbc = SB("meanbc", [128, 512], F32)
    B.msqbc = SB("msqbc", [128, 512], F32)
    B.rstdbc = SB("rstdbc", [128, 512], F32)
    B.ytmp = SB("ytmp", [128, 512], F32)
    B.pb8h = nc.alloc_psum_tensor("pb8", [128, 4096], F32)
    B.pb8 = B.pb8h[:]
    B.pb = [B.pb8h[:, i * 512:(i + 1) * 512] for i in range(8)]
    p.gop("memset", B.ones128[:], 1.0)
    p.gop("memset", B.ones128f[:], 1.0)
    p.gop("memset", B.onesbig[:], 1.0)
    p.gop("memset", B.onesrow[:], 1.0)
    p.gop("memset", B.ident[:], 0.0)
    p.gop("affine_select", B.ident[:], B.ones128[:], [[1, 128]], OP.is_equal, 0.0,
          base=0, channel_multiplier=-1)
    p.gop("affine_select", B.identf[:], B.ones128f[:], [[1, 128]], OP.is_equal, 0.0,
          base=0, channel_multiplier=-1)
    p.gop("iota", B.iotacol[:], [[1, 1]], base=0, channel_multiplier=1,
          allow_small_or_imprecise_dtypes=True)
    p.gop("memset", B.epscol[:], LN_EPS)
    p.gop("iota", B.iotarow32[:], [[1, 32]], base=0, channel_multiplier=0,
          allow_small_or_imprecise_dtypes=True)
    return B


def load_x(nc, p, B, xin):
    for c in range(NCH):
        p.dma("sync", B.XR[:, c * T:(c + 1) * T], xin[c * 128:(c + 1) * 128, :], inc="dx")
    return p.cnt["dx"]


def store_x(nc, p, B, xout, vwait):
    for c in range(NCH):
        p.dma("sync", xout[c * 128:(c + 1) * 128, :], B.XR[:, c * T:(c + 1) * T],
              inc="do", waits=[("v", vwait)] if c == 0 else ())
    p.wait("sync", "do", p.cnt["do"])


def emit_ln(nc, p, B, x_ready_dx=0, v_ready=0, svar=True):
    done = []
    for tc in range(NTC):
        ia, ib = (2 * tc) % 4, (2 * tc) % 4 + 1
        bA, bB = B.pb[ia], B.pb[ib]
        ms = {}
        for c in range(NCH):
            db = c % 2
            sl = slice(db * 512, (db + 1) * 512)
            xsl = B.XR[:, c * T + tc * 512: c * T + (tc + 1) * 512]
            w = list([("dx", x_ready_dx), ("v", v_ready)] if (tc == 0 and c == 0) else [])
            if c >= 2:
                w.append(("pe", ms[c - 2]))
            vcp = p.vop("tensor_copy", B.xbfs[:, sl], xsl, waits=w)
            scp = p.sop("activation", B.xsqs[:, sl], xsl, AF.Square, waits=w)
            w2_ = [("v", vcp), ("s", scp)]
            if c == 0:
                w2_ += p.bank_free[ia] + p.bank_free[ib]
            p.mm(bA[:, 0:512], B.ones128[:], B.xbfs[:, sl],
                 start=(c == 0), stop=(c == NCH - 1), waits=w2_)
            ms[c] = p.mm(bB[:, 0:512], B.ones128[:], B.xsqs[:, sl],
                         start=(c == 0), stop=(c == NCH - 1), inc="pe")
        pe_done = p.cnt["pe"]
        p.vop("tensor_scalar", B.meanbc[:], bA[:, 0:512], 1.0 / D, None, OP.mult,
              waits=[("pe", pe_done)])
        p.vop("tensor_scalar", B.msqbc[:], bB[:, 0:512], 1.0 / D, None, OP.mult)
        p.vop("tensor_tensor", B.rstdbc[:], B.meanbc[:], B.meanbc[:], OP.mult)
        vfree = p.vop("tensor_tensor", B.msqbc[:], B.msqbc[:], B.rstdbc[:], OP.subtract)
        p.bank_free[ia] = p.bank_free[ib] = [("v", vfree)]
        p.sop("activation", B.rstdbc[:], B.msqbc[:], AF.Ln, bias=B.epscol[:],
              waits=[("v", vfree)])
        if svar:
            p.sop("activation", B.svarbc[:, tc * 512:(tc + 1) * 512], B.rstdbc[:],
                  AF.Exp, scale=0.5)
        ssq = p.sop("activation", B.rstdbc[:], B.rstdbc[:], AF.Exp, scale=-0.5)
        p.wait("vector", "s", ssq)
        last = None
        for c in range(NCH):
            xsl = B.XR[:, c * T + tc * 512: c * T + (tc + 1) * 512]
            nsl = B.XN[:, c * T + tc * 512: c * T + (tc + 1) * 512]
            p.vop("tensor_tensor", B.ytmp[:], xsl, B.meanbc[:], OP.subtract)
            last = p.vop("tensor_tensor", nsl, B.ytmp[:], B.rstdbc[:], OP.mult)
        done.append(last)
    return done


def moe_buffers(nc, B):
    SB = nc.alloc_sbuf_tensor
    if not hasattr(B, "svarbc"):
        B.svarbc = SB("svarbc", [128, T], BF16)
    B.rwt = SB("rwt", [128, NCH * 32], F32)
    B.rwsc = SB("rwsc", [128, 32], F32)
    B.rbrow = SB("rbrow", [1, 32], BF16)
    B.b1c = SB("sb_b1c", [128, 64], F32)
    B.b2sb = SB("b2sb", [32, 1024], BF16)
    B.idxcols = SB("idxcols", [128, 16], BF16)
    B.idxbc = SB("idxbc", [128, 2048], BF16)
    B.MT = SB("MT", [32, 2048], BF16)
    B.eqs = SB("eqs", [128, 512], BF16)
    B.lmax = SB("lmax", [128, 1], F32)
    B.HT = SB("HT", [128, 8 * 2 * 1024], BF16)
    B.Z2 = SB("Z2", [128, 8 * 1024], F32)
    B.w1sb = SB("w1sb", [128, 2 * 2048], BF16)
    B.w2sb = SB("w2sb", [128, 2 * 2048], BF16)


def emit_moe_weight_preload(nc, p, B, W):
    for c in range(NCH):
        p.dma("gpsimd", B.rwt[:, c * 32:(c + 1) * 32],
              W["rw_t"][c * 128:(c + 1) * 128, :], inc="dg")
    p.dma("gpsimd", B.rbrow[:], W["rb"], inc="dg")
    p.dma("gpsimd", B.b2sb[:], W["b2"], inc="dg")
    p.dma("sync", B.b1c[:], W["b1c"], inc="dx")
    B.b1c_dx = p.cnt["dx"]
    dg = p.cnt["dg"]
    bk = 4
    for c in range(NCH):
        p.mm(B.pb[bk][:, 0:32], B.ones128f[:], B.rwt[:, c * 32:(c + 1) * 32],
             start=(c == 0), stop=(c == NCH - 1),
             waits=[("dg", dg), ("g", 2)] if c == 0 else (),
             inc="pe" if c == NCH - 1 else None)
    p.vop("tensor_scalar", B.rwsc[:], B.pb[bk][:, 0:32], 1.0 / D, None, OP.mult,
          waits=[("pe", p.cnt["pe"])])
    vlast = 0
    for c in range(NCH):
        vlast = p.vop("tensor_tensor", B.rwt[:, c * 32:(c + 1) * 32],
                      B.rwt[:, c * 32:(c + 1) * 32], B.rwsc[:], OP.subtract)
    p.bank_free[bk] = [("v", vlast)]
    B.rw_ready = vlast
    return p.cnt["dg"]


def emit_moe(nc, p, B, W, ln_done, small_dg):
    B.relu_war = {}
    for tt in range(16):
        bk = 5 + tt % 2
        lb = B.pb[bk]
        w = list(p.bank_free[bk]) + [("v", ln_done[tt // 4])]
        if tt == 0:
            w += [("v", B.rw_ready), ("dg", small_dg)]
        for c in range(NCH):
            p.mm(lb[:, 0:32],
                 B.XR[:, c * T + tt * 128: c * T + (tt + 1) * 128],
                 B.rwt[:, c * 32:(c + 1) * 32],
                 start=(c == 0), stop=False, waits=w if c == 0 else ())
        p.mm(lb[:, 0:32], B.svarbc[0:1, tt * 128:(tt + 1) * 128],
             B.rbrow[0:1, 0:32], start=False, stop=True, inc="pe")
        pe_l = p.cnt["pe"]
        p.vop("tensor_reduce", B.lmax[:], lb[:, 0:32], axis=AX.X,
              op=OP.max, waits=[("pe", pe_l)])
        p.vop("tensor_scalar", B.lsel[:], lb[:, 0:32], B.lmax[:], None, OP.is_equal)
        p.vop("scalar_tensor_tensor", B.lsel[:], B.lsel[:], -10000.0, B.iotarow32[:],
              OP.mult, OP.add)
        p.vop("tensor_reduce", B.lmax[:], B.lsel[:], axis=AX.X, op=OP.min)
        vr = p.vop("tensor_scalar", B.idxcols[:, tt:tt + 1], B.lmax[:], 10000.0,
                   None, OP.add)
        p.bank_free[bk] = [("v", vr)]
    v_idx = p.cnt["v"]
    p7h = B.pb[7][:, 0:512].bitcast(BF16)
    p6h = B.pb[6][:, 0:512].bitcast(BF16)
    p.wait("tensor", "v", v_idx)
    for wv in p.bank_free[7] + p.bank_free[6]:
        p.wait("tensor", wv[0], wv[1])
    for tt in range(16):
        dst = p7h if tt < 8 else p6h
        p.tr(dst[0:1, (tt % 8) * 128:(tt % 8 + 1) * 128],
             B.idxcols[:, tt:tt + 1], B.ident[:],
             inc="pe" if tt == 15 else None)
    for half in range(2):
        ph = p7h if half == 0 else p6h
        vir = p.vop("tensor_copy", B.xbfs[0:1, 0:1024], ph[0:1, 0:1024],
                    waits=[("pe", p.cnt["pe"])] if half == 0 else ())
        for q in range(2):
            tc = half * 2 + q
            p.mm(B.pb[7][:, 0:512], B.ones128[0:1, :],
                 B.xbfs[0:1, q * 512:(q + 1) * 512],
                 start=True, stop=True,
                 waits=[("v", vir)] + p.bank_free[7], inc="pe")
            vb = p.vop("tensor_copy", B.idxbc[:, tc * 512:(tc + 1) * 512],
                       B.pb[7][:, 0:512], waits=[("pe", p.cnt["pe"])])
            p.bank_free[7] = [("v", vb)]
    p.vop("tensor_scalar", B.MT[:], B.idxbc[0:32, :], B.iotacol[0:32, :], None,
          OP.is_equal)
    v_mt = p.cnt["v"]
    if B.dbg:
        p.dma("sync", B.dbg["XN"], B.XN[:], inc="do", waits=[("v", ln_done[-1])])
        p.dma("sync", B.dbg["idxbc"], B.idxbc[:], inc="do", waits=[("v", v_mt)])

    w1_rd = {}
    w2_rd = {}
    for th in range(2):
        t0 = th * 1024
        for eg in range(4):
            for ei in range(8):
                e = eg * 8 + ei
                db = ei % 2
                dgw = p.dma("gpsimd", B.w1sb[:, db * 2048:(db + 1) * 2048],
                            W["w1"][e].rearrange("(c p) j -> p c j", p=128),
                            inc="dg", waits=[("pe", w1_rd.get(db, 0))])
                for tq in range(2):
                    veq = p.vop("tensor_scalar", B.eqs[:, 0:512],
                                B.idxbc[:, t0 + tq * 512: t0 + (tq + 1) * 512],
                                float(e), None, OP.is_equal,
                                waits=[("v", v_mt)] if (th == 0 and eg == 0 and ei == 0 and tq == 0) else ())
                    for jb in range(2):
                        bk = jb * 2 + tq
                        bank = B.pb[bk]
                        w = [("dg", dgw)] + p.bank_free[bk]
                        mlast = None
                        for ic in range(NCH):
                            mlast = p.mm(bank[:, 0:512],
                                 B.w1sb[:, db * 2048 + ic * 256 + jb * 128:
                                        db * 2048 + ic * 256 + (jb + 1) * 128],
                                 B.XN[:, ic * T + t0 + tq * 512:
                                      ic * T + t0 + (tq + 1) * 512],
                                 start=(ic == 0), stop=(ic == NCH - 1),
                                 waits=w if ic == 0 else (),
                                 inc="pe" if ic == NCH - 1 else None)
                        w1_rd[db] = mlast
                        hsl = B.HT[:, (ei * 2 + jb) * 1024 + tq * 512:
                                   (ei * 2 + jb) * 1024 + (tq + 1) * 512]
                        tsl = (B.xbfs if bk < 2 else B.xsqs)[:, (bk % 2) * 512:
                                                             (bk % 2) * 512 + 512]
                        srl = p.sop("activation", tsl, bank[:, 0:512], AF.Relu,
                                    bias=B.b1c[:, e * 2 + jb: e * 2 + jb + 1],
                                    waits=[("pe", mlast), ("dx", B.b1c_dx),
                                           ("v", B.relu_war.get(bk, 0))])
                        vh = p.vop("tensor_tensor", hsl, tsl,
                                   B.eqs[:, 0:512], OP.mult,
                                   waits=[("s", srl), ("v", veq)])
                        B.relu_war[bk] = vh
                        p.bank_free[bk] = [("v", vh)]
            v_ht = p.cnt["v"]
            if B.dbg and th == 0 and eg == 0:
                p.dma("sync", B.dbg["HT"], B.HT[:], inc="do", waits=[("v", v_ht)])
            for oh in range(2):
                for ei in range(8):
                    e = eg * 8 + ei
                    pr = (ei // 2) % 2
                    db = pr * 2 + ei % 2
                    if ei % 2 == 0:
                        dgw = p.dma("gpsimd", B.w2sb[:, pr * 2048:(pr + 1) * 2048],
                                    W["w2"][e:e + 2, :, oh * 512:(oh + 1) * 512]
                                    .rearrange("e (c p) o -> p e c o", p=128),
                                    inc="dg", waits=[("pe", w2_rd.get(pr, 0))])
                    for obi in range(4):
                        for tq in range(2):
                            bk = obi * 2 + tq
                            bank = B.pb[bk]
                            ob = oh * 4 + obi
                            if ei == 0:
                                w = [("dg", dgw), ("v", v_ht)] + p.bank_free[bk]
                                if eg == 0:
                                    p.mm(bank[:, 0:512],
                                         B.b2sb[:, ob * 128:(ob + 1) * 128],
                                         B.MT[:, t0 + tq * 512: t0 + (tq + 1) * 512],
                                         start=True, stop=False, waits=w)
                                    w = []
                            else:
                                w = [("dg", dgw)]
                            for jb in range(2):
                                start = (ei == 0 and jb == 0 and eg != 0)
                                islast = (ei == 7 and jb == 1)
                                expert_last = (obi == 3 and tq == 1 and jb == 1
                                               and ei % 2 == 1)
                                mn = p.mm(bank[:, 0:512],
                                     B.w2sb[:, db * 1024 + jb * 512 + obi * 128:
                                            db * 1024 + jb * 512 + (obi + 1) * 128],
                                     B.HT[:, (ei * 2 + jb) * 1024 + tq * 512:
                                          (ei * 2 + jb) * 1024 + (tq + 1) * 512],
                                     start=start, stop=islast,
                                     waits=w if jb == 0 else (),
                                     inc="pe" if (islast or expert_last) else None)
                                if expert_last:
                                    w2_rd[pr] = mn
                                w = []
                pe_grp = p.cnt["pe"]
                for obi in range(4):
                    for tq in range(2):
                        bk = obi * 2 + tq
                        bank = B.pb[bk]
                        ob = oh * 4 + obi
                        zsl = B.Z2[:, ob * 1024 + tq * 512: ob * 1024 + (tq + 1) * 512]
                        if eg == 0:
                            vz = p.vop("tensor_copy", zsl, bank[:, 0:512],
                                       waits=[("pe", pe_grp)])
                        else:
                            vz = p.vop("tensor_tensor", zsl, zsl, bank[:, 0:512],
                                       OP.add, waits=[("pe", pe_grp)])
                        p.bank_free[bk] = [("v", vz)]
        if B.dbg and th == 0:
            p.dma("sync", B.dbg["Z2"], B.Z2[:], inc="do", waits=[("v", p.cnt["v"])])
        for ob in range(8):
            for tq in range(2):
                zsl = B.Z2[:, ob * 1024 + tq * 512: ob * 1024 + (tq + 1) * 512]
                xsl = B.XR[:, ob * T + t0 + tq * 512: ob * T + t0 + (tq + 1) * 512]
                p.vop("tensor_scalar", B.ytmp[:], zsl, 0.0, 0.5, OP.max, OP.mult)
                p.vop("tensor_tensor", xsl, xsl, B.ytmp[:], OP.add)
    return p.cnt["v"]


def moer_buffers(nc, B):
    SB = nc.alloc_sbuf_tensor
    if not hasattr(B, "svarbc"):
        B.svarbc = SB("svarbc", [128, T], BF16)
    B.rwt = SB("rwt", [128, NCH * 32], F32)
    B.rwsc = SB("rwsc", [128, 32], F32)
    B.rbrow = SB("rbrow", [1, 32], BF16)
    B.b1c = SB("sb_b1c", [128, 64], F32)
    B.b2t = SB("sb_b2t", [128, 256], F32)
    B.idxcols = SB("idxcols", [128, 16], F32)
    B.MT = SB("MT", [32, 2048], BF16)
    B.lmax = SB("lmax", [128, 1], F32)
    B.iote = SB("iote", [32, 1], BF16)
    B.Ltile = SB("Ltile", [128, 2048], BF16)
    B.ranksb = B.Ltile[0:32, 0:2048]
    B.w1f32 = SB("w1f32", [128, 2048], F32)
    B.w2f32 = B.w1f32
    B.srow = B.w1f32[0:1, 0:2048]
    B.Mtm = B.w1f32[:].bitcast(BF16)[:, 2048:2560]
    B.lsel = B.rwsc
    B.scoli = SB("scoli", [128, 16], I32)
    B.tokids = SB("tokids", [128, 16], I32)
    B.xg = SB("xg", [128, 4 * 1024], BF16)
    B.xgT = SB("xgT", [128, 4 * 1024], BF16)
    B.hT = SB("hT", [128, 4 * 256], BF16)
    B.yT = SB("yT", [128, 4 * 1024], BF16)
    B.permsb = SB("permsb", [128, 32], I32)
    B.dgfence = SB("dgfence", [16, 1], I32)
    B.w1sb = SB("w1sb", [128, 4 * 2048], BF16)
    B.w2sb = SB("w2sb", [128, 4 * 2048], BF16)
    B.ytm = B.xg
    B.xtms = B.yT


def emit_moer_preload(nc, p, B, W):
    for c in range(NCH):
        p.dma("gpsimd", B.rwt[:, c * 32:(c + 1) * 32],
              W["rw_t"][c * 128:(c + 1) * 128, :], inc="dg")
    p.dma("gpsimd", B.rbrow[:], W["rb"], inc="dg")
    p.dma("sync", B.b1c[:], W["b1c"], inc="dx")
    p.dma("sync", B.b2t[:], W["b2t"], inc="dx")
    B.b1c_dx = p.cnt["dx"]
    dg = p.cnt["dg"]
    p.vop("tensor_copy", B.iote[:], B.iotacol[0:32, :], waits=[("g", p.cnt["g"])])
    bk = 4
    for c in range(NCH):
        p.mm(B.pb[bk][:, 0:32], B.ones128f[:], B.rwt[:, c * 32:(c + 1) * 32],
             start=(c == 0), stop=(c == NCH - 1),
             waits=([("dg", dg), ("g", 2)] + p.bank_free[bk]) if c == 0 else (),
             inc="pe" if c == NCH - 1 else None)
    p.vop("tensor_scalar", B.rwsc[:], B.pb[bk][:, 0:32], 1.0 / D, None, OP.mult,
          waits=[("pe", p.cnt["pe"])])
    vlast = 0
    for c in range(NCH):
        vlast = p.vop("tensor_tensor", B.rwt[:, c * 32:(c + 1) * 32],
                      B.rwt[:, c * 32:(c + 1) * 32], B.rwsc[:], OP.subtract)
    p.bank_free[bk] = [("v", vlast)]
    B.rw_ready = vlast
    return p.cnt["dg"]


def emit_moe_routed(nc, p, B, W, ln_done, small_dg, scr):
    import dataclasses as _dc
    p2h = B.pb8h[:].bitcast(BF16)

    for tt in range(16):
        bk = 5 + tt % 2
        lb = B.pb[bk]
        w = list(p.bank_free[bk]) + [("v", ln_done[tt // 4])]
        if tt == 0:
            w += [("v", B.rw_ready), ("dg", small_dg), ("dx", B.b1c_dx)]
        for c in range(NCH):
            p.mm(lb[:, 0:32],
                 B.XR[:, c * T + tt * 128: c * T + (tt + 1) * 128],
                 B.rwt[:, c * 32:(c + 1) * 32],
                 start=(c == 0), stop=False, waits=w if c == 0 else ())
        p.mm(lb[:, 0:32], B.svarbc[0:1, tt * 128:(tt + 1) * 128],
             B.rbrow[0:1, 0:32], start=False, stop=True, inc="pe")
        pe_l = p.cnt["pe"]
        p.vop("tensor_reduce", B.lmax[:], lb[:, 0:32], axis=AX.X,
              op=OP.max, waits=[("pe", pe_l)])
        p.vop("tensor_scalar", B.lsel[:], lb[:, 0:32], B.lmax[:], None, OP.is_equal)
        p.vop("scalar_tensor_tensor", B.lsel[:], B.lsel[:], -10000.0, B.iotarow32[:],
              OP.mult, OP.add)
        p.vop("tensor_reduce", B.lmax[:], B.lsel[:], axis=AX.X, op=OP.min)
        p.vop("tensor_scalar", B.idxcols[:, tt:tt + 1], B.lmax[:], 10000.0,
              None, OP.add)
        vr = p.vop("tensor_scalar", B.Mtm[:, tt * 32:(tt + 1) * 32], B.iotarow32[:],
                   B.idxcols[:, tt:tt + 1], None, OP.is_equal)
        p.bank_free[bk] = [("v", vr)]
    v_idx = p.cnt["v"]

    st_dma = {}
    trp = {}
    for tile in range(16):
        par = tile % 2
        pbase = (4 + par) * 1024
        pw = []
        if tile >= 2:
            pw += p.bank_free[4 + par]
        for c in range(NCH):
            trp[par] = p.tr(p2h[:, pbase + c * 128: pbase + (c + 1) * 128],
                            B.XN[:, c * T + tile * 128: c * T + tile * 128 + 128],
                            B.ident[:], waits=pw if c == 0 else (),
                            inc="pe" if c == NCH - 1 else None)
        vw = [("pe", trp[par])]
        if tile >= 2:
            vw.append(("do", st_dma[tile - 2]))
        vv = p.vop("tensor_copy", B.xtms[:, par * 1024:(par + 1) * 1024],
                   p2h[:, pbase:pbase + 1024], waits=vw)
        p.bank_free[4 + par] = [("v", vv)]
        st_dma[tile] = p.dma("sync", scr["xn_tm"][tile * 128:(tile + 1) * 128, :],
                             B.xtms[:, par * 1024:(par + 1) * 1024], inc="do",
                             waits=[("v", vv)])
    xn_tm_done = p.cnt["do"]

    p.wait("tensor", "v", v_idx)
    for w_ in p.bank_free[0] + p.bank_free[1]:
        p.wait("tensor", w_[0], w_[1])
    for tile in range(16):
        p.tr(p2h[0:32, tile * 128:(tile + 1) * 128],
             B.Mtm[:, tile * 32:(tile + 1) * 32], B.ident[:],
             inc="pe" if tile == 15 else None)
    vmt = p.vop("tensor_copy", B.MT[:], p2h[0:32, 0:2048],
                waits=[("pe", p.cnt["pe"])])
    p.bank_free[0] = p.bank_free[1] = [("v", vmt)]

    g_l = {}
    mm_l = {}
    for c in range(16):
        g_l[c] = p.gop("affine_select", B.Ltile[:],
                       B.onesbig[:], [[1, 2048]], OP.is_gt, 0.0,
                       base=-(c * 128), channel_multiplier=-1,
                       waits=[("pe", mm_l[c - 1])] if c >= 1 else ())
        w = [("g", g_l[c])]
        if c == 0:
            w += p.bank_free[0] + p.bank_free[1] + p.bank_free[2] + p.bank_free[3]
        for n in range(4):
            mm_l[c] = p.mm(B.pb8[0:32, n * 512:(n + 1) * 512],
                           B.Mtm[:, c * 32:(c + 1) * 32],
                           B.Ltile[:, n * 512:(n + 1) * 512],
                           start=(c == 0), stop=(c == 15),
                           waits=w if n == 0 else (), inc="pe")
    pe_rank = p.cnt["pe"]
    p.vop("tensor_scalar", B.ranksb, B.pb8[0:32, 0:2048], 127.0, 32.0,
          OP.min, OP.mult, waits=[("pe", pe_rank)])
    v_mt = p.vop("tensor_tensor", B.ranksb, B.ranksb, B.MT[:], OP.mult)
    for i in range(4):
        p.bank_free[i] = [("v", v_mt)]
    p.wait("tensor", "v", v_mt)
    for n in range(4):
        p.mm(B.pb8[0:1, n * 512:(n + 1) * 512], B.ones128[0:32, 0:1],
             B.ranksb[:, n * 512:(n + 1) * 512], start=True, stop=False)
        p.mm(B.pb8[0:1, n * 512:(n + 1) * 512], B.iote[:],
             B.MT[:, n * 512:(n + 1) * 512], start=False, stop=True,
             inc="pe" if n == 3 else None)
    v_srow = p.vop("tensor_copy", B.srow, B.pb8[0:1, 0:2048],
                   waits=[("pe", p.cnt["pe"])])
    for i in range(4):
        p.bank_free[i] = [("v", v_srow)]
    p.wait("tensor", "v", v_srow)
    for tile in range(16):
        p.tr(bass.AP(B.pb8h, tile, [[4096, 128], [1, 1]]),
             B.srow[0:1, tile * 128:(tile + 1) * 128], B.identf[0:1, 0:1],
             inc="pe" if tile == 15 else None)
    B.pe_scoli = p.cnt["pe"]
    p.vop("tensor_copy", B.scoli[:],
          bass.AP(B.pb8h, 0, [[4096, 128], [1, 16]]),
          waits=[("pe", p.cnt["pe"])])
    vsc = p.cnt["v"]
    p.bank_free[0] = [("v", vsc)]
    g0 = p.gop("memset", B.tokids[:], 1 << 20)
    gi = None
    for hf in range(2):
        gi = p.dma("gpsimd",
                   scr["perm2"].rearrange("(r e) one -> r (e one)", r=128)[:, hf * 16:(hf + 1) * 16],
                   B.tokids[:], inc="dg", waits=[("g", g0)] if hf == 0 else ())
    g1 = p.gop("iota", B.tokids[:], [[128, 16]], base=0, channel_multiplier=1,
               allow_small_or_imprecise_dtypes=True, waits=[("dg", gi)])
    gsc = None
    for tile in range(16):
        gsc = p.idma("gpsimd", scr["perm2"],
                     bass.IndirectOffsetOnAxis(ap=B.scoli[:, tile:tile + 1], axis=0),
                     B.tokids[:, tile:tile + 1], None, inc="dg",
                     waits=[("g", g1), ("v", vsc)] if tile == 0 else ())
    gsc = p.dma("gpsimd", B.dgfence[:], scr["perm2"][0:16, :], inc="dg")
    gld = p.dma("sync", B.permsb[:],
                scr["perm2"].rearrange("(r e) one -> r (e one)", r=128),
                inc="dx", waits=[("dg", gsc)])

    w1_rd = {}
    w2_rd = {}
    xg_rd = {}
    cast_s = {}
    gath = {}
    scat = {}
    vyt = {}
    for g4 in range(8):
        es = [g4 * 4 + i for i in range(4)]
        for e in es:
            par = e % 4
            df1 = p.dma("sync", B.w1f32[:],
                        W["w1"][e].rearrange("(c p) j -> p c j", p=128),
                        inc="dx", waits=[("s", cast_s.get("slab", 0))] +
                        ([("pe", B.pe_scoli)] if e == 0 else []))
            cast_s["slab"] = p.sop(
                "activation", B.w1sb[:, par * 2048:(par + 1) * 2048],
                B.w1f32[:], AF.Identity,
                waits=[("dx", df1), ("pe", w1_rd.get(par, 0))])
            w1_rd[("cast", e)] = cast_s["slab"]
            gw = [("dx", gld), ("do", xn_tm_done)] if e == 0 else []
            gath[e] = p.idma("gpsimd", B.xg[:, par * 1024:(par + 1) * 1024], None,
                             scr["xn_tm"],
                             bass.IndirectOffsetOnAxis(ap=B.permsb[:, e:e + 1], axis=0),
                             inc="dg", waits=gw + [("pe", xg_rd.get(par, 0)),
                                                  ("dg", scat.get(par, 0)),
                                                  ("v", vyt.get(par, 0))])
        gfence = p.dma("gpsimd", B.dgfence[:], scr["perm2"][0:16, :], inc="dg")
        vxgt = {}
        for e in es:
            par = e % 4
            bko = 0 if par % 2 == 0 else 4
            for c in range(NCH):
                p.tr(p2h[:, bko * 1024 + c * 128: bko * 1024 + (c + 1) * 128],
                     B.xg[:, par * 1024 + c * 128: par * 1024 + (c + 1) * 128],
                     B.ident[:],
                     waits=([("dg", gath.get(e + 1, gfence))] + p.bank_free[bko])
                           if c == 0 else (),
                     inc="pe" if c == NCH - 1 else None)
            xg_rd[par] = p.cnt["pe"]
            vxgt[e] = p.vop("tensor_copy", B.xgT[:, par * 1024:(par + 1) * 1024],
                            p2h[:, bko * 1024:(bko + 1) * 1024],
                            waits=[("pe", p.cnt["pe"])])
            p.bank_free[bko] = [("v", vxgt[e])]
        sh = {}
        for e in es:
            par = e % 4
            mbk = 1 if par % 2 == 0 else 5
            p.wait("tensor", "v", vxgt[e])
            for jb in range(2):
                for ic in range(NCH):
                    p.mm(B.pb8[:, mbk * 512 + jb * 128: mbk * 512 + (jb + 1) * 128],
                         B.w1sb[:, par * 2048 + ic * 256 + jb * 128:
                                par * 2048 + ic * 256 + (jb + 1) * 128],
                         B.xgT[:, par * 1024 + ic * 128: par * 1024 + (ic + 1) * 128],
                         start=(ic == 0), stop=(ic == NCH - 1),
                         waits=([("s", w1_rd[("cast", e)])] + p.bank_free[mbk])
                               if (jb == 0 and ic == 0) else (),
                         inc="pe" if ic == NCH - 1 else None)
            w1_rd[par] = p.cnt["pe"]
            s_ = None
            for jb in range(2):
                s_ = p.sop("activation",
                           B.hT[:, par * 256 + jb * 128: par * 256 + (jb + 1) * 128],
                           B.pb8[:, mbk * 512 + jb * 128: mbk * 512 + (jb + 1) * 128],
                           AF.Relu, bias=B.b1c[:, e * 2 + jb: e * 2 + jb + 1],
                           waits=[("pe", w1_rd[par])])
            sh[e] = s_
            p.bank_free[mbk] = [("s", s_)]
            df2 = p.dma("sync", B.w2f32[:],
                        W["w2"][e].rearrange("(c p) o -> p c o", p=128),
                        inc="dx", waits=[("s", cast_s["slab"])])
            cast_s["slab"] = p.sop(
                "activation", B.w2sb[:, par * 2048:(par + 1) * 2048],
                B.w2f32[:], AF.Identity,
                waits=[("dx", df2), ("pe", w2_rd.get(par, 0))])
            w2_rd[("cast", e)] = cast_s["slab"]
        sy = {}
        for e in es:
            par = e % 4
            bko = 0 if par % 2 == 0 else 4
            p.wait("tensor", "s", sh[e])
            for ob in range(8):
                obk = bko + 2 + ob // 4
                slot = ob % 4
                for jb in range(2):
                    p.mm(B.pb8[:, obk * 512 + slot * 128: obk * 512 + (slot + 1) * 128],
                         B.w2sb[:, par * 2048 + jb * 1024 + ob * 128:
                                par * 2048 + jb * 1024 + (ob + 1) * 128],
                         B.hT[:, par * 256 + jb * 128: par * 256 + (jb + 1) * 128],
                         start=(jb == 0), stop=(jb == 1),
                         waits=([("s", w2_rd[("cast", e)])] + p.bank_free[obk])
                               if (slot == 0 and jb == 0) else (),
                         inc="pe" if jb == 1 else None)
            w2_rd[par] = p.cnt["pe"]
            s_ = None
            for ob in range(8):
                obk = bko + 2 + ob // 4
                slot = ob % 4
                s_ = p.sop("activation",
                           B.yT[:, par * 1024 + ob * 128: par * 1024 + (ob + 1) * 128],
                           B.pb8[:, obk * 512 + slot * 128: obk * 512 + (slot + 1) * 128],
                           AF.Relu, bias=B.b2t[:, e * 8 + ob: e * 8 + ob + 1],
                           waits=[("pe", w2_rd[par])] +
                                 ([("do", xn_tm_done)] if e < 4 and ob == 0 else []))
            sy[e] = s_
            p.bank_free[bko + 2] = [("s", s_)]
            p.bank_free[bko + 3] = [("s", s_)]
        for e in es:
            par = e % 4
            bko = 0 if par % 2 == 0 else 4
            p.wait("tensor", "s", sy[e])
            for c in range(NCH):
                p.tr(p2h[:, bko * 1024 + c * 128: bko * 1024 + (c + 1) * 128],
                     B.yT[:, par * 1024 + c * 128: par * 1024 + (c + 1) * 128],
                     B.ident[:], waits=p.bank_free[bko] if c == 0 else (),
                     inc="pe" if c == NCH - 1 else None)
            vytm = p.vop("tensor_copy", B.ytm[:, par * 1024:(par + 1) * 1024],
                         p2h[:, bko * 1024:(bko + 1) * 1024],
                         waits=[("pe", p.cnt["pe"])])
            vyt[par] = vytm
            p.bank_free[bko] = [("v", vytm)]
            scat[par] = p.idma("gpsimd", scr["y_tm"],
                               bass.IndirectOffsetOnAxis(ap=B.permsb[:, e:e + 1], axis=0),
                               B.ytm[:, par * 1024:(par + 1) * 1024], None,
                               inc="dg", waits=[("v", vytm)])
    p.dma("gpsimd", B.dgfence[:], scr["perm2"][0:16, :], inc="dg")
    y_done = p.cnt["dg"]

    ld = {}
    tr_rd = {}
    for tile in range(16):
        par = tile % 2
        bko = 0 if par == 0 else 4
        ld[tile] = p.dma("sync", B.xg[:, par * 1024:(par + 1) * 1024],
                         scr["y_tm"][tile * 128:(tile + 1) * 128, :], inc="dx",
                         waits=([("dg", y_done)] if tile == 0 else []) +
                               ([("pe", tr_rd[par])] if tile >= 2 else []))
        for c in range(NCH):
            tr_rd[par] = p.tr(
                p2h[:, bko * 1024 + c * 128: bko * 1024 + (c + 1) * 128],
                B.xg[:, par * 1024 + c * 128: par * 1024 + (c + 1) * 128],
                B.ident[:],
                waits=([("dx", ld[tile])] + p.bank_free[bko]) if c == 0 else (),
                inc="pe")
        vz = p.vop("scalar_tensor_tensor",
                   bass.AP(B.XR, tile * 128, [[16384, 128], [2048, 8], [1, 128]]),
                   _dc.replace(p2h, offset=bko * 1024, ap=[[8192, 128], [128, 8], [1, 128]]),
                   0.5,
                   bass.AP(B.XR, tile * 128, [[16384, 128], [2048, 8], [1, 128]]),
                   OP.mult, OP.add, waits=[("pe", tr_rd[par])])
        p.bank_free[bko] = [("v", vz)]
    return p.cnt["v"]


def moe_dram(nc):
    return {
        "rw_t": nc.dram_tensor("rw_t", [1024, 32], F32, kind="ExternalInput").ap(),
        "rb": nc.dram_tensor("rb", [1, 32], F32, kind="ExternalInput").ap(),
        "w1": nc.dram_tensor("w1", [32, 1024, 256], F32, kind="ExternalInput").ap(),
        "b1c": nc.dram_tensor("b1c", [128, 64], F32, kind="ExternalInput").ap(),
        "w2": nc.dram_tensor("w2", [32, 256, 1024], F32, kind="ExternalInput").ap(),
        "b2": nc.dram_tensor("b2", [32, 1024], F32, kind="ExternalInput").ap(),
    }


def build_moe_program(debug=False):
    nc = bass.Bass("TRN2", target_bir_lowering=False)
    xin = nc.dram_tensor("xin", [D, T], F32, kind="ExternalInput").ap()
    W = moe_dram(nc)
    xout = nc.dram_tensor("xout", [D, T], F32, kind="ExternalOutput").ap()
    p = P(nc)
    B = make_common(nc, p)
    moe_buffers(nc, B)
    dbg = {}
    if debug:
        dbg["XN"] = nc.dram_tensor("dbg_XN", [128, NCH * T], BF16, kind="ExternalOutput").ap()
        dbg["idxbc"] = nc.dram_tensor("dbg_idxbc", [128, T], BF16, kind="ExternalOutput").ap()
        dbg["HT"] = nc.dram_tensor("dbg_HT", [128, 8 * 2 * 1024], BF16, kind="ExternalOutput").ap()
        dbg["Z2"] = nc.dram_tensor("dbg_Z2", [128, 8 * 1024], F32, kind="ExternalOutput").ap()
    B.dbg = dbg
    dx = load_x(nc, p, B, xin)
    small_dg = emit_moe_weight_preload(nc, p, B, W)
    ln_done = emit_ln(nc, p, B, dx)
    vend = emit_moe(nc, p, B, W, ln_done, small_dg)
    store_x(nc, p, B, xout, vend)
    p.realize()
    return nc


def moe_host_weights(inputs, prefix="L0_ffn"):
    return {
        "rw_t": np.ascontiguousarray(np.asarray(inputs[prefix + "_router_w"]).T),
        "rb": np.ascontiguousarray(np.asarray(inputs[prefix + "_router_b"])[None, :]),
        "w1": np.ascontiguousarray(np.asarray(inputs[prefix + "_weights1"])),
        "b1c": np.ascontiguousarray(np.asarray(inputs[prefix + "_bias1"]).reshape(32, 2, 128).transpose(2, 0, 1).reshape(128, 64)),
        "w2": np.ascontiguousarray(np.asarray(inputs[prefix + "_weights2"])),
        "b2": np.ascontiguousarray(np.asarray(inputs[prefix + "_bias2"])),
    }


def l1ffn_buffers(nc, B):
    SB = nc.alloc_sbuf_tensor
    B.GH = SB("GH", [128, 32768], BF16)
    B.Hv = B.XR[:].bitcast(BF16)
    B.wA = SB("wA", [128, 2 * 4096], BF16)
    B.wB = SB("wB", [128, 2 * 1024], BF16)
    B.wC = SB("wC", [128, 2 * 1024], BF16)
    B.s1c = SB("s1c", [128, 32], F32)
    B.s2c = SB("s2c", [128, 32], F32)
    B.sh1c = SB("sh1c", [128, 32], F32)
    B.sh2c = SB("sh2c", [128, 32], F32)
    B.s3c = SB("s3c", [128, 8], F32)
    B.s3h = SB("s3h", [128, 8], F32)



def emit_l1ffn_preload(nc, p, B, W):
    for nm in ("s1c", "s2c", "sh1c", "sh2c", "s3c"):
        p.dma("sync", getattr(B, nm)[:], W[nm], inc="dx")
    dxs = p.cnt["dx"]
    p.vop("tensor_scalar", B.s3h[:], B.s3c[:], 0.5, None, OP.mult,
          waits=[("dx", dxs)])
    return p.cnt["v"]


def emit_l1ffn(nc, p, B, W, xin, xout, ln_done, scales_v, hv_waits=()):
    wArd = {}
    wBrd = {}
    wCrd = {}
    for tch in range(2):
        t0 = tch * 1024
        for ob in range(32):
            db = ob % 2
            dg1 = p.dma("gpsimd", B.wB[:, db * 1024:(db + 1) * 1024],
                        W["w1"][ob], inc="dg", waits=[("pe", wBrd.get(db, 0))])
            dg2 = p.dma("gpsimd", B.wC[:, db * 1024:(db + 1) * 1024],
                        W["w2"][ob], inc="dg", waits=[("pe", wCrd.get(db, 0))])
            bk1 = [4 * db, 4 * db + 1]
            bk2 = [4 * db + 2, 4 * db + 3]
            for c in range(8):
                for n in range(2):
                    w = []
                    if c == 0:
                        w = [("dg", dg1), ("v", ln_done[tch * 2 + n])] + p.bank_free[bk1[n]]
                        if tch == 0 and ob == 0 and n == 0:
                            w.append(("v", scales_v))
                    p.mm(B.pb[bk1[n]][:, 0:512],
                         B.wB[:, db * 1024 + c * 128:(db * 1024) + (c + 1) * 128],
                         B.XN[:, c * T + t0 + n * 512: c * T + t0 + (n + 1) * 512],
                         start=(c == 0), stop=(c == 7), waits=w,
                         inc="pe" if c == 7 else None)
            wBrd[db] = p.cnt["pe"]
            pe1 = [p.cnt["pe"] - 1, p.cnt["pe"]]
            for c in range(8):
                for n in range(2):
                    w = []
                    if c == 0:
                        w = [("dg", dg2)] + p.bank_free[bk2[n]]
                    p.mm(B.pb[bk2[n]][:, 0:512],
                         B.wC[:, db * 1024 + c * 128:(db * 1024) + (c + 1) * 128],
                         B.XN[:, c * T + t0 + n * 512: c * T + t0 + (n + 1) * 512],
                         start=(c == 0), stop=(c == 7), waits=w,
                         inc="pe" if c == 7 else None)
            wCrd[db] = p.cnt["pe"]
            pe2 = [p.cnt["pe"] - 1, p.cnt["pe"]]
            silut = [B.meanbc, B.msqbc]
            ssl = []
            for n in range(2):
                sw = [("pe", pe1[n]), ("v", p.cnt["v"])]
                if tch == 0 and ob == 0:
                    sw.append(("v", scales_v))
                ssl.append(p.sop("activation", silut[n][:, 0:512],
                                 B.pb[bk1[n]][:, 0:512], AF.Silu,
                                 scale=B.s1c[:, ob:ob + 1], waits=sw))
            vh = None
            for n in range(2):
                hw = [("pe", pe2[n]), ("s", ssl[n])]
                if tch == 0 and ob == 0 and n == 0:
                    hw += list(hv_waits)
                vh = p.vop("scalar_tensor_tensor",
                           B.Hv[:, ob * 1024 + n * 512: ob * 1024 + (n + 1) * 512],
                           B.pb[bk2[n]][:, 0:512], B.s2c[:, ob:ob + 1],
                           silut[n][:, 0:512], OP.mult, OP.mult,
                           waits=hw)
                p.bank_free[bk2[n]] = [("v", vh)]
            p.bank_free[bk1[0]] = [("v", vh)]
            p.bank_free[bk1[1]] = [("v", vh)]
        v_h = p.cnt["v"]
        for chain, (wkey, scol, src, dst) in enumerate((
                ("h1", B.sh1c, B.Hv, B.GH), ("h2", B.sh2c, B.GH, B.Hv))):
            s_rel = None
            for ob in range(32):
                db = ob % 2
                dga = p.dma("gpsimd", B.wA[:, db * 4096:(db + 1) * 4096],
                            W[wkey][ob], inc="dg", waits=[("pe", wArd.get(db, 0))])
                bks = [4 * db, 4 * db + 1]
                for c in range(32):
                    for n in range(2):
                        w = []
                        if c == 0:
                            w = [("dg", dga)] + p.bank_free[bks[n]]
                            if ob == 0 and n == 0:
                                if chain == 0:
                                    w.append(("v", v_h))
                                else:
                                    w.append(("s", s_gel))
                        p.mm(B.pb[bks[n]][:, 0:512],
                             B.wA[:, db * 4096 + c * 128:(db * 4096) + (c + 1) * 128],
                             src[:, c * 1024 + n * 512: c * 1024 + (n + 1) * 512],
                             start=(c == 0), stop=(c == 31), waits=w,
                             inc="pe" if c == 31 else None)
                wArd[db] = p.cnt["pe"]
                pes = [p.cnt["pe"] - 1, p.cnt["pe"]]
                for n in range(2):
                    s_rel = p.sop("activation",
                                  dst[:, ob * 1024 + n * 512: ob * 1024 + (n + 1) * 512],
                                  B.pb[bks[n]][:, 0:512], AF.Gelu,
                                  scale=scol[:, ob:ob + 1], waits=[("pe", pes[n])])
                    p.bank_free[bks[n]] = [("s", s_rel)]
            s_gel = s_rel
        for ob in range(8):
            db = ob % 2
            dga = p.dma("gpsimd", B.wA[:, db * 4096:(db + 1) * 4096],
                        W["w3"][ob], inc="dg", waits=[("pe", wArd.get(db, 0))])
            bks = [4 * db, 4 * db + 1]
            for c in range(32):
                for n in range(2):
                    w = []
                    if c == 0:
                        w = [("dg", dga)] + p.bank_free[bks[n]]
                        if ob == 0 and n == 0:
                            w.append(("s", s_gel))
                    p.mm(B.pb[bks[n]][:, 0:512],
                         B.wA[:, db * 4096 + c * 128:(db * 4096) + (c + 1) * 128],
                         B.Hv[:, c * 1024 + n * 512: c * 1024 + (n + 1) * 512],
                         start=(c == 0), stop=(c == 31), waits=w,
                         inc="pe" if c == 31 else None)
            wArd[db] = p.cnt["pe"]
            pes = [p.cnt["pe"] - 1, p.cnt["pe"]]
            for n in range(2):
                dxr = p.dma("sync", B.ytmp[:, 0:512],
                            xin[ob * 128:(ob + 1) * 128, t0 + n * 512:t0 + (n + 1) * 512],
                            inc="dx", waits=[("do", p.cnt["do"]), ("v", p.cnt["v"])])
                vr = p.vop("scalar_tensor_tensor", B.ytmp[:, 0:512],
                           B.pb[bks[n]][:, 0:512], B.s3h[:, ob:ob + 1],
                           B.ytmp[:, 0:512], OP.mult, OP.add,
                           waits=[("pe", pes[n]), ("dx", dxr)])
                p.bank_free[bks[n]] = [("v", vr)]
                p.dma("sync", xout[ob * 128:(ob + 1) * 128, t0 + n * 512:t0 + (n + 1) * 512],
                      B.ytmp[:, 0:512], inc="do", waits=[("v", vr)])
    p.wait("sync", "do", p.cnt["do"])
    return p.cnt["v"]


def l1ffn_dram(nc):
    return {
        "w1": nc.dram_tensor("f_w1", [32, 128, 8, 128], F32, kind="ExternalInput").ap(),
        "w2": nc.dram_tensor("f_w2", [32, 128, 8, 128], F32, kind="ExternalInput").ap(),
        "h1": nc.dram_tensor("f_h1", [32, 128, 32, 128], F32, kind="ExternalInput").ap(),
        "h2": nc.dram_tensor("f_h2", [32, 128, 32, 128], F32, kind="ExternalInput").ap(),
        "w3": nc.dram_tensor("f_w3", [8, 128, 32, 128], F32, kind="ExternalInput").ap(),
        "s1c": nc.dram_tensor("f_s1c", [128, 32], F32, kind="ExternalInput").ap(),
        "s2c": nc.dram_tensor("f_s2c", [128, 32], F32, kind="ExternalInput").ap(),
        "sh1c": nc.dram_tensor("f_sh1c", [128, 32], F32, kind="ExternalInput").ap(),
        "sh2c": nc.dram_tensor("f_sh2c", [128, 32], F32, kind="ExternalInput").ap(),
        "s3c": nc.dram_tensor("f_s3c", [128, 8], F32, kind="ExternalInput").ap(),
    }


def build_l1ffn_program():
    nc = bass.Bass("TRN2", target_bir_lowering=False)
    xin = nc.dram_tensor("xin", [D, T], F32, kind="ExternalInput").ap()
    W = l1ffn_dram(nc)
    xout = nc.dram_tensor("xout", [D, T], F32, kind="ExternalOutput").ap()
    p = P(nc)
    B = make_common(nc, p)
    l1ffn_buffers(nc, B)
    dx = load_x(nc, p, B, xin)
    scales_v = emit_l1ffn_preload(nc, p, B, W)
    ln_done = emit_ln(nc, p, B, dx, svar=False)
    emit_l1ffn(nc, p, B, W, xin, xout, ln_done, scales_v)
    p.realize()
    return nc


def _tile_oc(w, P=128, Q=128):
    O, I = w.shape
    t = w.reshape(O // Q, Q, I // P, P)
    return np.ascontiguousarray(t.transpose(0, 3, 2, 1))


def l1ffn_host_weights(inputs, prefix="L1_ffn"):
    g = lambda n: np.asarray(inputs[prefix + "_" + n])
    colmaj = lambda s, nb: np.ascontiguousarray(s.reshape(nb, 128).T)
    return {
        "f_w1": _tile_oc(g("w1_w")), "f_w2": _tile_oc(g("w2_w")),
        "f_h1": _tile_oc(g("h1_w")), "f_h2": _tile_oc(g("h2_w")),
        "f_w3": _tile_oc(g("w3_w")),
        "f_s1c": colmaj(g("w1_s"), 32), "f_s2c": colmaj(g("w2_s"), 32),
        "f_sh1c": colmaj(g("h1_s"), 32), "f_sh2c": colmaj(g("h2_s"), 32),
        "f_s3c": colmaj(g("w3_s"), 8),
    }


def l1attn_buffers(nc, B):
    SB = nc.alloc_sbuf_tensor
    B.zbs = B.ytmp[:, 0:128]
    B.QTs = SB("QTs", [64, 16 * 512], BF16)
    B.KTs = SB("KTs", [64, 16 * 512], BF16)
    B.VA = SB("VA", [128, 4 * 16 * 128], BF16)
    B.E2 = SB("E2", [128, 2 * 2048], BF16)
    B.CTs = SB("CTs", [128, 8 * 512], BF16)
    B.ATTs = SB("ATTs", [128, 8 * 512], F32)
    B.WS = SB("WS", [128, 2 * 1024], BF16)
    B.WV = SB("WV", [128, 2 * 1024], BF16)
    B.bqc = SB("bqc", [128, 8], F32)
    B.bkc = SB("bkc", [128, 8], F32)
    B.boc = SB("boc", [128, 8], F32)
    B.bvrow = SB("bvrow", [1, 1024], BF16)
    B.gamc = SB("gamc", [128, 8], F32)
    B.betc = SB("betc", [128, 8], F32)
    B.neg4c = SB("neg4c", [128, 1], F32)
    B.rz = SB("rz", [1, 512], F32)


def emit_l1attn_preload(nc, p, B, W):
    for nm in ("bqc", "bkc", "boc", "gamc", "betc"):
        p.dma("sync", getattr(B, nm)[:], W[nm], inc="dx")
    p.dma("gpsimd", B.bvrow[:], W["bv"], inc="dg")
    p.gop("memset", B.neg4c[:], -4.0)
    return p.cnt["dx"], p.cnt["dg"], p.cnt["g"]


def emit_l1attn(nc, p, B, W, ln_done, pre):
    dxs, dgs, gs = pre
    ws_rd = {}
    wv_rd = {}
    p.wait("tensor", "dx", dxs)
    for sg in range(4):
        t0 = sg * 512
        lnv = ln_done[sg]
        g0 = p.gop("memset", B.VA[:], 0.0,
                   waits=[("pe", p.cnt["pe"])] if sg > 0 else ())
        va_ones = bass.AP(B.VA, 0, [[4 * 16 * 128, 128], [128, 64], [1, 1]])
        g1 = p.gop("memset", va_ones, 1.0)
        for ic in range(8):
            db = ic % 2
            dgw = p.dma("gpsimd", B.WV[:, db * 1024:(db + 1) * 1024],
                        W["wv"][ic * 128:(ic + 1) * 128, :], inc="dg",
                        waits=[("pe", wv_rd.get(db, 0))])
            for tt in range(4):
                for n in range(2):
                    bk = tt * 2 + n
                    w = []
                    if ic == 0:
                        w = [("dg", dgw), ("v", lnv)] + p.bank_free[bk]
                    else:
                        w = [("dg", dgw)] if (tt == 0 and n == 0) else []
                    mn = p.mm(B.pb[bk][:, 0:512],
                              B.XN[:, ic * T + t0 + tt * 128: ic * T + t0 + (tt + 1) * 128],
                              B.WV[:, db * 1024 + n * 512: db * 1024 + (n + 1) * 512],
                              start=(ic == 0), stop=False, waits=w,
                              inc="pe" if (tt == 3 and n == 1) else None)
                    if tt == 3 and n == 1:
                        wv_rd[db] = mn
        for tt in range(4):
            for n in range(2):
                bk = tt * 2 + n
                p.mm(B.pb[bk][:, 0:512], B.onesrow[0:1, 0:128],
                     B.bvrow[0:1, n * 512:(n + 1) * 512],
                     start=False, stop=True, inc="pe")
        wv_rd[0] = wv_rd[1] = pe_v = p.cnt["pe"]
        vva = None
        for tt in range(4):
            for n in range(2):
                bk = tt * 2 + n
                dst = bass.AP(B.VA, tt * 2048 + n * 8 * 128 + 64,
                              [[4 * 16 * 128, 128], [128, 8], [1, 64]])
                srcv = bass.AP(B.pb8h, bk * 512, [[4096, 128], [64, 8], [1, 64]])
                vva = p.vop("tensor_copy", dst, srcv,
                            waits=[("pe", pe_v), ("g", g1)])
                p.bank_free[bk] = [("v", vva)]
        qk_done = None
        for wi, (wkey, bcol, dstbuf) in enumerate((("wq", B.bqc, B.QTs),
                                                   ("wk", B.bkc, B.KTs))):
            for cb in range(8):
                db = (wi * 8 + cb) % 2
                dgw = p.dma("gpsimd", B.WS[:, db * 1024:(db + 1) * 1024],
                            W[wkey][cb], inc="dg", waits=[("pe", ws_rd.get(db, 0))])
                bk = (wi * 8 + cb) % 4
                for ic in range(8):
                    w = []
                    if ic == 0:
                        w = [("dg", dgw)] + p.bank_free[bk]
                    mn = p.mm(B.pb[bk][:, 0:512],
                              B.WS[:, db * 1024 + ic * 128: db * 1024 + (ic + 1) * 128],
                              B.XN[:, ic * T + t0: ic * T + t0 + 512],
                              start=(ic == 0), stop=(ic == 7), waits=w,
                              inc="pe")
                    ws_rd[db] = mn
                p.sop("activation", dstbuf[0:64, (2 * cb) * 512:(2 * cb + 1) * 512],
                      B.pb8[0:64, bk * 512:(bk + 1) * 512], AF.Identity,
                      bias=bcol[0:64, cb:cb + 1],
                      waits=[("pe", p.cnt["pe"])])
                qk_done = p.sop("activation",
                                dstbuf[0:64, (2 * cb + 1) * 512:(2 * cb + 2) * 512],
                                B.pb8[64:128, bk * 512:(bk + 1) * 512], AF.Identity,
                                bias=bcol[64:128, cb:cb + 1])
                p.bank_free[bk] = [("s", qk_done)]
        pend_norm = [None]

        def flush_norm(p=p, B=B):
            if pend_norm[0] is None:
                return
            vz_, cbk_, gi_, q4_ = pend_norm[0]
            pend_norm[0] = None
            p.mm(B.pb[4][:, 0:512], B.ones128f[0:1, 0:128], B.rz[0:1, 0:512],
                 start=True, stop=True, waits=[("v", vz_)] + p.bank_free[4],
                 inc="pe")
            vzb = p.vop("tensor_copy", B.ytmp[:], B.pb[4][:, 0:512],
                        waits=[("pe", p.cnt["pe"])])
            p.bank_free[4] = [("v", vzb)]
            vct = None
            for par in range(2):
                dst = bass.AP(B.CTs, par * 64 * 4096 + (q4_ * 2) * 512 + gi_ * 128,
                              [[4096, 64], [512, 2], [1, 128]])
                csrc = bass.AP(B.pb8h, cbk_ * 512 + 64 * 4096 + par * 128,
                               [[4096, 64], [256, 2], [1, 128]])
                zsrc = bass.AP(B.ytmp, 64 * 512 + par * 128,
                               [[512, 64], [256, 2], [1, 128]])
                vct = p.vop("tensor_tensor", dst, csrc, zsrc, OP.mult)
            p.bank_free[cbk_] = [("v", vct)]

        def emit_S(gi, cq):
            qv = B.QTs[0:64, cq * 512 + gi * 128: cq * 512 + (gi + 1) * 128]
            for ck in range(16):
                kv = B.KTs[0:64, ck * 512 + gi * 128: ck * 512 + (gi + 1) * 128]
                w = []
                if ck % 4 == 0:
                    w = list(p.bank_free[ck // 4])
                    if gi == 0 and cq == 0:
                        w.append(("s", qk_done))
                p.mm(B.pb8[:, ck * 128:(ck + 1) * 128],
                     kv, qv, start=True, stop=True, waits=w,
                     inc="pe" if ck in (7, 15) else None)
            pe_s = p.cnt["pe"]
            eslot = cq % 2
            se = None
            for half in range(2):
                se = p.sop("activation",
                           B.E2[:, eslot * 2048 + half * 1024:
                                eslot * 2048 + (half + 1) * 1024],
                           B.pb8[:, half * 1024:(half + 1) * 1024],
                           AF.Exp, scale=0.125, bias=B.neg4c[:, 0:1],
                           waits=[("pe", pe_s - 1 + half)])
            for i in range(4):
                p.bank_free[i] = [("s", se)]
            return se

        def emit_ctx(gi, cq, se):
            eslot = cq % 2
            cbk = 6 + (cq // 4) % 2
            slot = cq % 4
            for ck in range(16):
                w = []
                if ck == 0:
                    w = [("s", se)] + (p.bank_free[cbk] if slot == 0 else [])
                    if gi == 0 and cq == 0:
                        w += [("v", vva), ("g", g1)]
                p.mm(B.pb8[:, cbk * 512 + slot * 128: cbk * 512 + (slot + 1) * 128],
                     B.VA[:, (gi * 16 + ck) * 128:(gi * 16 + ck + 1) * 128],
                     B.E2[:, eslot * 2048 + ck * 128: eslot * 2048 + (ck + 1) * 128],
                     start=(ck == 0), stop=(ck == 15), waits=w,
                     inc="pe" if ck == 15 else None)
            if slot == 3:
                vz = p.vop("reciprocal", B.rz[:],
                           bass.AP(B.pb8h, cbk * 512, [[4096, 1], [1, 512]]),
                           waits=[("pe", p.cnt["pe"])])
                pend_norm[0] = (vz, cbk, gi, cq // 4)

        prev = None
        for gi in range(4):
            for cq in range(16):
                se = emit_S(gi, cq)
                if cq % 4 == 1:
                    flush_norm()
                if prev is not None:
                    emit_ctx(prev[0], prev[1], prev[2])
                prev = (gi, cq, se)
        emit_ctx(prev[0], prev[1], prev[2])
        flush_norm()
        v_ct = p.cnt["v"]
        if B.adbg and sg == B.dbg_sg:
            p.dma("sync", B.adbg["QTs"], B.QTs[:], inc="do", waits=[("v", v_ct)])
            p.dma("sync", B.adbg["KTs"], B.KTs[:], inc="do")
            p.dma("sync", B.adbg["VA"], B.VA[:], inc="do")
            p.dma("sync", B.adbg["CTs"], B.CTs[:], inc="do")
            p.dma("sync", B.adbg["E2"], B.E2[:], inc="do")
        for ob in range(8):
            db = ob % 2
            dgw = p.dma("gpsimd", B.WS[:, db * 1024:(db + 1) * 1024],
                        W["wo"][ob], inc="dg", waits=[("pe", ws_rd.get(db, 0))])
            bk = ob % 4
            for ec in range(8):
                w = []
                if ec == 0:
                    w = [("dg", dgw), ("v", v_ct)] + p.bank_free[bk]
                mn = p.mm(B.pb[bk][:, 0:512],
                          B.WS[:, db * 1024 + ec * 128: db * 1024 + (ec + 1) * 128],
                          B.CTs[:, ec * 512:(ec + 1) * 512],
                          start=(ec == 0), stop=(ec == 7), waits=w, inc="pe")
                ws_rd[db] = mn
            vo = p.vop("scalar_tensor_tensor", B.ATTs[:, ob * 512:(ob + 1) * 512],
                       B.pb[bk][:, 0:512], B.boc[:, ob:ob + 1],
                       B.XN[:, ob * T + t0: ob * T + t0 + 512], OP.add, OP.add,
                       waits=[("pe", p.cnt["pe"])] + ([("s", p.cnt["s"])] if ob == 0 else []))
            p.bank_free[bk] = [("v", vo)]
        if B.adbg and sg == B.dbg_sg:
            p.dma("sync", B.adbg["ATTs"], B.ATTs[:], inc="do", waits=[("v", p.cnt["v"])])
        v_att = p.cnt["v"]
        ia, ib = 4, 5
        ms = {}
        for c in range(NCH):
            db2 = c % 2
            sl = slice(db2 * 512, (db2 + 1) * 512)
            asl = B.ATTs[:, c * 512:(c + 1) * 512]
            w = [("pe", ms[c - 2])] if c >= 2 else []
            vcp = p.vop("tensor_copy", B.xbfs[:, sl], asl, waits=w)
            scp = p.sop("activation", B.xsqs[:, sl], asl, AF.Square,
                        waits=w + ([("v", v_att)] if c == 0 else []))
            w2_ = [("v", vcp), ("s", scp)]
            if c == 0:
                w2_ += p.bank_free[ia] + p.bank_free[ib]
            p.mm(B.pb[ia][:, 0:512], B.ones128[:], B.xbfs[:, sl],
                 start=(c == 0), stop=(c == NCH - 1), waits=w2_)
            ms[c] = p.mm(B.pb[ib][:, 0:512], B.ones128[:], B.xsqs[:, sl],
                         start=(c == 0), stop=(c == NCH - 1), inc="pe")
        pe_done = p.cnt["pe"]
        p.vop("tensor_scalar", B.meanbc[:], B.pb[ia][:, 0:512], 1.0 / D, None,
              OP.mult, waits=[("pe", pe_done)])
        p.vop("tensor_scalar", B.msqbc[:], B.pb[ib][:, 0:512], 1.0 / D, None, OP.mult)
        p.vop("tensor_tensor", B.rstdbc[:], B.meanbc[:], B.meanbc[:], OP.mult)
        vfree = p.vop("tensor_tensor", B.msqbc[:], B.msqbc[:], B.rstdbc[:], OP.subtract)
        p.bank_free[ia] = p.bank_free[ib] = [("v", vfree)]
        p.sop("activation", B.rstdbc[:], B.msqbc[:], AF.Ln, bias=B.epscol[:],
              waits=[("v", vfree)])
        ssq = p.sop("activation", B.rstdbc[:], B.rstdbc[:], AF.Exp, scale=-0.5)
        p.wait("vector", "s", ssq)
        for c in range(NCH):
            asl = B.ATTs[:, c * 512:(c + 1) * 512]
            xsl = B.XR[:, c * T + t0: c * T + t0 + 512]
            p.vop("tensor_tensor", B.ytmp[:], asl, B.meanbc[:], OP.subtract)
            p.vop("tensor_tensor", B.ytmp[:], B.ytmp[:], B.rstdbc[:], OP.mult)
            p.vop("tensor_scalar", B.ytmp[:], B.ytmp[:], B.gamc[:, c:c + 1],
                  B.betc[:, c:c + 1], OP.mult, OP.add)
            p.vop("tensor_tensor", xsl, xsl, B.ytmp[:], OP.add)
    return p.cnt["v"]


def l1attn_dram(nc):
    return {
        "wq": nc.dram_tensor("a_wq", [8, 128, 8, 128], F32, kind="ExternalInput").ap(),
        "wk": nc.dram_tensor("a_wk", [8, 128, 8, 128], F32, kind="ExternalInput").ap(),
        "wv": nc.dram_tensor("a_wv", [1024, 1024], F32, kind="ExternalInput").ap(),
        "wo": nc.dram_tensor("a_wo", [8, 128, 8, 128], F32, kind="ExternalInput").ap(),
        "bqc": nc.dram_tensor("a_bqc", [128, 8], F32, kind="ExternalInput").ap(),
        "bkc": nc.dram_tensor("a_bkc", [128, 8], F32, kind="ExternalInput").ap(),
        "bv": nc.dram_tensor("a_bv", [1, 1024], F32, kind="ExternalInput").ap(),
        "boc": nc.dram_tensor("a_boc", [128, 8], F32, kind="ExternalInput").ap(),
        "gamc": nc.dram_tensor("a_gamc", [128, 8], F32, kind="ExternalInput").ap(),
        "betc": nc.dram_tensor("a_betc", [128, 8], F32, kind="ExternalInput").ap(),
    }


def build_l1attn_program(debug=False, dbg_sg=0):
    nc = bass.Bass("TRN2", target_bir_lowering=False)
    xin = nc.dram_tensor("xin", [D, T], F32, kind="ExternalInput").ap()
    W = l1attn_dram(nc)
    xout = nc.dram_tensor("xout", [D, T], F32, kind="ExternalOutput").ap()
    p = P(nc)
    B = make_common(nc, p)
    l1attn_buffers(nc, B)
    B.adbg = {}
    B.dbg_sg = dbg_sg
    if debug:
        for nm, sh, dt in (("QTs", [64, 8192], BF16), ("KTs", [64, 8192], BF16),
                           ("VA", [128, 8192], BF16), ("CTs", [128, 4096], BF16),
                           ("ATTs", [128, 4096], F32), ("E2", [128, 4096], BF16)):
            B.adbg[nm] = nc.dram_tensor("dbg_" + nm, sh, dt, kind="ExternalOutput").ap()
    dx = load_x(nc, p, B, xin)
    pre = emit_l1attn_preload(nc, p, B, W)
    ln_done = emit_ln(nc, p, B, dx, svar=False)
    vend = emit_l1attn(nc, p, B, W, ln_done, pre)
    store_x(nc, p, B, xout, vend)
    p.realize()
    return nc


def l1attn_host_weights(inputs, prefix="L1_attn"):
    g = lambda n: np.asarray(inputs[prefix + "_" + n])
    colmaj = lambda s: np.ascontiguousarray(s.reshape(8, 128).T)
    def tile_nat(w):
        t = w.reshape(8, 128, 8, 128)
        return np.ascontiguousarray(w.T.reshape(8, 128, 8, 128).transpose(0, 3, 2, 1))
    def tile_lhsT(w):
        t = w.reshape(8, 128, 8, 128)
        return np.ascontiguousarray(t.transpose(2, 1, 0, 3))
    return {
        "a_wq": tile_lhsT(g("wq")), "a_wk": tile_lhsT(g("wk")),
        "a_wv": np.ascontiguousarray(g("wv")), "a_wo": tile_lhsT(g("wo")),
        "a_bqc": colmaj(g("bq")), "a_bkc": colmaj(g("bk")),
        "a_bv": np.ascontiguousarray(g("bv")[None, :]),
        "a_boc": colmaj(g("bo")), "a_gamc": colmaj(g("gamma")),
        "a_betc": colmaj(g("beta")),
    }


def l0attn_buffers(nc, B):
    SB = nc.alloc_sbuf_tensor
    B.EQs = SB("EQs", [128, 4 * 1024], BF16)
    B.EKs = SB("EKs", [128, 4 * 1024], BF16)
    B.Vs = SB("Vs", [128, 4 * 1024], BF16)
    B.QTi = SB("QTi", [64, 2048], BF16)
    B.KTi = SB("KTi", [64, 2048], BF16)
    B.VTi = SB("VTi", [64, 2048], BF16)
    B.At = SB("At", [16, 2048], BF16)
    B.Vr = SB("Vr", [16, 64 + 128 * 64], BF16)
    B.rQ = SB("rQ", [128, 16], F32)
    B.rK = SB("rK", [128, 64], F32)
    B.rKT = SB("rKT", [64, 128], F32)
    B.mxc = SB("mxc", [128, 1], F32)
    B.WQ3 = SB("WQ3", [128, 2 * 1024], BF16)


def emit_l0attn(nc, p, B, W, ln_done):
    w3_rd = {}
    ident = B.ident
    p.gop("memset", B.Vr[0:16, 0:64], 0.0)
    for sg in range(4):
        t0 = sg * 512
        for wi, wkey in enumerate(("wqT", "wkT", "wvT")):
            dst = (B.EQs, B.EKs, B.Vs)[wi]
            for ic in range(8):
                db = ic % 2
                dgw = p.dma("gpsimd", B.WQ3[:, db * 1024:(db + 1) * 1024],
                            W[wkey][ic * 128:(ic + 1) * 128, :], inc="dg",
                            waits=[("pe", w3_rd.get(db, 0))])
                for tt in range(4):
                    for n in range(2):
                        bk = tt * 2 + n
                        w = []
                        if ic == 0:
                            w = [("dg", dgw)] + p.bank_free[bk]
                            if wi == 0 and tt == 0 and n == 0:
                                w.append(("v", ln_done[sg]))
                        elif tt == 0 and n == 0:
                            w = [("dg", dgw)]
                        mn = p.mm(B.pb[bk][:, 0:512],
                                  B.XN[:, ic * T + t0 + tt * 128: ic * T + t0 + (tt + 1) * 128],
                                  B.WQ3[:, db * 1024 + n * 512: db * 1024 + (n + 1) * 512],
                                  start=(ic == 0), stop=(ic == 7), waits=w,
                                  inc="pe" if (tt == 3 and n == 1) or ic == 7 else None)
                        if tt == 3 and n == 1:
                            w3_rd[db] = mn
            for tt in range(4):
                pe_t = p.cnt["pe"]
                if wi < 2:
                    vmx = p.vop("tensor_reduce", B.mxc[:],
                                bass.AP(B.pb8h, tt * 1024, [[4096, 128], [1, 1024]]),
                                axis=AX.X, op=OP.max, waits=[("pe", pe_t)])
                    p.vop("tensor_scalar_mul", B.mxc[:], B.mxc[:], -1.0)
                    p.wait("scalar", "v", p.cnt["v"])
                    se = p.sop("activation", dst[:, tt * 1024:(tt + 1) * 1024],
                               bass.AP(B.pb8h, tt * 1024, [[4096, 128], [1, 1024]]),
                               AF.Exp, bias=B.mxc[:], waits=[("pe", pe_t)])
                    rel = [("s", se)]
                else:
                    sv = p.sop("activation", dst[:, tt * 1024:(tt + 1) * 1024],
                               bass.AP(B.pb8h, tt * 1024, [[4096, 128], [1, 1024]]),
                               AF.Identity, waits=[("pe", pe_t)])
                    rel = [("s", sv)]
                p.bank_free[tt * 2] = rel
                p.bank_free[tt * 2 + 1] = rel
        for tt in range(4):
            toff = t0 + tt * 128
            eq = B.EQs[:, tt * 1024:(tt + 1) * 1024]
            ek = B.EKs[:, tt * 1024:(tt + 1) * 1024]
            vv = B.Vs[:, tt * 1024:(tt + 1) * 1024]
            p.wait("vector", "s", p.cnt["s"])
            p.vop("tensor_reduce", B.rQ[:],
                  bass.AP(B.EQs, tt * 1024, [[4096, 128], [64, 16], [1, 64]]),
                  axis=AX.X, op=OP.add)
            p.vop("reciprocal", B.rQ[:], B.rQ[:])
            p.vop("tensor_tensor",
                  bass.AP(B.EQs, tt * 1024, [[4096, 128], [64, 16], [1, 64]]),
                  bass.AP(B.EQs, tt * 1024, [[4096, 128], [64, 16], [1, 64]]),
                  bass.AP(B.rQ, 0, [[16, 128], [1, 16], [0, 64]]), OP.mult)
            p.vop("tensor_reduce", B.rK[:],
                  bass.AP(B.EKs, tt * 1024, [[4096, 128], [1, 64], [64, 16]]),
                  axis=AX.X, op=OP.add)
            vrk = p.vop("reciprocal", B.rK[:], B.rK[:])
            p2h = B.pb8h[:].bitcast(BF16)
            for src_, dstb, base in ((eq, B.QTi, 0), (ek, B.KTi, 2048), (vv, B.VTi, 4096)):
                pe_w = [("v", p.cnt["v"])]
                for h in range(16):
                    p.tr(p2h[0:64, base + h * 128:base + (h + 1) * 128],
                         src_[:, h * 64:(h + 1) * 64], ident[:],
                         waits=pe_w if h == 0 else (),
                         inc="pe" if h == 15 else None)
            p.tr(bass.AP(B.pb8h, 3072, [[4096, 64], [1, 128]]),
                 B.rK[:], B.identf[:], waits=[("v", vrk)], inc="pe")
            pe_tr = p.cnt["pe"]
            import dataclasses as _dc
            def _pview(off_bf16, pat):
                return _dc.replace(p2h, offset=off_bf16, ap=pat)
            vr1 = p.vop("tensor_copy", B.rKT[:],
                        bass.AP(B.pb8h, 3072, [[4096, 64], [1, 128]]),
                        waits=[("pe", pe_tr)])
            p.vop("tensor_copy",
                  bass.AP(B.QTi, 0, [[2048, 64], [1, 16], [16, 128]]),
                  _pview(0, [[8192, 64], [128, 16], [1, 128]]))
            v_qk = p.vop("scalar_tensor_tensor",
                         bass.AP(B.KTi, 0, [[2048, 64], [1, 16], [16, 128]]),
                         _pview(2048, [[8192, 64], [128, 16], [1, 128]]),
                         1.0,
                         bass.AP(B.rKT, 0, [[128, 64], [0, 16], [1, 128]]),
                         OP.mult, OP.mult)
            v_v = p.vop("tensor_copy",
                        bass.AP(B.VTi, 0, [[2048, 64], [1, 16], [16, 128]]),
                        _pview(4096, [[8192, 64], [128, 16], [1, 128]]))
            p.wait("tensor", "v", v_qk)
            for half in range(2):
                abase = 3072 if half == 0 else 2048
                bks = (6, 7) if half == 0 else (4, 5)
                for ti in range(64):
                    tok = half * 64 + ti
                    w0 = []
                    if ti == 0:
                        w0 = p.bank_free[bks[0]] + p.bank_free[bks[1]]
                        w0.append(("v", vr1 if half == 0 else v_v))
                    p.mm(bass.AP(B.pb8h, abase + ti * 16, [[4096, 16], [1, 16]]),
                         B.KTi[0:64, tok * 16:(tok + 1) * 16],
                         B.QTi[0:64, tok * 16:(tok + 1) * 16],
                         start=True, stop=True, waits=w0,
                         inc="pe" if ti == 63 else None)
                vsrc = bass.AP(B.pb8h, abase, [[4096, 16], [16, 64], [1, 2], [2, 8]])
                vdst = bass.AP(B.At, half * 1024, [[2048, 16], [16, 64], [8, 2], [1, 8]])
                va = p.vop("tensor_copy", vdst, vsrc, waits=[("pe", p.cnt["pe"])])
                p.bank_free[bks[0]] = [("v", va)]
                p.bank_free[bks[1]] = [("v", va)]
            for grp in range(4):
                vb0 = (grp % 2) * 2
                for ti in range(32):
                    tok = grp * 32 + ti
                    p.tr(p2h[0:16, vb0 * 1024 + ti * 64:vb0 * 1024 + (ti + 1) * 64],
                         B.VTi[0:64, tok * 16:(tok + 1) * 16], ident[0:64, 0:64],
                         waits=((p.bank_free[vb0] + p.bank_free[vb0 + 1]) if ti == 0 else ()),
                         inc="pe" if ti == 31 else None)
                vv2 = p.vop("tensor_copy",
                            B.Vr[0:16, 64 + grp * 2048: 64 + (grp + 1) * 2048],
                            p2h[0:16, vb0 * 1024:vb0 * 1024 + 2048],
                            waits=[("pe", p.cnt["pe"])])
                p.bank_free[vb0] = [("v", vv2)]
                p.bank_free[vb0 + 1] = [("v", vv2)]
            v_vr = p.cnt["v"]
            p.wait("tensor", "v", v_vr)
            for half in range(2):
                ebk, obk = (0, 1) if half == 0 else (2, 3)
                for ti in range(64):
                    tok = half * 64 + ti
                    w_e = (p.bank_free[ebk] + p.bank_free[obk]) if ti == 0 else ()
                    p.mm(bass.AP(B.pb8h, ebk * 512 + ti * 8, [[4096, 64], [1, 8]]),
                         B.Vr[0:16, 64 + tok * 64: 64 + (tok + 1) * 64],
                         B.At[0:16, tok * 16: tok * 16 + 8],
                         start=True, stop=True, waits=w_e)
                    p.mm(bass.AP(B.pb8h, obk * 512 + ti * 8, [[4096, 128], [1, 8]]),
                         B.Vr[0:16, tok * 64: tok * 64 + 128],
                         B.At[0:16, tok * 16 + 8: tok * 16 + 16],
                         start=True, stop=True,
                         inc="pe" if ti == 63 else None)
                pe_s2 = p.cnt["pe"]
                vlast = p.vop("scalar_tensor_tensor",
                              bass.AP(B.XR, toff + half * 64, [[16384, 64], [2048, 8], [1, 64]]),
                              bass.AP(B.pb8h, ebk * 512, [[4096, 64], [1, 8], [8, 64]]),
                              1.0,
                              bass.AP(B.XR, toff + half * 64, [[16384, 64], [2048, 8], [1, 64]]),
                              OP.mult, OP.add, waits=[("pe", pe_s2)])
                vlast = p.vop("scalar_tensor_tensor",
                              bass.AP(B.XR, 64 * 16384 + toff + half * 64,
                                      [[16384, 64], [2048, 8], [1, 64]]),
                              bass.AP(B.pb8h, obk * 512 + 64 * 4096, [[4096, 64], [1, 8], [8, 64]]),
                              1.0,
                              bass.AP(B.XR, 64 * 16384 + toff + half * 64,
                                      [[16384, 64], [2048, 8], [1, 64]]),
                              OP.mult, OP.add)
                p.bank_free[ebk] = [("v", vlast)]
                p.bank_free[obk] = [("v", vlast)]
    return p.cnt["v"]


def l0attn_dram(nc):
    return {
        "wqT": nc.dram_tensor("l0_wqT", [1024, 1024], F32, kind="ExternalInput").ap(),
        "wkT": nc.dram_tensor("l0_wkT", [1024, 1024], F32, kind="ExternalInput").ap(),
        "wvT": nc.dram_tensor("l0_wvT", [1024, 1024], F32, kind="ExternalInput").ap(),
    }


def build_l0attn_program():
    nc = bass.Bass("TRN2", target_bir_lowering=False)
    xin = nc.dram_tensor("xin", [D, T], F32, kind="ExternalInput").ap()
    W = l0attn_dram(nc)
    xout = nc.dram_tensor("xout", [D, T], F32, kind="ExternalOutput").ap()
    p = P(nc)
    B = make_common(nc, p)
    l0attn_buffers(nc, B)
    dx = load_x(nc, p, B, xin)
    ln_done = emit_ln(nc, p, B, dx, svar=False)
    vend = emit_l0attn(nc, p, B, W, ln_done)
    store_x(nc, p, B, xout, vend)
    p.realize()
    return nc


def l0attn_host_weights(inputs):
    return {
        "l0_wqT": np.ascontiguousarray(np.asarray(inputs["L0_attn_W_Q"]).T),
        "l0_wkT": np.ascontiguousarray(np.asarray(inputs["L0_attn_W_K"]).T),
        "l0_wvT": np.ascontiguousarray(np.asarray(inputs["L0_attn_W_V"]).T),
    }


_SBUF_TOP = 229344

_ARENA_GROUPS = {
    "moe": [("svarbc", [128, T], "BF16"), ("rwt", [128, NCH * 32], "F32"),
            ("rwsc", [128, 32], "F32"), ("rbrow", [1, 32], "BF16"),
            ("b1c", [128, 64], "F32"), ("b2t", [128, 256], "F32"),
            ("idxcols", [128, 16], "F32"), ("MT", [32, 2048], "BF16"),
            ("lmax", [128, 1], "F32"),
            ("iote", [32, 1], "BF16"),
            ("Ltile", [128, 2048], "BF16"),
            ("w1f32", [128, 2048], "F32"),
            ("tokids", [128, 16], "I32"), ("ytm", [128, 4 * 1024], "BF16"),
            ("xg", [128, 4 * 1024], "BF16"),
            ("xgT", [128, 4 * 1024], "BF16"), ("hT", [128, 4 * 256], "BF16"),
            ("yT", [128, 4 * 1024], "BF16"),
            ("w1sb", [128, 4 * 2048], "BF16"), ("w2sb", [128, 4 * 2048], "BF16")],
    "l0a": [("EQs", [128, 4 * 1024], "BF16"), ("EKs", [128, 4 * 1024], "BF16"),
            ("Vs", [128, 4 * 1024], "BF16"), ("QTi", [64, 2048], "BF16"),
            ("KTi", [64, 2048], "BF16"), ("VTi", [64, 2048], "BF16"),
            ("At", [16, 2048], "BF16"), ("Vr", [16, 64 + 128 * 64], "BF16"),
            ("rQ", [128, 16], "F32"), ("rK", [128, 64], "F32"),
            ("rKT", [64, 128], "F32"), ("mxc", [128, 1], "F32"),
            ("WQ3", [128, 2 * 1024], "BF16")],
    "l1f": [("GH", [128, 32768], "BF16"), ("wA", [128, 2 * 4096], "BF16"),
            ("wB", [128, 2 * 1024], "BF16"), ("wC", [128, 2 * 1024], "BF16"),
            ("s1c", [128, 32], "F32"), ("s2c", [128, 32], "F32"),
            ("sh1c", [128, 32], "F32"), ("sh2c", [128, 32], "F32"),
            ("s3c", [128, 8], "F32"), ("s3h", [128, 8], "F32")],
    "l1a": [("QTs", [64, 16 * 512], "BF16"), ("KTs", [64, 16 * 512], "BF16"),
            ("VA", [128, 4 * 16 * 128], "BF16"), ("E2", [128, 2 * 2048], "BF16"),
            ("CTs", [128, 8 * 512], "BF16"), ("ATTs", [128, 8 * 512], "F32"),
            ("WS", [128, 2 * 1024], "BF16"), ("WV", [128, 2 * 1024], "BF16"),
            ("bqc", [128, 8], "F32"), ("bkc", [128, 8], "F32"),
            ("boc", [128, 8], "F32"), ("bvrow", [1, 1024], "BF16"),
            ("gamc", [128, 8], "F32"), ("betc", [128, 8], "F32"),
            ("neg4c", [128, 1], "F32"), ("rz", [1, 512], "F32")],
}


def _alloc_arena(nc, base):
    dts = {"F32": F32, "BF16": BF16, "I32": I32}
    handles = {}
    maxend = base
    for phase, members in _ARENA_GROUPS.items():
        off = base
        hs = {}
        for name, shape, dt in members:
            nb = shape[-1] * (4 if dt == "F32" else 2)
            for s in shape[1:-1]:
                nb *= s
            off = (off + 31) & ~31
            hs[name] = nc.alloc_sbuf_tensor_at(phase + "_" + name, shape, dts[dt],
                                               offset=off)
            off += nb
        assert off <= _SBUF_TOP, (phase, off)
        maxend = max(maxend, off)
        handles[phase] = hs
    return handles


def build_full_program(skip=()):
    nc = bass.Bass("TRN2", target_bir_lowering=False)
    xin = nc.dram_tensor("xin", [D, T], F32, kind="ExternalInput").ap()
    Wm = moer_dram(nc)
    W0 = l0attn_dram(nc)
    Wf = l1ffn_dram(nc)
    Wa = l1attn_dram(nc)
    xout = nc.dram_tensor("xout", [D, T], F32, kind="ExternalOutput").ap()
    xsp = nc.dram_tensor("xspill", [D, T], F32, kind="Internal").ap()
    scr = {
        "xn_tm": nc.dram_tensor("s_xn_tm", [T, D], BF16, kind="Internal").ap(),
        "y_tm": nc.dram_tensor("s_y_tm", [T, D], BF16, kind="Internal").ap(),
        "perm2": nc.dram_tensor("s_perm2", [32 * 128, 1], I32, kind="Internal").ap(),
    }
    p = P(nc)
    B = make_common(nc, p)
    B.scoli = nc.alloc_sbuf_tensor("fx_scoli", [128, 16], I32)
    B.permsb = nc.alloc_sbuf_tensor("fx_permsb", [128, 32], I32)
    B.dgfence = nc.alloc_sbuf_tensor("fx_dgfence", [16, 1], I32)
    base = _SBUF_TOP - nc.sbuf_bytes_remaining
    handles = _alloc_arena(nc, base)

    def bind(phase):
        for name, h in handles[phase].items():
            setattr(B, name, h)
        if phase == "l1a":
            B.zbs = B.ytmp[:, 0:128]
        if phase == "moe":
            B.w2f32 = B.w1f32
            B.srow = B.w1f32[0:1, 0:2048]
            B.Mtm = B.w1f32[:].bitcast(BF16)[:, 2048:2560]
            B.lsel = B.rwsc
            B.ranksb = B.Ltile[0:32, 0:2048]
            B.xtms = B.yT
        if phase == "l1f":
            B.Hv = B.XR[:].bitcast(BF16)

    dx = load_x(nc, p, B, xin)
    if 1 not in skip:
        bind("moe")
        B.rdbg = {}
        sdg = emit_moer_preload(nc, p, B, Wm)
        ln = emit_ln(nc, p, B, x_ready_dx=dx, svar=True)
        emit_moe_routed(nc, p, B, Wm, ln, sdg, scr)
        phase_barrier(p)
    if 2 not in skip:
        bind("l0a")
        ln = emit_ln(nc, p, B, x_ready_dx=dx, v_ready=p.cnt["v"], svar=False)
        emit_l0attn(nc, p, B, W0, ln)
        phase_barrier(p)
    if 3 not in skip:
        bind("moe")
        B.rdbg = {}
        sdg = emit_moer_preload(nc, p, B, Wm)
        ln = emit_ln(nc, p, B, x_ready_dx=dx, v_ready=p.cnt["v"], svar=True)
        emit_moe_routed(nc, p, B, Wm, ln, sdg, scr)
        phase_barrier(p)
    if 4 not in skip:
        bind("l1f")
        sv = emit_l1ffn_preload(nc, p, B, Wf)
        ln = emit_ln(nc, p, B, x_ready_dx=dx, v_ready=p.cnt["v"], svar=False)
        for c in range(NCH):
            p.dma("sync", xsp[c * 128:(c + 1) * 128, :], B.XR[:, c * T:(c + 1) * T],
                  inc="do")
        spill_do = p.cnt["do"]
        emit_l1ffn(nc, p, B, Wf, xsp, xsp, ln, sv, hv_waits=[("do", spill_do)])
        phase_barrier(p)
    if 5 not in skip:
        bind("l1a")
        if 4 not in skip:
            for c in range(NCH):
                p.dma("sync", B.XR[:, c * T:(c + 1) * T], xsp[c * 128:(c + 1) * 128, :],
                      inc="dx", waits=[("do", p.cnt["do"])] if c == 0 else ())
        dx2 = p.cnt["dx"]
        pre = emit_l1attn_preload(nc, p, B, Wa)
        ln = emit_ln(nc, p, B, x_ready_dx=dx2, svar=False)
        emit_l1attn(nc, p, B, Wa, ln, pre)
        phase_barrier(p)
    if 6 not in skip:
        bind("l1f")
        sv = emit_l1ffn_preload(nc, p, B, Wf)
        ln = emit_ln(nc, p, B, x_ready_dx=dx, v_ready=p.cnt["v"], svar=False)
        for c in range(NCH):
            p.dma("sync", xsp[c * 128:(c + 1) * 128, :], B.XR[:, c * T:(c + 1) * T],
                  inc="do")
        spill_do = p.cnt["do"]
        emit_l1ffn(nc, p, B, Wf, xsp, xout, ln, sv, hv_waits=[("do", spill_do)])
    else:
        store_x(nc, p, B, xout, p.cnt["v"])
    p.realize()
    return nc


def build_moer_program(debug=False):
    nc = bass.Bass("TRN2", target_bir_lowering=False)
    xin = nc.dram_tensor("xin", [D, T], F32, kind="ExternalInput").ap()
    W = moer_dram(nc)
    xout = nc.dram_tensor("xout", [D, T], F32, kind="ExternalOutput").ap()
    scr = {
        "xn_tm": nc.dram_tensor("s_xn_tm", [T, D], BF16, kind="Internal").ap(),
        "y_tm": nc.dram_tensor("s_y_tm", [T, D], BF16, kind="Internal").ap(),
        "perm2": nc.dram_tensor("s_perm2", [32 * 128, 1], I32, kind="Internal").ap(),
    }
    p = P(nc)
    B = make_common(nc, p)
    moer_buffers(nc, B)
    B.rdbg = {}
    if debug:
        for nm, sh, dt in (("srow", [1, 2048], F32), ("permsb", [128, 32], I32),
                           ("MT", [32, 2048], BF16), ("ranksb", [32, 2048], BF16),
                           ("xn_tm", [T, D], BF16), ("y_tm", [T, D], BF16),
                           ("xg", [128, 4096], BF16), ("hT", [128, 1024], BF16)):
            B.rdbg[nm] = nc.dram_tensor("dbg_" + nm, sh, dt, kind="ExternalOutput").ap()
    dx = load_x(nc, p, B, xin)
    small_dg = emit_moer_preload(nc, p, B, W)
    ln_done = emit_ln(nc, p, B, x_ready_dx=dx, svar=True)
    vend = emit_moe_routed(nc, p, B, W, ln_done, small_dg, scr)
    if debug:
        p.dma("sync", B.rdbg["srow"], B.srow, inc="do", waits=[("v", vend)])
        p.dma("sync", B.rdbg["permsb"], B.permsb[:], inc="do")
        p.dma("sync", B.rdbg["MT"], B.MT[:], inc="do")
        p.dma("sync", B.rdbg["ranksb"], B.ranksb, inc="do")
        p.dma("sync", B.rdbg["xn_tm"], scr["xn_tm"], inc="do")
        p.dma("sync", B.rdbg["y_tm"], scr["y_tm"], inc="do")
        p.dma("sync", B.rdbg["xg"], B.xg[:], inc="do")
        p.dma("sync", B.rdbg["hT"], B.hT[:], inc="do")
    store_x(nc, p, B, xout, vend)
    p.realize()
    return nc


def moer_dram(nc):
    return {
        "rw_t": nc.dram_tensor("rw_t", [1024, 32], F32, kind="ExternalInput").ap(),
        "rb": nc.dram_tensor("rb", [1, 32], F32, kind="ExternalInput").ap(),
        "w1": nc.dram_tensor("w1", [32, 1024, 256], F32, kind="ExternalInput").ap(),
        "b1c": nc.dram_tensor("b1c", [128, 64], F32, kind="ExternalInput").ap(),
        "w2": nc.dram_tensor("w2", [32, 256, 1024], F32, kind="ExternalInput").ap(),
        "b2t": nc.dram_tensor("b2t", [128, 256], F32, kind="ExternalInput").ap(),
    }


def moer_host_weights(inputs, prefix="L0_ffn"):
    b2 = np.asarray(inputs[prefix + "_bias2"])
    return {
        "rw_t": np.ascontiguousarray(np.asarray(inputs[prefix + "_router_w"]).T),
        "rb": np.ascontiguousarray(np.asarray(inputs[prefix + "_router_b"])[None, :]),
        "w1": np.ascontiguousarray(np.asarray(inputs[prefix + "_weights1"])),
        "b1c": np.ascontiguousarray(np.asarray(inputs[prefix + "_bias1"]).reshape(32, 2, 128).transpose(2, 0, 1).reshape(128, 64)),
        "w2": np.ascontiguousarray(np.asarray(inputs[prefix + "_weights2"])),
        "b2t": np.ascontiguousarray(b2.reshape(32, 8, 128).transpose(2, 0, 1).reshape(128, 256)),
    }


_PROGRAMS = {}


def _get_program(name):
    if name not in _PROGRAMS:
        _PROGRAMS[name] = {"moe": build_moe_program, "l1ffn": build_l1ffn_program, "l1attn": build_l1attn_program, "l0attn": build_l0attn_program, "full": build_full_program, "moer": build_moer_program}[name]()
    return _PROGRAMS[name]


def kernel(**inputs):
    x = np.asarray(inputs["L0_attn_input_seq_embs"])
    xt = [np.ascontiguousarray(x[i].T) for i in range(8)]
    wmap = {}
    wmap.update(moer_host_weights(inputs))
    wmap.update(l0attn_host_weights(inputs))
    wmap.update(l1ffn_host_weights(inputs))
    wmap.update(l1attn_host_weights(inputs))
    nc = _get_program("full")
    in_maps = [dict(wmap, xin=xt[i]) for i in range(8)]
    res = run_bass_kernel_spmd(nc, in_maps, core_ids=list(range(8)))
    return np.stack([np.ascontiguousarray(r["xout"].T) for r in res.results])


# version 12 (final)
# speedup vs baseline: 1.1150x; 1.0239x over previous
import numpy as np
import ml_dtypes
import concourse.bass as bass
import concourse.mybir as mybir
from concourse.bass_utils import run_bass_kernel_spmd

F32 = mybir.dt.float32
BF16 = mybir.dt.bfloat16
I32 = mybir.dt.int32
AF = mybir.ActivationFunctionType
OP = mybir.AluOpType
AX = mybir.AxisListType

D = 1024
T = 2048
NCH = 8
NTC = 4
LN_EPS = 1e-5

ENGS = ("tensor", "vector", "scalar", "gpsimd", "sync")


class P:

    def __init__(self, nc):
        self.nc = nc
        self.q = {e: [] for e in ENGS}
        self.sems = {}
        self.cnt = {}
        for name in ("v", "s", "g", "pe", "dx", "dg", "do"):
            self.sems[name] = nc.alloc_semaphore("sem_" + name)
            self.cnt[name] = 0
        self.bank_free = [[] for _ in range(8)]

    def wait(self, eng, sem, val):
        if val <= 0:
            return
        s = self.sems[sem]
        self.q[eng].append(lambda E: E.wait_ge(s, val))

    def op(self, eng, meth, *args, inc=None, incby=1, waits=(), **kw):
        for sem, val in waits:
            self.wait(eng, sem, val)
        n = None
        if inc is not None:
            self.cnt[inc] += incby
            n = self.cnt[inc]
            s = self.sems[inc]

            def fn(E, meth=meth, args=args, kw=kw, s=s, incby=incby):
                getattr(E, meth)(*args, **kw).then_inc(s, incby)
        else:
            def fn(E, meth=meth, args=args, kw=kw):
                getattr(E, meth)(*args, **kw)
        self.q[eng].append(fn)
        return n

    def vop(self, meth, *args, waits=(), **kw):
        w = [("v", self.cnt["v"])] + list(waits)
        return self.op("vector", meth, *args, inc="v", waits=w, **kw)

    def sop(self, meth, *args, waits=(), **kw):
        w = [("s", self.cnt["s"])] + list(waits)
        return self.op("scalar", meth, *args, inc="s", waits=w, **kw)

    def gop(self, meth, *args, waits=(), **kw):
        w = [("g", self.cnt["g"])] + list(waits)
        return self.op("gpsimd", meth, *args, inc="g", waits=w, **kw)

    def mm(self, out, lhsT, rhs, start, stop, waits=(), inc=None, **kw):
        return self.op("tensor", "matmul", out, lhsT, rhs, start=start, stop=stop,
                       waits=waits, inc=inc, **kw)

    def tr(self, out, in_, ident, waits=(), inc=None):
        return self.op("tensor", "transpose", out, in_, ident, waits=waits, inc=inc)

    def dma(self, queue, out, in_, inc, waits=(), **kw):
        return self.op(queue, "dma_start", out=out, in_=in_, inc=inc, incby=16,
                       waits=waits, **kw)

    def idma(self, queue, out, out_off, in_, in_off, inc, waits=()):
        return self.op(queue, "indirect_dma_start", out, out_off, in_, in_off,
                       inc=inc, incby=16, waits=waits)

    def realize(self):
        with self.nc.Block() as block:
            for eng in ENGS:
                fns = self.q[eng]
                if fns:
                    getattr(block, eng)(lambda E, fns=fns: [f(E) for f in fns])
        self.q = {e: [] for e in ENGS}


def phase_barrier(p):
    snap = [(nm, p.cnt[nm]) for nm in ("v", "s", "g", "pe", "dx", "dg", "do")]
    for eng in ENGS:
        for nm, val in snap:
            p.wait(eng, nm, val)


class Buffers:
    dbg = {}
    adbg = {}
    dbg_sg = -1


def make_common(nc, p):
    B = Buffers()
    SB = nc.alloc_sbuf_tensor
    B.XR = SB("XR", [128, NCH * T], F32)
    B.XN = SB("XN", [128, NCH * T], BF16)
    B.ones128 = SB("ones128", [128, 128], BF16)
    B.ones128f = SB("ones128f", [128, 128], F32)
    B.onesbig = SB("onesbig", [128, 2048], BF16)
    B.onesrow = SB("onesrow", [1, 512], BF16)
    B.ident = SB("ident", [128, 128], BF16)
    B.identf = SB("identf", [128, 128], F32)
    B.iotacol = SB("iotacol", [128, 1], F32)
    B.epscol = SB("epscol", [128, 1], F32)
    B.iotarow32 = SB("iotarow32", [128, 32], F32)
    B.xbfs = SB("xbfs", [128, 2 * 512], BF16)
    B.xsqs = SB("xsqs", [128, 2 * 512], BF16)
    B.meanbc = SB("meanbc", [128, 512], F32)
    B.msqbc = SB("msqbc", [128, 512], F32)
    B.rstdbc = SB("rstdbc", [128, 512], F32)
    B.ytmp = SB("ytmp", [128, 512], F32)
    B.pb8h = nc.alloc_psum_tensor("pb8", [128, 4096], F32)
    B.pb8 = B.pb8h[:]
    B.pb = [B.pb8h[:, i * 512:(i + 1) * 512] for i in range(8)]
    p.gop("memset", B.ones128[:], 1.0)
    p.gop("memset", B.ones128f[:], 1.0)
    p.gop("memset", B.onesbig[:], 1.0)
    p.gop("memset", B.onesrow[:], 1.0)
    p.gop("memset", B.ident[:], 0.0)
    p.gop("affine_select", B.ident[:], B.ones128[:], [[1, 128]], OP.is_equal, 0.0,
          base=0, channel_multiplier=-1)
    p.gop("affine_select", B.identf[:], B.ones128f[:], [[1, 128]], OP.is_equal, 0.0,
          base=0, channel_multiplier=-1)
    p.gop("iota", B.iotacol[:], [[1, 1]], base=0, channel_multiplier=1,
          allow_small_or_imprecise_dtypes=True)
    p.gop("memset", B.epscol[:], LN_EPS)
    p.gop("iota", B.iotarow32[:], [[1, 32]], base=0, channel_multiplier=0,
          allow_small_or_imprecise_dtypes=True)
    return B


def load_x(nc, p, B, xin):
    for c in range(NCH):
        p.dma("sync", B.XR[:, c * T:(c + 1) * T], xin[c * 128:(c + 1) * 128, :], inc="dx")
    return p.cnt["dx"]


def store_x(nc, p, B, xout, vwait):
    for c in range(NCH):
        p.dma("sync", xout[c * 128:(c + 1) * 128, :], B.XR[:, c * T:(c + 1) * T],
              inc="do", waits=[("v", vwait)] if c == 0 else ())
    p.wait("sync", "do", p.cnt["do"])


def emit_ln(nc, p, B, x_ready_dx=0, v_ready=0, svar=True):
    done = []
    for tc in range(NTC):
        ia, ib = (2 * tc) % 4, (2 * tc) % 4 + 1
        bA, bB = B.pb[ia], B.pb[ib]
        ms = {}
        for c in range(NCH):
            db = c % 2
            sl = slice(db * 512, (db + 1) * 512)
            xsl = B.XR[:, c * T + tc * 512: c * T + (tc + 1) * 512]
            w = list([("dx", x_ready_dx), ("v", v_ready)] if (tc == 0 and c == 0) else [])
            if c >= 2:
                w.append(("pe", ms[c - 2]))
            vcp = p.vop("tensor_copy", B.xbfs[:, sl], xsl, waits=w)
            scp = p.sop("activation", B.xsqs[:, sl], xsl, AF.Square, waits=w)
            w2_ = [("v", vcp), ("s", scp)]
            if c == 0:
                w2_ += p.bank_free[ia] + p.bank_free[ib]
            p.mm(bA[:, 0:512], B.ones128[:], B.xbfs[:, sl],
                 start=(c == 0), stop=(c == NCH - 1), waits=w2_)
            ms[c] = p.mm(bB[:, 0:512], B.ones128[:], B.xsqs[:, sl],
                         start=(c == 0), stop=(c == NCH - 1), inc="pe")
        pe_done = p.cnt["pe"]
        p.vop("tensor_scalar", B.meanbc[:], bA[:, 0:512], 1.0 / D, None, OP.mult,
              waits=[("pe", pe_done)])
        p.vop("tensor_scalar", B.msqbc[:], bB[:, 0:512], 1.0 / D, None, OP.mult)
        p.vop("tensor_tensor", B.rstdbc[:], B.meanbc[:], B.meanbc[:], OP.mult)
        vfree = p.vop("tensor_tensor", B.msqbc[:], B.msqbc[:], B.rstdbc[:], OP.subtract)
        p.bank_free[ia] = p.bank_free[ib] = [("v", vfree)]
        p.sop("activation", B.rstdbc[:], B.msqbc[:], AF.Ln, bias=B.epscol[:],
              waits=[("v", vfree)])
        if svar:
            p.sop("activation", B.svarbc[:, tc * 512:(tc + 1) * 512], B.rstdbc[:],
                  AF.Exp, scale=0.5)
        ssq = p.sop("activation", B.rstdbc[:], B.rstdbc[:], AF.Exp, scale=-0.5)
        p.wait("vector", "s", ssq)
        last = None
        for c in range(NCH):
            xsl = B.XR[:, c * T + tc * 512: c * T + (tc + 1) * 512]
            nsl = B.XN[:, c * T + tc * 512: c * T + (tc + 1) * 512]
            p.vop("tensor_tensor", B.ytmp[:], xsl, B.meanbc[:], OP.subtract)
            last = p.vop("tensor_tensor", nsl, B.ytmp[:], B.rstdbc[:], OP.mult)
        done.append(last)
    return done


def moe_buffers(nc, B):
    SB = nc.alloc_sbuf_tensor
    if not hasattr(B, "svarbc"):
        B.svarbc = SB("svarbc", [128, T], BF16)
    B.rwt = SB("rwt", [128, NCH * 32], F32)
    B.rwsc = SB("rwsc", [128, 32], F32)
    B.rbrow = SB("rbrow", [1, 32], BF16)
    B.b1c = SB("sb_b1c", [128, 64], F32)
    B.b2sb = SB("b2sb", [32, 1024], BF16)
    B.idxcols = SB("idxcols", [128, 16], BF16)
    B.idxbc = SB("idxbc", [128, 2048], BF16)
    B.MT = SB("MT", [32, 2048], BF16)
    B.eqs = SB("eqs", [128, 512], BF16)
    B.lmax = SB("lmax", [128, 1], F32)
    B.HT = SB("HT", [128, 8 * 2 * 1024], BF16)
    B.Z2 = SB("Z2", [128, 8 * 1024], F32)
    B.w1sb = SB("w1sb", [128, 2 * 2048], BF16)
    B.w2sb = SB("w2sb", [128, 2 * 2048], BF16)


def emit_moe_weight_preload(nc, p, B, W):
    for c in range(NCH):
        p.dma("gpsimd", B.rwt[:, c * 32:(c + 1) * 32],
              W["rw_t"][c * 128:(c + 1) * 128, :], inc="dg")
    p.dma("gpsimd", B.rbrow[:], W["rb"], inc="dg")
    p.dma("gpsimd", B.b2sb[:], W["b2"], inc="dg")
    p.dma("sync", B.b1c[:], W["b1c"], inc="dx")
    B.b1c_dx = p.cnt["dx"]
    dg = p.cnt["dg"]
    bk = 4
    for c in range(NCH):
        p.mm(B.pb[bk][:, 0:32], B.ones128f[:], B.rwt[:, c * 32:(c + 1) * 32],
             start=(c == 0), stop=(c == NCH - 1),
             waits=[("dg", dg), ("g", 2)] if c == 0 else (),
             inc="pe" if c == NCH - 1 else None)
    p.vop("tensor_scalar", B.rwsc[:], B.pb[bk][:, 0:32], 1.0 / D, None, OP.mult,
          waits=[("pe", p.cnt["pe"])])
    vlast = 0
    for c in range(NCH):
        vlast = p.vop("tensor_tensor", B.rwt[:, c * 32:(c + 1) * 32],
                      B.rwt[:, c * 32:(c + 1) * 32], B.rwsc[:], OP.subtract)
    p.bank_free[bk] = [("v", vlast)]
    B.rw_ready = vlast
    return p.cnt["dg"]


def emit_moe(nc, p, B, W, ln_done, small_dg):
    B.relu_war = {}
    for tt in range(16):
        bk = 5 + tt % 2
        lb = B.pb[bk]
        w = list(p.bank_free[bk]) + [("v", ln_done[tt // 4])]
        if tt == 0:
            w += [("v", B.rw_ready), ("dg", small_dg)]
        for c in range(NCH):
            p.mm(lb[:, 0:32],
                 B.XR[:, c * T + tt * 128: c * T + (tt + 1) * 128],
                 B.rwt[:, c * 32:(c + 1) * 32],
                 start=(c == 0), stop=False, waits=w if c == 0 else ())
        p.mm(lb[:, 0:32], B.svarbc[0:1, tt * 128:(tt + 1) * 128],
             B.rbrow[0:1, 0:32], start=False, stop=True, inc="pe")
        pe_l = p.cnt["pe"]
        p.vop("tensor_reduce", B.lmax[:], lb[:, 0:32], axis=AX.X,
              op=OP.max, waits=[("pe", pe_l)])
        p.vop("tensor_scalar", B.lsel[:], lb[:, 0:32], B.lmax[:], None, OP.is_equal)
        p.vop("scalar_tensor_tensor", B.lsel[:], B.lsel[:], -10000.0, B.iotarow32[:],
              OP.mult, OP.add)
        p.vop("tensor_reduce", B.lmax[:], B.lsel[:], axis=AX.X, op=OP.min)
        vr = p.vop("tensor_scalar", B.idxcols[:, tt:tt + 1], B.lmax[:], 10000.0,
                   None, OP.add)
        p.bank_free[bk] = [("v", vr)]
    v_idx = p.cnt["v"]
    p7h = B.pb[7][:, 0:512].bitcast(BF16)
    p6h = B.pb[6][:, 0:512].bitcast(BF16)
    p.wait("tensor", "v", v_idx)
    for wv in p.bank_free[7] + p.bank_free[6]:
        p.wait("tensor", wv[0], wv[1])
    for tt in range(16):
        dst = p7h if tt < 8 else p6h
        p.tr(dst[0:1, (tt % 8) * 128:(tt % 8 + 1) * 128],
             B.idxcols[:, tt:tt + 1], B.ident[:],
             inc="pe" if tt == 15 else None)
    for half in range(2):
        ph = p7h if half == 0 else p6h
        vir = p.vop("tensor_copy", B.xbfs[0:1, 0:1024], ph[0:1, 0:1024],
                    waits=[("pe", p.cnt["pe"])] if half == 0 else ())
        for q in range(2):
            tc = half * 2 + q
            p.mm(B.pb[7][:, 0:512], B.ones128[0:1, :],
                 B.xbfs[0:1, q * 512:(q + 1) * 512],
                 start=True, stop=True,
                 waits=[("v", vir)] + p.bank_free[7], inc="pe")
            vb = p.vop("tensor_copy", B.idxbc[:, tc * 512:(tc + 1) * 512],
                       B.pb[7][:, 0:512], waits=[("pe", p.cnt["pe"])])
            p.bank_free[7] = [("v", vb)]
    p.vop("tensor_scalar", B.MT[:], B.idxbc[0:32, :], B.iotacol[0:32, :], None,
          OP.is_equal)
    v_mt = p.cnt["v"]
    if B.dbg:
        p.dma("sync", B.dbg["XN"], B.XN[:], inc="do", waits=[("v", ln_done[-1])])
        p.dma("sync", B.dbg["idxbc"], B.idxbc[:], inc="do", waits=[("v", v_mt)])

    w1_rd = {}
    w2_rd = {}
    for th in range(2):
        t0 = th * 1024
        for eg in range(4):
            for ei in range(8):
                e = eg * 8 + ei
                db = ei % 2
                dgw = p.dma("gpsimd", B.w1sb[:, db * 2048:(db + 1) * 2048],
                            W["w1"][e].rearrange("(c p) j -> p c j", p=128),
                            inc="dg", waits=[("pe", w1_rd.get(db, 0))])
                for tq in range(2):
                    veq = p.vop("tensor_scalar", B.eqs[:, 0:512],
                                B.idxbc[:, t0 + tq * 512: t0 + (tq + 1) * 512],
                                float(e), None, OP.is_equal,
                                waits=[("v", v_mt)] if (th == 0 and eg == 0 and ei == 0 and tq == 0) else ())
                    for jb in range(2):
                        bk = jb * 2 + tq
                        bank = B.pb[bk]
                        w = [("dg", dgw)] + p.bank_free[bk]
                        mlast = None
                        for ic in range(NCH):
                            mlast = p.mm(bank[:, 0:512],
                                 B.w1sb[:, db * 2048 + ic * 256 + jb * 128:
                                        db * 2048 + ic * 256 + (jb + 1) * 128],
                                 B.XN[:, ic * T + t0 + tq * 512:
                                      ic * T + t0 + (tq + 1) * 512],
                                 start=(ic == 0), stop=(ic == NCH - 1),
                                 waits=w if ic == 0 else (),
                                 inc="pe" if ic == NCH - 1 else None)
                        w1_rd[db] = mlast
                        hsl = B.HT[:, (ei * 2 + jb) * 1024 + tq * 512:
                                   (ei * 2 + jb) * 1024 + (tq + 1) * 512]
                        tsl = (B.xbfs if bk < 2 else B.xsqs)[:, (bk % 2) * 512:
                                                             (bk % 2) * 512 + 512]
                        srl = p.sop("activation", tsl, bank[:, 0:512], AF.Relu,
                                    bias=B.b1c[:, e * 2 + jb: e * 2 + jb + 1],
                                    waits=[("pe", mlast), ("dx", B.b1c_dx),
                                           ("v", B.relu_war.get(bk, 0))])
                        vh = p.vop("tensor_tensor", hsl, tsl,
                                   B.eqs[:, 0:512], OP.mult,
                                   waits=[("s", srl), ("v", veq)])
                        B.relu_war[bk] = vh
                        p.bank_free[bk] = [("v", vh)]
            v_ht = p.cnt["v"]
            if B.dbg and th == 0 and eg == 0:
                p.dma("sync", B.dbg["HT"], B.HT[:], inc="do", waits=[("v", v_ht)])
            for oh in range(2):
                for ei in range(8):
                    e = eg * 8 + ei
                    pr = (ei // 2) % 2
                    db = pr * 2 + ei % 2
                    if ei % 2 == 0:
                        dgw = p.dma("gpsimd", B.w2sb[:, pr * 2048:(pr + 1) * 2048],
                                    W["w2"][e:e + 2, :, oh * 512:(oh + 1) * 512]
                                    .rearrange("e (c p) o -> p e c o", p=128),
                                    inc="dg", waits=[("pe", w2_rd.get(pr, 0))])
                    for obi in range(4):
                        for tq in range(2):
                            bk = obi * 2 + tq
                            bank = B.pb[bk]
                            ob = oh * 4 + obi
                            if ei == 0:
                                w = [("dg", dgw), ("v", v_ht)] + p.bank_free[bk]
                                if eg == 0:
                                    p.mm(bank[:, 0:512],
                                         B.b2sb[:, ob * 128:(ob + 1) * 128],
                                         B.MT[:, t0 + tq * 512: t0 + (tq + 1) * 512],
                                         start=True, stop=False, waits=w)
                                    w = []
                            else:
                                w = [("dg", dgw)]
                            for jb in range(2):
                                start = (ei == 0 and jb == 0 and eg != 0)
                                islast = (ei == 7 and jb == 1)
                                expert_last = (obi == 3 and tq == 1 and jb == 1
                                               and ei % 2 == 1)
                                mn = p.mm(bank[:, 0:512],
                                     B.w2sb[:, db * 1024 + jb * 512 + obi * 128:
                                            db * 1024 + jb * 512 + (obi + 1) * 128],
                                     B.HT[:, (ei * 2 + jb) * 1024 + tq * 512:
                                          (ei * 2 + jb) * 1024 + (tq + 1) * 512],
                                     start=start, stop=islast,
                                     waits=w if jb == 0 else (),
                                     inc="pe" if (islast or expert_last) else None)
                                if expert_last:
                                    w2_rd[pr] = mn
                                w = []
                pe_grp = p.cnt["pe"]
                for obi in range(4):
                    for tq in range(2):
                        bk = obi * 2 + tq
                        bank = B.pb[bk]
                        ob = oh * 4 + obi
                        zsl = B.Z2[:, ob * 1024 + tq * 512: ob * 1024 + (tq + 1) * 512]
                        if eg == 0:
                            vz = p.vop("tensor_copy", zsl, bank[:, 0:512],
                                       waits=[("pe", pe_grp)])
                        else:
                            vz = p.vop("tensor_tensor", zsl, zsl, bank[:, 0:512],
                                       OP.add, waits=[("pe", pe_grp)])
                        p.bank_free[bk] = [("v", vz)]
        if B.dbg and th == 0:
            p.dma("sync", B.dbg["Z2"], B.Z2[:], inc="do", waits=[("v", p.cnt["v"])])
        for ob in range(8):
            for tq in range(2):
                zsl = B.Z2[:, ob * 1024 + tq * 512: ob * 1024 + (tq + 1) * 512]
                xsl = B.XR[:, ob * T + t0 + tq * 512: ob * T + t0 + (tq + 1) * 512]
                p.vop("tensor_scalar", B.ytmp[:], zsl, 0.0, 0.5, OP.max, OP.mult)
                p.vop("tensor_tensor", xsl, xsl, B.ytmp[:], OP.add)
    return p.cnt["v"]


def moer_buffers(nc, B):
    SB = nc.alloc_sbuf_tensor
    if not hasattr(B, "svarbc"):
        B.svarbc = SB("svarbc", [128, T], BF16)
    B.rwt = SB("rwt", [128, NCH * 32], F32)
    B.rwsc = SB("rwsc", [128, 32], F32)
    B.rbrow = SB("rbrow", [1, 32], BF16)
    B.b1c = SB("sb_b1c", [128, 64], F32)
    B.b2row = SB("sb_b2row", [1, 2048], F32)
    B.idxcols = SB("idxcols", [128, 16], F32)
    B.MT = SB("MT", [32, 2048], BF16)
    B.lmax = SB("lmax", [128, 1], F32)
    B.iote = SB("iote", [32, 1], BF16)
    B.Ltile = SB("Ltile", [128, 2048], BF16)
    B.ranksb = B.Ltile[0:32, 0:2048]
    B.w1f32 = SB("w1f32", [128, 2048], F32)
    B.w2f32 = SB("w2f32s", [128, 2048], F32)
    B.srow = B.w2f32[0:1, 0:2048]
    B.Mtm = B.w2f32[:].bitcast(BF16)[:, 2048:2560]
    B.lsel = B.rwsc
    B.scoli = SB("scoli", [128, 16], I32)
    B.tokids = SB("tokids", [128, 16], I32)
    B.xg = SB("xg", [128, 4 * 1024], BF16)
    B.xgT = SB("xgT", [128, 4 * 1024], BF16)
    B.hT = SB("hT", [128, 4 * 256], BF16)
    B.permsb = SB("permsb", [128, 32], I32)
    B.dgfence = SB("dgfence", [16, 1], I32)
    B.w1sb = SB("w1sb", [128, 4 * 2048], BF16)
    B.w2sb = SB("w2sb", [128, 4 * 2048], BF16)
    B.ytm = B.xg
    B.xtms = SB("xtms", [128, 2 * 1024], BF16)


def emit_moer_preload(nc, p, B, W):
    for c in range(NCH):
        p.dma("gpsimd", B.rwt[:, c * 32:(c + 1) * 32],
              W["rw_t"][c * 128:(c + 1) * 128, :], inc="dg")
    p.dma("gpsimd", B.rbrow[:], W["rb"], inc="dg")
    p.dma("sync", B.b1c[:], W["b1c"], inc="dx")
    B.b1c_dx = p.cnt["dx"]
    dg = p.cnt["dg"]
    p.vop("tensor_copy", B.iote[:], B.iotacol[0:32, :], waits=[("g", p.cnt["g"])])
    bk = 4
    for c in range(NCH):
        p.mm(B.pb[bk][:, 0:32], B.ones128f[:], B.rwt[:, c * 32:(c + 1) * 32],
             start=(c == 0), stop=(c == NCH - 1),
             waits=([("dg", dg), ("g", 2)] + p.bank_free[bk]) if c == 0 else (),
             inc="pe" if c == NCH - 1 else None)
    p.vop("tensor_scalar", B.rwsc[:], B.pb[bk][:, 0:32], 1.0 / D, None, OP.mult,
          waits=[("pe", p.cnt["pe"])])
    vlast = 0
    for c in range(NCH):
        vlast = p.vop("tensor_tensor", B.rwt[:, c * 32:(c + 1) * 32],
                      B.rwt[:, c * 32:(c + 1) * 32], B.rwsc[:], OP.subtract)
    p.bank_free[bk] = [("v", vlast)]
    B.rw_ready = vlast
    return p.cnt["dg"]


def emit_moe_routed(nc, p, B, W, ln_done, small_dg, scr):
    import dataclasses as _dc
    p2h = B.pb8h[:].bitcast(BF16)

    for tt in range(16):
        bk = 5 + tt % 2
        lb = B.pb[bk]
        w = list(p.bank_free[bk]) + [("v", ln_done[tt // 4])]
        if tt == 0:
            w += [("v", B.rw_ready), ("dg", small_dg), ("dx", B.b1c_dx)]
        for c in range(NCH):
            p.mm(lb[:, 0:32],
                 B.XR[:, c * T + tt * 128: c * T + (tt + 1) * 128],
                 B.rwt[:, c * 32:(c + 1) * 32],
                 start=(c == 0), stop=False, waits=w if c == 0 else ())
        p.mm(lb[:, 0:32], B.svarbc[0:1, tt * 128:(tt + 1) * 128],
             B.rbrow[0:1, 0:32], start=False, stop=True, inc="pe")
        pe_l = p.cnt["pe"]
        p.vop("tensor_reduce", B.lmax[:], lb[:, 0:32], axis=AX.X,
              op=OP.max, waits=[("pe", pe_l)])
        p.vop("tensor_scalar", B.lsel[:], lb[:, 0:32], B.lmax[:], None, OP.is_equal)
        p.vop("scalar_tensor_tensor", B.lsel[:], B.lsel[:], -10000.0, B.iotarow32[:],
              OP.mult, OP.add)
        p.vop("tensor_reduce", B.lmax[:], B.lsel[:], axis=AX.X, op=OP.min)
        p.vop("tensor_scalar", B.idxcols[:, tt:tt + 1], B.lmax[:], 10000.0,
              None, OP.add)
        vr = p.vop("tensor_scalar", B.Mtm[:, tt * 32:(tt + 1) * 32], B.iotarow32[:],
                   B.idxcols[:, tt:tt + 1], None, OP.is_equal)
        p.bank_free[bk] = [("v", vr)]
    v_idx = p.cnt["v"]

    st_dma = {}
    trp = {}
    for tile in range(16):
        par = tile % 2
        pbase = (4 + par) * 1024
        pw = []
        if tile >= 2:
            pw += p.bank_free[4 + par]
        for c in range(NCH):
            trp[par] = p.tr(p2h[:, pbase + c * 128: pbase + (c + 1) * 128],
                            B.XN[:, c * T + tile * 128: c * T + tile * 128 + 128],
                            B.ident[:], waits=pw if c == 0 else (),
                            inc="pe" if c == NCH - 1 else None)
        vw = [("pe", trp[par])]
        if tile >= 2:
            vw.append(("do", st_dma[tile - 2]))
        vv = p.vop("tensor_copy", B.xtms[:, par * 1024:(par + 1) * 1024],
                   p2h[:, pbase:pbase + 1024], waits=vw)
        p.bank_free[4 + par] = [("v", vv)]
        st_dma[tile] = p.dma("sync", scr["xn_tm"][tile * 128:(tile + 1) * 128, :],
                             B.xtms[:, par * 1024:(par + 1) * 1024], inc="do",
                             waits=[("v", vv)])
    p.dma("sync", B.gfsink[:].bitcast(F32)[:, 0:8], W["b1c"][:, 0:8], inc="do")
    xn_tm_done = p.cnt["do"]

    p.wait("tensor", "v", v_idx)
    for w_ in p.bank_free[0] + p.bank_free[1]:
        p.wait("tensor", w_[0], w_[1])
    for tile in range(16):
        p.tr(p2h[0:32, tile * 128:(tile + 1) * 128],
             B.Mtm[:, tile * 32:(tile + 1) * 32], B.ident[:],
             inc="pe" if tile == 15 else None)
    vmt = p.vop("tensor_copy", B.MT[:], p2h[0:32, 0:2048],
                waits=[("pe", p.cnt["pe"])])
    p.bank_free[0] = p.bank_free[1] = [("v", vmt)]

    g_l = {}
    mm_l = {}
    for c in range(16):
        g_l[c] = p.gop("affine_select", B.Ltile[:],
                       B.onesbig[:], [[1, 2048]], OP.is_gt, 0.0,
                       base=-(c * 128), channel_multiplier=-1,
                       waits=[("pe", mm_l[c - 1])] if c >= 1 else ())
        w = [("g", g_l[c])]
        if c == 0:
            w += p.bank_free[0] + p.bank_free[1] + p.bank_free[2] + p.bank_free[3]
        for n in range(4):
            mm_l[c] = p.mm(B.pb8[0:32, n * 512:(n + 1) * 512],
                           B.Mtm[:, c * 32:(c + 1) * 32],
                           B.Ltile[:, n * 512:(n + 1) * 512],
                           start=(c == 0), stop=(c == 15),
                           waits=w if n == 0 else (), inc="pe")
    pe_rank = p.cnt["pe"]
    p.vop("tensor_scalar", B.ranksb, B.pb8[0:32, 0:2048], 127.0, 32.0,
          OP.min, OP.mult, waits=[("pe", pe_rank)])
    v_mt = p.vop("tensor_tensor", B.ranksb, B.ranksb, B.MT[:], OP.mult)
    for i in range(4):
        p.bank_free[i] = [("v", v_mt)]
    p.wait("tensor", "v", v_mt)
    for n in range(4):
        p.mm(B.pb8[0:1, n * 512:(n + 1) * 512], B.ones128[0:32, 0:1],
             B.ranksb[:, n * 512:(n + 1) * 512], start=True, stop=False)
        p.mm(B.pb8[0:1, n * 512:(n + 1) * 512], B.iote[:],
             B.MT[:, n * 512:(n + 1) * 512], start=False, stop=True,
             inc="pe" if n == 3 else None)
    v_srow = p.vop("tensor_copy", B.srow, B.pb8[0:1, 0:2048],
                   waits=[("pe", p.cnt["pe"])])
    for i in range(4):
        p.bank_free[i] = [("v", v_srow)]
    p.wait("tensor", "v", v_srow)
    for tile in range(16):
        p.tr(bass.AP(B.pb8h, tile, [[4096, 128], [1, 1]]),
             B.srow[0:1, tile * 128:(tile + 1) * 128], B.identf[0:1, 0:1],
             inc="pe" if tile == 15 else None)
    B.pe_scoli = p.cnt["pe"]
    p.vop("tensor_copy", B.scoli[:],
          bass.AP(B.pb8h, 0, [[4096, 128], [1, 16]]),
          waits=[("pe", p.cnt["pe"])])
    vsc = p.cnt["v"]
    p.bank_free[0] = [("v", vsc)]
    g0 = p.gop("memset", B.tokids[:], 1 << 20)
    gi = None
    for hf in range(2):
        gi = p.dma("gpsimd",
                   scr["perm2"].rearrange("(r e) one -> r (e one)", r=128)[:, hf * 16:(hf + 1) * 16],
                   B.tokids[:], inc="dg", waits=[("g", g0)] if hf == 0 else ())
    g1 = p.gop("iota", B.tokids[:], [[128, 16]], base=0, channel_multiplier=1,
               allow_small_or_imprecise_dtypes=True, waits=[("dg", gi)])
    gsc = None
    for tile in range(16):
        gsc = p.idma("gpsimd", scr["perm2"],
                     bass.IndirectOffsetOnAxis(ap=B.scoli[:, tile:tile + 1], axis=0),
                     B.tokids[:, tile:tile + 1], None, inc="dg",
                     waits=[("g", g1), ("v", vsc)] if tile == 0 else ())
    gsc = p.idma("gpsimd", B.tokids[:, 0:1], None, scr["perm2"],
                 bass.IndirectOffsetOnAxis(ap=B.scoli[:, 0:1], axis=0), inc="dg")
    gld = p.dma("sync", B.permsb[:],
                scr["perm2"].rearrange("(r e) one -> r (e one)", r=128),
                inc="dx", waits=[("dg", gsc)])

    w1_rd = {}
    w2_rd = {}
    xg_rd = {}
    b2_rd = {}
    cast1 = {}
    cast2 = {}
    b2ld = {}
    chain = {}
    gath = {}
    scat = {}

    def wload(b):
        if b >= 8:
            return
        items = []
        for e in range(b * 4, b * 4 + 4):
            items += [("w1", e), ("w2", e), ("b2", e)]

        def emit_dma(kind, e):
            par = e % 4
            sl = e % 2
            if kind == "w1":
                return p.dma("sync", B.w1f32[:],
                             W["w1"][e].rearrange("(c p) j -> p c j", p=128),
                             inc="dx", waits=[("v", chain.get("w1", 0))])
            if kind == "w2":
                return p.dma("sync", B.w2f32[:],
                             W["w2"][e].rearrange("(c p) o -> p c o", p=128),
                             inc="dx", waits=[("v", chain.get("w2", 0))] +
                             ([("pe", B.pe_scoli)] if e == 0 else []))
            return p.dma("sync", B.b2row[0:1, sl * 1024:(sl + 1) * 1024],
                         W["b2"][e:e + 1, :], inc="dx",
                         waits=[("pe", b2_rd.get(sl, 0))])

        def emit_cast(kind, e, guard):
            par = e % 4
            if kind == "w1":
                cast1[e] = chain["w1"] = p.vop(
                    "tensor_copy", B.w1sb[:, par * 2048:(par + 1) * 2048],
                    B.w1f32[:], waits=[("dx", guard), ("pe", w1_rd.get(par, 0))])
            elif kind == "w2":
                cast2[e] = chain["w2"] = p.vop(
                    "tensor_copy", B.w2sb[:, par * 2048:(par + 1) * 2048],
                    B.w2f32[:], waits=[("dx", guard), ("pe", w2_rd.get(par, 0))])
            else:
                b2ld[e] = guard

        pend = []
        for it in items:
            g = emit_dma(*it)
            if it[0] != "b2":
                for pk, pe_ in pend:
                    emit_cast(pk, pe_, g)
                pend = []
            pend.append(it)
        g = p.dma("sync", B.gfsink[:].bitcast(F32)[:, 0:8], W["b1c"][:, 0:8],
                  inc="dx")
        for pk, pe_ in pend:
            emit_cast(pk, pe_, g)

    wload(0)
    for g4 in range(8):
        es = [g4 * 4 + i for i in range(4)]
        for e in es:
            par = e % 4
            gw = [("dx", gld), ("do", xn_tm_done)] if e == 0 else []
            gath[e] = p.idma("gpsimd", B.xg[:, par * 1024:(par + 1) * 1024], None,
                             scr["xn_tm"],
                             bass.IndirectOffsetOnAxis(ap=B.permsb[:, e:e + 1], axis=0),
                             inc="dg", waits=gw + [("pe", xg_rd.get(par, 0)),
                                                  ("dg", scat.get(par, 0))])
        gf = {}
        for k in range(2):
            gf[k] = p.idma("gpsimd", B.gfsink[:], None, scr["xn_tm"][:, 0:16],
                           bass.IndirectOffsetOnAxis(ap=B.permsb[:, 0:1], axis=0),
                           inc="dg")
        gfence = gf[1]
        guard = {es[0]: gath[es[2]], es[1]: gath[es[3]], es[2]: gf[0], es[3]: gf[1]}
        vxgt = {}
        for e in es:
            par = e % 4
            bko = 0 if par % 2 == 0 else 4
            for c in range(NCH):
                p.tr(p2h[:, bko * 1024 + c * 128: bko * 1024 + (c + 1) * 128],
                     B.xg[:, par * 1024 + c * 128: par * 1024 + (c + 1) * 128],
                     B.ident[:],
                     waits=([("dg", guard[e])] + p.bank_free[bko])
                           if c == 0 else (),
                     inc="pe" if c == NCH - 1 else None)
            xg_rd[par] = p.cnt["pe"]
            vxgt[e] = p.vop("tensor_copy", B.xgT[:, par * 1024:(par + 1) * 1024],
                            p2h[:, bko * 1024:(bko + 1) * 1024],
                            waits=[("pe", p.cnt["pe"])])
            if scr.get("dbgx") is not None:
                p.dma("sync", scr["dbgx"][e], B.xgT[:, par * 1024:(par + 1) * 1024],
                      inc="do", waits=[("v", vxgt[e])])
            p.bank_free[bko] = [("v", vxgt[e])]
        sh = {}
        for e in es:
            par = e % 4
            mbk = 1 if par % 2 == 0 else 5
            p.wait("tensor", "v", vxgt[e])
            for jb in range(2):
                for ic in range(NCH):
                    p.mm(B.pb8[:, mbk * 512 + jb * 128: mbk * 512 + (jb + 1) * 128],
                         B.w1sb[:, par * 2048 + ic * 256 + jb * 128:
                                par * 2048 + ic * 256 + (jb + 1) * 128],
                         B.xgT[:, par * 1024 + ic * 128: par * 1024 + (ic + 1) * 128],
                         start=(ic == 0), stop=(ic == NCH - 1),
                         waits=([("v", cast1[e])] + p.bank_free[mbk])
                               if (jb == 0 and ic == 0) else (),
                         inc="pe" if ic == NCH - 1 else None)
            w1_rd[par] = p.cnt["pe"]
            s_ = None
            for jb in range(2):
                s_ = p.sop("activation",
                           B.hT[:, par * 256 + jb * 128: par * 256 + (jb + 1) * 128],
                           B.pb8[:, mbk * 512 + jb * 128: mbk * 512 + (jb + 1) * 128],
                           AF.Relu, bias=B.b1c[:, e * 2 + jb: e * 2 + jb + 1],
                           waits=[("pe", w1_rd[par])])
            sh[e] = s_
            p.bank_free[mbk] = [("s", s_)]
        sy = {}
        for e in es:
            par = e % 4
            sl = e % 2
            bko = 0 if par % 2 == 0 else 4
            p.wait("tensor", "s", sh[e])
            for nb in range(2):
                obk = bko + 2 + nb
                for jb in range(2):
                    p.mm(B.pb8[:, obk * 512:(obk + 1) * 512],
                         B.hT[:, par * 256 + jb * 128: par * 256 + (jb + 1) * 128],
                         B.w2sb[:, par * 2048 + jb * 1024 + nb * 512:
                                par * 2048 + jb * 1024 + (nb + 1) * 512],
                         start=(jb == 0), stop=(jb == 1),
                         waits=([("v", cast2[e])] + p.bank_free[obk] +
                                ([("dx", b2ld[e])] if nb == 0 else []))
                               if jb == 0 else (),
                         inc="pe" if jb == 1 else None)
            w2_rd[par] = p.cnt["pe"]
            b2_rd[sl] = p.cnt["pe"]
            s_ = None
            for nb in range(2):
                obk = bko + 2 + nb
                s_ = p.vop("tensor_scalar",
                           B.ytm[:, par * 1024 + nb * 512: par * 1024 + (nb + 1) * 512],
                           B.pb8[:, obk * 512:(obk + 1) * 512], 0.0, None, OP.max,
                           waits=[("pe", w2_rd[par])] +
                                 ([("dg", scat.get(par, 0))] if nb == 0 else []))
            sy[e] = s_
            p.bank_free[bko + 2] = [("v", sy[e])]
            p.bank_free[bko + 3] = [("v", sy[e])]
        vdrain = p.vop("tensor_copy", B.gfsink[:, 0:16], B.gfsink[:, 0:16])
        wload(g4 + 1)
        for e in es:
            par = e % 4
            scat[par] = p.idma("gpsimd", scr["y_tm"],
                               bass.IndirectOffsetOnAxis(ap=B.permsb[:, e:e + 1], axis=0),
                               B.ytm[:, par * 1024:(par + 1) * 1024], None,
                               inc="dg", waits=[("v", vdrain)])
    p.idma("gpsimd", scr["perm2"],
           bass.IndirectOffsetOnAxis(ap=B.scoli[:, 0:1], axis=0),
           B.tokids[:, 0:1], None, inc="dg")
    p.idma("gpsimd", B.gfsink[:], None, scr["y_tm"][:, 0:16],
           bass.IndirectOffsetOnAxis(ap=B.permsb[:, 0:1], axis=0), inc="dg")
    y_done = p.cnt["dg"]

    ld = {}
    tr_rd = {}
    for tile in range(16):
        par = tile % 2
        bko = 0 if par == 0 else 4
        ld[tile] = p.dma("sync", B.xg[:, par * 1024:(par + 1) * 1024],
                         scr["y_tm"][tile * 128:(tile + 1) * 128, :], inc="dx",
                         waits=([("dg", y_done)] if tile == 0 else []) +
                               ([("pe", tr_rd[par])] if tile >= 2 else []))
        for c in range(NCH):
            tr_rd[par] = p.tr(
                p2h[:, bko * 1024 + c * 128: bko * 1024 + (c + 1) * 128],
                B.xg[:, par * 1024 + c * 128: par * 1024 + (c + 1) * 128],
                B.ident[:],
                waits=([("dx", ld[tile])] + p.bank_free[bko]) if c == 0 else (),
                inc="pe")
        vz = p.vop("scalar_tensor_tensor",
                   bass.AP(B.XR, tile * 128, [[16384, 128], [2048, 8], [1, 128]]),
                   _dc.replace(p2h, offset=bko * 1024, ap=[[8192, 128], [128, 8], [1, 128]]),
                   0.5,
                   bass.AP(B.XR, tile * 128, [[16384, 128], [2048, 8], [1, 128]]),
                   OP.mult, OP.add, waits=[("pe", tr_rd[par])])
        p.bank_free[bko] = [("v", vz)]
    return p.cnt["v"]


def moe_dram(nc):
    return {
        "rw_t": nc.dram_tensor("rw_t", [1024, 32], F32, kind="ExternalInput").ap(),
        "rb": nc.dram_tensor("rb", [1, 32], F32, kind="ExternalInput").ap(),
        "w1": nc.dram_tensor("w1", [32, 1024, 256], F32, kind="ExternalInput").ap(),
        "b1c": nc.dram_tensor("b1c", [128, 64], F32, kind="ExternalInput").ap(),
        "w2": nc.dram_tensor("w2", [32, 256, 1024], F32, kind="ExternalInput").ap(),
        "b2": nc.dram_tensor("b2", [32, 1024], F32, kind="ExternalInput").ap(),
    }


def build_moe_program(debug=False):
    nc = bass.Bass("TRN2", target_bir_lowering=False)
    xin = nc.dram_tensor("xin", [D, T], F32, kind="ExternalInput").ap()
    W = moe_dram(nc)
    xout = nc.dram_tensor("xout", [D, T], F32, kind="ExternalOutput").ap()
    p = P(nc)
    B = make_common(nc, p)
    moe_buffers(nc, B)
    dbg = {}
    if debug:
        dbg["XN"] = nc.dram_tensor("dbg_XN", [128, NCH * T], BF16, kind="ExternalOutput").ap()
        dbg["idxbc"] = nc.dram_tensor("dbg_idxbc", [128, T], BF16, kind="ExternalOutput").ap()
        dbg["HT"] = nc.dram_tensor("dbg_HT", [128, 8 * 2 * 1024], BF16, kind="ExternalOutput").ap()
        dbg["Z2"] = nc.dram_tensor("dbg_Z2", [128, 8 * 1024], F32, kind="ExternalOutput").ap()
    B.dbg = dbg
    dx = load_x(nc, p, B, xin)
    small_dg = emit_moe_weight_preload(nc, p, B, W)
    ln_done = emit_ln(nc, p, B, dx)
    vend = emit_moe(nc, p, B, W, ln_done, small_dg)
    store_x(nc, p, B, xout, vend)
    p.realize()
    return nc


def moe_host_weights(inputs, prefix="L0_ffn"):
    return {
        "rw_t": np.ascontiguousarray(np.asarray(inputs[prefix + "_router_w"]).T),
        "rb": np.ascontiguousarray(np.asarray(inputs[prefix + "_router_b"])[None, :]),
        "w1": np.ascontiguousarray(np.asarray(inputs[prefix + "_weights1"])),
        "b1c": np.ascontiguousarray(np.asarray(inputs[prefix + "_bias1"]).reshape(32, 2, 128).transpose(2, 0, 1).reshape(128, 64)),
        "w2": np.ascontiguousarray(np.asarray(inputs[prefix + "_weights2"])),
        "b2": np.ascontiguousarray(np.asarray(inputs[prefix + "_bias2"])),
    }


def l1ffn_buffers(nc, B):
    SB = nc.alloc_sbuf_tensor
    B.GH = SB("GH", [128, 32768], BF16)
    B.Hv = B.XR[:].bitcast(BF16)
    B.wA = SB("wA", [128, 2 * 4096], BF16)
    B.wB = SB("wB", [128, 2 * 1024], BF16)
    B.wC = SB("wC", [128, 2 * 1024], BF16)
    B.s1c = SB("s1c", [128, 32], F32)
    B.s2c = SB("s2c", [128, 32], F32)
    B.sh1c = SB("sh1c", [128, 32], F32)
    B.sh2c = SB("sh2c", [128, 32], F32)
    B.s3c = SB("s3c", [128, 8], F32)
    B.s3h = SB("s3h", [128, 8], F32)



def emit_l1ffn_preload(nc, p, B, W):
    for nm in ("s1c", "s2c", "sh1c", "sh2c", "s3c"):
        p.dma("sync", getattr(B, nm)[:], W[nm], inc="dx")
    dxs = p.cnt["dx"]
    p.vop("tensor_scalar", B.s3h[:], B.s3c[:], 0.5, None, OP.mult,
          waits=[("dx", dxs)])
    return p.cnt["v"]


def emit_l1ffn(nc, p, B, W, xin, xout, ln_done, scales_v, hv_waits=()):
    wArd = {}
    wBrd = {}
    wCrd = {}
    for tch in range(2):
        t0 = tch * 1024
        for ob in range(32):
            db = ob % 2
            dg1 = p.dma("gpsimd", B.wB[:, db * 1024:(db + 1) * 1024],
                        W["w1"][ob], inc="dg", waits=[("pe", wBrd.get(db, 0))])
            dg2 = p.dma("gpsimd", B.wC[:, db * 1024:(db + 1) * 1024],
                        W["w2"][ob], inc="dg", waits=[("pe", wCrd.get(db, 0))])
            bk1 = [4 * db, 4 * db + 1]
            bk2 = [4 * db + 2, 4 * db + 3]
            for c in range(8):
                for n in range(2):
                    w = []
                    if c == 0:
                        w = [("dg", dg1), ("v", ln_done[tch * 2 + n])] + p.bank_free[bk1[n]]
                        if tch == 0 and ob == 0 and n == 0:
                            w.append(("v", scales_v))
                    p.mm(B.pb[bk1[n]][:, 0:512],
                         B.wB[:, db * 1024 + c * 128:(db * 1024) + (c + 1) * 128],
                         B.XN[:, c * T + t0 + n * 512: c * T + t0 + (n + 1) * 512],
                         start=(c == 0), stop=(c == 7), waits=w,
                         inc="pe" if c == 7 else None)
            wBrd[db] = p.cnt["pe"]
            pe1 = [p.cnt["pe"] - 1, p.cnt["pe"]]
            for c in range(8):
                for n in range(2):
                    w = []
                    if c == 0:
                        w = [("dg", dg2)] + p.bank_free[bk2[n]]
                    p.mm(B.pb[bk2[n]][:, 0:512],
                         B.wC[:, db * 1024 + c * 128:(db * 1024) + (c + 1) * 128],
                         B.XN[:, c * T + t0 + n * 512: c * T + t0 + (n + 1) * 512],
                         start=(c == 0), stop=(c == 7), waits=w,
                         inc="pe" if c == 7 else None)
            wCrd[db] = p.cnt["pe"]
            pe2 = [p.cnt["pe"] - 1, p.cnt["pe"]]
            silut = [B.meanbc, B.msqbc]
            ssl = []
            for n in range(2):
                sw = [("pe", pe1[n]), ("v", p.cnt["v"])]
                if tch == 0 and ob == 0:
                    sw.append(("v", scales_v))
                ssl.append(p.sop("activation", silut[n][:, 0:512],
                                 B.pb[bk1[n]][:, 0:512], AF.Silu,
                                 scale=B.s1c[:, ob:ob + 1], waits=sw))
            vh = None
            for n in range(2):
                hw = [("pe", pe2[n]), ("s", ssl[n])]
                if tch == 0 and ob == 0 and n == 0:
                    hw += list(hv_waits)
                vh = p.vop("scalar_tensor_tensor",
                           B.Hv[:, ob * 1024 + n * 512: ob * 1024 + (n + 1) * 512],
                           B.pb[bk2[n]][:, 0:512], B.s2c[:, ob:ob + 1],
                           silut[n][:, 0:512], OP.mult, OP.mult,
                           waits=hw)
                p.bank_free[bk2[n]] = [("v", vh)]
            p.bank_free[bk1[0]] = [("v", vh)]
            p.bank_free[bk1[1]] = [("v", vh)]
        v_h = p.cnt["v"]
        for chain, (wkey, scol, src, dst) in enumerate((
                ("h1", B.sh1c, B.Hv, B.GH), ("h2", B.sh2c, B.GH, B.Hv))):
            s_rel = None
            for ob in range(32):
                db = ob % 2
                dga = p.dma("gpsimd", B.wA[:, db * 4096:(db + 1) * 4096],
                            W[wkey][ob], inc="dg", waits=[("pe", wArd.get(db, 0))])
                bks = [4 * db, 4 * db + 1]
                for c in range(32):
                    for n in range(2):
                        w = []
                        if c == 0:
                            w = [("dg", dga)] + p.bank_free[bks[n]]
                            if ob == 0 and n == 0:
                                if chain == 0:
                                    w.append(("v", v_h))
                                else:
                                    w.append(("s", s_gel))
                        p.mm(B.pb[bks[n]][:, 0:512],
                             B.wA[:, db * 4096 + c * 128:(db * 4096) + (c + 1) * 128],
                             src[:, c * 1024 + n * 512: c * 1024 + (n + 1) * 512],
                             start=(c == 0), stop=(c == 31), waits=w,
                             inc="pe" if c == 31 else None)
                wArd[db] = p.cnt["pe"]
                pes = [p.cnt["pe"] - 1, p.cnt["pe"]]
                for n in range(2):
                    s_rel = p.sop("activation",
                                  dst[:, ob * 1024 + n * 512: ob * 1024 + (n + 1) * 512],
                                  B.pb[bks[n]][:, 0:512], AF.Gelu,
                                  scale=scol[:, ob:ob + 1], waits=[("pe", pes[n])])
                    p.bank_free[bks[n]] = [("s", s_rel)]
            s_gel = s_rel
        for ob in range(8):
            db = ob % 2
            dga = p.dma("gpsimd", B.wA[:, db * 4096:(db + 1) * 4096],
                        W["w3"][ob], inc="dg", waits=[("pe", wArd.get(db, 0))])
            bks = [4 * db, 4 * db + 1]
            for c in range(32):
                for n in range(2):
                    w = []
                    if c == 0:
                        w = [("dg", dga)] + p.bank_free[bks[n]]
                        if ob == 0 and n == 0:
                            w.append(("s", s_gel))
                    p.mm(B.pb[bks[n]][:, 0:512],
                         B.wA[:, db * 4096 + c * 128:(db * 4096) + (c + 1) * 128],
                         B.Hv[:, c * 1024 + n * 512: c * 1024 + (n + 1) * 512],
                         start=(c == 0), stop=(c == 31), waits=w,
                         inc="pe" if c == 31 else None)
            wArd[db] = p.cnt["pe"]
            pes = [p.cnt["pe"] - 1, p.cnt["pe"]]
            for n in range(2):
                dxr = p.dma("sync", B.ytmp[:, 0:512],
                            xin[ob * 128:(ob + 1) * 128, t0 + n * 512:t0 + (n + 1) * 512],
                            inc="dx", waits=[("do", p.cnt["do"]), ("v", p.cnt["v"])])
                vr = p.vop("scalar_tensor_tensor", B.ytmp[:, 0:512],
                           B.pb[bks[n]][:, 0:512], B.s3h[:, ob:ob + 1],
                           B.ytmp[:, 0:512], OP.mult, OP.add,
                           waits=[("pe", pes[n]), ("dx", dxr)])
                p.bank_free[bks[n]] = [("v", vr)]
                p.dma("sync", xout[ob * 128:(ob + 1) * 128, t0 + n * 512:t0 + (n + 1) * 512],
                      B.ytmp[:, 0:512], inc="do", waits=[("v", vr)])
    p.wait("sync", "do", p.cnt["do"])
    return p.cnt["v"]


def l1ffn_dram(nc):
    return {
        "w1": nc.dram_tensor("f_w1", [32, 128, 8, 128], F32, kind="ExternalInput").ap(),
        "w2": nc.dram_tensor("f_w2", [32, 128, 8, 128], F32, kind="ExternalInput").ap(),
        "h1": nc.dram_tensor("f_h1", [32, 128, 32, 128], F32, kind="ExternalInput").ap(),
        "h2": nc.dram_tensor("f_h2", [32, 128, 32, 128], F32, kind="ExternalInput").ap(),
        "w3": nc.dram_tensor("f_w3", [8, 128, 32, 128], F32, kind="ExternalInput").ap(),
        "s1c": nc.dram_tensor("f_s1c", [128, 32], F32, kind="ExternalInput").ap(),
        "s2c": nc.dram_tensor("f_s2c", [128, 32], F32, kind="ExternalInput").ap(),
        "sh1c": nc.dram_tensor("f_sh1c", [128, 32], F32, kind="ExternalInput").ap(),
        "sh2c": nc.dram_tensor("f_sh2c", [128, 32], F32, kind="ExternalInput").ap(),
        "s3c": nc.dram_tensor("f_s3c", [128, 8], F32, kind="ExternalInput").ap(),
    }


def build_l1ffn_program():
    nc = bass.Bass("TRN2", target_bir_lowering=False)
    xin = nc.dram_tensor("xin", [D, T], F32, kind="ExternalInput").ap()
    W = l1ffn_dram(nc)
    xout = nc.dram_tensor("xout", [D, T], F32, kind="ExternalOutput").ap()
    p = P(nc)
    B = make_common(nc, p)
    l1ffn_buffers(nc, B)
    dx = load_x(nc, p, B, xin)
    scales_v = emit_l1ffn_preload(nc, p, B, W)
    ln_done = emit_ln(nc, p, B, dx, svar=False)
    emit_l1ffn(nc, p, B, W, xin, xout, ln_done, scales_v)
    p.realize()
    return nc


def _tile_oc(w, P=128, Q=128):
    O, I = w.shape
    t = w.reshape(O // Q, Q, I // P, P)
    return np.ascontiguousarray(t.transpose(0, 3, 2, 1))


def l1ffn_host_weights(inputs, prefix="L1_ffn"):
    g = lambda n: np.asarray(inputs[prefix + "_" + n])
    colmaj = lambda s, nb: np.ascontiguousarray(s.reshape(nb, 128).T)
    return {
        "f_w1": _tile_oc(g("w1_w")), "f_w2": _tile_oc(g("w2_w")),
        "f_h1": _tile_oc(g("h1_w")), "f_h2": _tile_oc(g("h2_w")),
        "f_w3": _tile_oc(g("w3_w")),
        "f_s1c": colmaj(g("w1_s"), 32), "f_s2c": colmaj(g("w2_s"), 32),
        "f_sh1c": colmaj(g("h1_s"), 32), "f_sh2c": colmaj(g("h2_s"), 32),
        "f_s3c": colmaj(g("w3_s"), 8),
    }


def l1attn_buffers(nc, B):
    SB = nc.alloc_sbuf_tensor
    B.zbs = B.ytmp[:, 0:128]
    B.QTs = SB("QTs", [64, 16 * 512], BF16)
    B.KTs = SB("KTs", [64, 16 * 512], BF16)
    B.VA = SB("VA", [128, 4 * 16 * 128], BF16)
    B.E2 = SB("E2", [128, 2 * 2048], BF16)
    B.CTs = SB("CTs", [128, 8 * 512], BF16)
    B.ATTs = SB("ATTs", [128, 8 * 512], F32)
    B.WS = SB("WS", [128, 2 * 1024], BF16)
    B.WV = SB("WV", [128, 2 * 1024], BF16)
    B.bqc = SB("bqc", [128, 8], F32)
    B.bkc = SB("bkc", [128, 8], F32)
    B.boc = SB("boc", [128, 8], F32)
    B.bvrow = SB("bvrow", [1, 1024], BF16)
    B.gamc = SB("gamc", [128, 8], F32)
    B.betc = SB("betc", [128, 8], F32)
    B.neg4c = SB("neg4c", [128, 1], F32)
    B.rz = SB("rz", [1, 512], F32)


def emit_l1attn_preload(nc, p, B, W):
    for nm in ("bqc", "bkc", "boc", "gamc", "betc"):
        p.dma("sync", getattr(B, nm)[:], W[nm], inc="dx")
    p.dma("gpsimd", B.bvrow[:], W["bv"], inc="dg")
    p.gop("memset", B.neg4c[:], -4.0)
    return p.cnt["dx"], p.cnt["dg"], p.cnt["g"]


def emit_l1attn(nc, p, B, W, ln_done, pre):
    dxs, dgs, gs = pre
    ws_rd = {}
    wv_rd = {}
    p.wait("tensor", "dx", dxs)
    for sg in range(4):
        t0 = sg * 512
        lnv = ln_done[sg]
        g0 = p.gop("memset", B.VA[:], 0.0,
                   waits=[("pe", p.cnt["pe"])] if sg > 0 else ())
        va_ones = bass.AP(B.VA, 0, [[4 * 16 * 128, 128], [128, 64], [1, 1]])
        g1 = p.gop("memset", va_ones, 1.0)
        for ic in range(8):
            db = ic % 2
            dgw = p.dma("gpsimd", B.WV[:, db * 1024:(db + 1) * 1024],
                        W["wv"][ic * 128:(ic + 1) * 128, :], inc="dg",
                        waits=[("pe", wv_rd.get(db, 0))])
            for tt in range(4):
                for n in range(2):
                    bk = tt * 2 + n
                    w = []
                    if ic == 0:
                        w = [("dg", dgw), ("v", lnv)] + p.bank_free[bk]
                    else:
                        w = [("dg", dgw)] if (tt == 0 and n == 0) else []
                    mn = p.mm(B.pb[bk][:, 0:512],
                              B.XN[:, ic * T + t0 + tt * 128: ic * T + t0 + (tt + 1) * 128],
                              B.WV[:, db * 1024 + n * 512: db * 1024 + (n + 1) * 512],
                              start=(ic == 0), stop=False, waits=w,
                              inc="pe" if (tt == 3 and n == 1) else None)
                    if tt == 3 and n == 1:
                        wv_rd[db] = mn
        for tt in range(4):
            for n in range(2):
                bk = tt * 2 + n
                p.mm(B.pb[bk][:, 0:512], B.onesrow[0:1, 0:128],
                     B.bvrow[0:1, n * 512:(n + 1) * 512],
                     start=False, stop=True, inc="pe")
        wv_rd[0] = wv_rd[1] = pe_v = p.cnt["pe"]
        vva = None
        for tt in range(4):
            for n in range(2):
                bk = tt * 2 + n
                dst = bass.AP(B.VA, tt * 2048 + n * 8 * 128 + 64,
                              [[4 * 16 * 128, 128], [128, 8], [1, 64]])
                srcv = bass.AP(B.pb8h, bk * 512, [[4096, 128], [64, 8], [1, 64]])
                vva = p.vop("tensor_copy", dst, srcv,
                            waits=[("pe", pe_v), ("g", g1)])
                p.bank_free[bk] = [("v", vva)]
        qk_done = None
        for wi, (wkey, bcol, dstbuf) in enumerate((("wq", B.bqc, B.QTs),
                                                   ("wk", B.bkc, B.KTs))):
            for cb in range(8):
                db = (wi * 8 + cb) % 2
                dgw = p.dma("gpsimd", B.WS[:, db * 1024:(db + 1) * 1024],
                            W[wkey][cb], inc="dg", waits=[("pe", ws_rd.get(db, 0))])
                bk = (wi * 8 + cb) % 4
                for ic in range(8):
                    w = []
                    if ic == 0:
                        w = [("dg", dgw)] + p.bank_free[bk]
                    mn = p.mm(B.pb[bk][:, 0:512],
                              B.WS[:, db * 1024 + ic * 128: db * 1024 + (ic + 1) * 128],
                              B.XN[:, ic * T + t0: ic * T + t0 + 512],
                              start=(ic == 0), stop=(ic == 7), waits=w,
                              inc="pe")
                    ws_rd[db] = mn
                p.sop("activation", dstbuf[0:64, (2 * cb) * 512:(2 * cb + 1) * 512],
                      B.pb8[0:64, bk * 512:(bk + 1) * 512], AF.Identity,
                      bias=bcol[0:64, cb:cb + 1],
                      waits=[("pe", p.cnt["pe"])])
                qk_done = p.sop("activation",
                                dstbuf[0:64, (2 * cb + 1) * 512:(2 * cb + 2) * 512],
                                B.pb8[64:128, bk * 512:(bk + 1) * 512], AF.Identity,
                                bias=bcol[64:128, cb:cb + 1])
                p.bank_free[bk] = [("s", qk_done)]
        pend_norm = [None]

        def flush_norm(p=p, B=B):
            if pend_norm[0] is None:
                return
            vz_, cbk_, gi_, q4_ = pend_norm[0]
            pend_norm[0] = None
            p.mm(B.pb[4][:, 0:512], B.ones128f[0:1, 0:128], B.rz[0:1, 0:512],
                 start=True, stop=True, waits=[("v", vz_)] + p.bank_free[4],
                 inc="pe")
            vzb = p.vop("tensor_copy", B.ytmp[:], B.pb[4][:, 0:512],
                        waits=[("pe", p.cnt["pe"])])
            p.bank_free[4] = [("v", vzb)]
            vct = None
            for par in range(2):
                dst = bass.AP(B.CTs, par * 64 * 4096 + (q4_ * 2) * 512 + gi_ * 128,
                              [[4096, 64], [512, 2], [1, 128]])
                csrc = bass.AP(B.pb8h, cbk_ * 512 + 64 * 4096 + par * 128,
                               [[4096, 64], [256, 2], [1, 128]])
                zsrc = bass.AP(B.ytmp, 64 * 512 + par * 128,
                               [[512, 64], [256, 2], [1, 128]])
                vct = p.vop("tensor_tensor", dst, csrc, zsrc, OP.mult)
            p.bank_free[cbk_] = [("v", vct)]

        def emit_S(gi, cq):
            qv = B.QTs[0:64, cq * 512 + gi * 128: cq * 512 + (gi + 1) * 128]
            for ck in range(16):
                kv = B.KTs[0:64, ck * 512 + gi * 128: ck * 512 + (gi + 1) * 128]
                w = []
                if ck % 4 == 0:
                    w = list(p.bank_free[ck // 4])
                    if gi == 0 and cq == 0:
                        w.append(("s", qk_done))
                p.mm(B.pb8[:, ck * 128:(ck + 1) * 128],
                     kv, qv, start=True, stop=True, waits=w,
                     inc="pe" if ck in (7, 15) else None)
            pe_s = p.cnt["pe"]
            eslot = cq % 2
            se = None
            for half in range(2):
                se = p.sop("activation",
                           B.E2[:, eslot * 2048 + half * 1024:
                                eslot * 2048 + (half + 1) * 1024],
                           B.pb8[:, half * 1024:(half + 1) * 1024],
                           AF.Exp, scale=0.125, bias=B.neg4c[:, 0:1],
                           waits=[("pe", pe_s - 1 + half)])
            for i in range(4):
                p.bank_free[i] = [("s", se)]
            return se

        def emit_ctx(gi, cq, se):
            eslot = cq % 2
            cbk = 6 + (cq // 4) % 2
            slot = cq % 4
            for ck in range(16):
                w = []
                if ck == 0:
                    w = [("s", se)] + (p.bank_free[cbk] if slot == 0 else [])
                    if gi == 0 and cq == 0:
                        w += [("v", vva), ("g", g1)]
                p.mm(B.pb8[:, cbk * 512 + slot * 128: cbk * 512 + (slot + 1) * 128],
                     B.VA[:, (gi * 16 + ck) * 128:(gi * 16 + ck + 1) * 128],
                     B.E2[:, eslot * 2048 + ck * 128: eslot * 2048 + (ck + 1) * 128],
                     start=(ck == 0), stop=(ck == 15), waits=w,
                     inc="pe" if ck == 15 else None)
            if slot == 3:
                vz = p.vop("reciprocal", B.rz[:],
                           bass.AP(B.pb8h, cbk * 512, [[4096, 1], [1, 512]]),
                           waits=[("pe", p.cnt["pe"])])
                pend_norm[0] = (vz, cbk, gi, cq // 4)

        prev = None
        for gi in range(4):
            for cq in range(16):
                se = emit_S(gi, cq)
                if cq % 4 == 1:
                    flush_norm()
                if prev is not None:
                    emit_ctx(prev[0], prev[1], prev[2])
                prev = (gi, cq, se)
        emit_ctx(prev[0], prev[1], prev[2])
        flush_norm()
        v_ct = p.cnt["v"]
        if B.adbg and sg == B.dbg_sg:
            p.dma("sync", B.adbg["QTs"], B.QTs[:], inc="do", waits=[("v", v_ct)])
            p.dma("sync", B.adbg["KTs"], B.KTs[:], inc="do")
            p.dma("sync", B.adbg["VA"], B.VA[:], inc="do")
            p.dma("sync", B.adbg["CTs"], B.CTs[:], inc="do")
            p.dma("sync", B.adbg["E2"], B.E2[:], inc="do")
        for ob in range(8):
            db = ob % 2
            dgw = p.dma("gpsimd", B.WS[:, db * 1024:(db + 1) * 1024],
                        W["wo"][ob], inc="dg", waits=[("pe", ws_rd.get(db, 0))])
            bk = ob % 4
            for ec in range(8):
                w = []
                if ec == 0:
                    w = [("dg", dgw), ("v", v_ct)] + p.bank_free[bk]
                mn = p.mm(B.pb[bk][:, 0:512],
                          B.WS[:, db * 1024 + ec * 128: db * 1024 + (ec + 1) * 128],
                          B.CTs[:, ec * 512:(ec + 1) * 512],
                          start=(ec == 0), stop=(ec == 7), waits=w, inc="pe")
                ws_rd[db] = mn
            vo = p.vop("scalar_tensor_tensor", B.ATTs[:, ob * 512:(ob + 1) * 512],
                       B.pb[bk][:, 0:512], B.boc[:, ob:ob + 1],
                       B.XN[:, ob * T + t0: ob * T + t0 + 512], OP.add, OP.add,
                       waits=[("pe", p.cnt["pe"])] + ([("s", p.cnt["s"])] if ob == 0 else []))
            p.bank_free[bk] = [("v", vo)]
        if B.adbg and sg == B.dbg_sg:
            p.dma("sync", B.adbg["ATTs"], B.ATTs[:], inc="do", waits=[("v", p.cnt["v"])])
        v_att = p.cnt["v"]
        ia, ib = 4, 5
        ms = {}
        for c in range(NCH):
            db2 = c % 2
            sl = slice(db2 * 512, (db2 + 1) * 512)
            asl = B.ATTs[:, c * 512:(c + 1) * 512]
            w = [("pe", ms[c - 2])] if c >= 2 else []
            vcp = p.vop("tensor_copy", B.xbfs[:, sl], asl, waits=w)
            scp = p.sop("activation", B.xsqs[:, sl], asl, AF.Square,
                        waits=w + ([("v", v_att)] if c == 0 else []))
            w2_ = [("v", vcp), ("s", scp)]
            if c == 0:
                w2_ += p.bank_free[ia] + p.bank_free[ib]
            p.mm(B.pb[ia][:, 0:512], B.ones128[:], B.xbfs[:, sl],
                 start=(c == 0), stop=(c == NCH - 1), waits=w2_)
            ms[c] = p.mm(B.pb[ib][:, 0:512], B.ones128[:], B.xsqs[:, sl],
                         start=(c == 0), stop=(c == NCH - 1), inc="pe")
        pe_done = p.cnt["pe"]
        p.vop("tensor_scalar", B.meanbc[:], B.pb[ia][:, 0:512], 1.0 / D, None,
              OP.mult, waits=[("pe", pe_done)])
        p.vop("tensor_scalar", B.msqbc[:], B.pb[ib][:, 0:512], 1.0 / D, None, OP.mult)
        p.vop("tensor_tensor", B.rstdbc[:], B.meanbc[:], B.meanbc[:], OP.mult)
        vfree = p.vop("tensor_tensor", B.msqbc[:], B.msqbc[:], B.rstdbc[:], OP.subtract)
        p.bank_free[ia] = p.bank_free[ib] = [("v", vfree)]
        p.sop("activation", B.rstdbc[:], B.msqbc[:], AF.Ln, bias=B.epscol[:],
              waits=[("v", vfree)])
        ssq = p.sop("activation", B.rstdbc[:], B.rstdbc[:], AF.Exp, scale=-0.5)
        p.wait("vector", "s", ssq)
        for c in range(NCH):
            asl = B.ATTs[:, c * 512:(c + 1) * 512]
            xsl = B.XR[:, c * T + t0: c * T + t0 + 512]
            p.vop("tensor_tensor", B.ytmp[:], asl, B.meanbc[:], OP.subtract)
            p.vop("tensor_tensor", B.ytmp[:], B.ytmp[:], B.rstdbc[:], OP.mult)
            p.vop("tensor_scalar", B.ytmp[:], B.ytmp[:], B.gamc[:, c:c + 1],
                  B.betc[:, c:c + 1], OP.mult, OP.add)
            p.vop("tensor_tensor", xsl, xsl, B.ytmp[:], OP.add)
    return p.cnt["v"]


def l1attn_dram(nc):
    return {
        "wq": nc.dram_tensor("a_wq", [8, 128, 8, 128], F32, kind="ExternalInput").ap(),
        "wk": nc.dram_tensor("a_wk", [8, 128, 8, 128], F32, kind="ExternalInput").ap(),
        "wv": nc.dram_tensor("a_wv", [1024, 1024], F32, kind="ExternalInput").ap(),
        "wo": nc.dram_tensor("a_wo", [8, 128, 8, 128], F32, kind="ExternalInput").ap(),
        "bqc": nc.dram_tensor("a_bqc", [128, 8], F32, kind="ExternalInput").ap(),
        "bkc": nc.dram_tensor("a_bkc", [128, 8], F32, kind="ExternalInput").ap(),
        "bv": nc.dram_tensor("a_bv", [1, 1024], F32, kind="ExternalInput").ap(),
        "boc": nc.dram_tensor("a_boc", [128, 8], F32, kind="ExternalInput").ap(),
        "gamc": nc.dram_tensor("a_gamc", [128, 8], F32, kind="ExternalInput").ap(),
        "betc": nc.dram_tensor("a_betc", [128, 8], F32, kind="ExternalInput").ap(),
    }


def build_l1attn_program(debug=False, dbg_sg=0):
    nc = bass.Bass("TRN2", target_bir_lowering=False)
    xin = nc.dram_tensor("xin", [D, T], F32, kind="ExternalInput").ap()
    W = l1attn_dram(nc)
    xout = nc.dram_tensor("xout", [D, T], F32, kind="ExternalOutput").ap()
    p = P(nc)
    B = make_common(nc, p)
    l1attn_buffers(nc, B)
    B.adbg = {}
    B.dbg_sg = dbg_sg
    if debug:
        for nm, sh, dt in (("QTs", [64, 8192], BF16), ("KTs", [64, 8192], BF16),
                           ("VA", [128, 8192], BF16), ("CTs", [128, 4096], BF16),
                           ("ATTs", [128, 4096], F32), ("E2", [128, 4096], BF16)):
            B.adbg[nm] = nc.dram_tensor("dbg_" + nm, sh, dt, kind="ExternalOutput").ap()
    dx = load_x(nc, p, B, xin)
    pre = emit_l1attn_preload(nc, p, B, W)
    ln_done = emit_ln(nc, p, B, dx, svar=False)
    vend = emit_l1attn(nc, p, B, W, ln_done, pre)
    store_x(nc, p, B, xout, vend)
    p.realize()
    return nc


def l1attn_host_weights(inputs, prefix="L1_attn"):
    g = lambda n: np.asarray(inputs[prefix + "_" + n])
    colmaj = lambda s: np.ascontiguousarray(s.reshape(8, 128).T)
    def tile_nat(w):
        t = w.reshape(8, 128, 8, 128)
        return np.ascontiguousarray(w.T.reshape(8, 128, 8, 128).transpose(0, 3, 2, 1))
    def tile_lhsT(w):
        t = w.reshape(8, 128, 8, 128)
        return np.ascontiguousarray(t.transpose(2, 1, 0, 3))
    return {
        "a_wq": tile_lhsT(g("wq")), "a_wk": tile_lhsT(g("wk")),
        "a_wv": np.ascontiguousarray(g("wv")), "a_wo": tile_lhsT(g("wo")),
        "a_bqc": colmaj(g("bq")), "a_bkc": colmaj(g("bk")),
        "a_bv": np.ascontiguousarray(g("bv")[None, :]),
        "a_boc": colmaj(g("bo")), "a_gamc": colmaj(g("gamma")),
        "a_betc": colmaj(g("beta")),
    }


def l0attn_buffers(nc, B):
    SB = nc.alloc_sbuf_tensor
    B.EQs = SB("EQs", [128, 4 * 1024], BF16)
    B.EKs = SB("EKs", [128, 4 * 1024], BF16)
    B.Vs = SB("Vs", [128, 4 * 1024], BF16)
    B.QTi = SB("QTi", [64, 2048], BF16)
    B.KTi = SB("KTi", [64, 2048], BF16)
    B.VTi = SB("VTi", [64, 2048], BF16)
    B.At = SB("At", [16, 2048], BF16)
    B.Vr = SB("Vr", [16, 64 + 128 * 64], BF16)
    B.rQ = SB("rQ", [128, 16], F32)
    B.rK = SB("rK", [128, 64], F32)
    B.rKT = SB("rKT", [64, 128], F32)
    B.mxc = SB("mxc", [128, 1], F32)
    B.WQ3 = SB("WQ3", [128, 2 * 1024], BF16)


def emit_l0attn(nc, p, B, W, ln_done):
    w3_rd = {}
    ident = B.ident
    p.gop("memset", B.Vr[0:16, 0:64], 0.0)
    for sg in range(4):
        t0 = sg * 512
        for wi, wkey in enumerate(("wqT", "wkT", "wvT")):
            dst = (B.EQs, B.EKs, B.Vs)[wi]
            for ic in range(8):
                db = ic % 2
                dgw = p.dma("gpsimd", B.WQ3[:, db * 1024:(db + 1) * 1024],
                            W[wkey][ic * 128:(ic + 1) * 128, :], inc="dg",
                            waits=[("pe", w3_rd.get(db, 0))])
                for tt in range(4):
                    for n in range(2):
                        bk = tt * 2 + n
                        w = []
                        if ic == 0:
                            w = [("dg", dgw)] + p.bank_free[bk]
                            if wi == 0 and tt == 0 and n == 0:
                                w.append(("v", ln_done[sg]))
                        elif tt == 0 and n == 0:
                            w = [("dg", dgw)]
                        mn = p.mm(B.pb[bk][:, 0:512],
                                  B.XN[:, ic * T + t0 + tt * 128: ic * T + t0 + (tt + 1) * 128],
                                  B.WQ3[:, db * 1024 + n * 512: db * 1024 + (n + 1) * 512],
                                  start=(ic == 0), stop=(ic == 7), waits=w,
                                  inc="pe" if (tt == 3 and n == 1) or ic == 7 else None)
                        if tt == 3 and n == 1:
                            w3_rd[db] = mn
            for tt in range(4):
                pe_t = p.cnt["pe"]
                if wi < 2:
                    vmx = p.vop("tensor_reduce", B.mxc[:],
                                bass.AP(B.pb8h, tt * 1024, [[4096, 128], [1, 1024]]),
                                axis=AX.X, op=OP.max, waits=[("pe", pe_t)])
                    p.vop("tensor_scalar_mul", B.mxc[:], B.mxc[:], -1.0)
                    p.wait("scalar", "v", p.cnt["v"])
                    se = p.sop("activation", dst[:, tt * 1024:(tt + 1) * 1024],
                               bass.AP(B.pb8h, tt * 1024, [[4096, 128], [1, 1024]]),
                               AF.Exp, bias=B.mxc[:], waits=[("pe", pe_t)])
                    rel = [("s", se)]
                else:
                    sv = p.sop("activation", dst[:, tt * 1024:(tt + 1) * 1024],
                               bass.AP(B.pb8h, tt * 1024, [[4096, 128], [1, 1024]]),
                               AF.Identity, waits=[("pe", pe_t)])
                    rel = [("s", sv)]
                p.bank_free[tt * 2] = rel
                p.bank_free[tt * 2 + 1] = rel
        for tt in range(4):
            toff = t0 + tt * 128
            eq = B.EQs[:, tt * 1024:(tt + 1) * 1024]
            ek = B.EKs[:, tt * 1024:(tt + 1) * 1024]
            vv = B.Vs[:, tt * 1024:(tt + 1) * 1024]
            p.wait("vector", "s", p.cnt["s"])
            p.vop("tensor_reduce", B.rQ[:],
                  bass.AP(B.EQs, tt * 1024, [[4096, 128], [64, 16], [1, 64]]),
                  axis=AX.X, op=OP.add)
            p.vop("reciprocal", B.rQ[:], B.rQ[:])
            p.vop("tensor_tensor",
                  bass.AP(B.EQs, tt * 1024, [[4096, 128], [64, 16], [1, 64]]),
                  bass.AP(B.EQs, tt * 1024, [[4096, 128], [64, 16], [1, 64]]),
                  bass.AP(B.rQ, 0, [[16, 128], [1, 16], [0, 64]]), OP.mult)
            p.vop("tensor_reduce", B.rK[:],
                  bass.AP(B.EKs, tt * 1024, [[4096, 128], [1, 64], [64, 16]]),
                  axis=AX.X, op=OP.add)
            vrk = p.vop("reciprocal", B.rK[:], B.rK[:])
            p2h = B.pb8h[:].bitcast(BF16)
            for src_, dstb, base in ((eq, B.QTi, 0), (ek, B.KTi, 2048), (vv, B.VTi, 4096)):
                pe_w = [("v", p.cnt["v"])]
                for h in range(16):
                    p.tr(p2h[0:64, base + h * 128:base + (h + 1) * 128],
                         src_[:, h * 64:(h + 1) * 64], ident[:],
                         waits=pe_w if h == 0 else (),
                         inc="pe" if h == 15 else None)
            p.tr(bass.AP(B.pb8h, 3072, [[4096, 64], [1, 128]]),
                 B.rK[:], B.identf[:], waits=[("v", vrk)], inc="pe")
            pe_tr = p.cnt["pe"]
            import dataclasses as _dc
            def _pview(off_bf16, pat):
                return _dc.replace(p2h, offset=off_bf16, ap=pat)
            vr1 = p.vop("tensor_copy", B.rKT[:],
                        bass.AP(B.pb8h, 3072, [[4096, 64], [1, 128]]),
                        waits=[("pe", pe_tr)])
            p.vop("tensor_copy",
                  bass.AP(B.QTi, 0, [[2048, 64], [1, 16], [16, 128]]),
                  _pview(0, [[8192, 64], [128, 16], [1, 128]]))
            v_qk = p.vop("scalar_tensor_tensor",
                         bass.AP(B.KTi, 0, [[2048, 64], [1, 16], [16, 128]]),
                         _pview(2048, [[8192, 64], [128, 16], [1, 128]]),
                         1.0,
                         bass.AP(B.rKT, 0, [[128, 64], [0, 16], [1, 128]]),
                         OP.mult, OP.mult)
            v_v = p.vop("tensor_copy",
                        bass.AP(B.VTi, 0, [[2048, 64], [1, 16], [16, 128]]),
                        _pview(4096, [[8192, 64], [128, 16], [1, 128]]))
            p.wait("tensor", "v", v_qk)
            for half in range(2):
                abase = 3072 if half == 0 else 2048
                bks = (6, 7) if half == 0 else (4, 5)
                for ti in range(64):
                    tok = half * 64 + ti
                    w0 = []
                    if ti == 0:
                        w0 = p.bank_free[bks[0]] + p.bank_free[bks[1]]
                        w0.append(("v", vr1 if half == 0 else v_v))
                    p.mm(bass.AP(B.pb8h, abase + ti * 16, [[4096, 16], [1, 16]]),
                         B.KTi[0:64, tok * 16:(tok + 1) * 16],
                         B.QTi[0:64, tok * 16:(tok + 1) * 16],
                         start=True, stop=True, waits=w0,
                         inc="pe" if ti == 63 else None)
                vsrc = bass.AP(B.pb8h, abase, [[4096, 16], [16, 64], [1, 2], [2, 8]])
                vdst = bass.AP(B.At, half * 1024, [[2048, 16], [16, 64], [8, 2], [1, 8]])
                va = p.vop("tensor_copy", vdst, vsrc, waits=[("pe", p.cnt["pe"])])
                p.bank_free[bks[0]] = [("v", va)]
                p.bank_free[bks[1]] = [("v", va)]
            for grp in range(4):
                vb0 = (grp % 2) * 2
                for ti in range(32):
                    tok = grp * 32 + ti
                    p.tr(p2h[0:16, vb0 * 1024 + ti * 64:vb0 * 1024 + (ti + 1) * 64],
                         B.VTi[0:64, tok * 16:(tok + 1) * 16], ident[0:64, 0:64],
                         waits=((p.bank_free[vb0] + p.bank_free[vb0 + 1]) if ti == 0 else ()),
                         inc="pe" if ti == 31 else None)
                vv2 = p.vop("tensor_copy",
                            B.Vr[0:16, 64 + grp * 2048: 64 + (grp + 1) * 2048],
                            p2h[0:16, vb0 * 1024:vb0 * 1024 + 2048],
                            waits=[("pe", p.cnt["pe"])])
                p.bank_free[vb0] = [("v", vv2)]
                p.bank_free[vb0 + 1] = [("v", vv2)]
            v_vr = p.cnt["v"]
            p.wait("tensor", "v", v_vr)
            for half in range(2):
                ebk, obk = (0, 1) if half == 0 else (2, 3)
                for ti in range(64):
                    tok = half * 64 + ti
                    w_e = (p.bank_free[ebk] + p.bank_free[obk]) if ti == 0 else ()
                    p.mm(bass.AP(B.pb8h, ebk * 512 + ti * 8, [[4096, 64], [1, 8]]),
                         B.Vr[0:16, 64 + tok * 64: 64 + (tok + 1) * 64],
                         B.At[0:16, tok * 16: tok * 16 + 8],
                         start=True, stop=True, waits=w_e)
                    p.mm(bass.AP(B.pb8h, obk * 512 + ti * 8, [[4096, 128], [1, 8]]),
                         B.Vr[0:16, tok * 64: tok * 64 + 128],
                         B.At[0:16, tok * 16 + 8: tok * 16 + 16],
                         start=True, stop=True,
                         inc="pe" if ti == 63 else None)
                pe_s2 = p.cnt["pe"]
                vlast = p.vop("scalar_tensor_tensor",
                              bass.AP(B.XR, toff + half * 64, [[16384, 64], [2048, 8], [1, 64]]),
                              bass.AP(B.pb8h, ebk * 512, [[4096, 64], [1, 8], [8, 64]]),
                              1.0,
                              bass.AP(B.XR, toff + half * 64, [[16384, 64], [2048, 8], [1, 64]]),
                              OP.mult, OP.add, waits=[("pe", pe_s2)])
                vlast = p.vop("scalar_tensor_tensor",
                              bass.AP(B.XR, 64 * 16384 + toff + half * 64,
                                      [[16384, 64], [2048, 8], [1, 64]]),
                              bass.AP(B.pb8h, obk * 512 + 64 * 4096, [[4096, 64], [1, 8], [8, 64]]),
                              1.0,
                              bass.AP(B.XR, 64 * 16384 + toff + half * 64,
                                      [[16384, 64], [2048, 8], [1, 64]]),
                              OP.mult, OP.add)
                p.bank_free[ebk] = [("v", vlast)]
                p.bank_free[obk] = [("v", vlast)]
    return p.cnt["v"]


def l0attn_dram(nc):
    return {
        "wqT": nc.dram_tensor("l0_wqT", [1024, 1024], F32, kind="ExternalInput").ap(),
        "wkT": nc.dram_tensor("l0_wkT", [1024, 1024], F32, kind="ExternalInput").ap(),
        "wvT": nc.dram_tensor("l0_wvT", [1024, 1024], F32, kind="ExternalInput").ap(),
    }


def build_l0attn_program():
    nc = bass.Bass("TRN2", target_bir_lowering=False)
    xin = nc.dram_tensor("xin", [D, T], F32, kind="ExternalInput").ap()
    W = l0attn_dram(nc)
    xout = nc.dram_tensor("xout", [D, T], F32, kind="ExternalOutput").ap()
    p = P(nc)
    B = make_common(nc, p)
    l0attn_buffers(nc, B)
    dx = load_x(nc, p, B, xin)
    ln_done = emit_ln(nc, p, B, dx, svar=False)
    vend = emit_l0attn(nc, p, B, W, ln_done)
    store_x(nc, p, B, xout, vend)
    p.realize()
    return nc


def l0attn_host_weights(inputs):
    return {
        "l0_wqT": np.ascontiguousarray(np.asarray(inputs["L0_attn_W_Q"]).T),
        "l0_wkT": np.ascontiguousarray(np.asarray(inputs["L0_attn_W_K"]).T),
        "l0_wvT": np.ascontiguousarray(np.asarray(inputs["L0_attn_W_V"]).T),
    }


_SBUF_TOP = 229344

_ARENA_GROUPS = {
    "moe": [("svarbc", [128, T], "BF16"), ("rwt", [128, NCH * 32], "F32"),
            ("rwsc", [128, 32], "F32"), ("rbrow", [1, 32], "BF16"),
            ("b1c", [128, 64], "F32"), ("b2row", [1, 2048], "F32"),
            ("idxcols", [128, 16], "F32"),
            ("lmax", [128, 1], "F32"),
            ("iote", [32, 1], "BF16"),
            ("Ltile", [128, 2048], "BF16"),
            ("w1f32", [128, 2048], "F32"), ("w2f32s", [128, 2048], "F32"),
            ("tokids", [128, 16], "I32"), ("ytm", [128, 4 * 1024], "BF16"),
            ("xg", [128, 4 * 1024], "BF16"),
            ("xgT", [128, 4 * 1024], "BF16"), ("hT", [128, 4 * 256], "BF16"),

            ("w1sb", [128, 4 * 2048], "BF16"), ("w2sb", [128, 4 * 2048], "BF16")],
    "l0a": [("EQs", [128, 4 * 1024], "BF16"), ("EKs", [128, 4 * 1024], "BF16"),
            ("Vs", [128, 4 * 1024], "BF16"), ("QTi", [64, 2048], "BF16"),
            ("KTi", [64, 2048], "BF16"), ("VTi", [64, 2048], "BF16"),
            ("At", [16, 2048], "BF16"), ("Vr", [16, 64 + 128 * 64], "BF16"),
            ("rQ", [128, 16], "F32"), ("rK", [128, 64], "F32"),
            ("rKT", [64, 128], "F32"), ("mxc", [128, 1], "F32"),
            ("WQ3", [128, 2 * 1024], "BF16")],
    "l1f": [("GH", [128, 32768], "BF16"), ("wA", [128, 2 * 4096], "BF16"),
            ("wB", [128, 2 * 1024], "BF16"), ("wC", [128, 2 * 1024], "BF16"),
            ("s1c", [128, 32], "F32"), ("s2c", [128, 32], "F32"),
            ("sh1c", [128, 32], "F32"), ("sh2c", [128, 32], "F32"),
            ("s3c", [128, 8], "F32"), ("s3h", [128, 8], "F32")],
    "l1a": [("QTs", [64, 16 * 512], "BF16"), ("KTs", [64, 16 * 512], "BF16"),
            ("VA", [128, 4 * 16 * 128], "BF16"), ("E2", [128, 2 * 2048], "BF16"),
            ("CTs", [128, 8 * 512], "BF16"), ("ATTs", [128, 8 * 512], "F32"),
            ("WS", [128, 2 * 1024], "BF16"), ("WV", [128, 2 * 1024], "BF16"),
            ("bqc", [128, 8], "F32"), ("bkc", [128, 8], "F32"),
            ("boc", [128, 8], "F32"), ("bvrow", [1, 1024], "BF16"),
            ("gamc", [128, 8], "F32"), ("betc", [128, 8], "F32"),
            ("neg4c", [128, 1], "F32"), ("rz", [1, 512], "F32")],
}


def _alloc_arena(nc, base):
    dts = {"F32": F32, "BF16": BF16, "I32": I32}
    handles = {}
    maxend = base
    for phase, members in _ARENA_GROUPS.items():
        off = base
        hs = {}
        for name, shape, dt in members:
            nb = shape[-1] * (4 if dt == "F32" else 2)
            for s in shape[1:-1]:
                nb *= s
            off = (off + 31) & ~31
            hs[name] = nc.alloc_sbuf_tensor_at(phase + "_" + name, shape, dts[dt],
                                               offset=off)
            off += nb
        assert off <= _SBUF_TOP, (phase, off)
        maxend = max(maxend, off)
        handles[phase] = hs
    return handles


def build_full_program(skip=()):
    nc = bass.Bass("TRN2", target_bir_lowering=False)
    xin = nc.dram_tensor("xin", [D, T], F32, kind="ExternalInput").ap()
    Wm = moer_dram(nc)
    W0 = l0attn_dram(nc)
    Wf = l1ffn_dram(nc)
    Wa = l1attn_dram(nc)
    xout = nc.dram_tensor("xout", [D, T], F32, kind="ExternalOutput").ap()
    xsp = nc.dram_tensor("xspill", [D, T], F32, kind="Internal").ap()
    import os as _os
    _dbg = _os.environ.get("MKDBG", "") == "1"
    _dbgx = nc.dram_tensor("dbg_xgt", [32, 128, 1024], BF16,
                           kind="ExternalOutput").ap() if _dbg else None
    scr = {
        "xn_tm": nc.dram_tensor("s_xn_tm", [T, D], BF16,
                                kind="ExternalOutput" if _dbg else "Internal").ap(),
        "y_tm": nc.dram_tensor("s_y_tm", [T, D], BF16,
                               kind="ExternalOutput" if _dbg else "Internal").ap(),
        "perm2": nc.dram_tensor("s_perm2", [32 * 128, 1], I32,
                                kind="ExternalOutput" if _dbg else "Internal").ap(),
    }
    p = P(nc)
    B = make_common(nc, p)
    B.scoli = nc.alloc_sbuf_tensor("fx_scoli", [128, 16], I32)
    B.permsb = nc.alloc_sbuf_tensor("fx_permsb", [128, 32], I32)
    B.dgfence = nc.alloc_sbuf_tensor("fx_dgfence", [16, 1], I32)
    B.gfsink = nc.alloc_sbuf_tensor("fx_gfsink", [128, 16], BF16)
    base = _SBUF_TOP - nc.sbuf_bytes_remaining
    handles = _alloc_arena(nc, base)

    def bind(phase):
        for name, h in handles[phase].items():
            setattr(B, name, h)
        if phase == "l1a":
            B.zbs = B.ytmp[:, 0:128]
        if phase == "moe":
            B.w2f32 = B.w2f32s
            B.srow = B.w2f32s[0:1, 0:2048]
            B.Mtm = B.w2f32s[:].bitcast(BF16)[:, 2048:2560]
            B.lsel = B.rwsc
            B.ranksb = B.Ltile[0:32, 0:2048]
            B.xtms = B.xgT[:, 2048:4096]
            B.MT = B.xgT[0:32, 0:2048]
        if phase == "l1f":
            B.Hv = B.XR[:].bitcast(BF16)

    scr["dbgx"] = _dbgx
    dx = load_x(nc, p, B, xin)
    if 1 not in skip:
        bind("moe")
        B.rdbg = {}
        sdg = emit_moer_preload(nc, p, B, Wm)
        ln = emit_ln(nc, p, B, x_ready_dx=dx, svar=True)
        emit_moe_routed(nc, p, B, Wm, ln, sdg, scr)
        phase_barrier(p)
    if 2 not in skip:
        bind("l0a")
        ln = emit_ln(nc, p, B, x_ready_dx=dx, v_ready=p.cnt["v"], svar=False)
        emit_l0attn(nc, p, B, W0, ln)
        phase_barrier(p)
    if 3 not in skip:
        bind("moe")
        B.rdbg = {}
        sdg = emit_moer_preload(nc, p, B, Wm)
        ln = emit_ln(nc, p, B, x_ready_dx=dx, v_ready=p.cnt["v"], svar=True)
        emit_moe_routed(nc, p, B, Wm, ln, sdg, scr)
        phase_barrier(p)
    if 4 not in skip:
        bind("l1f")
        sv = emit_l1ffn_preload(nc, p, B, Wf)
        ln = emit_ln(nc, p, B, x_ready_dx=dx, v_ready=p.cnt["v"], svar=False)
        for c in range(NCH):
            p.dma("sync", xsp[c * 128:(c + 1) * 128, :], B.XR[:, c * T:(c + 1) * T],
                  inc="do")
        spill_do = p.cnt["do"]
        emit_l1ffn(nc, p, B, Wf, xsp, xsp, ln, sv, hv_waits=[("do", spill_do)])
        phase_barrier(p)
    if 5 not in skip:
        bind("l1a")
        if 4 not in skip:
            for c in range(NCH):
                p.dma("sync", B.XR[:, c * T:(c + 1) * T], xsp[c * 128:(c + 1) * 128, :],
                      inc="dx", waits=[("do", p.cnt["do"])] if c == 0 else ())
        dx2 = p.cnt["dx"]
        pre = emit_l1attn_preload(nc, p, B, Wa)
        ln = emit_ln(nc, p, B, x_ready_dx=dx2, svar=False)
        emit_l1attn(nc, p, B, Wa, ln, pre)
        phase_barrier(p)
    if 6 not in skip:
        bind("l1f")
        sv = emit_l1ffn_preload(nc, p, B, Wf)
        ln = emit_ln(nc, p, B, x_ready_dx=dx, v_ready=p.cnt["v"], svar=False)
        for c in range(NCH):
            p.dma("sync", xsp[c * 128:(c + 1) * 128, :], B.XR[:, c * T:(c + 1) * T],
                  inc="do")
        spill_do = p.cnt["do"]
        emit_l1ffn(nc, p, B, Wf, xsp, xout, ln, sv, hv_waits=[("do", spill_do)])
    else:
        store_x(nc, p, B, xout, p.cnt["v"])
    p.realize()
    return nc


def build_moer_program(debug=False):
    nc = bass.Bass("TRN2", target_bir_lowering=False)
    xin = nc.dram_tensor("xin", [D, T], F32, kind="ExternalInput").ap()
    W = moer_dram(nc)
    xout = nc.dram_tensor("xout", [D, T], F32, kind="ExternalOutput").ap()
    scr = {
        "xn_tm": nc.dram_tensor("s_xn_tm", [T, D], BF16, kind="Internal").ap(),
        "y_tm": nc.dram_tensor("s_y_tm", [T, D], BF16, kind="Internal").ap(),
        "perm2": nc.dram_tensor("s_perm2", [32 * 128, 1], I32, kind="Internal").ap(),
    }
    p = P(nc)
    B = make_common(nc, p)
    moer_buffers(nc, B)
    B.rdbg = {}
    if debug:
        for nm, sh, dt in (("srow", [1, 2048], F32), ("permsb", [128, 32], I32),
                           ("MT", [32, 2048], BF16), ("ranksb", [32, 2048], BF16),
                           ("xn_tm", [T, D], BF16), ("y_tm", [T, D], BF16),
                           ("xg", [128, 4096], BF16), ("hT", [128, 1024], BF16)):
            B.rdbg[nm] = nc.dram_tensor("dbg_" + nm, sh, dt, kind="ExternalOutput").ap()
    dx = load_x(nc, p, B, xin)
    small_dg = emit_moer_preload(nc, p, B, W)
    ln_done = emit_ln(nc, p, B, x_ready_dx=dx, svar=True)
    vend = emit_moe_routed(nc, p, B, W, ln_done, small_dg, scr)
    if debug:
        p.dma("sync", B.rdbg["srow"], B.srow, inc="do", waits=[("v", vend)])
        p.dma("sync", B.rdbg["permsb"], B.permsb[:], inc="do")
        p.dma("sync", B.rdbg["MT"], B.MT[:], inc="do")
        p.dma("sync", B.rdbg["ranksb"], B.ranksb, inc="do")
        p.dma("sync", B.rdbg["xn_tm"], scr["xn_tm"], inc="do")
        p.dma("sync", B.rdbg["y_tm"], scr["y_tm"], inc="do")
        p.dma("sync", B.rdbg["xg"], B.xg[:], inc="do")
        p.dma("sync", B.rdbg["hT"], B.hT[:], inc="do")
    store_x(nc, p, B, xout, vend)
    p.realize()
    return nc


def moer_dram(nc):
    return {
        "rw_t": nc.dram_tensor("rw_t", [1024, 32], F32, kind="ExternalInput").ap(),
        "rb": nc.dram_tensor("rb", [1, 32], F32, kind="ExternalInput").ap(),
        "w1": nc.dram_tensor("w1", [32, 1024, 256], F32, kind="ExternalInput").ap(),
        "b1c": nc.dram_tensor("b1c", [128, 64], F32, kind="ExternalInput").ap(),
        "w2": nc.dram_tensor("w2", [32, 256, 1024], F32, kind="ExternalInput").ap(),
        "b2": nc.dram_tensor("b2", [32, 1024], F32, kind="ExternalInput").ap(),
    }


def moer_host_weights(inputs, prefix="L0_ffn"):
    b2 = np.asarray(inputs[prefix + "_bias2"])
    return {
        "rw_t": np.ascontiguousarray(np.asarray(inputs[prefix + "_router_w"]).T),
        "rb": np.ascontiguousarray(np.asarray(inputs[prefix + "_router_b"])[None, :]),
        "w1": np.ascontiguousarray(np.asarray(inputs[prefix + "_weights1"])),
        "b1c": np.ascontiguousarray(np.asarray(inputs[prefix + "_bias1"]).reshape(32, 2, 128).transpose(2, 0, 1).reshape(128, 64)),
        "w2": np.ascontiguousarray(np.asarray(inputs[prefix + "_weights2"])),
        "b2": np.ascontiguousarray(b2),
    }


_PROGRAMS = {}


def _get_program(name):
    if name not in _PROGRAMS:
        _PROGRAMS[name] = {"moe": build_moe_program, "l1ffn": build_l1ffn_program, "l1attn": build_l1attn_program, "l0attn": build_l0attn_program, "full": build_full_program, "moer": build_moer_program}[name]()
    return _PROGRAMS[name]


def kernel(**inputs):
    x = np.asarray(inputs["L0_attn_input_seq_embs"])
    xt = [np.ascontiguousarray(x[i].T) for i in range(8)]
    wmap = {}
    wmap.update(moer_host_weights(inputs))
    wmap.update(l0attn_host_weights(inputs))
    wmap.update(l1ffn_host_weights(inputs))
    wmap.update(l1attn_host_weights(inputs))
    nc = _get_program("full")
    in_maps = [dict(wmap, xin=xt[i]) for i in range(8)]
    res = run_bass_kernel_spmd(nc, in_maps, core_ids=list(range(8)))
    return np.stack([np.ascontiguousarray(r["xout"].T) for r in res.results])
